# version 05
# baseline (speedup 1.0000x reference)
import math
from contextlib import ExitStack

import numpy as np
import ml_dtypes

import concourse.bass as bass
import concourse.mybir as mybir
from concourse.bass_utils import run_bass_kernel_spmd

F32 = mybir.dt.float32
BF16 = mybir.dt.bfloat16
AF = mybir.ActivationFunctionType
ALU = mybir.AluOpType
AX = mybir.AxisListType
bfdt = ml_dtypes.bfloat16

B, T, D = 2, 8192, 2048
QH, KVH, HD = 16, 8, 128
WIN = 1024
E, EH = 32, 64
ATTN_SCALE = 0.08838834764831845
EPS_RMS = 1e-6
EPS_LN = 1e-5
ROPE_BASE = 10000.0

NCORE = 8
TO = 2048
HALO = 1024
TG = 512
NT = 16
NG = 4
NGALL = 6
NCH = 16
RING = 12
NEG = -1.0e30
QPAIR_A = [0, 1, 4, 5, 8, 9, 12, 13]


def q_block_of(n):
    j = (n // 4) * 2 + (n % 2)
    half = 0 if (n % 4) < 2 else 1
    return j, half



def host_tables():
    c = {}
    c["ident"] = np.eye(128, dtype=np.float32)
    c["ones"] = np.ones((128, 128), dtype=np.float32)
    r = np.arange(128)[:, None]
    col = np.arange(128)[None, :]
    c["tri"] = np.concatenate(
        [np.where(col > r, 0.0, NEG), np.where(col <= r, 0.0, NEG)],
        axis=1).astype(np.float32)
    eq = np.zeros((8, 128, 16), dtype=np.float32)
    for j in range(8):
        a = QPAIR_A[j]
        eq[j, 0:64, a] = 1.0
        eq[j, 64:128, a + 2] = 1.0
    c["eyeq"] = eq.transpose(1, 0, 2).reshape(128, 8 * 16)
    ek = np.zeros((4, 128, 8), dtype=np.float32)
    for p in range(4):
        ek[p, 0:64, 2 * p] = 1.0
        ek[p, 64:128, 2 * p + 1] = 1.0
    c["eyek"] = ek.transpose(1, 0, 2).reshape(128, 4 * 8)
    pk = np.zeros((4, 8, 128), dtype=np.float32)
    for p in range(4):
        pk[p, 2 * p, 0:64] = 1.0
        pk[p, 2 * p + 1, 64:128] = 1.0
    c["pselk"] = pk.transpose(1, 0, 2).reshape(8, 4 * 128)
    ii = np.arange(32)
    c["idxlt"] = (ii[None, :] < ii[:, None]).astype(np.float32)
    return c


def host_weights(qk, kvk, outk, wv, wo, sim):
    s = {}
    qb = np.empty((8, 2, NCH, 128, 128), dtype=np.float32)
    for j in range(8):
        a = QPAIR_A[j]
        for v in range(2):
            blk = np.concatenate([qk[a][:, v * 64:(v + 1) * 64],
                                  qk[a + 2][:, v * 64:(v + 1) * 64]], axis=1)
            qb[j, v] = blk.reshape(NCH, 128, 128)
    s["Wq"] = np.ascontiguousarray(qb.reshape(16, NCH, 128, 128))
    kb = np.empty((16, NCH, 128, 128), dtype=np.float32)
    kk, vv = kvk[0], kvk[1]
    for p in range(4):
        for v in range(2):
            blk = np.concatenate([kk[2 * p][:, v * 64:(v + 1) * 64],
                                  kk[2 * p + 1][:, v * 64:(v + 1) * 64]], axis=1)
            kb[p * 2 + v] = blk.reshape(NCH, 128, 128)
    for h in range(8):
        kb[8 + h] = vv[h].reshape(NCH, 128, 128)
    s["Wkv"] = np.ascontiguousarray(kb)
    ob = np.empty((NCH, QH, 128, 128), dtype=np.float32)
    for n in range(QH):
        for c in range(NCH):
            ob[c, n] = outk[n][:, c * 128:(c + 1) * 128]
    s["Wout"] = np.ascontiguousarray(ob)
    s["WvT"] = np.ascontiguousarray(wv.transpose(0, 2, 1)).reshape(E, D * EH)
    s["WoF"] = np.ascontiguousarray(wo).reshape(E, EH * D)
    s["simM"] = np.ascontiguousarray(sim)
    return s


def host_core_inputs(core, x, qs, ks, gates, shared):
    b, cs = core // 4, core % 4
    t0 = cs * TO
    d = {}
    xs = np.zeros((HALO + TO, D), dtype=np.float32)
    lo = max(0, t0 - HALO)
    xs[HALO - (t0 - lo):HALO] = x[b, lo:t0]
    xs[HALO:] = x[b, t0:t0 + TO]
    d["xT"] = np.ascontiguousarray(xs.T)

    pos = np.arange(t0 - HALO, t0 + TO, dtype=np.float64)
    pos = np.maximum(pos, 0.0)
    frac = np.arange(0, HD, 2, dtype=np.float64) / HD
    ts = ROPE_BASE ** frac
    ang = pos[None, :] / ts[:, None]
    d["costab"] = np.cos(ang).astype(np.float32)
    d["sintab"] = np.sin(ang).astype(np.float32)

    dead = np.zeros((NT, 1024), dtype=np.float32)
    for i in range(NT):
        st = cs * NT + i
        keypos = st * 128 - 1024 + np.arange(1024)
        dead[i] = np.where(keypos < 0, NEG, 0.0)
    d["deadrows"] = dead

    d["qsstack"] = np.stack([np.concatenate([qs[0:64], qs[0:64]]),
                             np.concatenate([qs[64:128], qs[64:128]])],
                            1).astype(np.float32)
    d["ksstack"] = np.stack([np.concatenate([ks[0:64], ks[0:64]]),
                             np.concatenate([ks[64:128], ks[64:128]])],
                            1).astype(np.float32)
    d["gates"] = gates.reshape(1, E).astype(np.float32)
    bs = np.zeros((128, 2), np.float32)
    bs[:, b] = 1.0
    d["bsel"] = bs
    d.update(shared)
    return d



class Op:
    __slots__ = ("eng", "fn", "deps", "is_dma", "need_inc", "val", "tag",
                 "stream", "semkey", "val_wait", "batch_start", "gidx",
                 "psum")

    def __init__(self, eng, fn, deps, is_dma, tag, stream):
        self.eng, self.fn, self.deps = eng, fn, deps
        self.is_dma = is_dma
        self.need_inc = False
        self.val = None
        self.val_wait = None
        self.tag = tag
        self.stream = stream
        self.semkey = None
        self.batch_start = False
        self.gidx = -1
        self.psum = None


ENG_SEM = {"tensor": "sT", "vector": "sV", "scalar": "sS", "gpsimd": "sG",
           "dsync": "qS", "dscalar": "qA", "dgp": "qG", "coll": "qC"}
ENG_EXEC = {"tensor": "tensor", "vector": "vector", "scalar": "scalar",
            "gpsimd": "gpsimd", "dsync": "sync", "dscalar": "scalar",
            "dgp": "gpsimd", "coll": "gpsimd"}


class MK:
    def __init__(self, debug=(), stop=None, serial=False, capture=False,
                 bank_edges=None):
        self.debug = set(debug)
        self.stop = stop
        self.serial = serial
        self.capture = capture
        self.bank_edges = bank_edges
        self.nc = bass.Bass("TRN2", target_bir_lowering=False)
        self.ops = []
        self.es = ExitStack()
        self.sems = {}
        self.dram = {}
        self.sb = {}
        self.ps = {}
        self.final_deps = []
        self.war = {}
        self.colT_ready = {"colkv": [None] * 16, "colq": [None] * 16}
        self.kslot_ready = {}
        self.vslot_ready = {}
        self.q_ready = {}
        self.acol_ready = [None] * 4
        self.enc_ready = {}

    def op(self, eng, fn, deps=(), tag=None, stream="s"):
        o = Op(eng, fn, [d for d in deps if d is not None],
               eng in ("dsync", "dscalar", "dgp", "coll"), tag, stream)
        o.gidx = len(self.ops)
        self.ops.append(o)
        return o

    def dbg(self, name, shape, dtype, src_fn, deps):
        if name not in self.debug:
            return None
        ap = self.nc.dram_tensor("dbg_" + name, list(shape), dtype,
                                 kind="ExternalOutput").ap()
        o = self.op("dgp", lambda ap=ap, f=src_fn:
                    self.nc.gpsimd.dma_start(out=ap, in_=f()),
                    deps, tag="dbg_" + name, stream="dbg")
        self.final_deps.append(o)
        return o

    def warlist(self, key):
        return self.war.get(key, [])

    def setwar(self, key, ops):
        self.war[key] = list(ops)

    def addwar(self, key, *ops):
        self.war.setdefault(key, []).extend(ops)

    def takewar(self, key):
        v = self.war.get(key, [])
        self.war[key] = []
        return v

    def _alloc(self):
        nc, es = self.nc, self.es

        def sbuf(name, shape, dt):
            self.sb[name] = es.enter_context(
                nc.sbuf_tensor("sb_" + name, list(shape), dt))

        def psum(name, shape, dt):
            self.ps[name] = es.enter_context(
                nc.psum_tensor(name, list(shape), dt))

        def dram_in(name, shape, dt=F32):
            self.dram[name] = nc.dram_tensor(
                name, list(shape), dt, kind="ExternalInput").ap()

        dram_in("xT", [D, HALO + TO])
        dram_in("costab", [64, HALO + TO])
        dram_in("sintab", [64, HALO + TO])
        dram_in("deadrows", [NT, 1024])
        dram_in("qsstack", [128, 2])
        dram_in("ksstack", [128, 2])
        dram_in("ident", [128, 128])
        dram_in("ones", [128, 128])
        dram_in("tri", [128, 256])
        dram_in("eyeq", [128, 128])
        dram_in("eyek", [128, 32])
        dram_in("pselk", [8, 512])
        dram_in("idxlt", [32, 32])
        dram_in("bsel", [128, 2])
        dram_in("Wq", [16, NCH, 128, 128])
        dram_in("Wkv", [16, NCH, 128, 128])
        dram_in("Wout", [NCH, QH, 128, 128])
        dram_in("WvT", [E, 16384])
        dram_in("WoF", [E, 16384])
        dram_in("simM", [D, E])
        dram_in("gates", [1, E])

        self.dram["yT"] = nc.dram_tensor("yT", [D, TO], F32,
                                         kind="ExternalOutput").ap()
        self.dram["xmid"] = nc.dram_tensor("xmid", [NCH, 128, TO], BF16,
                                           kind="Internal").ap()
        self.dram["foldin"] = nc.dram_tensor("foldin", [1, 65536], BF16,
                                             kind="Internal").ap()
        self.dram["foldout"] = nc.dram_tensor("foldout", [NCORE, 65536], BF16,
                                              kind="Internal").ap()
        self.dram["seqin"] = nc.dram_tensor("seqin", [128, 18], F32,
                                            kind="Internal").ap()
        self.dram["seqout"] = nc.dram_tensor("seqout", [128 * NCORE, 18], F32,
                                             kind="Internal").ap()

        sbuf("xfm", [128, NCH * TG], BF16)
        sbuf("q1", [128, 8 * TG], BF16)
        sbuf("q2", [128, 8 * TG], BF16)
        sbuf("enc", [128, QH * TG], BF16)
        sbuf("k1", [128, 4 * RING * 128], BF16)
        sbuf("k2", [128, 4 * RING * 128], BF16)
        sbuf("vring", [128, RING * KVH * 128], BF16)
        sbuf("x32", [128, NCH * TG], F32)
        sbuf("P", [128, 2 * 1152], BF16)
        sbuf("PT", [128, 2 * 1152], BF16)
        sbuf("wbuf", [128, 2 * 2048], BF16)
        sbuf("out32", [128, 2 * TG], F32)
        sbuf("sq", [128, 2 * TG], BF16)
        sbuf("fz", [128, 2 * TG], BF16)
        sbuf("rope", [128, 3 * TG], F32)
        sbuf("tbl", [128, 2 * TG], F32)
        sbuf("bc", [128, 2 * TG], F32)
        sbuf("vh", [128, TG], BF16)
        sbuf("acol", [128, 4 * 16], F32)
        sbuf("small", [128, 8], F32)
        sbuf("dead", [1, 1024], BF16)
        sbuf("colsc", [128, 1], F32)
        sbuf("colscb", [128, 1], BF16)
        sbuf("ident", [128, 128], BF16)
        sbuf("fident", [128, 128], F32)
        sbuf("pselk", [8, 512], BF16)
        sbuf("ones", [128, 128], BF16)
        sbuf("fones", [128, 1], F32)
        sbuf("onesr", [1, 32], F32)
        sbuf("tri", [128, 256], F32)
        sbuf("eyeq", [128, 128], BF16)
        sbuf("eyek", [128, 32], BF16)
        sbuf("idxlt", [32, 32], F32)
        sbuf("bsel", [128, 2], F32)
        sbuf("qsstack", [128, 2], F32)
        sbuf("ksstack", [128, 2], F32)
        for nm in ("m1", "rstd1", "rowa", "rowb"):
            sbuf(nm, [1, TG], F32)
        sbuf("rowar", [1, 3 * TG], F32)
        sbuf("rowbf", [1, 2 * TG], BF16)
        sb = self.sb
        sb["m2"] = sb["rowar"][0:1, 0:TG]
        sb["rstd2"] = sb["rowar"][0:1, TG:2 * TG]
        sb["rstd3"] = sb["rowar"][0:1, 2 * TG:3 * TG]
        sb["m4"] = sb["rowar"][0:1, 0:TG]
        sb["rstd4"] = sb["rowar"][0:1, TG:2 * TG]
        sb["rowc"] = sb["rowb"]
        sb["m2b"] = sb["rowbf"][0:1, 0:TG]
        sb["rstd2b"] = sb["rowbf"][0:1, TG:2 * TG]
        sb["rstd3b"] = sb["rowbf"][0:1, 0:TG]
        sb["rstd4b"] = sb["rowbf"][0:1, 0:TG]
        sb["negm4b"] = sb["rowbf"][0:1, TG:2 * TG]
        sb["rstd1b"] = sb["rowbf"][0:1, 0:TG]
        sb["m1b"] = sb["rowbf"][0:1, TG:2 * TG]
        sbuf("m3", [1, NG * TG], BF16)
        sbuf("aq", [16, TG], F32)
        sb["ak"] = sb["aq"][0:8, :]
        sbuf("akb", [8, TG], BF16)
        sbuf("seqp", [128, 16], F32)
        sbuf("seqa", [128, 16], F32)
        sbuf("seqsb", [128, 18], F32)
        sbuf("seqg", [128, NCORE * 18], F32)
        sbuf("sumw", [1, 8], F32)
        sbuf("srep", [128, 2 * 18], F32)
        sbuf("snorm", [128, 16], F32)
        sb["simsb"] = sb["x32"][:, 0:512]
        sbuf("simsq", [128, 32], F32)
        sbuf("gat", [32, 12], F32)
        sbuf("gatr", [1, 96], F32)
        sbuf("cst", [128, 2], F32)
        sbuf("gatm", [32, 64], F32)
        sbuf("grow", [1, 64], F32)
        sbuf("gsc", [1, 8], F32)
        sbuf("rmb", [32, 4], BF16)
        sbuf("wveff", [128, NCH * EH], BF16)
        sbuf("colv", [1, EH], BF16)
        sbuf("colwo", [64, 1], F32)
        sbuf("colwob", [64, 1], BF16)
        sb["x32ld"] = sb["out32"][:]
        sb["comb"] = sb["enc"][0:64, 0:TO]
        sb["woeff"] = sb["q1"][0:64, 0:D]
        sb["xmidld"] = sb["q1"][:, 2048:2048 + 2 * TG]
        sb["wfold"] = sb["q2"][0:32, 0:2048]
        sb["foldrow"] = sb["q2"][0:1, 2048:4096]
        sb["wob1"] = sb["enc"][0:64, 2048:2048 + D]

        psum("big0", [128, 1536], F32)
        psum("big1", [128, 1536], F32)
        psum("pt", [128, 1024], BF16)
        psum("sm", [128, 512], F32)

        for s in sorted(set(ENG_SEM.values())):
            self.sems[s] = es.enter_context(nc.semaphore(s))

    def emit_consts(self):
        nc, sb, dr = self.nc, self.sb, self.dram
        self.cload = []
        for nm in ("ident", "ones", "eyeq", "eyek", "pselk"):
            o = self.op("dgp", lambda nm=nm: nc.gpsimd.dma_start(
                out=self.sb[nm][:], in_=self.dram[nm]), (), tag="ld_" + nm, stream="c")
            self.cload.append(o)
        for nm in ("idxlt", "qsstack", "ksstack", "bsel", "tri"):
            o = self.op("dsync", lambda nm=nm: nc.sync.dma_start(
                out=self.sb[nm][:], in_=self.dram[nm]), (), tag="ld_" + nm, stream="c2")
            self.cload.append(o)
        o = self.op("dsync", lambda: nc.sync.dma_start(
            out=self.sb["fident"][:], in_=self.dram["ident"]), (), tag="ld_fid", stream="c2")
        self.cload.append(o)
        m0 = self.op("vector", lambda: nc.vector.memset(sb["seqa"][:], 0.0))
        c0 = self.op("vector", lambda: nc.vector.memset(
            sb["cst"][:, 0:1], EPS_LN))
        c1 = self.op("vector", lambda: nc.vector.memset(
            sb["cst"][:, 1:2], EPS_RMS))
        self.cload += [c0, c1]
        m1 = self.op("vector", lambda: nc.vector.memset(sb["fones"][:], 1.0))
        m2 = self.op("vector", lambda: nc.vector.memset(sb["onesr"][:], 1.0))
        self.memsets = [m0, m1, m2]
        self.seqa_last = m0

    def emit_group_proj(self, gg):
        nc, sb, ps, dr = self.nc, self.sb, self.ps, self.dram
        own = gg >= 2
        g = gg - 2
        col0 = gg * TG
        gt0 = gg * 4 - 8

        src = dr["xT"][:, col0:col0 + TG].rearrange("(c p) t -> p c t", p=128)
        ld_x = self.op("dgp", lambda src=src: nc.gpsimd.dma_start(
            out=sb["xfm"][:].rearrange("p (c t) -> p c t", c=NCH), in_=src),
            self.takewar("xfm") + self.cload, tag=f"ldx{gg}", stream="x")
        self.ld_x = ld_x

        tbl_deps = self.takewar("tbl")
        ld_t = []
        for half in (0, 1):
            for j, nm in ((0, "costab"), (1, "sintab")):
                o = self.op("dsync", lambda half=half, j=j, nm=nm:
                            nc.sync.dma_start(
                                out=sb["tbl"][64 * half:64 * half + 64,
                                              j * TG:(j + 1) * TG],
                                in_=dr[nm][:, col0:col0 + TG]),
                            tbl_deps, tag=f"ldt{gg}", stream="tbl")
                ld_t.append(o)
        self.ld_t = ld_t

        mean_mms, ssq_mms = [], []
        for c in range(NCH):
            mean_mms.append(self.op("tensor", lambda c=c: nc.tensor.matmul(
                ps["big1"][0:1, 512:512 + TG],
                sb["ones"][:, 0:1], sb["xfm"][:, c * TG:(c + 1) * TG],
                start=(c == 0), stop=(c == NCH - 1)),
                [ld_x] + (self.takewar("b1_4") if c == 0 else []),
                tag=f"meanmm{gg}_{c}"))
        sq_ops = []
        for c in range(NCH):
            slot = c % 2
            sqo = self.op("scalar", lambda c=c, slot=slot:
                          nc.scalar.activation(
                              sb["sq"][:, slot * TG:(slot + 1) * TG],
                              sb["xfm"][:, c * TG:(c + 1) * TG], AF.Square),
                          [ld_x] + self.takewar(f"sq{slot}"),
                          tag=f"sq{gg}_{c}")
            sq_ops.append(sqo)
            mo = self.op("tensor", lambda c=c, slot=slot:
                         nc.tensor.matmul(
                ps["big1"][0:1, 1024:1024 + TG],
                sb["ones"][:, 0:1], sb["sq"][:, slot * TG:(slot + 1) * TG],
                start=(c == 0), stop=(c == NCH - 1)),
                [sqo] + (self.takewar("b1_5") if c == 0 else []),
                tag=f"ssqmm{gg}_{c}")
            ssq_mms.append(mo)
            self.setwar(f"sq{slot}", [mo])

        m1 = self.op("scalar", lambda: nc.scalar.activation(
            sb["m1"][:], ps["big1"][0:1, 512:512 + TG], AF.Copy,
            scale=1.0 / D), [mean_mms[-1]] + self.takewar("rowsc"),
            tag=f"m1_{gg}")
        self.addwar("b1_4", m1)
        m1sq = self.op("scalar", lambda: nc.scalar.activation(
            sb["rowa"][:], sb["m1"][:], AF.Square), [m1])
        msq = self.op("scalar", lambda: nc.scalar.activation(
            sb["rowb"][:], ps["big1"][0:1, 1024:1024 + TG], AF.Copy,
            scale=1.0 / D), [ssq_mms[-1]], tag=f"msq{gg}")
        self.addwar("b1_5", msq)
        var = self.op("vector", lambda: nc.vector.tensor_sub(
            sb["rowb"][:], sb["rowb"][:], sb["rowa"][:]), [msq, m1sq])
        sd = self.op("scalar", lambda: nc.scalar.activation(
            sb["rowa"][:], sb["rowb"][:], AF.Sqrt, bias=sb["cst"][0:1, 0:1]),
            [var], tag=f"sd{gg}")
        rstd1 = self.op("vector", lambda: nc.vector.reciprocal(
            sb["rstd1"][:], sb["rowa"][:]), [sd], tag=f"rstd1_{gg}")
        rstd1b = self.op("vector", lambda: nc.vector.tensor_copy(
            sb["rstd1b"][:], sb["rstd1"][:]),
            [rstd1] + self.takewar("rbf0"), tag=f"rstd1b_{gg}")
        m1b = self.op("vector", lambda: nc.vector.tensor_copy(
            sb["m1b"][:], sb["m1"][:]), [m1] + self.takewar("rbf1"),
            tag=f"m1b_{gg}")
        bcmm = self.op("tensor", lambda: nc.tensor.matmul(
            ps["sm"][:, 0:TG], sb["ones"][0:1, :], sb["rstd1b"][:],
            start=True, stop=True),
            [rstd1b] + self.takewar("smbank"), tag=f"bc1mm{gg}")
        self.addwar("rbf0", bcmm)
        bc1 = self.op("vector", lambda: nc.vector.tensor_copy(
            sb["bc"][:, 0:TG], ps["sm"][:, 0:TG]),
            [bcmm] + self.takewar("bc0"), tag=f"bc1_{gg}")
        bcmm2 = self.op("tensor", lambda: nc.tensor.matmul(
            ps["sm"][:, 0:TG], sb["ones"][0:1, :], sb["m1b"][:],
            start=True, stop=True), [m1b, bc1], tag=f"bcm1mm{gg}")
        self.addwar("rbf1", bcmm2)
        m1bc = self.op("vector", lambda: nc.vector.tensor_copy(
            sb["bc"][:, TG:2 * TG], ps["sm"][:, 0:TG]),
            [bcmm2] + self.takewar("bc1dst"), tag=f"m1bc_{gg}")
        self.addwar("smbank", m1bc)
        self.bc1_op = bc1
        self.setwar("rowsc", [m1, m1sq, msq, var, sd, rstd1, rstd1b, m1b,
                              bcmm, bcmm2])

        xz_ops = []
        for c in range(NCH):
            o = self.op("vector", lambda c=c: nc.vector.tensor_sub(
                sb["xfm"][:, c * TG:(c + 1) * TG],
                sb["xfm"][:, c * TG:(c + 1) * TG], sb["bc"][:, TG:2 * TG]),
                [mean_mms[c], sq_ops[c], m1bc], tag=f"xz{gg}_{c}")
            xz_ops.append(o)
            self.addwar("bc1dst", o)
        self.xz_ops = xz_ops

        self._proj_pass(gg, kv=True)
        if own:
            self._proj_pass(gg, kv=False)
            self._emit_aq_rows_finalize(gg)

    def _stream_block(self, wname, gg, b):
        nc, sb, dr = self.nc, self.sb, self.dram
        slot = b % 2
        deps = self.takewar(f"wslot{slot}")
        ops = []
        for h8 in range(2):
            src = dr[wname][b, h8 * 8:(h8 + 1) * 8].rearrange(
                "c p q -> p c q")
            o = self.op("dgp", lambda src=src, slot=slot, h8=h8:
                        nc.gpsimd.dma_start(
                            out=sb["wbuf"][:, slot * 2048 + h8 * 1024:
                                           slot * 2048 + (h8 + 1) * 1024
                                           ].rearrange("p (c q) -> p c q",
                                                       c=8),
                            in_=src), deps, tag=f"w{wname}{gg}_{b}_{h8}", stream=f"w{slot}")
            ops.append(o)
        return ops

    def _block_mms(self, wname, gg, b, pslot, wdma):
        nc, sb, ps = self.nc, self.sb, self.ps
        slot = b % 2
        mm = None
        for c in range(NCH):
            deps = [wdma[c // 8], self.xz_ops[c]]
            if c == 0:
                deps += self.takewar(f"big0_{pslot}")
            mm = self.op("tensor", lambda c=c, slot=slot, pslot=pslot:
                         nc.tensor.matmul(
                             ps["big0"][:, pslot * TG:(pslot + 1) * TG],
                             sb["wbuf"][:, slot * 2048 + c * 128:
                                        slot * 2048 + (c + 1) * 128],
                             sb["xfm"][:, c * TG:(c + 1) * TG],
                             start=(c == 0), stop=(c == NCH - 1)),
                         deps, tag=f"mm{wname}{gg}_{b}_{c}")
        self.setwar(f"wslot{slot}", [mm])
        self.addwar("xfm", mm)
        return mm

    def _proj_pass(self, gg, kv):
        nc, sb, ps = self.nc, self.sb, self.ps
        wname = "Wkv" if kv else "Wq"
        gt0 = gg * 4 - 8
        slot0 = gt0 % RING

        if kv:
            units = [("k", p) for p in range(4)] + [("v", h) for h in range(8)]
        else:
            units = [("q", j) for j in range(8)]

        for kind, idx in units:
            if kind in ("k", "q"):
                b1 = 2 * idx
                wd1 = self._stream_block(wname, gg, b1)
                wd2 = self._stream_block(wname, gg, b1 + 1)
                mm1 = self._block_mms(wname, gg, b1, 0, wd1)
                mm2 = self._block_mms(wname, gg, b1 + 1, 1, wd2)
                self._pair_post(gg, kind, idx, mm1, mm2, slot0)
            else:
                b = 8 + idx
                wd = self._stream_block(wname, gg, b)
                mm = self._block_mms(wname, gg, b, idx % 2, wd)
                self._v_post(gg, idx, mm, slot0)

    def _pair_post(self, gg, kind, idx, mm1, mm2, slot0):
        nc, sb, ps = self.nc, self.sb, self.ps
        isq = kind == "q"
        stack = sb["qsstack"] if isq else sb["ksstack"]
        nunits = 8 if isq else 4
        rows_n = 16 if isq else 8
        eye = sb["eyeq"] if isq else sb["eyek"]
        eyew = 16 if isq else 8
        rowkey = "aqrows" if isq else "akrows"

        sq_ops = []
        for half, mm in ((0, mm1), (1, mm2)):
            so = self.op("scalar", lambda half=half: nc.scalar.activation(
                sb["sq"][:, half * TG:(half + 1) * TG],
                ps["big0"][:, half * TG:(half + 1) * TG], AF.Square),
                [mm] + self.takewar(f"sq{half}"),
                tag=f"rsq{gg}{kind}{idx}_{half}")
            sq_ops.append(so)
        eye_last = self.war.get(rowkey + "_last", None)
        for half in (0, 1):
            deps = [sq_ops[half]]
            if eye_last is not None:
                deps.append(eye_last)
            if isq:
                start = (idx == 0 and half == 0)
                stop = (idx == nunits - 1 and half == 1)
            else:
                start = (half == 0)
                stop = (half == 1)
            if start:
                deps += self.takewar("b1_3")
            eye_last = self.op("tensor", lambda half=half, idx=idx,
                               start=start, stop=stop, rows_n=rows_n,
                               eye=eye, eyew=eyew:
                               nc.tensor.matmul(
                ps["big1"][0:rows_n, 0:TG],
                eye[:, idx * eyew:(idx + 1) * eyew],
                sb["sq"][:, half * TG:(half + 1) * TG],
                start=start, stop=stop),
                deps, tag=f"eye{gg}{kind}{idx}_{half}")
            self.setwar(f"sq{half}", [eye_last])
        self.war[rowkey + "_last"] = eye_last
        if idx == nunits - 1:
            self.war[rowkey + "_done"] = eye_last

        rope_war = self.takewar("rope")
        t1 = self.op("vector", lambda: nc.vector.scalar_tensor_tensor(
            sb["rope"][:, 0:TG], ps["big0"][:, 0:TG], stack[:, 0:1],
            sb["tbl"][:, 0:TG], ALU.mult, ALU.mult),
            [mm1] + self.ld_t + self.cload + rope_war,
            tag=f"t1_{gg}{kind}{idx}")
        t2 = self.op("vector", lambda: nc.vector.scalar_tensor_tensor(
            sb["rope"][:, TG:2 * TG], ps["big0"][:, TG:2 * TG],
            stack[:, 1:2], sb["tbl"][:, TG:2 * TG], ALU.mult, ALU.mult),
            [mm2] + self.ld_t + rope_war, tag=f"t2_{gg}{kind}{idx}")
        r1 = self.op("vector", lambda: nc.vector.tensor_sub(
            sb["rope"][:, 0:TG], sb["rope"][:, 0:TG], sb["rope"][:, TG:2 * TG]),
            [t1, t2], tag=f"r1_{gg}{kind}{idx}")
        t3 = self.op("vector", lambda: nc.vector.scalar_tensor_tensor(
            sb["rope"][:, TG:2 * TG], ps["big0"][:, TG:2 * TG],
            stack[:, 1:2], sb["tbl"][:, 0:TG], ALU.mult, ALU.mult),
            [mm2, r1], tag=f"t3_{gg}{kind}{idx}")
        t4 = self.op("vector", lambda: nc.vector.scalar_tensor_tensor(
            sb["rope"][:, 2 * TG:3 * TG], ps["big0"][:, 0:TG],
            stack[:, 0:1], sb["tbl"][:, TG:2 * TG], ALU.mult, ALU.mult),
            [mm1], tag=f"t4_{gg}{kind}{idx}")
        r2 = self.op("vector", lambda: nc.vector.tensor_add(
            sb["rope"][:, TG:2 * TG], sb["rope"][:, TG:2 * TG],
            sb["rope"][:, 2 * TG:3 * TG]),
            [t3, t4], tag=f"r2_{gg}{kind}{idx}")
        self.addwar("tbl", t1, t2, t3, t4)
        self.setwar("big0_0", [t1, t4, sq_ops[0]])
        self.setwar("big0_1", [t2, t3, sq_ops[1]])

        if isq:
            d1 = self.takewar(f"q1_{idx}")
            o1 = self.op("vector", lambda idx=idx: nc.vector.tensor_copy(
                sb["q1"][:, idx * TG:(idx + 1) * TG], sb["rope"][:, 0:TG]),
                [r1] + d1, tag=f"qw1_{gg}_{idx}")
            d2 = self.takewar(f"q2_{idx}")
            o2 = self.op("vector", lambda idx=idx: nc.vector.tensor_copy(
                sb["q2"][:, idx * TG:(idx + 1) * TG],
                sb["rope"][:, TG:2 * TG]), [r2] + d2,
                tag=f"qw2_{gg}_{idx}")
            self.q_ready[idx] = [o1, o2]
            self.setwar("rope", [o1, o2])
        else:
            self._ak_pair_finalize(gg, idx)
            pk = idx
            base = pk * RING * 128 + slot0 * 128
            dst_deps = []
            for s in range(4):
                dst_deps += self.takewar(f"kw_{slot0 + s}_{pk}")
            ops_out = []
            for (src_slot, tens, prt) in ((0, "k1", 0), (0, "k1", 64),
                                          (1, "k2", 0), (1, "k2", 64)):
                o = self.op("vector", lambda src_slot=src_slot, tens=tens,
                            prt=prt, base=base:
                            nc.vector.tensor_mul(
                    sb[tens][prt:prt + 64, base:base + TG],
                    sb["rope"][prt:prt + 64,
                               src_slot * TG:(src_slot + 1) * TG],
                    sb["bc"][prt:prt + 64, TG:2 * TG]),
                    [r1, r2, self.akbc_op] + dst_deps,
                    tag=f"kw{gg}_{idx}_{tens}_{prt}")
                ops_out.append(o)
                self.addwar("bc1dst", o)
            self.setwar("rope", ops_out)
            for s in range(4):
                self.kslot_ready[(slot0 + s, pk)] = ops_out

    def _ak_pair_finalize(self, gg, p):
        nc, sb, ps = self.nc, self.sb, self.ps
        eye_last = self.war["akrows_last"]
        e1 = self.op("scalar", lambda: nc.scalar.activation(
            sb["ak"][:], sb["bc"][0:8, 0:TG], AF.Square),
            [self.bc1_op] + self.takewar("aq_war"), tag=f"ak1_{gg}{p}")
        self.addwar("bc0", e1)
        e2 = self.op("vector", lambda: nc.vector.scalar_tensor_tensor(
            sb["ak"][:], sb["ak"][:], 1.0 / HD,
            ps["big1"][0:8, 0:TG], ALU.mult, ALU.mult),
            [e1, eye_last], tag=f"ak2_{gg}{p}")
        self.addwar("b1_3", e2)
        e3 = self.op("scalar", lambda: nc.scalar.activation(
            sb["ak"][:], sb["ak"][:], AF.Sqrt, bias=sb["cst"][0:8, 1:2]),
            [e2], tag=f"ak3_{gg}{p}")
        e4 = self.op("vector", lambda: nc.vector.reciprocal(
            sb["ak"][:], sb["ak"][:]), [e3], tag=f"ak4_{gg}{p}")
        e5 = self.op("vector", lambda: nc.vector.tensor_mul(
            sb["ak"][:], sb["ak"][:], sb["bc"][0:8, 0:TG]),
            [e4, self.bc1_op], tag=f"ak5_{gg}{p}")
        self.addwar("bc0", e5)
        e6 = self.op("vector", lambda: nc.vector.tensor_copy(
            sb["akb"][:], sb["ak"][:]), [e5], tag=f"ak6_{gg}{p}")
        self.setwar("aq_war", [e6])
        bcd = self.takewar("smbank")
        bm = self.op("tensor", lambda p=p: nc.tensor.matmul(
            ps["sm"][:, 0:TG], sb["pselk"][:, p * 128:(p + 1) * 128],
            sb["akb"][:], start=True, stop=True),
            [e6] + bcd, tag=f"akbc{gg}{p}")
        ev = self.op("vector", lambda: nc.vector.tensor_copy(
            sb["bc"][:, TG:2 * TG], ps["sm"][:, 0:TG]),
            [bm] + self.takewar("bc1dst"), tag=f"akbcev{gg}{p}")
        self.addwar("smbank", ev)
        self.akbc_op = ev

    def _v_post(self, gg, h, mm, slot0):
        nc, sb, ps = self.nc, self.sb, self.ps
        pslot = h % 2
        ev = self.op("vector", lambda pslot=pslot:
                     nc.vector.scalar_tensor_tensor(
            sb["vh"][:], ps["big0"][:, pslot * TG:(pslot + 1) * TG], 0.0,
            sb["bc"][:, 0:TG], ALU.bypass, ALU.mult),
            [mm, self.bc1_op] + self.takewar("vh"), tag=f"vev{gg}_{h}")
        self.addwar("bc0", ev)
        self.setwar(f"big0_{pslot}", [ev])
        tps = []
        ptw = self.takewar("ptbank")
        for ti in range(4):
            o = self.op("tensor", lambda ti=ti: nc.tensor.transpose(
                ps["pt"][:, ti * 128:(ti + 1) * 128],
                sb["vh"][:, ti * 128:(ti + 1) * 128], sb["ident"][:]),
                [ev] + (ptw if ti == 0 else []), tag=f"vtp{gg}_{h}_{ti}")
            tps.append(o)
        dep_ring = []
        for s in range(4):
            dep_ring += self.takewar(f"vw_{slot0 + s}_{h}")
        evv = self.op("vector", lambda h=h, slot0=slot0:
                      nc.vector.tensor_copy(
            sb["vring"][:].rearrange("p (s h q) -> p s h q",
                                     s=RING, h=KVH)[:, slot0:slot0 + 4, h],
            ps["pt"][:, 0:512].rearrange("p (s q) -> p s q", s=4)),
            tps + dep_ring, tag=f"vrev{gg}_{h}")
        self.setwar("ptbank", [evv])
        self.addwar("vh", tps[-1])
        for s in range(4):
            self.vslot_ready[(slot0 + s, h)] = evv

    def _emit_aq_rows_finalize(self, gg):
        nc, sb, ps = self.nc, self.sb, self.ps
        eye_done = self.war["aqrows_done"]
        e1 = self.op("scalar", lambda: nc.scalar.activation(
            sb["aq"][:], sb["bc"][0:16, 0:TG], AF.Square),
            [self.bc1_op] + self.takewar("aq_war"), tag=f"aq1_{gg}")
        self.addwar("bc0", e1)
        e2 = self.op("vector", lambda: nc.vector.scalar_tensor_tensor(
            sb["aq"][:], sb["aq"][:], 1.0 / HD,
            ps["big1"][0:16, 0:TG], ALU.mult, ALU.mult),
            [e1, eye_done], tag=f"aq2_{gg}")
        self.addwar("b1_3", e2)
        e3 = self.op("scalar", lambda: nc.scalar.activation(
            sb["aq"][:], sb["aq"][:], AF.Sqrt,
            bias=sb["cst"][0:16, 1:2]), [e2])
        e4 = self.op("vector", lambda: nc.vector.reciprocal(
            sb["aq"][:], sb["aq"][:]), [e3])
        e5 = self.op("vector", lambda: nc.vector.scalar_tensor_tensor(
            sb["aq"][:], sb["bc"][0:16, 0:TG], ATTN_SCALE,
            sb["aq"][:], ALU.mult, ALU.mult), [e4, self.bc1_op],
            tag=f"aq5_{gg}")
        self.addwar("bc0", e5)
        self.setwar("aq_war", [])
        for ti in range(4):
            smw = self.takewar("smbank")
            tp = self.op("tensor", lambda ti=ti: nc.tensor.transpose(
                ps["sm"][0:128, 16 * (ti % 2):16 * (ti % 2) + 16],
                sb["aq"][:, ti * 128:(ti + 1) * 128], sb["fident"][0:16, 0:16]),
                [e5] + smw, tag=f"acoltp{gg}_{ti}")
            ev = self.op("vector", lambda ti=ti: nc.vector.tensor_copy(
                sb["acol"][:, ti * 16:(ti + 1) * 16],
                ps["sm"][0:128, 16 * (ti % 2):16 * (ti % 2) + 16]),
                [tp] + self.takewar(f"acol{ti}"), tag=f"acolev{gg}_{ti}")
            self.addwar("smbank", ev)
            self.addwar("aq_war", tp)
            self.acol_ready[ti] = ev

    def emit_group_attn(self, gg):
        nc, sb, ps = self.nc, self.sb, self.ps
        g = gg - 2
        gt0 = g * 4

        self.enc_ready = {}
        self.setwar("S0", self.takewar("big0_0") + self.takewar("big0_1") +
                    self.warlist("S0"))
        self.setwar("S1", self.takewar("b1_3") + self.takewar("b1_4") +
                    self.takewar("b1_5") + self.warlist("S1"))
        self.setwar("encbank", self.takewar("smbank") +
                    self.warlist("encbank"))

        for tix in range(4):
            ti = gt0 + tix
            self._tile_attn(g, ti, tix)

    def _srun_pieces(self, ti):
        slots = [(ti - 8 + jj) % RING for jj in range(9)]
        runs = []
        rs, rlen = slots[0], 1
        for s in slots[1:]:
            if s == rs + rlen:
                rlen += 1
            else:
                runs.append((rs, rlen))
                rs, rlen = s, 1
        runs.append((rs, rlen))
        pieces = []
        col = 0
        for rs, rlen in runs:
            off = 0
            while off < rlen * 128:
                take = min(512 - (col % 512) if (col % 512) else 512,
                           rlen * 128 - off)
                take = min(take, 512)
                pieces.append((col, take, rs, off))
                col += take
                off += take
        assert col == 1152
        return pieces

    def _tile_attn(self, g, ti, tix):
        nc, sb, ps = self.nc, self.sb, self.ps
        pieces = self._srun_pieces(ti)

        dslot = 0
        dld = self.op("dgp", lambda ti=ti, dslot=dslot: nc.gpsimd.dma_start(
            out=sb["dead"][:, dslot * 1024:(dslot + 1) * 1024],
            in_=self.dram["deadrows"][ti:ti + 1, :]),
            self.takewar(f"dead{dslot}"), tag=f"dld{ti}", stream="dead")

        stage = {}

        def s_stage(h):
            par = h % 2
            big = ps["big0"] if par == 0 else ps["big1"]
            kvh = h // 2
            pk = h // 4
            qj, qhalf = q_block_of(h)
            base = 64 * qhalf
            slot_war = self.takewar(f"S{par}")
            mms = []
            bank_started = set()
            last_in_bank = {}
            plan = []
            for (col, ncols, rs, off) in pieces:
                plan.append((col // 512, "s", (col, ncols, rs, off)))
            for bk in (0, 1):
                plan.append((bk, "d", (bk * 512,)))
            bank_count = {}
            for bk, kind, a in plan:
                bank_count[bk] = bank_count.get(bk, 0) + 1
            bank_seen = {}
            qd = self.q_ready.get(qj, [])
            for bk, kind, a in plan:
                bank_seen[bk] = bank_seen.get(bk, 0) + 1
                is_first = bank_seen[bk] == 1
                closes = bank_seen[bk] == bank_count[bk]
                if kind == "s":
                    col, ncols, rs, off = a
                    kdep = []
                    for s in range(off // 128, (off + ncols + 127) // 128):
                        kdep += self.kslot_ready.get((rs + s, pk), [])
                    for half, (qt, kt) in enumerate((("q1", "k1"), ("q2", "k2"))):
                        deps = list(qd) + kdep
                        if is_first and half == 0:
                            deps += slot_war
                        st = is_first and half == 0
                        sp = closes and half == 1
                        o = self.op("tensor", lambda par=par, col=col,
                                    ncols=ncols, rs=rs, off=off, base=base,
                                    qt=qt, kt=kt, qj=qj, tix=tix, pk=pk,
                                    st=st, sp=sp, big=big:
                                    nc.tensor.matmul(
                            big[:, col:col + ncols],
                            sb[qt][base:base + 64,
                                   qj * TG + tix * 128:qj * TG + (tix + 1) * 128],
                            sb[kt][base:base + 64,
                                   pk * RING * 128 + rs * 128 + off:
                                   pk * RING * 128 + rs * 128 + off + ncols],
                            start=st, stop=sp),
                            deps, tag=f"S{ti}_{h}_{col}_{half}")
                        mms.append(o)
                        for s in range(off // 128, (off + ncols + 127) // 128):
                            self.addwar(f"kw_{rs + s}_{pk}", o)
                        self.addwar(f"q1_{qj}" if half == 0 else f"q2_{qj}", o)
                else:
                    c0 = a[0]
                    sp = closes
                    o = self.op("tensor", lambda par=par, c0=c0, dslot=dslot,
                                sp=sp, big=big: nc.tensor.matmul(
                        big[:, c0:c0 + 512],
                        sb["ones"][0:1, 0:128],
                        sb["dead"][:, dslot * 1024 + c0:dslot * 1024 + c0 + 512],
                        start=False, stop=sp),
                        [dld], tag=f"Sdead{ti}_{h}_{c0}")
                    mms.append(o)
                    self.addwar(f"dead{dslot}", o)
            return mms

        def softmax_stage(h, mms):
            par = h % 2
            big = ps["big0"] if par == 0 else ps["big1"]
            mk0 = self.op("vector", lambda big=big: nc.vector.tensor_add(
                big[:, 0:128], big[:, 0:128], sb["tri"][:, 0:128]),
                [mms[-1]] + self.cload, tag=f"mask0_{ti}_{h}")
            mk8 = self.op("vector", lambda big=big: nc.vector.tensor_add(
                big[:, 1024:1152], big[:, 1024:1152], sb["tri"][:, 128:256]),
                [mms[-1]], tag=f"mask8_{ti}_{h}")
            mx = self.op("vector", lambda big=big, par=par:
                         nc.vector.reduce_max(
                sb["small"][:, par:par + 1], big[:, 0:1152],
                axis=AX.X, negate=True),
                [mk0, mk8] + self.takewar(f"small{par}"),
                tag=f"max{ti}_{h}")
            nm2 = self.op("vector", lambda par=par, h=h, tix=tix:
                          nc.vector.tensor_mul(
                sb["small"][:, par:par + 1], sb["small"][:, par:par + 1],
                sb["acol"][:, tix * 16 + h:tix * 16 + h + 1]),
                [mx, self.acol_ready[tix]], tag=f"nm2_{ti}_{h}")
            pslot = par
            pw = self.takewar(f"P{pslot}")
            ex = self.op("scalar", lambda big=big, par=par, pslot=pslot,
                         h=h, tix=tix: nc.scalar.activation(
                sb["P"][:, pslot * 1152:(pslot + 1) * 1152],
                big[:, 0:1152], AF.Exp,
                bias=sb["small"][:, par:par + 1],
                scale=sb["acol"][:, tix * 16 + h:tix * 16 + h + 1],
                accum_out=sb["small"][:, 2 + par:3 + par]),
                [nm2] + pw, tag=f"exp{ti}_{h}")
            self.addwar(f"acol{tix}", ex)
            rc = self.op("vector", lambda par=par: nc.vector.reciprocal(
                sb["small"][:, 4 + par:5 + par],
                sb["small"][:, 2 + par:3 + par]), [ex], tag=f"rcp{ti}_{h}")
            pn = self.op("vector", lambda pslot=pslot, par=par:
                         nc.vector.tensor_scalar_mul(
                sb["P"][:, pslot * 1152:(pslot + 1) * 1152],
                sb["P"][:, pslot * 1152:(pslot + 1) * 1152],
                sb["small"][:, 4 + par:5 + par]), [rc, ex],
                tag=f"pn{ti}_{h}")
            self.addwar(f"small{par}", ex, rc, pn)
            self.setwar(f"S{par}", [ex, mx])
            return pn

        def pv_stage(h, pn):
            par = h % 2
            pslot = par
            kvh = h // 2
            ptw = self.takewar("ptbank")
            tps = []
            for jj in range(8):
                o = self.op("tensor", lambda jj=jj, pslot=pslot:
                            nc.tensor.transpose(
                    ps["pt"][:, jj * 128:(jj + 1) * 128],
                    sb["P"][:, pslot * 1152 + jj * 128:
                            pslot * 1152 + (jj + 1) * 128],
                    sb["ident"][:]),
                    [pn] + (ptw if jj == 0 else []), tag=f"tp{ti}_{h}_{jj}")
                tps.append(o)
            ptdst = self.takewar(f"PT{pslot}")
            ev1 = self.op("vector", lambda pslot=pslot: nc.vector.tensor_copy(
                sb["PT"][:, pslot * 1152:pslot * 1152 + 1024],
                ps["pt"][:, 0:1024]), tps + ptdst, tag=f"ptev1_{ti}_{h}")
            tp8 = self.op("tensor", lambda pslot=pslot: nc.tensor.transpose(
                ps["pt"][:, 0:128],
                sb["P"][:, pslot * 1152 + 1024:pslot * 1152 + 1152],
                sb["ident"][:]), [pn, ev1], tag=f"tp8_{ti}_{h}")
            ev2 = self.op("vector", lambda pslot=pslot: nc.vector.tensor_copy(
                sb["PT"][:, pslot * 1152 + 1024:pslot * 1152 + 1152],
                ps["pt"][:, 0:128]), [tp8], tag=f"ptev2_{ti}_{h}")
            self.setwar("ptbank", [ev2])
            self.setwar(f"P{pslot}", [tps[-1], tp8])
            slots = [(ti - 8 + jj) % RING for jj in range(9)]
            encw = self.takewar("encbank")
            pv_last = None
            for jj in range(9):
                vdep = [self.vslot_ready.get((slots[jj], kvh))]
                deps = vdep + ([ev1] if jj < 8 else [ev2])
                if jj == 0:
                    deps += encw
                pv_last = self.op("tensor", lambda jj=jj, kvh=kvh,
                                  slots=slots, pslot=pslot:
                                  nc.tensor.matmul(
                    ps["sm"][:, 0:128],
                    sb["vring"][:, slots[jj] * KVH * 128 + kvh * 128:
                                slots[jj] * KVH * 128 + (kvh + 1) * 128],
                    sb["PT"][:, pslot * 1152 + jj * 128:
                             pslot * 1152 + (jj + 1) * 128],
                    start=(jj == 0), stop=(jj == 8)),
                    deps, tag=f"pv{ti}_{h}_{jj}")
                self.addwar(f"vw_{slots[jj]}_{kvh}", pv_last)
            self.addwar(f"PT{pslot}", pv_last)
            encdst = self.takewar(f"encdst_{h}_{tix}")
            eev = self.op("vector", lambda h=h, tix=tix: nc.vector.tensor_copy(
                sb["enc"][:, h * TG + tix * 128:h * TG + (tix + 1) * 128],
                ps["sm"][:, 0:128]), [pv_last] + encdst,
                tag=f"encev{ti}_{h}")
            self.setwar("encbank", [eev])
            self.enc_ready.setdefault(h, []).append(eev)
            return eev

        prev = {}
        for h in range(QH + 1):
            if h < QH:
                prev[h] = s_stage(h)
            if h >= 1:
                pn = softmax_stage(h - 1, prev.pop(h - 1))
                pv_stage(h - 1, pn)

    def emit_group_out(self, gg):
        nc, sb, ps = self.nc, self.sb, self.ps
        g = gg - 2
        col0 = gg * TG

        s0war = self.takewar("S0")
        self.setwar("big0_0", s0war + self.warlist("big0_0"))
        self.addwar("big0_1", *s0war)
        s1war = self.takewar("S1")
        self.addwar("b1_3", *s1war)
        self.addwar("b1_4", *s1war)
        self.addwar("b1_5", *s1war)
        self.setwar("smbank", self.takewar("encbank") +
                    self.warlist("smbank"))

        mean_last = None
        ssq_last = None
        evac_ops = []
        for c in range(NCH):
            slot = c % 2
            wd = []
            wdeps = self.takewar(f"wslot{slot}")
            for h8 in range(2):
                src = self.dram["Wout"][c, h8 * 8:(h8 + 1) * 8].rearrange(
                    "n p q -> p n q")
                o = self.op("dgp", lambda src=src, slot=slot, h8=h8:
                            nc.gpsimd.dma_start(
                    out=sb["wbuf"][:, slot * 2048 + h8 * 1024:
                                   slot * 2048 + (h8 + 1) * 1024
                                   ].rearrange("p (n q) -> p n q", n=8),
                    in_=src), wdeps, tag=f"wout{g}_{c}_{h8}", stream=f"w{slot}")
                wd.append(o)
            mm = None
            for n in range(QH):
                deps = [wd[n // 8]] + self.enc_ready.get(n, [])
                if n == 0:
                    deps += self.takewar(f"big0_{slot}")
                mm = self.op("tensor", lambda n=n, slot=slot: nc.tensor.matmul(
                    ps["big0"][:, slot * TG:(slot + 1) * TG],
                    sb["wbuf"][:, slot * 2048 + n * 128:
                               slot * 2048 + (n + 1) * 128],
                    sb["enc"][:, n * TG:(n + 1) * TG],
                    start=(n == 0), stop=(n == QH - 1)),
                    deps, tag=f"op{g}_{c}_{n}")
                for tix4 in range(4):
                    self.addwar(f"encdst_{n}_{tix4}", mm)
            self.setwar(f"wslot{slot}", [mm])
            ev = self.op("scalar", lambda c=c, slot=slot: nc.scalar.activation(
                sb["x32"][:, c * TG:(c + 1) * TG],
                ps["big0"][:, slot * TG:(slot + 1) * TG], AF.Copy),
                [mm] + self.takewar(f"x32_{c}"), tag=f"atev{g}_{c}")
            cb = self.op("vector", lambda c=c, slot=slot: nc.vector.tensor_copy(
                sb["sq"][:, slot * TG:(slot + 1) * TG],
                sb["x32"][:, c * TG:(c + 1) * TG]),
                [ev] + self.takewar(f"sq{slot}"), tag=f"atb{g}_{c}")
            mean_last = self.op("tensor", lambda slot=slot, c=c:
                                nc.tensor.matmul(
                ps["big1"][0:1, 512:512 + TG], sb["ones"][:, 0:1],
                sb["sq"][:, slot * TG:(slot + 1) * TG],
                start=(c == 0), stop=(c == NCH - 1)),
                [cb] + (self.takewar("b1_4") if c == 0 else []),
                tag=f"m2mm{g}_{c}")
            self.setwar(f"sq{slot}", [mean_last])
            sq2 = self.op("scalar", lambda c=c, slot=slot: nc.scalar.activation(
                sb["fz"][:, slot * TG:(slot + 1) * TG],
                ps["big0"][:, slot * TG:(slot + 1) * TG], AF.Square),
                [mm, ev] + self.takewar(f"fz{slot}"), tag=f"atsq{g}_{c}")
            self.setwar(f"big0_{slot}", [ev, sq2])
            ssq_last = self.op("tensor", lambda slot=slot, c=c:
                               nc.tensor.matmul(
                ps["big1"][0:1, 1024:1024 + TG], sb["ones"][:, 0:1],
                sb["fz"][:, slot * TG:(slot + 1) * TG],
                start=(c == 0), stop=(c == NCH - 1)),
                [sq2] + (self.takewar("b1_5") if c == 0 else []),
                tag=f"s2mm{g}_{c}")
            self.setwar(f"fz{slot}", [ssq_last])
            evac_ops.append(ev)

        m2 = self.op("scalar", lambda: nc.scalar.activation(
            sb["m2"][:], ps["big1"][0:1, 512:512 + TG], AF.Copy,
            scale=1.0 / D), [mean_last] + self.takewar("rowsc"),
            tag=f"m2_{g}")
        self.addwar("b1_4", m2)
        m2sq = self.op("scalar", lambda: nc.scalar.activation(
            sb["rowa"][:], sb["m2"][:], AF.Square), [m2])
        msq2 = self.op("scalar", lambda: nc.scalar.activation(
            sb["rowb"][:], ps["big1"][0:1, 1024:1024 + TG], AF.Copy,
            scale=1.0 / D), [ssq_last], tag=f"msq2_{g}")
        self.addwar("b1_5", msq2)
        var2 = self.op("vector", lambda: nc.vector.tensor_sub(
            sb["rowa"][:], sb["rowb"][:], sb["rowa"][:]), [msq2, m2sq])
        sd2 = self.op("scalar", lambda: nc.scalar.activation(
            sb["rowb"][:], sb["rowa"][:], AF.Sqrt,
            bias=sb["cst"][0:1, 0:1]), [var2])
        rstd2 = self.op("vector", lambda: nc.vector.reciprocal(
            sb["rstd2"][:], sb["rowb"][:]), [sd2], tag=f"rstd2_{g}")
        m2b = self.op("vector", lambda: nc.vector.tensor_copy(
            sb["m2b"][:], sb["m2"][:]), [m2] + self.takewar("rbf0"))
        rstd2b = self.op("vector", lambda: nc.vector.tensor_copy(
            sb["rstd2b"][:], sb["rstd2"][:]), [rstd2] + self.takewar("rbf1"))
        bcm = self.op("tensor", lambda: nc.tensor.matmul(
            ps["sm"][:, 0:TG], sb["ones"][0:1, :], sb["m2b"][:],
            start=True, stop=True), [m2b] + self.takewar("smbank"),
            tag=f"bcm2_{g}")
        bc_m2 = self.op("vector", lambda: nc.vector.tensor_copy(
            sb["bc"][:, 0:TG], ps["sm"][:, 0:TG]),
            [bcm] + self.takewar("bc0"), tag=f"bcm2e_{g}")
        bcr = self.op("tensor", lambda: nc.tensor.matmul(
            ps["sm"][:, 0:TG], sb["ones"][0:1, :], sb["rstd2b"][:],
            start=True, stop=True), [rstd2b, bc_m2], tag=f"bcr2_{g}")
        self.addwar("rbf1", bcr)
        bc_r2 = self.op("vector", lambda: nc.vector.tensor_copy(
            sb["bc"][:, TG:2 * TG], ps["sm"][:, 0:TG]),
            [bcr] + self.takewar("bc1dst"), tag=f"bcr2e_{g}")
        self.addwar("smbank", bc_r2)
        self.setwar("rowsc", [m2, m2sq, msq2, var2, sd2, rstd2, m2b,
                              rstd2b, bcm, bcr])

        mean3_last = None
        ssq3_last = None
        for c in range(NCH):
            slot = c % 2
            xld = self.op("dsync", lambda c=c, slot=slot, col0=col0:
                          nc.sync.dma_start(
                out=sb["x32ld"][:, slot * TG:(slot + 1) * TG],
                in_=self.dram["xT"][c * 128:(c + 1) * 128,
                                    col0:col0 + TG]),
                self.takewar(f"xo32_{slot}"), tag=f"xld{g}_{c}", stream=f"xl{slot}")
            u1 = self.op("vector", lambda c=c: nc.vector.tensor_sub(
                sb["x32"][:, c * TG:(c + 1) * TG],
                sb["x32"][:, c * TG:(c + 1) * TG], sb["bc"][:, 0:TG]),
                [evac_ops[c], bc_m2], tag=f"sw2a{g}_{c}")
            u2 = self.op("vector", lambda c=c: nc.vector.tensor_mul(
                sb["x32"][:, c * TG:(c + 1) * TG],
                sb["x32"][:, c * TG:(c + 1) * TG], sb["bc"][:, TG:2 * TG]),
                [u1, bc_r2], tag=f"sw2b{g}_{c}")
            self.addwar("bc0", u1)
            self.addwar("bc1dst", u2)
            u3 = self.op("vector", lambda c=c, slot=slot: nc.vector.tensor_add(
                sb["x32"][:, c * TG:(c + 1) * TG],
                sb["x32"][:, c * TG:(c + 1) * TG],
                sb["x32ld"][:, slot * TG:(slot + 1) * TG]),
                [u2, xld], tag=f"sw2c{g}_{c}")
            self.addwar(f"xo32_{slot}", u3)
            self.sweep2_u3 = getattr(self, "sweep2_u3", {})
            self.sweep2_u3[c] = u3
            xmb = self.op("vector", lambda c=c, slot=slot: nc.vector.tensor_copy(
                sb["sq"][:, slot * TG:(slot + 1) * TG],
                sb["x32"][:, c * TG:(c + 1) * TG]),
                [u3] + self.takewar(f"sq{slot}"), tag=f"xmb{g}_{c}")
            st = self.op("dsync", lambda c=c, slot=slot, g=g:
                         nc.sync.dma_start(
                out=self.dram["xmid"][c, :, g * TG:(g + 1) * TG],
                in_=sb["sq"][:, slot * TG:(slot + 1) * TG]),
                [xmb], tag=f"xmst{g}_{c}", stream=f"xms{slot}")
            self.xmid_stored.append(st)
            mean3_last = self.op("tensor", lambda slot=slot, c=c:
                                 nc.tensor.matmul(
                ps["big1"][0:1, 512:512 + TG], sb["ones"][:, 0:1],
                sb["sq"][:, slot * TG:(slot + 1) * TG],
                start=(c == 0), stop=(c == NCH - 1)),
                [xmb] + (self.takewar("b1_4") if c == 0 else []),
                tag=f"m3mm{g}_{c}")
            self.setwar(f"sq{slot}", [mean3_last, st])
            sq3 = self.op("scalar", lambda c=c, slot=slot: nc.scalar.activation(
                sb["fz"][:, slot * TG:(slot + 1) * TG],
                sb["x32"][:, c * TG:(c + 1) * TG], AF.Square),
                [u3] + self.takewar(f"fz{slot}"), tag=f"sq3{g}_{c}")
            ssq3_last = self.op("tensor", lambda slot=slot, c=c:
                                nc.tensor.matmul(
                ps["big1"][0:1, 1024:1024 + TG], sb["ones"][:, 0:1],
                sb["fz"][:, slot * TG:(slot + 1) * TG],
                start=(c == 0), stop=(c == NCH - 1)),
                [sq3] + (self.takewar("b1_5") if c == 0 else []),
                tag=f"s3mm{g}_{c}")
            self.setwar(f"fz{slot}", [ssq3_last])

        m3 = self.op("scalar", lambda: nc.scalar.activation(
            sb["rowc"][:], ps["big1"][0:1, 512:512 + TG], AF.Copy,
            scale=1.0 / D), [mean3_last] + self.takewar("rowsc"),
            tag=f"m3_{g}")
        self.addwar("b1_4", m3)
        negm3 = self.op("scalar", lambda g=g: nc.scalar.activation(
            sb["m3"][:, g * TG:(g + 1) * TG], sb["rowc"][:], AF.Copy,
            scale=-1.0), [m3], tag=f"negm3_{g}")
        self.negm3_ops[g] = negm3
        m3sq = self.op("scalar", lambda: nc.scalar.activation(
            sb["rowa"][:], sb["rowc"][:], AF.Square), [m3])
        msq3 = self.op("scalar", lambda: nc.scalar.activation(
            sb["rowb"][:], ps["big1"][0:1, 1024:1024 + TG], AF.Copy,
            scale=1.0 / D), [ssq3_last, negm3, m3sq], tag=f"msq3_{g}")
        self.addwar("b1_5", msq3)
        var3 = self.op("vector", lambda: nc.vector.tensor_sub(
            sb["rowa"][:], sb["rowb"][:], sb["rowa"][:]), [msq3, m3sq])
        sd3 = self.op("scalar", lambda: nc.scalar.activation(
            sb["rowb"][:], sb["rowa"][:], AF.Sqrt,
            bias=sb["cst"][0:1, 0:1]), [var3])
        rstd3 = self.op("vector", lambda: nc.vector.reciprocal(
            sb["rstd3"][:], sb["rowb"][:]), [sd3], tag=f"rstd3_{g}")
        rstd3b = self.op("vector", lambda: nc.vector.tensor_copy(
            sb["rstd3b"][:], sb["rstd3"][:]),
            [rstd3] + self.takewar("rbf0"))
        bcm3 = self.op("tensor", lambda: nc.tensor.matmul(
            ps["sm"][:, 0:TG], sb["ones"][0:1, :], sb["rstd3b"][:],
            start=True, stop=True), [rstd3b] + self.takewar("smbank"),
            tag=f"bc3_{g}")
        self.addwar("rbf0", bcm3)
        bc_r3 = self.op("vector", lambda: nc.vector.tensor_copy(
            sb["bc"][:, 0:TG], ps["sm"][:, 0:TG]),
            [bcm3, bc_m2] + self.takewar("bc0"), tag=f"bc3e_{g}")
        self.addwar("smbank", bc_r3)
        self.setwar("rowsc", [m3, negm3, m3sq, msq3, var3, sd3, rstd3,
                              rstd3b, bcm3])

        stt_ops = []
        u3s = self.sweep2_u3
        for c in range(NCH):
            slot = c % 2
            o = self.op("vector", lambda c=c, slot=slot:
                        nc.vector.scalar_tensor_tensor(
                sb["fz"][:, slot * TG:(slot + 1) * TG],
                sb["x32"][:, c * TG:(c + 1) * TG], 0.0,
                sb["bc"][:, 0:TG], ALU.bypass, ALU.mult,
                accum_out=sb["seqp"][:, c:c + 1]),
                [bc_r3, u3s[c]] + self.takewar(f"fz{slot}") +
                self.takewar(f"seqp{c}"), tag=f"stt{g}_{c}")
            self.addwar("bc0", o)
            self.setwar(f"fz{slot}", [o])
            self.addwar(f"x32_{c}", o)
            stt_ops.append(o)
        acc = self.op("vector", lambda: nc.vector.tensor_add(
            sb["seqa"][:], sb["seqa"][:], sb["seqp"][:]),
            stt_ops + [self.seqa_last], tag=f"seqacc{g}")
        self.seqa_last = acc
        for c in range(NCH):
            self.addwar(f"seqp{c}", acc)
        sg = self.op("vector", lambda g=g: nc.vector.tensor_mul(
            sb["rowc"][:], sb["rstd3"][:], sb["m3"][:, g * TG:(g + 1) * TG]),
            [rstd3, negm3, m3sq], tag=f"sig1_{g}")
        sgs = self.op("vector", lambda g=g: nc.vector.reduce_sum(
            sb["sumw"][0:1, g:g + 1], sb["rowc"][:], axis=AX.X),
            [sg], tag=f"sig2_{g}")
        self.sumw_ops = getattr(self, "sumw_ops", []) + [sgs]
        self.addwar("rowsc", sg, sgs)

    def _row_transpose(self, tag, src_fn, dst_fn, deps, region):
        nc, sb, ps = self.nc, self.sb, self.ps
        smw = self.takewar("smbank")
        tp = self.op("tensor", lambda: nc.tensor.transpose(
            ps["sm"][0:32, region:region + 1] if True else None,
            src_fn(), sb["fident"][0:1, 0:1]), deps + smw,
            tag=tag + "_tp")
        ev = self.op("vector", lambda: nc.vector.tensor_copy(
            dst_fn(), ps["sm"][0:32, region:region + 1]), [tp],
            tag=tag + "_ev")
        self.addwar("smbank", ev)
        return ev

    def _col_transpose(self, tag, src_fn, dst_fn, deps, region):
        nc, sb, ps = self.nc, self.sb, self.ps
        smw = self.takewar("smbank")
        tp = self.op("tensor", lambda: nc.tensor.transpose(
            ps["sm"][0:1, region:region + 32],
            src_fn(), sb["fident"][0:32, 0:32]), deps + smw,
            tag=tag + "_tp")
        ev = self.op("vector", lambda: nc.vector.tensor_copy(
            dst_fn(), ps["sm"][0:1, region:region + 32]), [tp],
            tag=tag + "_ev")
        self.addwar("smbank", ev)
        return ev

    def emit_gating(self):
        nc, sb, ps, dr = self.nc, self.sb, self.ps, self.dram

        freed = []
        for j in range(8):
            freed += self.takewar(f"q1_{j}") + self.takewar(f"q2_{j}")
        for n in range(QH):
            for t4 in range(4):
                freed += self.takewar(f"encdst_{n}_{t4}")
        for k in ("wfold0", "foldrow0", "woeff", "wob1", "wveff",
                  "xmidld0", "xmidld1", "comb_ps"):
            self.setwar(k, freed + self.warlist(k))

        z = self.op("vector", lambda: nc.vector.memset(
            sb["seqsb"][:], 0.0), (), tag="seqz")
        cp = self.op("vector", lambda: nc.vector.tensor_copy(
            sb["seqsb"][:, 0:16], sb["seqa"][:]), [z, self.seqa_last],
            tag="seqcp")
        ssum = self.op("vector", lambda: nc.vector.reduce_sum(
            sb["sumw"][0:1, 6:7], sb["sumw"][0:1, 0:4], axis=AX.X),
            self.sumw_ops, tag="sigsum")
        sput = self.op("vector", lambda: nc.vector.tensor_copy(
            sb["seqsb"][0:1, 16:17], sb["sumw"][0:1, 6:7]), [ssum, z],
            tag="sigput")
        st = self.op("dsync", lambda: nc.sync.dma_start(
            out=dr["seqin"], in_=sb["seqsb"][:]), [cp, sput], tag="seqst", stream="g")
        coll = self.op("coll", lambda: nc.gpsimd.collective_compute(
            "AllGather", ALU.bypass,
            replica_groups=[list(range(NCORE))],
            ins=[dr["seqin"][:]], outs=[dr["seqout"][:]]), [st], tag="ag")
        ldg = self.op("dsync", lambda: nc.sync.dma_start(
            out=sb["seqg"][:].rearrange("p (r c) -> p r c", r=NCORE),
            in_=dr["seqout"].rearrange("(r p) c -> p r c", p=128)),
            [coll], tag="seqld", stream="g")

        sums = {}
        for b in range(2):
            t1 = self.op("vector", lambda b=b: nc.vector.tensor_add(
                sb["srep"][:, b * 18:(b + 1) * 18],
                sb["seqg"][:, (4 * b) * 18:(4 * b) * 18 + 18],
                sb["seqg"][:, (4 * b + 1) * 18:(4 * b + 1) * 18 + 18]),
                [ldg], tag=f"bs{b}a")
            t2 = self.op("vector", lambda b=b: nc.vector.tensor_add(
                sb["srep"][:, b * 18:(b + 1) * 18],
                sb["srep"][:, b * 18:(b + 1) * 18],
                sb["seqg"][:, (4 * b + 2) * 18:(4 * b + 2) * 18 + 18]),
                [t1], tag=f"bs{b}b")
            t3 = self.op("vector", lambda b=b: nc.vector.tensor_add(
                sb["srep"][:, b * 18:(b + 1) * 18],
                sb["srep"][:, b * 18:(b + 1) * 18],
                sb["seqg"][:, (4 * b + 3) * 18:(4 * b + 3) * 18 + 18]),
                [t2], tag=f"bs{b}c")
            sums[b] = t3

        sld = self.op("dsync", lambda: nc.sync.dma_start(
            out=sb["simsb"][:].rearrange("p (c e) -> p c e", c=NCH),
            in_=dr["simM"].rearrange("(c p) e -> p c e", p=128)),
            self.takewar("x32_0"), tag="simld", stream="g")
        gld = self.op("dsync", lambda: nc.sync.dma_start(
            out=sb["gatr"][0:1, 0:32], in_=dr["gates"]), (), tag="gld",
            stream="g")
        sig = self.op("scalar", lambda: nc.scalar.activation(
            sb["gatr"][0:1, 32:64], sb["gatr"][0:1, 0:32], AF.Sigmoid),
            [gld], tag="sig")
        sigc = self._row_transpose("sigT", lambda: sb["gatr"][0:1, 32:64],
                                   lambda: sb["gat"][0:32, 8:9],
                                   [sig], 160)

        simsq_last = None
        for c in range(NCH):
            sqq = self.op("scalar", lambda c=c: nc.scalar.activation(
                sb["simsq"][:], sb["simsb"][:, c * 32:(c + 1) * 32],
                AF.Square), [sld] + ([simsq_last] if simsq_last else []),
                tag=f"simsq{c}")
            self.addwar("x32_0", sqq)
            simsq_last = self.op("tensor", lambda c=c: nc.tensor.matmul(
                ps["sm"][0:1, 8:40], sb["fones"][:], sb["simsq"][:],
                start=(c == 0), stop=(c == NCH - 1)),
                [sqq, self.memsets[1]] +
                (self.takewar("smbank") if c == 0 else []),
                tag=f"simmm{c}")
        scn = self.op("scalar", lambda: nc.scalar.activation(
            sb["grow"][0:1, 0:32], ps["sm"][0:1, 8:40], AF.Sqrt),
            [simsq_last], tag="scn")
        scr = self.op("vector", lambda: nc.vector.reciprocal(
            sb["grow"][0:1, 32:64], sb["grow"][0:1, 0:32]), [scn], tag="scr")
        self.addwar("smbank", scn)
        scrc = self._row_transpose("scrT", lambda: sb["grow"][0:1, 32:64],
                                   lambda: sb["gat"][0:32, 9:10],
                                   [scr], 128)
        self.sim_ops = [simsq_last, scrc, sigc]

        for b in range(2):
            self._gate_batch(b, sums[b])

        self.dbg("routing", [32, 8], F32,
                 lambda: sb["gat"][0:32, 0:8], self.routing_ops)
        self._fold()

    def _gate_batch(self, b, sum_op):
        nc, sb, ps, dr = self.nc, self.sb, self.ps, self.dram
        gw = self.takewar("gatew")
        sgbb = self.op("vector", lambda b=b: nc.vector.tensor_copy(
            sb["colscb"][0:1, 0:1], sb["srep"][0:1, b * 18 + 16:b * 18 + 17]),
            [sum_op] + self.warlist("colscb_war") + gw, tag=f"sgbb{b}")
        smw = self.takewar("smbank")
        sgm = self.op("tensor", lambda: nc.tensor.matmul(
            ps["sm"][:, 0:1], sb["ones"][0:1, :], sb["colscb"][0:1, 0:1],
            start=True, stop=True), [sgbb] + smw, tag=f"sgmm{b}")
        sgev = self.op("vector", lambda: nc.vector.tensor_copy(
            sb["small"][:, 6:7], ps["sm"][:, 0:1]), [sgm] + gw,
            tag=f"sgev{b}")
        self.addwar("smbank", sgev)
        sn = self.op("vector", lambda b=b: nc.vector.tensor_scalar(
            sb["snorm"][:], sb["srep"][:, b * 18:b * 18 + 16],
            sb["small"][:, 6:7], 1.0 / T, ALU.add, ALU.mult),
            [sum_op, sgev] + gw, tag=f"snorm{b}")
        nsq = self.op("scalar", lambda: nc.scalar.activation(
            sb["bc"][:, 0:16], sb["snorm"][:], AF.Square,
            accum_out=sb["colsc"][:, 0:1]),
            [sn] + self.takewar("bc0") + self.takewar("colsc_g") + gw,
            tag=f"nsq{b}")
        nmm = self.op("tensor", lambda: nc.tensor.matmul(
            ps["sm"][0:1, 4:5], sb["colsc"][:], sb["fones"][:],
            start=True, stop=True),
            [nsq, self.memsets[1], sgev] + self.takewar("smbank2"),
            tag=f"nmm{b}")
        nev = self.op("scalar", lambda: nc.scalar.activation(
            sb["gsc"][0:1, 0:1], ps["sm"][0:1, 4:5], AF.Sqrt),
            [nmm] + gw, tag=f"nev{b}")
        nrc = self.op("vector", lambda: nc.vector.reciprocal(
            sb["gsc"][0:1, 1:2], sb["gsc"][0:1, 0:1]), [nev], tag=f"nrc{b}")
        self.addwar("smbank2", nev)
        nrb = self.op("vector", lambda: nc.vector.tensor_copy(
            sb["colscb"][0:1, 0:1], sb["gsc"][0:1, 1:2]),
            [nrc, sgm] + self.warlist("colscb_war"), tag=f"nrb{b}")
        smw2 = self.takewar("smbank")
        nrm = self.op("tensor", lambda: nc.tensor.matmul(
            ps["sm"][:, 1:2], sb["ones"][0:1, :], sb["colscb"][0:1, 0:1],
            start=True, stop=True), [nrb] + smw2, tag=f"nrm{b}")
        nrbc = self.op("vector", lambda: nc.vector.tensor_copy(
            sb["small"][:, 7:8], ps["sm"][:, 1:2]), [nrm] + gw,
            tag=f"nrbc{b}")
        self.addwar("smbank", nrbc)
        aff_r = None
        aff_c = None
        b00 = self.takewar("big0_0")
        b01 = self.takewar("big0_1")
        for c in range(NCH):
            aff_r = self.op("tensor", lambda c=c: nc.tensor.matmul(
                ps["big0"][0:1, 0:32],
                sb["snorm"][:, c:c + 1], sb["simsb"][:, c * 32:(c + 1) * 32],
                start=(c == 0), stop=(c == NCH - 1)),
                [sn, self.sim_ops[0]] + (b00 if c == 0 else []),
                tag=f"affr{b}_{c}")
            aff_c = self.op("tensor", lambda c=c: nc.tensor.matmul(
                ps["big0"][0:32, 512:513],
                sb["simsb"][:, c * 32:(c + 1) * 32], sb["snorm"][:, c:c + 1],
                start=(c == 0), stop=(c == NCH - 1)),
                [sn] + (b01 if c == 0 else []), tag=f"affc{b}_{c}")
            self.addwar("x32_0", aff_r, aff_c)
        lr1 = self.op("vector", lambda: nc.vector.tensor_scalar_mul(
            sb["gatr"][0:1, 64:96], ps["big0"][0:1, 0:32],
            sb["small"][0:1, 7:8]), [aff_r, nrbc] + gw, tag=f"lr1{b}")
        self.addwar("big0_0", lr1)
        lr2 = self.op("vector", lambda: nc.vector.tensor_mul(
            sb["gatr"][0:1, 64:96], sb["gatr"][0:1, 64:96],
            sb["grow"][0:1, 32:64]), [lr1, self.sim_ops[1]], tag=f"lr2{b}")
        lrow = self.op("vector", lambda: nc.vector.tensor_sub(
            sb["gatr"][0:1, 64:96], sb["gatr"][0:1, 64:96],
            sb["gatr"][0:1, 32:64]), [lr2], tag=f"lrow{b}")
        lc1 = self.op("vector", lambda: nc.vector.tensor_scalar_mul(
            sb["gat"][0:32, 2:3], ps["big0"][0:32, 512:513],
            sb["small"][0:32, 7:8]), [aff_c, nrbc] + gw, tag=f"lc1{b}")
        self.addwar("big0_1", lc1)
        lc2 = self.op("vector", lambda: nc.vector.tensor_mul(
            sb["gat"][0:32, 2:3], sb["gat"][0:32, 2:3],
            sb["gat"][0:32, 9:10]), [lc1, self.sim_ops[1]], tag=f"lc2{b}")
        lcol = self.op("vector", lambda: nc.vector.tensor_sub(
            sb["gat"][0:32, 2:3], sb["gat"][0:32, 2:3], sb["gat"][0:32, 8:9]),
            [lc2, self.sim_ops[2]], tag=f"lcol{b}")
        smw = self.takewar("smbank")
        rmm = self.op("tensor", lambda: nc.tensor.matmul(
            ps["sm"][0:32, 192:224], sb["onesr"][:],
            sb["gatr"][0:1, 64:96], start=True, stop=True),
            [lrow, self.memsets[2]] + smw, tag=f"rmm{b}")
        rev = self.op("vector", lambda: nc.vector.tensor_copy(
            sb["gatm"][:, 0:32], ps["sm"][0:32, 192:224]), [rmm] + gw,
            tag=f"rev{b}")
        self.addwar("smbank", rev)
        gt_ = self.op("vector", lambda: nc.vector.tensor_single_scalar(
            sb["gatm"][:, 32:64], sb["gatm"][:, 0:32], sb["gat"][0:32, 2:3],
            ALU.is_gt), [rev, lcol], tag=f"ggt{b}")
        eq_ = self.op("vector", lambda: nc.vector.tensor_single_scalar(
            sb["gatm"][:, 0:32], sb["gatm"][:, 0:32], sb["gat"][0:32, 2:3],
            ALU.is_equal), [gt_], tag=f"geq{b}")
        tie = self.op("vector", lambda: nc.vector.tensor_mul(
            sb["gatm"][:, 0:32], sb["gatm"][:, 0:32], sb["idxlt"][:]),
            [eq_] + self.cload, tag=f"gtie{b}")
        rkp = self.op("vector", lambda: nc.vector.tensor_add(
            sb["gatm"][:, 0:32], sb["gatm"][:, 0:32], sb["gatm"][:, 32:64]),
            [tie], tag=f"grk{b}")
        rank = self.op("vector", lambda: nc.vector.reduce_sum(
            sb["gat"][0:32, 3:4], sb["gatm"][:, 0:32], axis=AX.X), [rkp],
            tag=f"grank{b}")
        fb_ = self.op("vector", lambda: nc.vector.tensor_single_scalar(
            sb["gat"][0:32, 4:5], sb["gat"][0:32, 3:4], 15.0, ALU.is_le),
            [rank], tag=f"gfb{b}")
        mc = self.op("vector", lambda: nc.vector.tensor_single_scalar(
            sb["gat"][0:32, 5:6], sb["gat"][0:32, 2:3], 0.0, ALU.is_gt),
            [lcol], tag=f"gmc{b}")
        cmm = self.op("tensor", lambda: nc.tensor.matmul(
            ps["sm"][0:1, 228:229], sb["gat"][0:32, 5:6],
            sb["fones"][0:32, 0:1], start=True, stop=True),
            [mc, rmm, self.memsets[1]], tag=f"cmm{b}")
        cev = self.op("vector", lambda: nc.vector.tensor_copy(
            sb["gsc"][0:1, 2:3], ps["sm"][0:1, 228:229]), [cmm] + gw,
            tag=f"cev{b}")
        self.addwar("smbank", cev)
        inz = self.op("vector", lambda: nc.vector.tensor_single_scalar(
            sb["gsc"][0:1, 3:4], sb["gsc"][0:1, 2:3], 0.0, ALU.is_equal),
            [cev], tag=f"ginz{b}")
        smw = self.takewar("smbank")
        ibm = self.op("tensor", lambda: nc.tensor.matmul(
            ps["sm"][0:32, 232:233], sb["onesr"][:],
            sb["gsc"][0:1, 3:4], start=True, stop=True),
            [inz] + smw, tag=f"gibm{b}")
        ibe = self.op("vector", lambda: nc.vector.tensor_copy(
            sb["gat"][0:32, 6:7], ps["sm"][0:32, 232:233]), [ibm] + gw,
            tag=f"gibe{b}")
        self.addwar("smbank", ibe)
        fbt = self.op("vector", lambda: nc.vector.tensor_mul(
            sb["gat"][0:32, 6:7], sb["gat"][0:32, 6:7], sb["gat"][0:32, 4:5]),
            [ibe, fb_], tag=f"gfbt{b}")
        mfin = self.op("vector", lambda: nc.vector.tensor_add(
            sb["gat"][0:32, 5:6], sb["gat"][0:32, 5:6], sb["gat"][0:32, 6:7]),
            [fbt, mc], tag=f"gmfin{b}")
        gcol = self.op("vector", lambda: nc.vector.tensor_relu(
            sb["gat"][0:32, 7:8], sb["gat"][0:32, 2:3]), [lcol, gt_, eq_],
            tag=f"ggcol{b}")
        z1 = self.op("vector", lambda: nc.vector.tensor_mul(
            sb["gat"][0:32, 7:8], sb["gat"][0:32, 7:8], sb["gat"][0:32, 5:6]),
            [gcol, mfin], tag=f"gz1{b}")
        z2 = self.op("vector", lambda: nc.vector.tensor_scalar(
            sb["gat"][0:32, 6:7], sb["gat"][0:32, 5:6], -1.0, 1.0,
            ALU.mult, ALU.add), [mfin, fbt], tag=f"gz2{b}")
        z3 = self.op("vector", lambda: nc.vector.tensor_scalar_mul(
            sb["gat"][0:32, 6:7], sb["gat"][0:32, 6:7], -1.0e38), [z2],
            tag=f"gz3{b}")
        zz = self.op("vector", lambda: nc.vector.tensor_add(
            sb["gat"][0:32, 7:8], sb["gat"][0:32, 7:8], sb["gat"][0:32, 6:7]),
            [z3, z1], tag=f"gzz{b}")
        zrow = self._col_transpose(f"zT{b}", lambda: sb["gat"][0:32, 7:8],
                                   lambda: sb["grow"][0:1, 0:32],
                                   [zz, self.sim_ops[1]], 240)
        mx = self.op("vector", lambda: nc.vector.reduce_max(
            sb["gsc"][0:1, 5:6], sb["grow"][0:1, 0:32], axis=AX.X,
            negate=True), [zrow], tag=f"gmx{b}")
        ex = self.op("scalar", lambda: nc.scalar.activation(
            sb["grow"][0:1, 0:32], sb["grow"][0:1, 0:32], AF.Exp,
            bias=sb["gsc"][0:1, 5:6], accum_out=sb["gsc"][0:1, 6:7]),
            [mx], tag=f"gex{b}")
        rcp = self.op("vector", lambda: nc.vector.reciprocal(
            sb["gsc"][0:1, 7:8], sb["gsc"][0:1, 6:7]), [ex], tag=f"grcp{b}")
        rt = self.op("vector", lambda: nc.vector.tensor_scalar_mul(
            sb["grow"][0:1, 0:32], sb["grow"][0:1, 0:32],
            sb["gsc"][0:1, 7:8]), [rcp, ex], tag=f"grt{b}")
        rce = self._row_transpose(f"rT{b}", lambda: sb["grow"][0:1, 0:32],
                                  lambda: sb["gat"][0:32, 0:1],
                                  [rt] + gw, 272)
        rmk = self.op("vector", lambda: nc.vector.tensor_mul(
            sb["gat"][0:32, 1:2], sb["gat"][0:32, 0:1], sb["gat"][0:32, 5:6]),
            [rce, mfin] + gw, tag=f"grmk{b}")
        rbM = self.op("vector", lambda b=b: nc.vector.tensor_copy(
            sb["rmb"][0:32, 2 * b:2 * b + 1], sb["gat"][0:32, 1:2]),
            [rmk], tag=f"grb{b}")
        rbP = self.op("vector", lambda b=b: nc.vector.tensor_copy(
            sb["rmb"][0:32, 2 * b + 1:2 * b + 2], sb["gat"][0:32, 0:1]),
            [rce, rmk], tag=f"grb2{b}")
        self.routing_ops = getattr(self, "routing_ops", []) + [rbM, rbP]
        self.setwar("gatew", [rbM, rbP, lrow, zrow, mx, rt, sn, nsq, nrc,
                              mfin, fbt, gcol, rank, inz, cev])

    def _fold(self):
        nc, sb, ps, dr = self.nc, self.sb, self.ps, self.dram
        NCHUNK = 8
        self.setwar("fold0", self.takewar("big0_0") + self.takewar("big0_1") +
                    self.takewar("S0"))
        self.setwar("fold1", self.takewar("b1_3") + self.takewar("b1_4") +
                    self.takewar("b1_5") + self.takewar("S1"))
        stores = []
        for wi, (wnm, bb) in enumerate((("WvT", 0), ("WvT", 1),
                                        ("WoF", 0), ("WoF", 1))):
            rcol = 2 * bb + (0 if wnm == "WvT" else 1)
            rop = self.routing_ops[rcol]
            for ch in range(NCHUNK):
                slot = 0
                wd = self.op("dgp", lambda ch=ch, slot=slot, wnm=wnm:
                             nc.gpsimd.dma_start(
                    out=sb["wfold"][:, slot * 2048:(slot + 1) * 2048],
                    in_=dr[wnm][:, ch * 2048:(ch + 1) * 2048]),
                    self.takewar(f"wfold{slot}"), tag=f"f{wnm}{bb}_{ch}",
                    stream="wf")
                mms = []
                for pp in range(4):
                    bigt = ps["big0"] if pp < 3 else ps["big1"]
                    pcol = (pp % 3) * 512
                    deps = [wd, rop]
                    if pp == 0:
                        deps += self.takewar("fold0")
                    if pp == 3:
                        deps += self.takewar("fold1")
                    o = self.op("tensor", lambda pp=pp, slot=slot, bigt=bigt,
                                pcol=pcol, rcol=rcol: nc.tensor.matmul(
                        bigt[0:1, pcol:pcol + 512],
                        sb["rmb"][0:32, rcol:rcol + 1],
                        sb["wfold"][0:32, slot * 2048 + pp * 512:
                                    slot * 2048 + (pp + 1) * 512],
                        start=True, stop=True), deps,
                        tag=f"fm{wnm}{bb}_{ch}_{pp}")
                    mms.append(o)
                ev1 = self.op("vector", lambda slot=slot:
                              nc.vector.tensor_copy(
                    sb["foldrow"][0:1, 0:1536],
                    ps["big0"][0:1, 0:1536]), mms[0:3] +
                    self.takewar("foldrow0"), tag=f"fe1{wnm}{bb}_{ch}")
                ev2 = self.op("vector", lambda slot=slot:
                              nc.vector.tensor_copy(
                    sb["foldrow"][0:1, 1536:2048],
                    ps["big1"][0:1, 0:512]), [mms[3]],
                    tag=f"fe2{wnm}{bb}_{ch}")
                self.setwar("fold0", [ev1])
                self.setwar("fold1", [ev2])
                sto = self.op("dsync", lambda ch=ch, wi=wi:
                              nc.sync.dma_start(
                    out=dr["foldin"][0:1, wi * 16384 + ch * 2048:
                                     wi * 16384 + (ch + 1) * 2048],
                    in_=sb["foldrow"][0:1, 0:2048]),
                    [ev1, ev2], tag=f"fs{wnm}{bb}_{ch}", stream="fst")
                self.setwar(f"wfold{slot}", mms)
                self.setwar("foldrow0", [sto])
                stores.append(sto)
        self.addwar("big0_0", *self.takewar("fold0"))
        self.addwar("b1_3", *self.takewar("fold1"))
        co = self.op("coll", lambda: nc.gpsimd.collective_compute(
            "AllGather", ALU.bypass,
            replica_groups=[list(range(NCORE))],
            ins=[dr["foldin"][:]], outs=[dr["foldout"][:]]), stores,
            tag="agw")
        fo = dr["foldout"]
        self.wveff_ld = []
        ld_b1 = []
        for r in range(NCORE):
            for cc in range(2):
                c = 2 * r + cc
                for bb, dstn in ((0, "wveff"), (1, "fz")):
                    src_ap = fo[r:r + 1,
                                bb * 16384 + cc * 8192:
                                bb * 16384 + (cc + 1) * 8192].rearrange(
                        "one (p h) -> (one p) h", p=128)
                    o = self.op("dgp", lambda src_ap=src_ap, c=c, dstn=dstn:
                                nc.gpsimd.dma_start(
                        out=sb[dstn][:, c * EH:(c + 1) * EH], in_=src_ap),
                        [co] + (self.takewar("wveff") if dstn == "wveff"
                                else self.takewar("fz0") +
                                self.takewar("fz1")),
                        tag=f"wveffld{bb}_{c}", stream="we")
                    (self.wveff_ld if bb == 0 else ld_b1).append(o)
        bl1 = self.op("vector", lambda: nc.vector.tensor_scalar_mul(
            sb["wveff"][:], sb["wveff"][:], sb["bsel"][:, 0:1]),
            self.wveff_ld + self.cload, tag="wvbl1")
        bl2 = self.op("vector", lambda: nc.vector.scalar_tensor_tensor(
            sb["wveff"][:], sb["fz"][:, 0:2 * TG], sb["bsel"][:, 1:2],
            sb["wveff"][:], ALU.mult, ALU.add),
            [bl1] + ld_b1, tag="wvbl2")
        self.addwar("fz0", bl2)
        self.addwar("fz1", bl2)
        self.wveff_ld = [bl2]
        self.woeff_ld = []
        ldo_b1 = []
        for r in range(NCORE):
            for bb, dstn in ((0, "woeff"), (1, "wob1")):
                src_ap = fo[r:r + 1,
                            32768 + bb * 16384:32768 + bb * 16384 + 16384
                            ][0:1, 0:16384].rearrange(
                    "one (h d) -> (one h) d", h=8)
                o = self.op("dgp", lambda src_ap=src_ap, r=r, dstn=dstn:
                            nc.gpsimd.dma_start(
                    out=sb[dstn][8 * r:8 * r + 8, :], in_=src_ap),
                    [co] + self.takewar("woeff" if dstn == "woeff"
                                        else "wob1"),
                    tag=f"woeffld{bb}_{r}", stream="we")
                (self.woeff_ld if bb == 0 else ldo_b1).append(o)
        bo1 = self.op("vector", lambda: nc.vector.tensor_scalar_mul(
            sb["woeff"][:], sb["woeff"][:], sb["bsel"][0:64, 0:1]),
            self.woeff_ld + self.cload, tag="wobl1")
        bo2 = self.op("vector", lambda: nc.vector.scalar_tensor_tensor(
            sb["woeff"][:], sb["wob1"][:], sb["bsel"][0:64, 1:2],
            sb["woeff"][:], ALU.mult, ALU.add),
            [bo1] + ldo_b1, tag="wobl2")
        self.addwar("wob1_done", bo2)
        self.woeff_ld = [bo2]

    def emit_ffn(self):
        nc, sb, ps, dr = self.nc, self.sb, self.ps, self.dram

        cvm = None
        for c in range(NCH):
            cvm = self.op("tensor", lambda c=c: nc.tensor.matmul(
                ps["sm"][0:1, 448:448 + EH], sb["ones"][:, 0:1],
                sb["wveff"][:, c * EH:(c + 1) * EH],
                start=(c == 0), stop=(c == NCH - 1)),
                self.wveff_ld + (self.takewar("smbank") if c == 0 else []),
                tag=f"cvm{c}")
        cve = self.op("vector", lambda: nc.vector.tensor_copy(
            sb["colv"][:], ps["sm"][0:1, 448:448 + EH]), [cvm], tag="cve")
        self.addwar("smbank", cve)
        cwo = self.op("scalar", lambda: nc.scalar.activation(
            sb["bc"][0:64, 0:TG], sb["woeff"][:, 0:TG], AF.Copy,
            accum_out=sb["colwo"][:]), self.woeff_ld + self.takewar("bc0"),
            tag="cwo1")
        cwos = [cwo]
        for q in range(1, 4):
            o = self.op("scalar", lambda q=q: nc.scalar.activation(
                sb["bc"][0:64, 0:TG], sb["woeff"][:, q * TG:(q + 1) * TG],
                AF.Copy, accum_out=sb["colsc"][0:64, 0:1]),
                [cwos[-1]] + self.takewar("colsc_g"), tag=f"cwo{q}")
            a = self.op("vector", lambda: nc.vector.tensor_add(
                sb["colwo"][:], sb["colwo"][:], sb["colsc"][0:64, 0:1]),
                [o, cwos[-1] if q == 1 else cwos[-1]], tag=f"cwoa{q}")
            self.addwar("colsc_g", a)
            cwos.append(a)
        cwob = self.op("vector", lambda: nc.vector.tensor_copy(
            sb["colwob"][:], sb["colwo"][:]), [cwos[-1]], tag="cwob")
        self.addwar("bc0", cwos[-1])

        self.dbg("wveff", [128, NCH * EH], BF16, lambda: self.sb["wveff"][:],
                 self.wveff_ld)
        self.dbg("woeff", [64, D], BF16, lambda: self.sb["woeff"][:],
                 self.woeff_ld)
        for f in range(4):
            self._ffn_fchunk(f, cve, cwob)
        self.dbg("comb", [64, TO], BF16, lambda: self.sb["comb"][:],
                 [o for o in self.ops if o.tag and o.tag.startswith("cemb")])


    def _ffn_fchunk(self, f, cve, cwob):
        nc, sb, ps, dr = self.nc, self.sb, self.ps, self.dram
        g = f
        mm = None
        for c in range(NCH):
            slot = c % 2
            xld = self.op("dgp", lambda c=c, f=f, slot=slot:
                          nc.gpsimd.dma_start(
                out=sb["xmidld"][:, slot * TG:(slot + 1) * TG],
                in_=dr["xmid"][c, :, f * TG:(f + 1) * TG]),
                self.takewar(f"xmidld{slot}") + self.xmid_stored,
                tag=f"cxl{f}_{c}", stream=f"xm{slot}")
            deps = [xld] + self.wveff_ld
            if c == 0:
                deps += self.takewar("comb_ps") + self.takewar("smbank")
            mm = self.op("tensor", lambda c=c, slot=slot: nc.tensor.matmul(
                ps["sm"][0:EH, 0:TG],
                sb["wveff"][:, c * EH:(c + 1) * EH],
                sb["xmidld"][:, slot * TG:(slot + 1) * TG],
                start=(c == 0), stop=False), deps, tag=f"cm{f}_{c}")
            self.addwar(f"xmidld{slot}", mm)
        corr = self.op("tensor", lambda f=f: nc.tensor.matmul(
            ps["sm"][0:EH, 0:TG], sb["colv"][:],
            sb["m3"][:, f * TG:(f + 1) * TG], start=False, stop=True),
            [mm, cve, self.negm3_ops[f]], tag=f"cmc{f}")
        cemb = self.op("vector", lambda f=f: nc.vector.tensor_copy(
            sb["comb"][:, f * TG:(f + 1) * TG], ps["sm"][0:EH, 0:TG]),
            [corr], tag=f"cemb{f}")
        self.setwar("comb_ps", [cemb])
        self.addwar("smbank", cemb)

        m4m = self.op("tensor", lambda f=f: nc.tensor.matmul(
            ps["big1"][0:1, 512:512 + TG], sb["colwob"][:],
            sb["comb"][:, f * TG:(f + 1) * TG], start=True, stop=True),
            [cemb, cwob] + self.takewar("b1_4"), tag=f"m4m{f}")
        m4 = self.op("scalar", lambda: nc.scalar.activation(
            sb["m4"][:], ps["big1"][0:1, 512:512 + TG], AF.Copy,
            scale=1.0 / D), [m4m] + self.takewar("rowsc"), tag=f"m4_{f}")
        self.addwar("b1_4", m4)
        negm4 = self.op("scalar", lambda: nc.scalar.activation(
            sb["negm4b"][:], sb["m4"][:], AF.Copy, scale=-1.0),
            [m4] + self.takewar("rbf1"))

        ssq_last = None
        evs = []
        for c in range(NCH):
            slot = c % 2
            deps = self.woeff_ld + [cemb, negm4]
            if c == 0:
                deps += self.takewar("big0_0") + self.takewar("big0_1")
            mmf = self.op("tensor", lambda c=c, slot=slot, f=f:
                          nc.tensor.matmul(
                ps["big0"][:, slot * TG:(slot + 1) * TG],
                sb["woeff"][:, c * 128:(c + 1) * 128],
                sb["comb"][:, f * TG:(f + 1) * TG],
                start=True, stop=False),
                deps + self.takewar(f"big0_{slot}"), tag=f"fm{f}_{c}")
            mc = self.op("tensor", lambda c=c, slot=slot: nc.tensor.matmul(
                ps["big0"][:, slot * TG:(slot + 1) * TG],
                sb["ones"][0:1, 0:128], sb["negm4b"][:],
                start=False, stop=True), [mmf], tag=f"fc{f}_{c}")
            self.addwar("rbf1", mc)
            ev = self.op("scalar", lambda c=c, slot=slot:
                         nc.scalar.activation(
                sb["x32"][:, c * TG:(c + 1) * TG],
                ps["big0"][:, slot * TG:(slot + 1) * TG], AF.Copy),
                [mc] + self.takewar(f"x32_{c}"), tag=f"fev{f}_{c}")
            sq4 = self.op("scalar", lambda c=c, slot=slot:
                          nc.scalar.activation(
                sb["fz"][:, slot * TG:(slot + 1) * TG],
                ps["big0"][:, slot * TG:(slot + 1) * TG], AF.Square),
                [mc, ev] + self.takewar(f"fz{slot}"), tag=f"fsq{f}_{c}")
            self.setwar(f"big0_{slot}", [ev, sq4])
            ssq_last = self.op("tensor", lambda slot=slot, c=c:
                               nc.tensor.matmul(
                ps["big1"][0:1, 1024:1024 + TG], sb["ones"][:, 0:1],
                sb["fz"][:, slot * TG:(slot + 1) * TG],
                start=(c == 0), stop=(c == NCH - 1)),
                [sq4] + (self.takewar("b1_5") if c == 0 else []),
                tag=f"fsmm{f}_{c}")
            self.setwar(f"fz{slot}", [ssq_last])
            evs.append(ev)

        msq4 = self.op("scalar", lambda: nc.scalar.activation(
            sb["rowb"][:], ps["big1"][0:1, 1024:1024 + TG], AF.Copy,
            scale=1.0 / D), [ssq_last] + self.takewar("rowsc"),
            tag=f"msq4_{f}")
        self.addwar("b1_5", msq4)
        sd4 = self.op("scalar", lambda: nc.scalar.activation(
            sb["rowa"][:], sb["rowb"][:], AF.Sqrt,
            bias=sb["cst"][0:1, 0:1]), [msq4])
        rstd4 = self.op("vector", lambda: nc.vector.reciprocal(
            sb["rstd4"][:], sb["rowa"][:]), [sd4], tag=f"rstd4_{f}")
        rstd4b = self.op("vector", lambda: nc.vector.tensor_copy(
            sb["rstd4b"][:], sb["rstd4"][:]), [rstd4] + self.takewar("rbf0"))
        bcm = self.op("tensor", lambda: nc.tensor.matmul(
            ps["sm"][:, 0:TG], sb["ones"][0:1, :], sb["rstd4b"][:],
            start=True, stop=True),
            [rstd4b] + self.takewar("smbank") + self.takewar("comb_ps"),
            tag=f"bc4_{f}")
        bc4 = self.op("vector", lambda: nc.vector.tensor_copy(
            sb["bc"][:, 0:TG], ps["sm"][:, 0:TG]),
            [bcm] + self.takewar("bc0"), tag=f"bc4e_{f}")
        self.addwar("smbank", bc4)
        self.addwar("rbf0", bcm)
        self.setwar("rowsc", [msq4, sd4, rstd4, rstd4b, bcm, m4, negm4])
        self.setwar("comb_ps", [bc4])

        for c in range(NCH):
            slot = c % 2
            xld = self.op("dgp", lambda c=c, f=f, slot=slot:
                          nc.gpsimd.dma_start(
                out=sb["xmidld"][:, slot * TG:(slot + 1) * TG],
                in_=dr["xmid"][c, :, f * TG:(f + 1) * TG]),
                self.takewar(f"xmidld{slot}"), tag=f"yxl{f}_{c}", stream=f"xm{slot}")
            u = self.op("vector", lambda c=c: nc.vector.tensor_mul(
                sb["x32"][:, c * TG:(c + 1) * TG],
                sb["x32"][:, c * TG:(c + 1) * TG], sb["bc"][:, 0:TG]),
                [evs[c], bc4], tag=f"yu{f}_{c}")
            self.addwar("bc0", u)
            oslot = c % 2
            y = self.op("vector", lambda c=c, slot=slot, oslot=oslot:
                        nc.vector.tensor_add(
                sb["out32"][:, oslot * TG:(oslot + 1) * TG],
                sb["x32"][:, c * TG:(c + 1) * TG],
                sb["xmidld"][:, slot * TG:(slot + 1) * TG]),
                [u, xld] + self.takewar(f"xo32_{oslot}"),
                tag=f"y{f}_{c}")
            self.addwar(f"xmidld{slot}", y)
            self.addwar(f"x32_{c}", y)
            sto = self.op("dsync", lambda c=c, f=f, oslot=oslot:
                          nc.sync.dma_start(
                out=dr["yT"][c * 128:(c + 1) * 128, f * TG:(f + 1) * TG],
                in_=sb["out32"][:, oslot * TG:(oslot + 1) * TG]),
                [y], tag=f"yst{f}_{c}", stream=f"y{oslot}")
            self.setwar(f"xo32_{oslot}", [sto])
            self.final_deps.append(sto)

    def build(self):
        self._alloc()
        self.emit_consts()
        self.xmid_stored = []
        self.negm3_ops = [None] * NG
        for gg in range(NGALL):
            self.emit_group_proj(gg)
            if gg >= 2:
                self.emit_group_attn(gg)
                self.emit_group_out(gg)
        sb = self.sb
        self.dbg("q1", [128, 8 * TG], BF16, lambda: sb["q1"][:],
                 sum((self.q_ready.get(j, []) for j in range(8)), []))
        self.dbg("q2", [128, 8 * TG], BF16, lambda: sb["q2"][:],
                 sum((self.q_ready.get(j, []) for j in range(8)), []))
        self.dbg("k1", [128, 4 * RING * 128], BF16, lambda: sb["k1"][:],
                 sum((self.kslot_ready.get((s, p), [])
                      for s in range(RING) for p in range(4)), []))
        self.dbg("k2", [128, 4 * RING * 128], BF16, lambda: sb["k2"][:],
                 sum((self.kslot_ready.get((s, p), [])
                      for s in range(RING) for p in range(4)), []))
        self.dbg("vring", [128, RING * KVH * 128], BF16,
                 lambda: sb["vring"][:],
                 [self.vslot_ready[k] for k in self.vslot_ready])
        self.dbg("enc", [128, QH * TG], BF16, lambda: sb["enc"][:],
                 sum((self.enc_ready.get(n, []) for n in range(QH)), []))
        self.dbg("x32", [128, NCH * TG], F32, lambda: sb["x32"][:],
                 [o for o in self.ops if o.tag and o.tag.startswith("sw2c3_")])
        self.dbg("seqa", [128, 16], F32, lambda: sb["seqa"][:],
                 [self.seqa_last])
        self.dbg("acol", [128, 64], F32, lambda: sb["acol"][:],
                 [a for a in self.acol_ready if a])
        self.dbg("P", [128, 2 * 1152], BF16, lambda: sb["P"][:],
                 [o for o in self.ops if o.tag and o.tag.startswith("pn15_")])
        if self.stop != "groups":
            self.emit_gating()
            self.emit_ffn()
        return self._emit()

    def _emit(self):
        nc = self.nc
        if self.bank_edges:
            for ia, ib in self.bank_edges:
                self.ops[ib].deps.append(self.ops[ia])
        if self.serial:
            prev = None
            for o in self.ops:
                if prev is not None:
                    o.deps = list(o.deps) + [prev]
                prev = o
        for o in self.ops:
            if o.is_dma:
                o.need_inc = True
            for d in o.deps:
                d.need_inc = True
        for o in self.final_deps:
            o.need_inc = True

        for o in self.ops:
            o.semkey = (ENG_SEM[o.eng] + "_" + o.stream) if o.is_dma \
                else ENG_SEM[o.eng]
        keys = sorted({o.semkey for o in self.ops if o.need_inc})
        for k in keys:
            if k not in self.sems:
                self.sems[k] = self.es.enter_context(nc.semaphore(k))
        counts = {}
        for o in self.ops:
            if o.need_inc:
                counts[o.semkey] = counts.get(o.semkey, 0) + \
                    (1 if o.eng == "coll" else (16 if o.is_dma else 1))
                o.val = counts[o.semkey]
        self.totals = dict(counts)
        know = {}
        per_key = {}
        for o in self.ops:
            k = {}
            for d in o.deps:
                kd = know.get(id(d))
                if kd:
                    for s, v in kd.items():
                        if k.get(s, 0) < v:
                            k[s] = v
                if d.need_inc:
                    if k.get(d.semkey, 0) < d.val:
                        k[d.semkey] = d.val
            know[id(o)] = k
            if o.is_dma and o.need_inc:
                per_key.setdefault(o.semkey, []).append(o)
        for kk, lst in per_key.items():
            i = 0
            while i < len(lst):
                j = i + 1
                while j < len(lst):
                    kj = know.get(id(lst[j]), {})
                    if kj.get(kk, 0) >= lst[j - 1].val:
                        break
                    j += 1
                end_val = lst[j - 1].val
                for t in range(i, j):
                    lst[t].val_wait = end_val
                lst[i].batch_start = True
                i = j
        for o in self.ops:
            if o.need_inc and o.val_wait is None:
                o.val_wait = o.val

        by_exec = {e: [] for e in
                   ("tensor", "vector", "scalar", "gpsimd", "sync")}
        for o in self.ops:
            by_exec[ENG_EXEC[o.eng]].append(o)
        watermark = {e: {} for e in by_exec}
        sems = self.sems

        def run_stream(eng_obj, ops, ename):
            wm = watermark[ename]
            for o in ops:
                for d in o.deps:
                    sk = d.semkey
                    v = d.val_wait
                    assert v is not None, (d.tag, o.tag)
                    if wm.get(sk, 0) < v:
                        eng_obj.wait_ge(sems[sk], v)
                        wm[sk] = v
                inst = o.fn()
                if self.capture:
                    o.psum = _extract_psum(inst)
                if o.need_inc:
                    inst.then_inc(sems[o.semkey],
                                  1 if o.eng == "coll" else
                                  (16 if o.is_dma else 1))
            if ename == "sync":
                for o in self.final_deps:
                    sk, v = o.semkey, o.val_wait
                    if wm.get(sk, 0) < v:
                        eng_obj.wait_ge(sems[sk], v)
                        wm[sk] = v
                for sk, tot in sorted(self.totals.items()):
                    if "_" in sk and wm.get(sk, 0) < tot:
                        eng_obj.wait_ge(sems[sk], tot)
                        wm[sk] = tot

        with self.es:
            block = self.es.enter_context(nc.Block())
            block.tensor(lambda e: run_stream(e, by_exec["tensor"], "tensor"))
            block.vector(lambda e: run_stream(e, by_exec["vector"], "vector"))
            block.scalar(lambda e: run_stream(e, by_exec["scalar"], "scalar"))
            block.gpsimd(lambda e: run_stream(e, by_exec["gpsimd"], "gpsimd"))
            block.sync(lambda e: run_stream(e, by_exec["sync"], "sync"))
        return nc



_PSUM_INFO = {"big0": (0, 1536, 4), "big1": (3, 1536, 4),
              "pt": (6, 1024, 2), "sm": (7, 512, 4)}


def _extract_psum(bi):
    mi = bi.ins
    acc = []

    def scan(lst, kind):
        for pap in lst:
            memref = getattr(pap, "memref", None)
            if memref not in _PSUM_INFO:
                continue
            b0, fsz, esz = _PSUM_INFO[memref]
            col0 = pap.offset % fsz
            ext = 0
            for pr in pap.ap:
                st, num = int(pr[0]), int(pr[1])
                if st % fsz == 0:
                    continue
                ext += abs(st) * (num - 1)
            lo = b0 + (col0 * esz) // 2048
            hi = b0 + ((col0 + ext) * esz) // 2048
            for b in range(lo, hi + 1):
                acc.append((b, kind))

    scan(mi.ins, "R")
    scan(mi.outs, "W")
    return acc or None


def compute_bank_edges(mk):
    from bisect import bisect_right
    ops = mk.ops
    CE = ("tensor", "vector", "scalar")
    streams = {}
    pos = {}
    for o in ops:
        e = ENG_EXEC[o.eng]
        lst = streams.setdefault(e, [])
        pos[id(o)] = len(lst)
        lst.append(o)
    INF = float("inf")
    icov = {}
    for e in CE:
        lst = streams.get(e, [])
        arr = [INF] * len(lst)
        nxt = INF
        for i in range(len(lst) - 1, -1, -1):
            o = lst[i]
            if (not o.is_dma) and o.need_inc:
                nxt = o.val
            arr[i] = nxt
        icov[e] = arr
    sem_of = {e: ENG_SEM[e] for e in CE}
    wmv = {e: {} for e in streams}
    knowpos = {e: {ae: -1 for ae in CE} for e in streams}
    hist = [{ae: [] for ae in CE} for _ in range(8)]
    edges = []
    for o in ops:
        e = ENG_EXEC[o.eng]
        wv, kp = wmv[e], knowpos[e]
        upd = False
        for d in o.deps:
            v = d.val_wait
            if v is None:
                continue
            sk = d.semkey
            if wv.get(sk, 0) < v:
                wv[sk] = v
                upd = True
        if upd:
            for ae in CE:
                v = wv.get(sem_of[ae], 0)
                if v:
                    p = bisect_right(icov[ae], v) - 1
                    if p > kp[ae]:
                        kp[ae] = p
        if o.psum and e in CE:
            mypos = pos[id(o)]
            for b in sorted({bb for bb, _k in o.psum}):
                hb = hist[b]
                for ae in CE:
                    if ae == e:
                        continue
                    lst = hb[ae]
                    if lst and lst[-1][0] > kp[ae]:
                        ap_, ag = lst[-1]
                        edges.append((ag, o.gidx))
                        kp[ae] = ap_
                hb[e].append((mypos, o.gidx))
    seen = set()
    out = []
    for ed in edges:
        if ed not in seen:
            seen.add(ed)
            out.append(ed)
    return out


def build_banked(**kw):
    m1 = MK(capture=True, **kw)
    m1.build()
    edges = compute_bank_edges(m1)
    m2 = MK(bank_edges=edges, **kw)
    nc = m2.build()
    return m2, nc, edges



_BUILD_CACHE = {}


def kernel(L0_attn_x, L0_attn_q_kernel, L0_attn_kv_kernel, L0_attn_out_kernel,
           L0_attn_q_norm_scale, L0_attn_k_norm_scale, L0_ffn_sim_matrix,
           L0_ffn_gates, L0_ffn_w_q, L0_ffn_w_k, L0_ffn_w_v, L0_ffn_o_weights):
    _ = (L0_ffn_w_q, L0_ffn_w_k)
    x = np.asarray(L0_attn_x, dtype=np.float32)
    shared = host_weights(np.asarray(L0_attn_q_kernel, np.float32),
                          np.asarray(L0_attn_kv_kernel, np.float32),
                          np.asarray(L0_attn_out_kernel, np.float32),
                          np.asarray(L0_ffn_w_v, np.float32),
                          np.asarray(L0_ffn_o_weights, np.float32),
                          np.asarray(L0_ffn_sim_matrix, np.float32))
    shared.update(host_tables())
    in_maps = []
    for core in range(NCORE):
        d = host_core_inputs(core, x,
                             np.asarray(L0_attn_q_norm_scale, np.float32),
                             np.asarray(L0_attn_k_norm_scale, np.float32),
                             np.asarray(L0_ffn_gates, np.float32), shared)
        d["WvT"] = np.ascontiguousarray(
            shared["WvT"][:, core * 16384:(core + 1) * 16384])
        d["WoF"] = np.ascontiguousarray(
            shared["WoF"][:, core * 16384:(core + 1) * 16384])
        in_maps.append(d)
    _mk, nc, _edges = build_banked()
    res = run_bass_kernel_spmd(nc, in_maps, core_ids=list(range(NCORE)))
    out = np.empty((B, T, D), dtype=np.float32)
    for core in range(NCORE):
        b, cs = core // 4, core % 4
        out[b, cs * TO:(cs + 1) * TO, :] = res.results[core]["yT"].T
    return out


# version 06
# speedup vs baseline: 1.0343x; 1.0343x over previous
import math
from contextlib import ExitStack

import numpy as np
import ml_dtypes

import concourse.bass as bass
import concourse.mybir as mybir
from concourse.bass_utils import run_bass_kernel_spmd

F32 = mybir.dt.float32
BF16 = mybir.dt.bfloat16
AF = mybir.ActivationFunctionType
ALU = mybir.AluOpType
AX = mybir.AxisListType
bfdt = ml_dtypes.bfloat16

B, T, D = 2, 8192, 2048
QH, KVH, HD = 16, 8, 128
WIN = 1024
E, EH = 32, 64
ATTN_SCALE = 0.08838834764831845
EPS_RMS = 1e-6
EPS_LN = 1e-5
ROPE_BASE = 10000.0

NCORE = 8
TO = 2048
HALO = 1024
TG = 512
NT = 16
NG = 4
NGALL = 6
NCH = 16
RING = 12
NEG = -1.0e30
QPAIR_A = [0, 1, 4, 5, 8, 9, 12, 13]


def q_block_of(n):
    j = (n // 4) * 2 + (n % 2)
    half = 0 if (n % 4) < 2 else 1
    return j, half



def host_tables():
    c = {}
    c["ident"] = np.eye(128, dtype=np.float32)
    c["ones"] = np.ones((128, 128), dtype=np.float32)
    r = np.arange(128)[:, None]
    col = np.arange(128)[None, :]
    c["tri"] = np.concatenate(
        [np.where(r > col, 0.0, NEG), np.where(r <= col, 0.0, NEG)],
        axis=1).astype(np.float32)
    aqs = np.zeros((16, 8, 128), dtype=np.float32)
    for j in range(8):
        a = QPAIR_A[j]
        aqs[a, j, 0:64] = 1.0
        aqs[a + 2, j, 64:128] = 1.0
    c["aqsel"] = aqs.reshape(16, 8 * 128)
    eq = np.zeros((8, 128, 16), dtype=np.float32)
    for j in range(8):
        a = QPAIR_A[j]
        eq[j, 0:64, a] = 1.0
        eq[j, 64:128, a + 2] = 1.0
    c["eyeq"] = eq.transpose(1, 0, 2).reshape(128, 8 * 16)
    ek = np.zeros((4, 128, 8), dtype=np.float32)
    for p in range(4):
        ek[p, 0:64, 2 * p] = 1.0
        ek[p, 64:128, 2 * p + 1] = 1.0
    c["eyek"] = ek.transpose(1, 0, 2).reshape(128, 4 * 8)
    pk = np.zeros((4, 8, 128), dtype=np.float32)
    for p in range(4):
        pk[p, 2 * p, 0:64] = 1.0
        pk[p, 2 * p + 1, 64:128] = 1.0
    c["pselk"] = pk.transpose(1, 0, 2).reshape(8, 4 * 128)
    ii = np.arange(32)
    c["idxlt"] = (ii[None, :] < ii[:, None]).astype(np.float32)
    return c


def host_weights(qk, kvk, outk, wv, wo, sim):
    s = {}
    qb = np.empty((8, 2, NCH, 128, 128), dtype=np.float32)
    for j in range(8):
        a = QPAIR_A[j]
        for v in range(2):
            blk = np.concatenate([qk[a][:, v * 64:(v + 1) * 64],
                                  qk[a + 2][:, v * 64:(v + 1) * 64]], axis=1)
            qb[j, v] = blk.reshape(NCH, 128, 128)
    s["Wq"] = np.ascontiguousarray(qb.reshape(16, NCH, 128, 128))
    kb = np.empty((16, NCH, 128, 128), dtype=np.float32)
    kk, vv = kvk[0], kvk[1]
    for p in range(4):
        for v in range(2):
            blk = np.concatenate([kk[2 * p][:, v * 64:(v + 1) * 64],
                                  kk[2 * p + 1][:, v * 64:(v + 1) * 64]], axis=1)
            kb[p * 2 + v] = blk.reshape(NCH, 128, 128)
    for h in range(8):
        kb[8 + h] = vv[h].reshape(NCH, 128, 128)
    s["Wkv"] = np.ascontiguousarray(kb)
    ob = np.empty((NCH, QH, 128, 128), dtype=np.float32)
    for n in range(QH):
        for c in range(NCH):
            ob[c, n] = outk[n][:, c * 128:(c + 1) * 128]
    s["Wout"] = np.ascontiguousarray(ob)
    s["WvT"] = np.ascontiguousarray(wv.transpose(0, 2, 1)).reshape(E, D * EH)
    s["WoF"] = np.ascontiguousarray(wo).reshape(E, EH * D)
    s["simM"] = np.ascontiguousarray(sim)
    return s


def host_core_inputs(core, x, qs, ks, gates, shared):
    b, cs = core // 4, core % 4
    t0 = cs * TO
    d = {}
    xs = np.zeros((HALO + TO, D), dtype=np.float32)
    lo = max(0, t0 - HALO)
    xs[HALO - (t0 - lo):HALO] = x[b, lo:t0]
    xs[HALO:] = x[b, t0:t0 + TO]
    d["xT"] = np.ascontiguousarray(xs.T)

    pos = np.arange(t0 - HALO, t0 + TO, dtype=np.float64)
    pos = np.maximum(pos, 0.0)
    frac = np.arange(0, HD, 2, dtype=np.float64) / HD
    ts = ROPE_BASE ** frac
    ang = pos[None, :] / ts[:, None]
    d["costab"] = np.cos(ang).astype(np.float32)
    d["sintab"] = np.sin(ang).astype(np.float32)

    sp = cs * TO + np.arange(2 * TG)
    nd = np.maximum(0, 1023 - sp).astype(np.float32)
    d["negnd"] = (-nd).reshape(1, 2 * TG)

    d["qsstack"] = np.stack([np.concatenate([qs[0:64], qs[0:64]]),
                             np.concatenate([qs[64:128], qs[64:128]])],
                            1).astype(np.float32)
    d["ksstack"] = np.stack([np.concatenate([ks[0:64], ks[0:64]]),
                             np.concatenate([ks[64:128], ks[64:128]])],
                            1).astype(np.float32)
    d["gates"] = gates.reshape(1, E).astype(np.float32)
    bs = np.zeros((128, 2), np.float32)
    bs[:, b] = 1.0
    d["bsel"] = bs
    d.update(shared)
    return d



class Op:
    __slots__ = ("eng", "fn", "deps", "is_dma", "need_inc", "val", "tag",
                 "stream", "semkey", "val_wait", "batch_start", "gidx",
                 "psum", "iname")

    def __init__(self, eng, fn, deps, is_dma, tag, stream):
        self.eng, self.fn, self.deps = eng, fn, deps
        self.is_dma = is_dma
        self.need_inc = False
        self.val = None
        self.val_wait = None
        self.tag = tag
        self.stream = stream
        self.semkey = None
        self.batch_start = False
        self.gidx = -1
        self.psum = None
        self.iname = None


ENG_SEM = {"tensor": "sT", "vector": "sV", "scalar": "sS", "gpsimd": "sG",
           "dsync": "qS", "dscalar": "qA", "dgp": "qG", "coll": "qC"}
ENG_EXEC = {"tensor": "tensor", "vector": "vector", "scalar": "scalar",
            "gpsimd": "gpsimd", "dsync": "sync", "dscalar": "scalar",
            "dgp": "gpsimd", "coll": "gpsimd"}


class MK:
    def __init__(self, debug=(), stop=None, serial=False, capture=False,
                 bank_edges=None):
        self.debug = set(debug)
        self.stop = stop
        self.serial = serial
        self.capture = capture
        self.bank_edges = bank_edges
        self.nc = bass.Bass("TRN2", target_bir_lowering=False)
        self.ops = []
        self.es = ExitStack()
        self.sems = {}
        self.dram = {}
        self.sb = {}
        self.ps = {}
        self.final_deps = []
        self.war = {}
        self.colT_ready = {"colkv": [None] * 16, "colq": [None] * 16}
        self.kslot_ready = {}
        self.vslot_ready = {}
        self.q_ready = {}
        self.acol_ready = [None] * 4
        self.enc_ready = {}

    def op(self, eng, fn, deps=(), tag=None, stream="s"):
        o = Op(eng, fn, [d for d in deps if d is not None],
               eng in ("dsync", "dscalar", "dgp", "coll"), tag, stream)
        o.gidx = len(self.ops)
        self.ops.append(o)
        return o

    def dbg(self, name, shape, dtype, src_fn, deps):
        if name not in self.debug:
            return None
        ap = self.nc.dram_tensor("dbg_" + name, list(shape), dtype,
                                 kind="ExternalOutput").ap()
        o = self.op("dgp", lambda ap=ap, f=src_fn:
                    self.nc.gpsimd.dma_start(out=ap, in_=f()),
                    deps, tag="dbg_" + name, stream="dbg")
        self.final_deps.append(o)
        return o

    def warlist(self, key):
        return self.war.get(key, [])

    def setwar(self, key, ops):
        self.war[key] = list(ops)

    def addwar(self, key, *ops):
        self.war.setdefault(key, []).extend(ops)

    def takewar(self, key):
        v = self.war.get(key, [])
        self.war[key] = []
        return v

    def _alloc(self):
        nc, es = self.nc, self.es

        def sbuf(name, shape, dt):
            self.sb[name] = es.enter_context(
                nc.sbuf_tensor("sb_" + name, list(shape), dt))

        def psum(name, shape, dt):
            self.ps[name] = es.enter_context(
                nc.psum_tensor(name, list(shape), dt))

        def dram_in(name, shape, dt=F32):
            self.dram[name] = nc.dram_tensor(
                name, list(shape), dt, kind="ExternalInput").ap()

        dram_in("xT", [D, HALO + TO])
        dram_in("costab", [64, HALO + TO])
        dram_in("sintab", [64, HALO + TO])
        dram_in("negnd", [1, 2 * TG])
        dram_in("aqsel", [16, 8 * 128])
        dram_in("qsstack", [128, 2])
        dram_in("ksstack", [128, 2])
        dram_in("ident", [128, 128])
        dram_in("ones", [128, 128])
        dram_in("tri", [128, 256])
        dram_in("eyeq", [128, 128])
        dram_in("eyek", [128, 32])
        dram_in("pselk", [8, 512])
        dram_in("idxlt", [32, 32])
        dram_in("bsel", [128, 2])
        dram_in("Wq", [16, NCH, 128, 128])
        dram_in("Wkv", [16, NCH, 128, 128])
        dram_in("Wout", [NCH, QH, 128, 128])
        dram_in("WvT", [E, 16384])
        dram_in("WoF", [E, 16384])
        dram_in("simM", [D, E])
        dram_in("gates", [1, E])

        self.dram["yT"] = nc.dram_tensor("yT", [D, TO], F32,
                                         kind="ExternalOutput").ap()
        self.dram["xmid"] = nc.dram_tensor("xmid", [NCH, 128, TO], BF16,
                                           kind="Internal").ap()
        self.dram["foldin"] = nc.dram_tensor("foldin", [1, 65536], BF16,
                                             kind="Internal").ap()
        self.dram["foldout"] = nc.dram_tensor("foldout", [NCORE, 65536], BF16,
                                              kind="Internal").ap()
        self.dram["seqin"] = nc.dram_tensor("seqin", [128, 18], F32,
                                            kind="Internal").ap()
        self.dram["seqout"] = nc.dram_tensor("seqout", [128 * NCORE, 18], F32,
                                             kind="Internal").ap()

        sbuf("xfm", [128, NCH * TG], BF16)
        sbuf("q1", [128, 8 * TG], BF16)
        sbuf("q2", [128, 8 * TG], BF16)
        sbuf("enc", [128, QH * TG], BF16)
        sbuf("k1", [128, 4 * RING * 128], BF16)
        sbuf("k2", [128, 4 * RING * 128], BF16)
        sbuf("vring", [128, RING * KVH * 128], BF16)
        sbuf("x32", [128, NCH * TG], F32)
        sbuf("PTr", [128, 2 * 4608], BF16)
        sbuf("wbuf", [128, 2 * 2048], BF16)
        sbuf("out32", [128, 2 * TG], F32)
        sbuf("sq", [128, 2 * TG], BF16)
        sbuf("fz", [128, 2 * TG], BF16)
        sbuf("rope", [128, 3 * TG], F32)
        self.sb["recd"] = self.sb["rope"][:, 0:2 * TG]
        self.sb["negnd"] = self.sb["rope"][0:1, 2 * TG:3 * TG]
        sbuf("tbl", [128, 2 * TG], BF16)
        sbuf("bc", [128, 2 * TG], F32)
        sbuf("vh", [128, TG], BF16)
        sbuf("small", [128, 2], F32)
        sbuf("aqsel", [16, 8 * 128], F32)
        sbuf("onesf", [1, 128], F32)
        sbuf("colsc", [128, 1], F32)
        sbuf("colscb", [128, 1], BF16)
        sbuf("ident", [128, 128], BF16)
        sbuf("fident", [128, 128], F32)
        sbuf("pselk", [8, 512], BF16)
        sbuf("ones", [128, 128], BF16)
        sbuf("fones", [128, 1], F32)
        sbuf("tri", [128, 256], F32)
        sbuf("eyeq", [128, 128], BF16)
        sbuf("eyek", [128, 32], BF16)
        sbuf("idxlt", [32, 32], F32)
        sbuf("bsel", [128, 2], F32)
        sbuf("qsstack", [128, 2], F32)
        sbuf("ksstack", [128, 2], F32)
        for nm in ("m1", "rstd1", "rowa", "rowb"):
            sbuf(nm, [1, TG], F32)
        sbuf("rowar", [1, 3 * TG], F32)
        sbuf("rowbf", [1, 2 * TG], BF16)
        sb = self.sb
        sb["m2"] = sb["rowar"][0:1, 0:TG]
        sb["rstd2"] = sb["rowar"][0:1, TG:2 * TG]
        sb["rstd3"] = sb["rowar"][0:1, 2 * TG:3 * TG]
        sb["m4"] = sb["rowar"][0:1, 0:TG]
        sb["rstd4"] = sb["rowar"][0:1, TG:2 * TG]
        sb["rowc"] = sb["rowb"]
        sb["m2b"] = sb["rowbf"][0:1, 0:TG]
        sb["rstd2b"] = sb["rowbf"][0:1, TG:2 * TG]
        sb["rstd3b"] = sb["rowbf"][0:1, 0:TG]
        sb["rstd4b"] = sb["rowbf"][0:1, 0:TG]
        sb["negm4b"] = sb["rowbf"][0:1, TG:2 * TG]
        sb["rstd1b"] = sb["rowbf"][0:1, 0:TG]
        sb["m1b"] = sb["rowbf"][0:1, TG:2 * TG]
        sbuf("m3", [1, NG * TG], BF16)
        sbuf("aq", [16, TG], F32)
        sb["ak"] = sb["aq"][0:8, :]
        sbuf("akb", [8, TG], BF16)
        sbuf("seqp", [128, 16], F32)
        sbuf("seqa", [128, 16], F32)
        sbuf("seqsb", [128, 18], F32)
        sbuf("seqg", [128, NCORE * 18], F32)
        sbuf("sumw", [1, 8], F32)
        sb["simsb"] = sb["x32"][:, 0:512]
        sbuf("cst", [128, 2], F32)
        sbuf("rmb", [32, 4], BF16)
        sb["srep"] = sb["rope"][:, 0:36]
        sb["snorm"] = sb["rope"][:, 36:52]
        sb["simsq"] = sb["rope"][:, 52:84]
        sb["gat"] = sb["rope"][0:32, 84:96]
        sb["gatm"] = sb["rope"][0:32, 96:160]
        sb["gatr"] = sb["rope"][0:1, 160:256]
        sb["grow"] = sb["rope"][0:1, 256:320]
        sb["gsc"] = sb["rope"][0:1, 320:328]
        sb["wveff"] = sb["PTr"][:, 0:NCH * EH]
        sb["colv"] = sb["PTr"][0:1, NCH * EH:NCH * EH + EH]
        sbuf("colwo", [64, 1], F32)
        sbuf("colwob", [64, 1], BF16)
        sb["x32ld"] = sb["out32"][:]
        sb["comb"] = sb["enc"][0:64, 0:TO]
        sb["woeff"] = sb["q1"][0:64, 0:D]
        sb["xmidld"] = sb["q1"][:, 2048:2048 + 2 * TG]
        sb["wfold"] = sb["q2"][0:32, 0:2048]
        sb["foldrow"] = sb["q2"][0:1, 2048:4096]
        sb["wob1"] = sb["enc"][0:64, 2048:2048 + D]

        psum("big0", [128, 1536], F32)
        psum("big1", [128, 1536], F32)
        psum("pt", [128, 512], F32)
        psum("sm", [128, 512], F32)

        for s in sorted(set(ENG_SEM.values())):
            self.sems[s] = es.enter_context(nc.semaphore(s))

    def emit_consts(self):
        nc, sb, dr = self.nc, self.sb, self.dram
        self.cload = []
        for nm in ("ident", "ones", "eyeq", "eyek", "pselk"):
            o = self.op("dgp", lambda nm=nm: nc.gpsimd.dma_start(
                out=self.sb[nm][:], in_=self.dram[nm]), (), tag="ld_" + nm, stream="c")
            self.cload.append(o)
        for nm in ("idxlt", "qsstack", "ksstack", "bsel", "tri", "aqsel"):
            o = self.op("dsync", lambda nm=nm: nc.sync.dma_start(
                out=self.sb[nm][:], in_=self.dram[nm]), (), tag="ld_" + nm, stream="c2")
            self.cload.append(o)
        o = self.op("dsync", lambda: nc.sync.dma_start(
            out=self.sb["fident"][:], in_=self.dram["ident"]), (), tag="ld_fid", stream="c2")
        self.cload.append(o)
        m0 = self.op("vector", lambda: nc.vector.memset(sb["seqa"][:], 0.0))
        c0 = self.op("vector", lambda: nc.vector.memset(
            sb["cst"][:, 0:1], EPS_LN))
        c1 = self.op("vector", lambda: nc.vector.memset(
            sb["cst"][:, 1:2], EPS_RMS))
        self.cload += [c0, c1]
        m1 = self.op("vector", lambda: nc.vector.memset(sb["fones"][:], 1.0))
        m2 = self.op("vector", lambda: nc.vector.memset(sb["onesf"][:], 1.0))
        self.cload.append(m2)
        self.memsets = [m0, m1]
        self.seqa_last = m0

    def emit_group_proj(self, gg):
        nc, sb, ps, dr = self.nc, self.sb, self.ps, self.dram
        own = gg >= 2
        g = gg - 2
        col0 = gg * TG
        gt0 = gg * 4 - 8

        src = dr["xT"][:, col0:col0 + TG].rearrange("(c p) t -> p c t", p=128)
        ld_x = self.op("dgp", lambda src=src: nc.gpsimd.dma_start(
            out=sb["xfm"][:].rearrange("p (c t) -> p c t", c=NCH), in_=src),
            self.takewar("xfm") + self.cload, tag=f"ldx{gg}", stream="x")
        self.ld_x = ld_x

        tbl_deps = self.takewar("tbl")
        ld_t = []
        for half in (0, 1):
            for j, nm in ((0, "costab"), (1, "sintab")):
                o = self.op("dgp", lambda half=half, j=j, nm=nm:
                            nc.gpsimd.dma_start(
                                out=sb["tbl"][64 * half:64 * half + 64,
                                              j * TG:(j + 1) * TG],
                                in_=dr[nm][:, col0:col0 + TG]),
                            tbl_deps, tag=f"ldt{gg}", stream="tbl")
                ld_t.append(o)
        self.ld_t = ld_t

        mean_mms, ssq_mms = [], []
        for c in range(NCH):
            mean_mms.append(self.op("tensor", lambda c=c: nc.tensor.matmul(
                ps["big1"][0:1, 512:512 + TG],
                sb["ones"][:, 0:1], sb["xfm"][:, c * TG:(c + 1) * TG],
                start=(c == 0), stop=(c == NCH - 1)),
                [ld_x] + (self.takewar("b1_4") if c == 0 else []),
                tag=f"meanmm{gg}_{c}"))
        sq_ops = []
        for c in range(NCH):
            slot = c % 2
            sqo = self.op("scalar", lambda c=c, slot=slot:
                          nc.scalar.activation(
                              sb["sq"][:, slot * TG:(slot + 1) * TG],
                              sb["xfm"][:, c * TG:(c + 1) * TG], AF.Square),
                          [ld_x] + self.takewar(f"sq{slot}"),
                          tag=f"sq{gg}_{c}")
            sq_ops.append(sqo)
            mo = self.op("tensor", lambda c=c, slot=slot:
                         nc.tensor.matmul(
                ps["big1"][0:1, 1024:1024 + TG],
                sb["ones"][:, 0:1], sb["sq"][:, slot * TG:(slot + 1) * TG],
                start=(c == 0), stop=(c == NCH - 1)),
                [sqo] + (self.takewar("b1_5") if c == 0 else []),
                tag=f"ssqmm{gg}_{c}")
            ssq_mms.append(mo)
            self.setwar(f"sq{slot}", [mo])

        m1 = self.op("scalar", lambda: nc.scalar.activation(
            sb["m1"][:], ps["big1"][0:1, 512:512 + TG], AF.Copy,
            scale=1.0 / D), [mean_mms[-1]] + self.takewar("rowsc"),
            tag=f"m1_{gg}")
        self.addwar("b1_4", m1)
        m1sq = self.op("scalar", lambda: nc.scalar.activation(
            sb["rowa"][:], sb["m1"][:], AF.Square), [m1])
        msq = self.op("scalar", lambda: nc.scalar.activation(
            sb["rowb"][:], ps["big1"][0:1, 1024:1024 + TG], AF.Copy,
            scale=1.0 / D), [ssq_mms[-1]], tag=f"msq{gg}")
        self.addwar("b1_5", msq)
        var = self.op("vector", lambda: nc.vector.tensor_sub(
            sb["rowb"][:], sb["rowb"][:], sb["rowa"][:]), [msq, m1sq])
        sd = self.op("scalar", lambda: nc.scalar.activation(
            sb["rowa"][:], sb["rowb"][:], AF.Sqrt, bias=sb["cst"][0:1, 0:1]),
            [var], tag=f"sd{gg}")
        rstd1 = self.op("vector", lambda: nc.vector.reciprocal(
            sb["rstd1"][:], sb["rowa"][:]), [sd], tag=f"rstd1_{gg}")
        rstd1b = self.op("vector", lambda: nc.vector.tensor_copy(
            sb["rstd1b"][:], sb["rstd1"][:]),
            [rstd1] + self.takewar("rbf0"), tag=f"rstd1b_{gg}")
        m1b = self.op("vector", lambda: nc.vector.tensor_copy(
            sb["m1b"][:], sb["m1"][:]), [m1] + self.takewar("rbf1"),
            tag=f"m1b_{gg}")
        bcmm = self.op("tensor", lambda: nc.tensor.matmul(
            ps["sm"][:, 0:TG], sb["ones"][0:1, :], sb["rstd1b"][:],
            start=True, stop=True),
            [rstd1b] + self.takewar("smbank"), tag=f"bc1mm{gg}")
        self.addwar("rbf0", bcmm)
        bc1 = self.op("vector", lambda: nc.vector.tensor_copy(
            sb["bc"][:, 0:TG], ps["sm"][:, 0:TG]),
            [bcmm] + self.takewar("bc0"), tag=f"bc1_{gg}")
        bcmm2 = self.op("tensor", lambda: nc.tensor.matmul(
            ps["sm"][:, 0:TG], sb["ones"][0:1, :], sb["m1b"][:],
            start=True, stop=True), [m1b, bc1], tag=f"bcm1mm{gg}")
        self.addwar("rbf1", bcmm2)
        m1bc = self.op("vector", lambda: nc.vector.tensor_copy(
            sb["bc"][:, TG:2 * TG], ps["sm"][:, 0:TG]),
            [bcmm2] + self.takewar("bc1dst"), tag=f"m1bc_{gg}")
        self.addwar("smbank", m1bc)
        self.bc1_op = bc1
        self.setwar("rowsc", [m1, m1sq, msq, var, sd, rstd1, rstd1b, m1b,
                              bcmm, bcmm2])

        xz_ops = []
        for c in range(NCH):
            o = self.op("vector", lambda c=c: nc.vector.tensor_sub(
                sb["xfm"][:, c * TG:(c + 1) * TG],
                sb["xfm"][:, c * TG:(c + 1) * TG], sb["bc"][:, TG:2 * TG]),
                [mean_mms[c], sq_ops[c], m1bc], tag=f"xz{gg}_{c}")
            xz_ops.append(o)
            self.addwar("bc1dst", o)
        self.xz_ops = xz_ops

        self._proj_pass(gg, kv=True)
        if own:
            self._proj_pass(gg, kv=False)
            self._emit_aq_rows_finalize(gg)

    def _stream_block(self, wname, gg, b):
        nc, sb, dr = self.nc, self.sb, self.dram
        slot = b % 2
        deps = self.takewar(f"wslot{slot}")
        ops = []
        for h8 in range(2):
            src = dr[wname][b, h8 * 8:(h8 + 1) * 8].rearrange(
                "c p q -> p c q")
            o = self.op("dgp", lambda src=src, slot=slot, h8=h8:
                        nc.gpsimd.dma_start(
                            out=sb["wbuf"][:, slot * 2048 + h8 * 1024:
                                           slot * 2048 + (h8 + 1) * 1024
                                           ].rearrange("p (c q) -> p c q",
                                                       c=8),
                            in_=src), deps, tag=f"w{wname}{gg}_{b}_{h8}", stream=f"w{slot}")
            ops.append(o)
        return ops

    def _block_mms(self, wname, gg, b, pslot, wdma):
        nc, sb, ps = self.nc, self.sb, self.ps
        slot = b % 2
        mm = None
        for c in range(NCH):
            deps = [wdma[c // 8], self.xz_ops[c]]
            if c == 0:
                deps += self.takewar(f"big0_{pslot}")
            mm = self.op("tensor", lambda c=c, slot=slot, pslot=pslot:
                         nc.tensor.matmul(
                             ps["big0"][:, pslot * TG:(pslot + 1) * TG],
                             sb["wbuf"][:, slot * 2048 + c * 128:
                                        slot * 2048 + (c + 1) * 128],
                             sb["xfm"][:, c * TG:(c + 1) * TG],
                             start=(c == 0), stop=(c == NCH - 1)),
                         deps, tag=f"mm{wname}{gg}_{b}_{c}")
        self.setwar(f"wslot{slot}", [mm])
        self.addwar("xfm", mm)
        return mm

    def _proj_pass(self, gg, kv):
        nc, sb, ps = self.nc, self.sb, self.ps
        wname = "Wkv" if kv else "Wq"
        gt0 = gg * 4 - 8
        slot0 = gt0 % RING

        if kv:
            units = [("k", p) for p in range(4)] + [("v", h) for h in range(8)]
        else:
            units = [("q", j) for j in range(8)]

        for kind, idx in units:
            if kind in ("k", "q"):
                b1 = 2 * idx
                wd1 = self._stream_block(wname, gg, b1)
                wd2 = self._stream_block(wname, gg, b1 + 1)
                mm1 = self._block_mms(wname, gg, b1, 0, wd1)
                mm2 = self._block_mms(wname, gg, b1 + 1, 1, wd2)
                self._pair_post(gg, kind, idx, mm1, mm2, slot0)
            else:
                b = 8 + idx
                wd = self._stream_block(wname, gg, b)
                mm = self._block_mms(wname, gg, b, idx % 2, wd)
                self._v_post(gg, idx, mm, slot0)

    def _pair_post(self, gg, kind, idx, mm1, mm2, slot0):
        nc, sb, ps = self.nc, self.sb, self.ps
        isq = kind == "q"
        stack = sb["qsstack"] if isq else sb["ksstack"]
        nunits = 8 if isq else 4
        rows_n = 16 if isq else 8
        eye = sb["eyeq"] if isq else sb["eyek"]
        eyew = 16 if isq else 8
        rowkey = "aqrows" if isq else "akrows"

        sq_ops = []
        for half, mm in ((0, mm1), (1, mm2)):
            so = self.op("scalar", lambda half=half: nc.scalar.activation(
                sb["sq"][:, half * TG:(half + 1) * TG],
                ps["big0"][:, half * TG:(half + 1) * TG], AF.Square),
                [mm] + self.takewar(f"sq{half}"),
                tag=f"rsq{gg}{kind}{idx}_{half}")
            sq_ops.append(so)
        eye_last = self.war.get(rowkey + "_last", None)
        for half in (0, 1):
            deps = [sq_ops[half]]
            if eye_last is not None:
                deps.append(eye_last)
            if isq:
                start = (idx == 0 and half == 0)
                stop = (idx == nunits - 1 and half == 1)
            else:
                start = (half == 0)
                stop = (half == 1)
            if start:
                deps += self.takewar("b1_3")
            eye_last = self.op("tensor", lambda half=half, idx=idx,
                               start=start, stop=stop, rows_n=rows_n,
                               eye=eye, eyew=eyew:
                               nc.tensor.matmul(
                ps["big1"][0:rows_n, 0:TG],
                eye[:, idx * eyew:(idx + 1) * eyew],
                sb["sq"][:, half * TG:(half + 1) * TG],
                start=start, stop=stop),
                deps, tag=f"eye{gg}{kind}{idx}_{half}")
            self.setwar(f"sq{half}", [eye_last])
        self.war[rowkey + "_last"] = eye_last
        if idx == nunits - 1:
            self.war[rowkey + "_done"] = eye_last

        rope_war = self.takewar("rope")
        t1 = self.op("vector", lambda: nc.vector.scalar_tensor_tensor(
            sb["rope"][:, 0:TG], ps["big0"][:, 0:TG], stack[:, 0:1],
            sb["tbl"][:, 0:TG], ALU.mult, ALU.mult),
            [mm1] + self.ld_t + self.cload + rope_war,
            tag=f"t1_{gg}{kind}{idx}")
        t2 = self.op("vector", lambda: nc.vector.scalar_tensor_tensor(
            sb["rope"][:, TG:2 * TG], ps["big0"][:, TG:2 * TG],
            stack[:, 1:2], sb["tbl"][:, TG:2 * TG], ALU.mult, ALU.mult),
            [mm2] + self.ld_t + rope_war, tag=f"t2_{gg}{kind}{idx}")
        r1 = self.op("vector", lambda: nc.vector.tensor_sub(
            sb["rope"][:, 0:TG], sb["rope"][:, 0:TG], sb["rope"][:, TG:2 * TG]),
            [t1, t2], tag=f"r1_{gg}{kind}{idx}")
        t3 = self.op("vector", lambda: nc.vector.scalar_tensor_tensor(
            sb["rope"][:, TG:2 * TG], ps["big0"][:, TG:2 * TG],
            stack[:, 1:2], sb["tbl"][:, 0:TG], ALU.mult, ALU.mult),
            [mm2, r1], tag=f"t3_{gg}{kind}{idx}")
        t4 = self.op("vector", lambda: nc.vector.scalar_tensor_tensor(
            sb["rope"][:, 2 * TG:3 * TG], ps["big0"][:, 0:TG],
            stack[:, 0:1], sb["tbl"][:, TG:2 * TG], ALU.mult, ALU.mult),
            [mm1], tag=f"t4_{gg}{kind}{idx}")
        r2 = self.op("vector", lambda: nc.vector.tensor_add(
            sb["rope"][:, TG:2 * TG], sb["rope"][:, TG:2 * TG],
            sb["rope"][:, 2 * TG:3 * TG]),
            [t3, t4], tag=f"r2_{gg}{kind}{idx}")
        self.addwar("tbl", t1, t2, t3, t4)
        self.setwar("big0_0", [t1, t4, sq_ops[0]])
        self.setwar("big0_1", [t2, t3, sq_ops[1]])

        if isq:
            d1 = self.takewar(f"q1_{idx}")
            o1 = self.op("vector", lambda idx=idx: nc.vector.tensor_copy(
                sb["q1"][:, idx * TG:(idx + 1) * TG], sb["rope"][:, 0:TG]),
                [r1] + d1, tag=f"qw1_{gg}_{idx}")
            d2 = self.takewar(f"q2_{idx}")
            o2 = self.op("vector", lambda idx=idx: nc.vector.tensor_copy(
                sb["q2"][:, idx * TG:(idx + 1) * TG],
                sb["rope"][:, TG:2 * TG]), [r2] + d2,
                tag=f"qw2_{gg}_{idx}")
            self.q_ready[idx] = [o1, o2]
            self.setwar("rope", [o1, o2])
        else:
            self._ak_pair_finalize(gg, idx)
            pk = idx
            base = pk * RING * 128 + slot0 * 128
            dst_deps = []
            for s in range(4):
                dst_deps += self.takewar(f"kw_{slot0 + s}_{pk}")
            ops_out = []
            for (src_slot, tens, prt) in ((0, "k1", 0), (0, "k1", 64),
                                          (1, "k2", 0), (1, "k2", 64)):
                o = self.op("vector", lambda src_slot=src_slot, tens=tens,
                            prt=prt, base=base:
                            nc.vector.tensor_mul(
                    sb[tens][prt:prt + 64, base:base + TG],
                    sb["rope"][prt:prt + 64,
                               src_slot * TG:(src_slot + 1) * TG],
                    sb["bc"][prt:prt + 64, TG:2 * TG]),
                    [r1, r2, self.akbc_op] + dst_deps,
                    tag=f"kw{gg}_{idx}_{tens}_{prt}")
                ops_out.append(o)
                self.addwar("bc1dst", o)
            self.setwar("rope", ops_out)
            for s in range(4):
                self.kslot_ready[(slot0 + s, pk)] = ops_out

    def _ak_pair_finalize(self, gg, p):
        nc, sb, ps = self.nc, self.sb, self.ps
        eye_last = self.war["akrows_last"]
        e1 = self.op("scalar", lambda: nc.scalar.activation(
            sb["ak"][:], sb["bc"][0:8, 0:TG], AF.Square),
            [self.bc1_op] + self.takewar("aq_war"), tag=f"ak1_{gg}{p}")
        self.addwar("bc0", e1)
        e2 = self.op("vector", lambda: nc.vector.scalar_tensor_tensor(
            sb["ak"][:], sb["ak"][:], 1.0 / HD,
            ps["big1"][0:8, 0:TG], ALU.mult, ALU.mult),
            [e1, eye_last], tag=f"ak2_{gg}{p}")
        self.addwar("b1_3", e2)
        e3 = self.op("scalar", lambda: nc.scalar.activation(
            sb["ak"][:], sb["ak"][:], AF.Sqrt, bias=sb["cst"][0:8, 1:2]),
            [e2], tag=f"ak3_{gg}{p}")
        e4 = self.op("vector", lambda: nc.vector.reciprocal(
            sb["ak"][:], sb["ak"][:]), [e3], tag=f"ak4_{gg}{p}")
        e5 = self.op("vector", lambda: nc.vector.tensor_mul(
            sb["ak"][:], sb["ak"][:], sb["bc"][0:8, 0:TG]),
            [e4, self.bc1_op], tag=f"ak5_{gg}{p}")
        self.addwar("bc0", e5)
        e6 = self.op("vector", lambda: nc.vector.tensor_copy(
            sb["akb"][:], sb["ak"][:]), [e5], tag=f"ak6_{gg}{p}")
        self.setwar("aq_war", [e6])
        bcd = self.takewar("smbank")
        bm = self.op("tensor", lambda p=p: nc.tensor.matmul(
            ps["sm"][:, 0:TG], sb["pselk"][:, p * 128:(p + 1) * 128],
            sb["akb"][:], start=True, stop=True),
            [e6] + bcd, tag=f"akbc{gg}{p}")
        ev = self.op("vector", lambda: nc.vector.tensor_copy(
            sb["bc"][:, TG:2 * TG], ps["sm"][:, 0:TG]),
            [bm] + self.takewar("bc1dst"), tag=f"akbcev{gg}{p}")
        self.addwar("smbank", ev)
        self.akbc_op = ev

    def _v_post(self, gg, h, mm, slot0):
        nc, sb, ps = self.nc, self.sb, self.ps
        pslot = h % 2
        ev = self.op("vector", lambda pslot=pslot:
                     nc.vector.scalar_tensor_tensor(
            sb["vh"][:], ps["big0"][:, pslot * TG:(pslot + 1) * TG], 0.0,
            sb["bc"][:, 0:TG], ALU.bypass, ALU.mult),
            [mm, self.bc1_op] + self.takewar("vh"), tag=f"vev{gg}_{h}")
        self.addwar("bc0", ev)
        self.setwar(f"big0_{pslot}", [ev])
        tps = []
        ptw = self.takewar("ptbank")
        for ti in range(4):
            o = self.op("tensor", lambda ti=ti: nc.tensor.matmul(
                ps["pt"][:, ti * 128:(ti + 1) * 128],
                sb["vh"][:, ti * 128:(ti + 1) * 128], sb["ident"][:],
                start=True, stop=True),
                [ev] + (ptw if ti == 0 else []), tag=f"vtp{gg}_{h}_{ti}")
            tps.append(o)
        dep_ring = []
        for s in range(4):
            dep_ring += self.takewar(f"vw_{slot0 + s}_{h}")
        evv = self.op("vector", lambda h=h, slot0=slot0:
                      nc.vector.tensor_copy(
            sb["vring"][:].rearrange("p (s h q) -> p s h q",
                                     s=RING, h=KVH)[:, slot0:slot0 + 4, h],
            ps["pt"][:, 0:512].rearrange("p (s q) -> p s q", s=4)),
            tps + dep_ring, tag=f"vrev{gg}_{h}")
        self.setwar("ptbank", [evv])
        self.addwar("vh", tps[-1])
        for s in range(4):
            self.vslot_ready[(slot0 + s, h)] = evv

    def _emit_aq_rows_finalize(self, gg):
        nc, sb, ps = self.nc, self.sb, self.ps
        eye_done = self.war["aqrows_done"]
        e1 = self.op("scalar", lambda: nc.scalar.activation(
            sb["aq"][:], sb["bc"][0:16, 0:TG], AF.Square),
            [self.bc1_op] + self.takewar("aq_war"), tag=f"aq1_{gg}")
        self.addwar("bc0", e1)
        e2 = self.op("vector", lambda: nc.vector.scalar_tensor_tensor(
            sb["aq"][:], sb["aq"][:], 1.0 / HD,
            ps["big1"][0:16, 0:TG], ALU.mult, ALU.mult),
            [e1, eye_done], tag=f"aq2_{gg}")
        self.addwar("b1_3", e2)
        e3 = self.op("scalar", lambda: nc.scalar.activation(
            sb["aq"][:], sb["aq"][:], AF.Sqrt,
            bias=sb["cst"][0:16, 1:2]), [e2])
        e4 = self.op("vector", lambda: nc.vector.reciprocal(
            sb["aq"][:], sb["aq"][:]), [e3])
        e5 = self.op("vector", lambda: nc.vector.scalar_tensor_tensor(
            sb["aq"][:], sb["bc"][0:16, 0:TG], ATTN_SCALE,
            sb["aq"][:], ALU.mult, ALU.mult), [e4, self.bc1_op],
            tag=f"aq5_{gg}")
        self.addwar("bc0", e5)
        self.setwar("aq_war", [])
        for j in range(8):
            bankkey = "ptbank" if j % 2 == 0 else "smbank"
            dps = ps["pt"] if j % 2 == 0 else ps["sm"]
            deps = [e5] + self.cload + self.takewar(bankkey)
            bm = self.op("tensor", lambda j=j, dps=dps: nc.tensor.matmul(
                dps[:, 0:TG], sb["aqsel"][:, j * 128:(j + 1) * 128],
                sb["aq"][0:16, 0:TG], start=True, stop=True),
                deps, tag=f"qsc{gg}_{j}")
            self.addwar("aq_war", bm)
            m1d = self.q_ready.get(j, [])
            o1 = self.op("vector", lambda j=j, dps=dps: nc.vector.tensor_mul(
                sb["q1"][:, j * TG:(j + 1) * TG],
                sb["q1"][:, j * TG:(j + 1) * TG], dps[:, 0:TG]),
                [bm] + list(m1d), tag=f"qm1{gg}_{j}")
            o2 = self.op("vector", lambda j=j, dps=dps: nc.vector.tensor_mul(
                sb["q2"][:, j * TG:(j + 1) * TG],
                sb["q2"][:, j * TG:(j + 1) * TG], dps[:, 0:TG]),
                [bm] + list(m1d), tag=f"qm2{gg}_{j}")
            self.setwar(bankkey, [o1, o2])
            self.addwar(f"q1_{j}", o1)
            self.addwar(f"q2_{j}", o2)
            self.q_ready[j] = [o1, o2]

    def emit_group_attn(self, gg):
        nc, sb, ps = self.nc, self.sb, self.ps
        g = gg - 2
        t0 = g * 4
        self.enc_ready = {}

        entry = {0: (self.takewar("big0_0") + self.takewar("big0_1") +
                     self.takewar("S0")),
                 1: (self.takewar("b1_3") + self.takewar("b1_4") +
                     self.takewar("b1_5") + self.takewar("S1"))}
        entry_used = {0: False, 1: False}

        ndld = None
        if g < 2:
            nddeps = (list(self.q_ready.get(7, ())) +
                      self.takewar("ndld") + self.cload)
            ndld = self.op("dsync", lambda g=g: nc.sync.dma_start(
                out=sb["negnd"][0:1, 0:TG],
                in_=self.dram["negnd"][0:1, g * TG:(g + 1) * TG]),
                nddeps, tag=f"ndld{gg}", stream="c2")

        ALIVE = []
        for jj in range(12):
            if jj < 4:
                ALIVE.append((0, (jj + 1) * 128))
            elif jj < 8:
                ALIVE.append((0, 512))
            else:
                ALIVE.append(((jj - 8) * 128, 512))
        PCOL = []
        c = 0
        for jj in range(12):
            PCOL.append(c)
            lo, hi = ALIVE[jj]
            c += hi - lo
        DORD = [3, 0, 1, 2] + list(range(4, 12))

        exps = {}
        enc_last = {}

        def s_block(h, jj):
            par = h % 2
            pk = h // 4
            qj, qhalf = q_block_of(h)
            base = 64 * qhalf
            slot = (t0 - 8 + jj) % RING
            lo, hi = ALIVE[jj]
            big = ps["big0"] if par == 0 else ps["big1"]
            pp = jj % 2
            soff = pp * TG
            last = None
            for half, (qt, kt) in enumerate((("q1", "k1"), ("q2", "k2"))):
                deps = list(self.q_ready.get(qj, ())) + \
                    self.kslot_ready.get((slot, pk), [])
                if half == 0:
                    deps += self.takewar(f"sbuf{par}{pp}")
                    if not entry_used[par]:
                        deps += entry[par]
                o = self.op("tensor", lambda par=par, lo=lo, hi=hi, soff=soff,
                            base=base, qt=qt, kt=kt, qj=qj, pk=pk, slot=slot,
                            half=half, big=big: nc.tensor.matmul(
                    big[:, soff + lo:soff + hi],
                    sb[kt][base:base + 64,
                           pk * RING * 128 + slot * 128:
                           pk * RING * 128 + (slot + 1) * 128],
                    sb[qt][base:base + 64, qj * TG + lo:qj * TG + hi],
                    start=(half == 0), stop=(half == 1)),
                    deps, tag=f"S{gg}_{h}_{jj}_{half}")
                self.addwar(f"kw_{slot}_{pk}", o)
                self.addwar(f"q1_{qj}" if half == 0 else f"q2_{qj}", o)
                last = o
            mask = None
            if jj < 4:
                mask = self.op("vector", lambda big=big, soff=soff, jj=jj:
                               nc.vector.tensor_add(
                    big[:, soff + jj * 128:soff + (jj + 1) * 128],
                    big[:, soff + jj * 128:soff + (jj + 1) * 128],
                    sb["tri"][:, 0:128]), [last] + self.cload,
                    tag=f"mask{gg}_{h}_{jj}")
            elif jj >= 8:
                mask = self.op("vector", lambda big=big, soff=soff, jj=jj:
                               nc.vector.tensor_add(
                    big[:, soff + (jj - 8) * 128:soff + (jj - 7) * 128],
                    big[:, soff + (jj - 8) * 128:soff + (jj - 7) * 128],
                    sb["tri"][:, 128:256]), [last],
                    tag=f"mask{gg}_{h}_{jj}")
            pwar = self.takewar(f"ptr{par}_{jj}")
            ex = self.op("scalar", lambda par=par, lo=lo, hi=hi, soff=soff,
                         jj=jj, big=big: nc.scalar.activation(
                sb["PTr"][:, par * 4608 + PCOL[jj]:
                          par * 4608 + PCOL[jj] + (hi - lo)],
                big[:, soff + lo:soff + hi], AF.Exp),
                ([mask] if mask is not None else [last]) + pwar,
                tag=f"exp{gg}_{h}_{jj}")
            self.setwar(f"sbuf{par}{pp}", [ex])
            return ex

        def pvd_block(h, jj, first, last_pv, last_d):
            par = h % 2
            kvh = h // 2
            slot = (t0 - 8 + jj) % RING
            lo, hi = ALIVE[jj]
            big = ps["big0"] if par == 0 else ps["big1"]
            ex = exps[h][jj]
            vr = self.vslot_ready.get((slot, kvh))
            deps = [ex] + ([vr] if vr is not None else [])
            if first:
                deps += self.takewar(f"encp{par}")
                if not entry_used[par]:
                    deps += entry[par]
                    entry_used[par] = True
            pv = self.op("tensor", lambda par=par, lo=lo, hi=hi, jj=jj,
                         slot=slot, kvh=kvh, first=first, last_pv=last_pv,
                         big=big: nc.tensor.matmul(
                big[:, 1024 + lo:1024 + hi],
                sb["vring"][:, slot * KVH * 128 + kvh * 128:
                            slot * KVH * 128 + (kvh + 1) * 128],
                sb["PTr"][:, par * 4608 + PCOL[jj]:
                          par * 4608 + PCOL[jj] + (hi - lo)],
                start=first, stop=last_pv), deps, tag=f"pv{gg}_{h}_{jj}")
            self.addwar(f"vw_{slot}_{kvh}", pv)
            dps = ps["pt"] if par == 0 else ps["sm"]
            ddeps = [ex] + self.cload
            if first:
                ddeps += self.takewar("ptbank" if par == 0 else "smbank")
            dm = self.op("tensor", lambda par=par, lo=lo, hi=hi, jj=jj,
                         first=first, last_d=last_d, dps=dps:
                         nc.tensor.matmul(
                dps[:, lo:hi], sb["ones"][:, 0:128],
                sb["PTr"][:, par * 4608 + PCOL[jj]:
                          par * 4608 + PCOL[jj] + (hi - lo)],
                start=first, stop=last_d), ddeps, tag=f"dm{gg}_{h}_{jj}")
            self.addwar(f"ptr{par}_{jj}", pv, dm)
            return pv, dm

        def finish_head(h, pvlast, dlast):
            par = h % 2
            dps = ps["pt"] if par == 0 else ps["sm"]
            if g < 2:
                dlast = self.op("tensor", lambda dps=dps:
                                nc.tensor.matmul(
                    dps[:, 0:TG], sb["onesf"][0:1, 0:128],
                    sb["negnd"][0:1, 0:TG],
                    start=False, stop=True), [dlast, ndld] + self.cload,
                    tag=f"dnd{gg}_{h}")
                self.addwar("ndld", dlast)
                self.addwar("rope", dlast)
            rc = self.op("vector", lambda par=par, dps=dps:
                         nc.vector.reciprocal(
                sb["recd"][:, par * TG:(par + 1) * TG], dps[:, 0:TG]),
                [dlast], tag=f"rcp{gg}_{h}")
            self.setwar("ptbank" if par == 0 else "smbank", [rc])
            big = ps["big0"] if par == 0 else ps["big1"]
            edeps = [pvlast, rc]
            for t in range(4):
                edeps += self.takewar(f"encdst_{h}_{t}")
            ev = self.op("vector", lambda par=par, h=h, big=big:
                         nc.vector.tensor_mul(
                sb["enc"][:, h * TG:(h + 1) * TG],
                sb["recd"][:, par * TG:(par + 1) * TG],
                big[:, 1024:1536]), edeps, tag=f"encev{gg}_{h}")
            self.setwar(f"encp{par}", [ev])
            self.addwar("rope", ev)
            self.enc_ready.setdefault(h, []).append(ev)
            enc_last[par] = ev

        for step in range(QH + 1):
            h = step
            hp = step - 1
            for jj in range(12):
                if h < QH:
                    exps.setdefault(h, {})[jj] = s_block(h, jj)
                if hp >= 0:
                    dj = DORD[jj]
                    pv, dm = pvd_block(hp, dj, jj == 0,
                                       jj == 11, jj == 11 and (g >= 2))
                    if jj == 11:
                        finish_head(hp, pv, dm)

        self.setwar("S0", self.warlist("sbuf00") + self.warlist("sbuf01") +
                    [enc_last[0]])
        self.setwar("S1", self.warlist("sbuf10") + self.warlist("sbuf11") +
                    [enc_last[1]])

    def emit_group_out(self, gg):
        nc, sb, ps = self.nc, self.sb, self.ps
        g = gg - 2
        col0 = gg * TG

        s0war = self.takewar("S0")
        self.setwar("big0_0", s0war + self.warlist("big0_0"))
        self.addwar("big0_1", *s0war)
        s1war = self.takewar("S1")
        self.addwar("b1_3", *s1war)
        self.addwar("b1_4", *s1war)
        self.addwar("b1_5", *s1war)
        self.setwar("smbank", self.takewar("encbank") +
                    self.warlist("smbank"))

        mean_last = None
        ssq_last = None
        evac_ops = []
        for c in range(NCH):
            slot = c % 2
            wd = []
            wdeps = self.takewar(f"wslot{slot}")
            for h8 in range(2):
                src = self.dram["Wout"][c, h8 * 8:(h8 + 1) * 8].rearrange(
                    "n p q -> p n q")
                o = self.op("dgp", lambda src=src, slot=slot, h8=h8:
                            nc.gpsimd.dma_start(
                    out=sb["wbuf"][:, slot * 2048 + h8 * 1024:
                                   slot * 2048 + (h8 + 1) * 1024
                                   ].rearrange("p (n q) -> p n q", n=8),
                    in_=src), wdeps, tag=f"wout{g}_{c}_{h8}", stream=f"w{slot}")
                wd.append(o)
            mm = None
            for n in range(QH):
                deps = [wd[n // 8]] + self.enc_ready.get(n, [])
                if n == 0:
                    deps += self.takewar(f"big0_{slot}")
                mm = self.op("tensor", lambda n=n, slot=slot: nc.tensor.matmul(
                    ps["big0"][:, slot * TG:(slot + 1) * TG],
                    sb["wbuf"][:, slot * 2048 + n * 128:
                               slot * 2048 + (n + 1) * 128],
                    sb["enc"][:, n * TG:(n + 1) * TG],
                    start=(n == 0), stop=(n == QH - 1)),
                    deps, tag=f"op{g}_{c}_{n}")
                for tix4 in range(4):
                    self.addwar(f"encdst_{n}_{tix4}", mm)
            self.setwar(f"wslot{slot}", [mm])
            ev = self.op("scalar", lambda c=c, slot=slot: nc.scalar.activation(
                sb["x32"][:, c * TG:(c + 1) * TG],
                ps["big0"][:, slot * TG:(slot + 1) * TG], AF.Copy),
                [mm] + self.takewar(f"x32_{c}"), tag=f"atev{g}_{c}")
            cb = self.op("vector", lambda c=c, slot=slot: nc.vector.tensor_copy(
                sb["sq"][:, slot * TG:(slot + 1) * TG],
                sb["x32"][:, c * TG:(c + 1) * TG]),
                [ev] + self.takewar(f"sq{slot}"), tag=f"atb{g}_{c}")
            mean_last = self.op("tensor", lambda slot=slot, c=c:
                                nc.tensor.matmul(
                ps["big1"][0:1, 512:512 + TG], sb["ones"][:, 0:1],
                sb["sq"][:, slot * TG:(slot + 1) * TG],
                start=(c == 0), stop=(c == NCH - 1)),
                [cb] + (self.takewar("b1_4") if c == 0 else []),
                tag=f"m2mm{g}_{c}")
            self.setwar(f"sq{slot}", [mean_last])
            sq2 = self.op("scalar", lambda c=c, slot=slot: nc.scalar.activation(
                sb["fz"][:, slot * TG:(slot + 1) * TG],
                ps["big0"][:, slot * TG:(slot + 1) * TG], AF.Square),
                [mm, ev] + self.takewar(f"fz{slot}"), tag=f"atsq{g}_{c}")
            self.setwar(f"big0_{slot}", [ev, sq2])
            ssq_last = self.op("tensor", lambda slot=slot, c=c:
                               nc.tensor.matmul(
                ps["big1"][0:1, 1024:1024 + TG], sb["ones"][:, 0:1],
                sb["fz"][:, slot * TG:(slot + 1) * TG],
                start=(c == 0), stop=(c == NCH - 1)),
                [sq2] + (self.takewar("b1_5") if c == 0 else []),
                tag=f"s2mm{g}_{c}")
            self.setwar(f"fz{slot}", [ssq_last])
            evac_ops.append(ev)

        m2 = self.op("scalar", lambda: nc.scalar.activation(
            sb["m2"][:], ps["big1"][0:1, 512:512 + TG], AF.Copy,
            scale=1.0 / D), [mean_last] + self.takewar("rowsc"),
            tag=f"m2_{g}")
        self.addwar("b1_4", m2)
        m2sq = self.op("scalar", lambda: nc.scalar.activation(
            sb["rowa"][:], sb["m2"][:], AF.Square), [m2])
        msq2 = self.op("scalar", lambda: nc.scalar.activation(
            sb["rowb"][:], ps["big1"][0:1, 1024:1024 + TG], AF.Copy,
            scale=1.0 / D), [ssq_last], tag=f"msq2_{g}")
        self.addwar("b1_5", msq2)
        var2 = self.op("vector", lambda: nc.vector.tensor_sub(
            sb["rowa"][:], sb["rowb"][:], sb["rowa"][:]), [msq2, m2sq])
        sd2 = self.op("scalar", lambda: nc.scalar.activation(
            sb["rowb"][:], sb["rowa"][:], AF.Sqrt,
            bias=sb["cst"][0:1, 0:1]), [var2])
        rstd2 = self.op("vector", lambda: nc.vector.reciprocal(
            sb["rstd2"][:], sb["rowb"][:]), [sd2], tag=f"rstd2_{g}")
        m2b = self.op("vector", lambda: nc.vector.tensor_copy(
            sb["m2b"][:], sb["m2"][:]), [m2] + self.takewar("rbf0"))
        rstd2b = self.op("vector", lambda: nc.vector.tensor_copy(
            sb["rstd2b"][:], sb["rstd2"][:]), [rstd2] + self.takewar("rbf1"))
        bcm = self.op("tensor", lambda: nc.tensor.matmul(
            ps["sm"][:, 0:TG], sb["ones"][0:1, :], sb["m2b"][:],
            start=True, stop=True), [m2b] + self.takewar("smbank"),
            tag=f"bcm2_{g}")
        bc_m2 = self.op("vector", lambda: nc.vector.tensor_copy(
            sb["bc"][:, 0:TG], ps["sm"][:, 0:TG]),
            [bcm] + self.takewar("bc0"), tag=f"bcm2e_{g}")
        bcr = self.op("tensor", lambda: nc.tensor.matmul(
            ps["sm"][:, 0:TG], sb["ones"][0:1, :], sb["rstd2b"][:],
            start=True, stop=True), [rstd2b, bc_m2], tag=f"bcr2_{g}")
        self.addwar("rbf1", bcr)
        bc_r2 = self.op("vector", lambda: nc.vector.tensor_copy(
            sb["bc"][:, TG:2 * TG], ps["sm"][:, 0:TG]),
            [bcr] + self.takewar("bc1dst"), tag=f"bcr2e_{g}")
        self.addwar("smbank", bc_r2)
        self.setwar("rowsc", [m2, m2sq, msq2, var2, sd2, rstd2, m2b,
                              rstd2b, bcm, bcr])

        mean3_last = None
        ssq3_last = None
        for c in range(NCH):
            slot = c % 2
            xld = self.op("dsync", lambda c=c, slot=slot, col0=col0:
                          nc.sync.dma_start(
                out=sb["x32ld"][:, slot * TG:(slot + 1) * TG],
                in_=self.dram["xT"][c * 128:(c + 1) * 128,
                                    col0:col0 + TG]),
                self.takewar(f"xo32_{slot}"), tag=f"xld{g}_{c}", stream=f"xl{slot}")
            u1 = self.op("vector", lambda c=c: nc.vector.tensor_sub(
                sb["x32"][:, c * TG:(c + 1) * TG],
                sb["x32"][:, c * TG:(c + 1) * TG], sb["bc"][:, 0:TG]),
                [evac_ops[c], bc_m2], tag=f"sw2a{g}_{c}")
            u2 = self.op("vector", lambda c=c: nc.vector.tensor_mul(
                sb["x32"][:, c * TG:(c + 1) * TG],
                sb["x32"][:, c * TG:(c + 1) * TG], sb["bc"][:, TG:2 * TG]),
                [u1, bc_r2], tag=f"sw2b{g}_{c}")
            self.addwar("bc0", u1)
            self.addwar("bc1dst", u2)
            u3 = self.op("vector", lambda c=c, slot=slot: nc.vector.tensor_add(
                sb["x32"][:, c * TG:(c + 1) * TG],
                sb["x32"][:, c * TG:(c + 1) * TG],
                sb["x32ld"][:, slot * TG:(slot + 1) * TG]),
                [u2, xld], tag=f"sw2c{g}_{c}")
            self.addwar(f"xo32_{slot}", u3)
            self.sweep2_u3 = getattr(self, "sweep2_u3", {})
            self.sweep2_u3[c] = u3
            xmb = self.op("vector", lambda c=c, slot=slot: nc.vector.tensor_copy(
                sb["sq"][:, slot * TG:(slot + 1) * TG],
                sb["x32"][:, c * TG:(c + 1) * TG]),
                [u3] + self.takewar(f"sq{slot}"), tag=f"xmb{g}_{c}")
            st = self.op("dsync", lambda c=c, slot=slot, g=g:
                         nc.sync.dma_start(
                out=self.dram["xmid"][c, :, g * TG:(g + 1) * TG],
                in_=sb["sq"][:, slot * TG:(slot + 1) * TG]),
                [xmb], tag=f"xmst{g}_{c}", stream=f"xms{slot}")
            self.xmid_stored.append(st)
            mean3_last = self.op("tensor", lambda slot=slot, c=c:
                                 nc.tensor.matmul(
                ps["big1"][0:1, 512:512 + TG], sb["ones"][:, 0:1],
                sb["sq"][:, slot * TG:(slot + 1) * TG],
                start=(c == 0), stop=(c == NCH - 1)),
                [xmb] + (self.takewar("b1_4") if c == 0 else []),
                tag=f"m3mm{g}_{c}")
            self.setwar(f"sq{slot}", [mean3_last, st])
            sq3 = self.op("scalar", lambda c=c, slot=slot: nc.scalar.activation(
                sb["fz"][:, slot * TG:(slot + 1) * TG],
                sb["x32"][:, c * TG:(c + 1) * TG], AF.Square),
                [u3] + self.takewar(f"fz{slot}"), tag=f"sq3{g}_{c}")
            ssq3_last = self.op("tensor", lambda slot=slot, c=c:
                                nc.tensor.matmul(
                ps["big1"][0:1, 1024:1024 + TG], sb["ones"][:, 0:1],
                sb["fz"][:, slot * TG:(slot + 1) * TG],
                start=(c == 0), stop=(c == NCH - 1)),
                [sq3] + (self.takewar("b1_5") if c == 0 else []),
                tag=f"s3mm{g}_{c}")
            self.setwar(f"fz{slot}", [ssq3_last])

        m3 = self.op("scalar", lambda: nc.scalar.activation(
            sb["rowc"][:], ps["big1"][0:1, 512:512 + TG], AF.Copy,
            scale=1.0 / D), [mean3_last] + self.takewar("rowsc"),
            tag=f"m3_{g}")
        self.addwar("b1_4", m3)
        negm3 = self.op("scalar", lambda g=g: nc.scalar.activation(
            sb["m3"][:, g * TG:(g + 1) * TG], sb["rowc"][:], AF.Copy,
            scale=-1.0), [m3], tag=f"negm3_{g}")
        self.negm3_ops[g] = negm3
        m3sq = self.op("scalar", lambda: nc.scalar.activation(
            sb["rowa"][:], sb["rowc"][:], AF.Square), [m3])
        msq3 = self.op("scalar", lambda: nc.scalar.activation(
            sb["rowb"][:], ps["big1"][0:1, 1024:1024 + TG], AF.Copy,
            scale=1.0 / D), [ssq3_last, negm3, m3sq], tag=f"msq3_{g}")
        self.addwar("b1_5", msq3)
        var3 = self.op("vector", lambda: nc.vector.tensor_sub(
            sb["rowa"][:], sb["rowb"][:], sb["rowa"][:]), [msq3, m3sq])
        sd3 = self.op("scalar", lambda: nc.scalar.activation(
            sb["rowb"][:], sb["rowa"][:], AF.Sqrt,
            bias=sb["cst"][0:1, 0:1]), [var3])
        rstd3 = self.op("vector", lambda: nc.vector.reciprocal(
            sb["rstd3"][:], sb["rowb"][:]), [sd3], tag=f"rstd3_{g}")
        rstd3b = self.op("vector", lambda: nc.vector.tensor_copy(
            sb["rstd3b"][:], sb["rstd3"][:]),
            [rstd3] + self.takewar("rbf0"))
        bcm3 = self.op("tensor", lambda: nc.tensor.matmul(
            ps["sm"][:, 0:TG], sb["ones"][0:1, :], sb["rstd3b"][:],
            start=True, stop=True), [rstd3b] + self.takewar("smbank"),
            tag=f"bc3_{g}")
        self.addwar("rbf0", bcm3)
        bc_r3 = self.op("vector", lambda: nc.vector.tensor_copy(
            sb["bc"][:, 0:TG], ps["sm"][:, 0:TG]),
            [bcm3, bc_m2] + self.takewar("bc0"), tag=f"bc3e_{g}")
        self.addwar("smbank", bc_r3)
        self.setwar("rowsc", [m3, negm3, m3sq, msq3, var3, sd3, rstd3,
                              rstd3b, bcm3])

        stt_ops = []
        u3s = self.sweep2_u3
        for c in range(NCH):
            slot = c % 2
            o = self.op("vector", lambda c=c, slot=slot:
                        nc.vector.scalar_tensor_tensor(
                sb["fz"][:, slot * TG:(slot + 1) * TG],
                sb["x32"][:, c * TG:(c + 1) * TG], 0.0,
                sb["bc"][:, 0:TG], ALU.bypass, ALU.mult,
                accum_out=sb["seqp"][:, c:c + 1]),
                [bc_r3, u3s[c]] + self.takewar(f"fz{slot}") +
                self.takewar(f"seqp{c}"), tag=f"stt{g}_{c}")
            self.addwar("bc0", o)
            self.setwar(f"fz{slot}", [o])
            self.addwar(f"x32_{c}", o)
            stt_ops.append(o)
        acc = self.op("vector", lambda: nc.vector.tensor_add(
            sb["seqa"][:], sb["seqa"][:], sb["seqp"][:]),
            stt_ops + [self.seqa_last], tag=f"seqacc{g}")
        self.seqa_last = acc
        for c in range(NCH):
            self.addwar(f"seqp{c}", acc)
        sg = self.op("vector", lambda g=g: nc.vector.tensor_mul(
            sb["rowc"][:], sb["rstd3"][:], sb["m3"][:, g * TG:(g + 1) * TG]),
            [rstd3, negm3, m3sq], tag=f"sig1_{g}")
        sgs = self.op("vector", lambda g=g: nc.vector.reduce_sum(
            sb["sumw"][0:1, g:g + 1], sb["rowc"][:], axis=AX.X),
            [sg], tag=f"sig2_{g}")
        self.sumw_ops = getattr(self, "sumw_ops", []) + [sgs]
        self.addwar("rowsc", sg, sgs)

    def _row_transpose(self, tag, src_fn, dst_fn, deps, region):
        nc, sb, ps = self.nc, self.sb, self.ps
        smw = self.takewar("smbank")
        tp = self.op("tensor", lambda: nc.tensor.transpose(
            ps["sm"][0:32, region:region + 1] if True else None,
            src_fn(), sb["fident"][0:1, 0:1]), deps + smw,
            tag=tag + "_tp")
        ev = self.op("vector", lambda: nc.vector.tensor_copy(
            dst_fn(), ps["sm"][0:32, region:region + 1]), [tp],
            tag=tag + "_ev")
        self.addwar("smbank", ev)
        return ev

    def _col_transpose(self, tag, src_fn, dst_fn, deps, region):
        nc, sb, ps = self.nc, self.sb, self.ps
        smw = self.takewar("smbank")
        tp = self.op("tensor", lambda: nc.tensor.transpose(
            ps["sm"][0:1, region:region + 32],
            src_fn(), sb["fident"][0:32, 0:32]), deps + smw,
            tag=tag + "_tp")
        ev = self.op("vector", lambda: nc.vector.tensor_copy(
            dst_fn(), ps["sm"][0:1, region:region + 32]), [tp],
            tag=tag + "_ev")
        self.addwar("smbank", ev)
        return ev

    def emit_gating(self):
        nc, sb, ps, dr = self.nc, self.sb, self.ps, self.dram

        freed = []
        for j in range(8):
            freed += self.takewar(f"q1_{j}") + self.takewar(f"q2_{j}")
        for n in range(QH):
            for t4 in range(4):
                freed += self.takewar(f"encdst_{n}_{t4}")
        for p in (0, 1):
            for jj in range(12):
                freed += self.takewar(f"ptr{p}_{jj}")
        for k in ("wfold0", "foldrow0", "woeff", "wob1", "wveff",
                  "xmidld0", "xmidld1", "comb_ps"):
            self.setwar(k, freed + self.warlist(k))

        ganchor = self.takewar("rope")
        self.ganchor = ganchor

        z = self.op("vector", lambda: nc.vector.memset(
            sb["seqsb"][:], 0.0), (), tag="seqz")
        cp = self.op("vector", lambda: nc.vector.tensor_copy(
            sb["seqsb"][:, 0:16], sb["seqa"][:]), [z, self.seqa_last],
            tag="seqcp")
        ssum = self.op("vector", lambda: nc.vector.reduce_sum(
            sb["sumw"][0:1, 6:7], sb["sumw"][0:1, 0:4], axis=AX.X),
            self.sumw_ops, tag="sigsum")
        sput = self.op("vector", lambda: nc.vector.tensor_copy(
            sb["seqsb"][0:1, 16:17], sb["sumw"][0:1, 6:7]), [ssum, z],
            tag="sigput")
        st = self.op("dsync", lambda: nc.sync.dma_start(
            out=dr["seqin"], in_=sb["seqsb"][:]), [cp, sput], tag="seqst", stream="g")
        coll = self.op("coll", lambda: nc.gpsimd.collective_compute(
            "AllGather", ALU.bypass,
            replica_groups=[list(range(NCORE))],
            ins=[dr["seqin"][:]], outs=[dr["seqout"][:]]), [st], tag="ag")
        ldg = self.op("dsync", lambda: nc.sync.dma_start(
            out=sb["seqg"][:].rearrange("p (r c) -> p r c", r=NCORE),
            in_=dr["seqout"].rearrange("(r p) c -> p r c", p=128)),
            [coll], tag="seqld", stream="g")

        sums = {}
        for b in range(2):
            t1 = self.op("vector", lambda b=b: nc.vector.tensor_add(
                sb["srep"][:, b * 18:(b + 1) * 18],
                sb["seqg"][:, (4 * b) * 18:(4 * b) * 18 + 18],
                sb["seqg"][:, (4 * b + 1) * 18:(4 * b + 1) * 18 + 18]),
                [ldg], tag=f"bs{b}a")
            t2 = self.op("vector", lambda b=b: nc.vector.tensor_add(
                sb["srep"][:, b * 18:(b + 1) * 18],
                sb["srep"][:, b * 18:(b + 1) * 18],
                sb["seqg"][:, (4 * b + 2) * 18:(4 * b + 2) * 18 + 18]),
                [t1], tag=f"bs{b}b")
            t3 = self.op("vector", lambda b=b: nc.vector.tensor_add(
                sb["srep"][:, b * 18:(b + 1) * 18],
                sb["srep"][:, b * 18:(b + 1) * 18],
                sb["seqg"][:, (4 * b + 3) * 18:(4 * b + 3) * 18 + 18]),
                [t2], tag=f"bs{b}c")
            sums[b] = t3

        sld = self.op("dsync", lambda: nc.sync.dma_start(
            out=sb["simsb"][:].rearrange("p (c e) -> p c e", c=NCH),
            in_=dr["simM"].rearrange("(c p) e -> p c e", p=128)),
            self.takewar("x32_0"), tag="simld", stream="g")
        gld = self.op("dsync", lambda: nc.sync.dma_start(
            out=sb["gatr"][0:1, 0:32], in_=dr["gates"]), ganchor, tag="gld",
            stream="g")
        sig = self.op("scalar", lambda: nc.scalar.activation(
            sb["gatr"][0:1, 32:64], sb["gatr"][0:1, 0:32], AF.Sigmoid),
            [gld], tag="sig")
        sigc = self._row_transpose("sigT", lambda: sb["gatr"][0:1, 32:64],
                                   lambda: sb["gat"][0:32, 8:9],
                                   [sig], 160)

        simsq_last = None
        for c in range(NCH):
            sqq = self.op("scalar", lambda c=c: nc.scalar.activation(
                sb["simsq"][:], sb["simsb"][:, c * 32:(c + 1) * 32],
                AF.Square), [sld] + ([simsq_last] if simsq_last else
                                     ganchor),
                tag=f"simsq{c}")
            self.addwar("x32_0", sqq)
            simsq_last = self.op("tensor", lambda c=c: nc.tensor.matmul(
                ps["sm"][0:1, 8:40], sb["fones"][:], sb["simsq"][:],
                start=(c == 0), stop=(c == NCH - 1)),
                [sqq, self.memsets[1]] +
                (self.takewar("smbank") if c == 0 else []),
                tag=f"simmm{c}")
        scn = self.op("scalar", lambda: nc.scalar.activation(
            sb["grow"][0:1, 0:32], ps["sm"][0:1, 8:40], AF.Sqrt),
            [simsq_last], tag="scn")
        scr = self.op("vector", lambda: nc.vector.reciprocal(
            sb["grow"][0:1, 32:64], sb["grow"][0:1, 0:32]), [scn], tag="scr")
        self.addwar("smbank", scn)
        scrc = self._row_transpose("scrT", lambda: sb["grow"][0:1, 32:64],
                                   lambda: sb["gat"][0:32, 9:10],
                                   [scr], 128)
        self.sim_ops = [simsq_last, scrc, sigc]

        for b in range(2):
            self._gate_batch(b, sums[b])

        self.dbg("routing", [32, 8], F32,
                 lambda: sb["gat"][0:32, 0:8], self.routing_ops)
        self._fold()

    def _gate_batch(self, b, sum_op):
        nc, sb, ps, dr = self.nc, self.sb, self.ps, self.dram
        gw = self.takewar("gatew")
        sgbb = self.op("vector", lambda b=b: nc.vector.tensor_copy(
            sb["colscb"][0:1, 0:1], sb["srep"][0:1, b * 18 + 16:b * 18 + 17]),
            [sum_op] + self.warlist("colscb_war") + gw, tag=f"sgbb{b}")
        smw = self.takewar("smbank")
        sgm = self.op("tensor", lambda: nc.tensor.matmul(
            ps["sm"][:, 0:1], sb["ones"][0:1, :], sb["colscb"][0:1, 0:1],
            start=True, stop=True), [sgbb] + smw, tag=f"sgmm{b}")
        sgev = self.op("vector", lambda: nc.vector.tensor_copy(
            sb["small"][:, 0:1], ps["sm"][:, 0:1]), [sgm] + gw,
            tag=f"sgev{b}")
        self.addwar("smbank", sgev)
        sn = self.op("vector", lambda b=b: nc.vector.tensor_scalar(
            sb["snorm"][:], sb["srep"][:, b * 18:b * 18 + 16],
            sb["small"][:, 0:1], 1.0 / T, ALU.add, ALU.mult),
            [sum_op, sgev] + gw, tag=f"snorm{b}")
        nsq = self.op("scalar", lambda: nc.scalar.activation(
            sb["bc"][:, 0:16], sb["snorm"][:], AF.Square,
            accum_out=sb["colsc"][:, 0:1]),
            [sn] + self.takewar("bc0") + self.takewar("colsc_g") + gw,
            tag=f"nsq{b}")
        nmm = self.op("tensor", lambda: nc.tensor.matmul(
            ps["sm"][0:1, 4:5], sb["colsc"][:], sb["fones"][:],
            start=True, stop=True),
            [nsq, self.memsets[1], sgev] + self.takewar("smbank2"),
            tag=f"nmm{b}")
        nev = self.op("scalar", lambda: nc.scalar.activation(
            sb["gsc"][0:1, 0:1], ps["sm"][0:1, 4:5], AF.Sqrt),
            [nmm] + gw, tag=f"nev{b}")
        nrc = self.op("vector", lambda: nc.vector.reciprocal(
            sb["gsc"][0:1, 1:2], sb["gsc"][0:1, 0:1]), [nev], tag=f"nrc{b}")
        self.addwar("smbank2", nev)
        nrb = self.op("vector", lambda: nc.vector.tensor_copy(
            sb["colscb"][0:1, 0:1], sb["gsc"][0:1, 1:2]),
            [nrc, sgm] + self.warlist("colscb_war"), tag=f"nrb{b}")
        smw2 = self.takewar("smbank")
        nrm = self.op("tensor", lambda: nc.tensor.matmul(
            ps["sm"][:, 1:2], sb["ones"][0:1, :], sb["colscb"][0:1, 0:1],
            start=True, stop=True), [nrb] + smw2, tag=f"nrm{b}")
        nrbc = self.op("vector", lambda: nc.vector.tensor_copy(
            sb["small"][:, 1:2], ps["sm"][:, 1:2]), [nrm] + gw,
            tag=f"nrbc{b}")
        self.addwar("smbank", nrbc)
        aff_r = None
        aff_c = None
        b00 = self.takewar("big0_0")
        b01 = self.takewar("big0_1")
        for c in range(NCH):
            aff_r = self.op("tensor", lambda c=c: nc.tensor.matmul(
                ps["big0"][0:1, 0:32],
                sb["snorm"][:, c:c + 1], sb["simsb"][:, c * 32:(c + 1) * 32],
                start=(c == 0), stop=(c == NCH - 1)),
                [sn, self.sim_ops[0]] + (b00 if c == 0 else []),
                tag=f"affr{b}_{c}")
            aff_c = self.op("tensor", lambda c=c: nc.tensor.matmul(
                ps["big0"][0:32, 512:513],
                sb["simsb"][:, c * 32:(c + 1) * 32], sb["snorm"][:, c:c + 1],
                start=(c == 0), stop=(c == NCH - 1)),
                [sn] + (b01 if c == 0 else []), tag=f"affc{b}_{c}")
            self.addwar("x32_0", aff_r, aff_c)
        lr1 = self.op("vector", lambda: nc.vector.tensor_scalar_mul(
            sb["gatr"][0:1, 64:96], ps["big0"][0:1, 0:32],
            sb["small"][0:1, 1:2]), [aff_r, nrbc] + gw, tag=f"lr1{b}")
        self.addwar("big0_0", lr1)
        lr2 = self.op("vector", lambda: nc.vector.tensor_mul(
            sb["gatr"][0:1, 64:96], sb["gatr"][0:1, 64:96],
            sb["grow"][0:1, 32:64]), [lr1, self.sim_ops[1]], tag=f"lr2{b}")
        lrow = self.op("vector", lambda: nc.vector.tensor_sub(
            sb["gatr"][0:1, 64:96], sb["gatr"][0:1, 64:96],
            sb["gatr"][0:1, 32:64]), [lr2], tag=f"lrow{b}")
        lc1 = self.op("vector", lambda: nc.vector.tensor_scalar_mul(
            sb["gat"][0:32, 2:3], ps["big0"][0:32, 512:513],
            sb["small"][0:32, 1:2]), [aff_c, nrbc] + gw, tag=f"lc1{b}")
        self.addwar("big0_1", lc1)
        lc2 = self.op("vector", lambda: nc.vector.tensor_mul(
            sb["gat"][0:32, 2:3], sb["gat"][0:32, 2:3],
            sb["gat"][0:32, 9:10]), [lc1, self.sim_ops[1]], tag=f"lc2{b}")
        lcol = self.op("vector", lambda: nc.vector.tensor_sub(
            sb["gat"][0:32, 2:3], sb["gat"][0:32, 2:3], sb["gat"][0:32, 8:9]),
            [lc2, self.sim_ops[2]], tag=f"lcol{b}")
        smw = self.takewar("smbank")
        rmm = self.op("tensor", lambda: nc.tensor.matmul(
            ps["sm"][0:32, 192:224], sb["onesf"][0:1, 0:32],
            sb["gatr"][0:1, 64:96], start=True, stop=True),
            [lrow] + self.cload + smw, tag=f"rmm{b}")
        rev = self.op("vector", lambda: nc.vector.tensor_copy(
            sb["gatm"][:, 0:32], ps["sm"][0:32, 192:224]), [rmm] + gw,
            tag=f"rev{b}")
        self.addwar("smbank", rev)
        gt_ = self.op("vector", lambda: nc.vector.tensor_single_scalar(
            sb["gatm"][:, 32:64], sb["gatm"][:, 0:32], sb["gat"][0:32, 2:3],
            ALU.is_gt), [rev, lcol], tag=f"ggt{b}")
        eq_ = self.op("vector", lambda: nc.vector.tensor_single_scalar(
            sb["gatm"][:, 0:32], sb["gatm"][:, 0:32], sb["gat"][0:32, 2:3],
            ALU.is_equal), [gt_], tag=f"geq{b}")
        tie = self.op("vector", lambda: nc.vector.tensor_mul(
            sb["gatm"][:, 0:32], sb["gatm"][:, 0:32], sb["idxlt"][:]),
            [eq_] + self.cload, tag=f"gtie{b}")
        rkp = self.op("vector", lambda: nc.vector.tensor_add(
            sb["gatm"][:, 0:32], sb["gatm"][:, 0:32], sb["gatm"][:, 32:64]),
            [tie], tag=f"grk{b}")
        rank = self.op("vector", lambda: nc.vector.reduce_sum(
            sb["gat"][0:32, 3:4], sb["gatm"][:, 0:32], axis=AX.X), [rkp],
            tag=f"grank{b}")
        fb_ = self.op("vector", lambda: nc.vector.tensor_single_scalar(
            sb["gat"][0:32, 4:5], sb["gat"][0:32, 3:4], 15.0, ALU.is_le),
            [rank], tag=f"gfb{b}")
        mc = self.op("vector", lambda: nc.vector.tensor_single_scalar(
            sb["gat"][0:32, 5:6], sb["gat"][0:32, 2:3], 0.0, ALU.is_gt),
            [lcol], tag=f"gmc{b}")
        cmm = self.op("tensor", lambda: nc.tensor.matmul(
            ps["sm"][0:1, 228:229], sb["gat"][0:32, 5:6],
            sb["fones"][0:32, 0:1], start=True, stop=True),
            [mc, rmm, self.memsets[1]], tag=f"cmm{b}")
        cev = self.op("vector", lambda: nc.vector.tensor_copy(
            sb["gsc"][0:1, 2:3], ps["sm"][0:1, 228:229]), [cmm] + gw,
            tag=f"cev{b}")
        self.addwar("smbank", cev)
        inz = self.op("vector", lambda: nc.vector.tensor_single_scalar(
            sb["gsc"][0:1, 3:4], sb["gsc"][0:1, 2:3], 0.0, ALU.is_equal),
            [cev], tag=f"ginz{b}")
        smw = self.takewar("smbank")
        ibm = self.op("tensor", lambda: nc.tensor.matmul(
            ps["sm"][0:32, 232:233], sb["onesf"][0:1, 0:32],
            sb["gsc"][0:1, 3:4], start=True, stop=True),
            [inz] + smw, tag=f"gibm{b}")
        ibe = self.op("vector", lambda: nc.vector.tensor_copy(
            sb["gat"][0:32, 6:7], ps["sm"][0:32, 232:233]), [ibm] + gw,
            tag=f"gibe{b}")
        self.addwar("smbank", ibe)
        fbt = self.op("vector", lambda: nc.vector.tensor_mul(
            sb["gat"][0:32, 6:7], sb["gat"][0:32, 6:7], sb["gat"][0:32, 4:5]),
            [ibe, fb_], tag=f"gfbt{b}")
        mfin = self.op("vector", lambda: nc.vector.tensor_add(
            sb["gat"][0:32, 5:6], sb["gat"][0:32, 5:6], sb["gat"][0:32, 6:7]),
            [fbt, mc], tag=f"gmfin{b}")
        gcol = self.op("vector", lambda: nc.vector.tensor_relu(
            sb["gat"][0:32, 7:8], sb["gat"][0:32, 2:3]), [lcol, gt_, eq_],
            tag=f"ggcol{b}")
        z1 = self.op("vector", lambda: nc.vector.tensor_mul(
            sb["gat"][0:32, 7:8], sb["gat"][0:32, 7:8], sb["gat"][0:32, 5:6]),
            [gcol, mfin], tag=f"gz1{b}")
        z2 = self.op("vector", lambda: nc.vector.tensor_scalar(
            sb["gat"][0:32, 6:7], sb["gat"][0:32, 5:6], -1.0, 1.0,
            ALU.mult, ALU.add), [mfin, fbt], tag=f"gz2{b}")
        z3 = self.op("vector", lambda: nc.vector.tensor_scalar_mul(
            sb["gat"][0:32, 6:7], sb["gat"][0:32, 6:7], -1.0e38), [z2],
            tag=f"gz3{b}")
        zz = self.op("vector", lambda: nc.vector.tensor_add(
            sb["gat"][0:32, 7:8], sb["gat"][0:32, 7:8], sb["gat"][0:32, 6:7]),
            [z3, z1], tag=f"gzz{b}")
        zrow = self._col_transpose(f"zT{b}", lambda: sb["gat"][0:32, 7:8],
                                   lambda: sb["grow"][0:1, 0:32],
                                   [zz, self.sim_ops[1]], 240)
        mx = self.op("vector", lambda: nc.vector.reduce_max(
            sb["gsc"][0:1, 5:6], sb["grow"][0:1, 0:32], axis=AX.X,
            negate=True), [zrow], tag=f"gmx{b}")
        ex = self.op("scalar", lambda: nc.scalar.activation(
            sb["grow"][0:1, 0:32], sb["grow"][0:1, 0:32], AF.Exp,
            bias=sb["gsc"][0:1, 5:6], accum_out=sb["gsc"][0:1, 6:7]),
            [mx], tag=f"gex{b}")
        rcp = self.op("vector", lambda: nc.vector.reciprocal(
            sb["gsc"][0:1, 7:8], sb["gsc"][0:1, 6:7]), [ex], tag=f"grcp{b}")
        rt = self.op("vector", lambda: nc.vector.tensor_scalar_mul(
            sb["grow"][0:1, 0:32], sb["grow"][0:1, 0:32],
            sb["gsc"][0:1, 7:8]), [rcp, ex], tag=f"grt{b}")
        rce = self._row_transpose(f"rT{b}", lambda: sb["grow"][0:1, 0:32],
                                  lambda: sb["gat"][0:32, 0:1],
                                  [rt] + gw, 272)
        rmk = self.op("vector", lambda: nc.vector.tensor_mul(
            sb["gat"][0:32, 1:2], sb["gat"][0:32, 0:1], sb["gat"][0:32, 5:6]),
            [rce, mfin] + gw, tag=f"grmk{b}")
        rbM = self.op("vector", lambda b=b: nc.vector.tensor_copy(
            sb["rmb"][0:32, 2 * b:2 * b + 1], sb["gat"][0:32, 1:2]),
            [rmk], tag=f"grb{b}")
        rbP = self.op("vector", lambda b=b: nc.vector.tensor_copy(
            sb["rmb"][0:32, 2 * b + 1:2 * b + 2], sb["gat"][0:32, 0:1]),
            [rce, rmk], tag=f"grb2{b}")
        self.routing_ops = getattr(self, "routing_ops", []) + [rbM, rbP]
        self.setwar("gatew", [rbM, rbP, lrow, zrow, mx, rt, sn, nsq, nrc,
                              mfin, fbt, gcol, rank, inz, cev])

    def _fold(self):
        nc, sb, ps, dr = self.nc, self.sb, self.ps, self.dram
        NCHUNK = 8
        self.setwar("fold0", self.takewar("big0_0") + self.takewar("big0_1") +
                    self.takewar("S0"))
        self.setwar("fold1", self.takewar("b1_3") + self.takewar("b1_4") +
                    self.takewar("b1_5") + self.takewar("S1"))
        stores = []
        for wi, (wnm, bb) in enumerate((("WvT", 0), ("WvT", 1),
                                        ("WoF", 0), ("WoF", 1))):
            rcol = 2 * bb + (0 if wnm == "WvT" else 1)
            rop = self.routing_ops[rcol]
            for ch in range(NCHUNK):
                slot = 0
                wd = self.op("dgp", lambda ch=ch, slot=slot, wnm=wnm:
                             nc.gpsimd.dma_start(
                    out=sb["wfold"][:, slot * 2048:(slot + 1) * 2048],
                    in_=dr[wnm][:, ch * 2048:(ch + 1) * 2048]),
                    self.takewar(f"wfold{slot}"), tag=f"f{wnm}{bb}_{ch}",
                    stream="wf")
                mms = []
                for pp in range(4):
                    bigt = ps["big0"] if pp < 3 else ps["big1"]
                    pcol = (pp % 3) * 512
                    deps = [wd, rop]
                    if pp == 0:
                        deps += self.takewar("fold0")
                    if pp == 3:
                        deps += self.takewar("fold1")
                    o = self.op("tensor", lambda pp=pp, slot=slot, bigt=bigt,
                                pcol=pcol, rcol=rcol: nc.tensor.matmul(
                        bigt[0:1, pcol:pcol + 512],
                        sb["rmb"][0:32, rcol:rcol + 1],
                        sb["wfold"][0:32, slot * 2048 + pp * 512:
                                    slot * 2048 + (pp + 1) * 512],
                        start=True, stop=True), deps,
                        tag=f"fm{wnm}{bb}_{ch}_{pp}")
                    mms.append(o)
                ev1 = self.op("vector", lambda slot=slot:
                              nc.vector.tensor_copy(
                    sb["foldrow"][0:1, 0:1536],
                    ps["big0"][0:1, 0:1536]), mms[0:3] +
                    self.takewar("foldrow0"), tag=f"fe1{wnm}{bb}_{ch}")
                ev2 = self.op("vector", lambda slot=slot:
                              nc.vector.tensor_copy(
                    sb["foldrow"][0:1, 1536:2048],
                    ps["big1"][0:1, 0:512]), [mms[3]],
                    tag=f"fe2{wnm}{bb}_{ch}")
                self.setwar("fold0", [ev1])
                self.setwar("fold1", [ev2])
                sto = self.op("dsync", lambda ch=ch, wi=wi:
                              nc.sync.dma_start(
                    out=dr["foldin"][0:1, wi * 16384 + ch * 2048:
                                     wi * 16384 + (ch + 1) * 2048],
                    in_=sb["foldrow"][0:1, 0:2048]),
                    [ev1, ev2], tag=f"fs{wnm}{bb}_{ch}", stream="fst")
                self.setwar(f"wfold{slot}", mms)
                self.setwar("foldrow0", [sto])
                stores.append(sto)
        self.addwar("big0_0", *self.takewar("fold0"))
        self.addwar("b1_3", *self.takewar("fold1"))
        co = self.op("coll", lambda: nc.gpsimd.collective_compute(
            "AllGather", ALU.bypass,
            replica_groups=[list(range(NCORE))],
            ins=[dr["foldin"][:]], outs=[dr["foldout"][:]]), stores,
            tag="agw")
        fo = dr["foldout"]
        self.wveff_ld = []
        ld_b1 = []
        for r in range(NCORE):
            for cc in range(2):
                c = 2 * r + cc
                for bb, dstn in ((0, "wveff"), (1, "fz")):
                    src_ap = fo[r:r + 1,
                                bb * 16384 + cc * 8192:
                                bb * 16384 + (cc + 1) * 8192].rearrange(
                        "one (p h) -> (one p) h", p=128)
                    o = self.op("dgp", lambda src_ap=src_ap, c=c, dstn=dstn:
                                nc.gpsimd.dma_start(
                        out=sb[dstn][:, c * EH:(c + 1) * EH], in_=src_ap),
                        [co] + (self.takewar("wveff") if dstn == "wveff"
                                else self.takewar("fz0") +
                                self.takewar("fz1")),
                        tag=f"wveffld{bb}_{c}", stream="we")
                    (self.wveff_ld if bb == 0 else ld_b1).append(o)
        bl1 = self.op("vector", lambda: nc.vector.tensor_scalar_mul(
            sb["wveff"][:], sb["wveff"][:], sb["bsel"][:, 0:1]),
            self.wveff_ld + self.cload, tag="wvbl1")
        bl2 = self.op("vector", lambda: nc.vector.scalar_tensor_tensor(
            sb["wveff"][:], sb["fz"][:, 0:2 * TG], sb["bsel"][:, 1:2],
            sb["wveff"][:], ALU.mult, ALU.add),
            [bl1] + ld_b1, tag="wvbl2")
        self.addwar("fz0", bl2)
        self.addwar("fz1", bl2)
        self.wveff_ld = [bl2]
        self.woeff_ld = []
        ldo_b1 = []
        for r in range(NCORE):
            for bb, dstn in ((0, "woeff"), (1, "wob1")):
                src_ap = fo[r:r + 1,
                            32768 + bb * 16384:32768 + bb * 16384 + 16384
                            ][0:1, 0:16384].rearrange(
                    "one (h d) -> (one h) d", h=8)
                o = self.op("dgp", lambda src_ap=src_ap, r=r, dstn=dstn:
                            nc.gpsimd.dma_start(
                    out=sb[dstn][8 * r:8 * r + 8, :], in_=src_ap),
                    [co] + self.takewar("woeff" if dstn == "woeff"
                                        else "wob1"),
                    tag=f"woeffld{bb}_{r}", stream="we")
                (self.woeff_ld if bb == 0 else ldo_b1).append(o)
        bo1 = self.op("vector", lambda: nc.vector.tensor_scalar_mul(
            sb["woeff"][:], sb["woeff"][:], sb["bsel"][0:64, 0:1]),
            self.woeff_ld + self.cload, tag="wobl1")
        bo2 = self.op("vector", lambda: nc.vector.scalar_tensor_tensor(
            sb["woeff"][:], sb["wob1"][:], sb["bsel"][0:64, 1:2],
            sb["woeff"][:], ALU.mult, ALU.add),
            [bo1] + ldo_b1, tag="wobl2")
        self.addwar("wob1_done", bo2)
        self.woeff_ld = [bo2]

    def emit_ffn(self):
        nc, sb, ps, dr = self.nc, self.sb, self.ps, self.dram

        cvm = None
        for c in range(NCH):
            cvm = self.op("tensor", lambda c=c: nc.tensor.matmul(
                ps["sm"][0:1, 448:448 + EH], sb["ones"][:, 0:1],
                sb["wveff"][:, c * EH:(c + 1) * EH],
                start=(c == 0), stop=(c == NCH - 1)),
                self.wveff_ld + (self.takewar("smbank") if c == 0 else []),
                tag=f"cvm{c}")
        cve = self.op("vector", lambda: nc.vector.tensor_copy(
            sb["colv"][:], ps["sm"][0:1, 448:448 + EH]), [cvm], tag="cve")
        self.addwar("smbank", cve)
        cwo = self.op("scalar", lambda: nc.scalar.activation(
            sb["bc"][0:64, 0:TG], sb["woeff"][:, 0:TG], AF.Copy,
            accum_out=sb["colwo"][:]), self.woeff_ld + self.takewar("bc0"),
            tag="cwo1")
        cwos = [cwo]
        for q in range(1, 4):
            o = self.op("scalar", lambda q=q: nc.scalar.activation(
                sb["bc"][0:64, 0:TG], sb["woeff"][:, q * TG:(q + 1) * TG],
                AF.Copy, accum_out=sb["colsc"][0:64, 0:1]),
                [cwos[-1]] + self.takewar("colsc_g"), tag=f"cwo{q}")
            a = self.op("vector", lambda: nc.vector.tensor_add(
                sb["colwo"][:], sb["colwo"][:], sb["colsc"][0:64, 0:1]),
                [o, cwos[-1] if q == 1 else cwos[-1]], tag=f"cwoa{q}")
            self.addwar("colsc_g", a)
            cwos.append(a)
        cwob = self.op("vector", lambda: nc.vector.tensor_copy(
            sb["colwob"][:], sb["colwo"][:]), [cwos[-1]], tag="cwob")
        self.addwar("bc0", cwos[-1])

        self.dbg("wveff", [128, NCH * EH], BF16, lambda: self.sb["wveff"][:],
                 self.wveff_ld)
        self.dbg("woeff", [64, D], BF16, lambda: self.sb["woeff"][:],
                 self.woeff_ld)
        for f in range(4):
            self._ffn_fchunk(f, cve, cwob)
        self.dbg("comb", [64, TO], BF16, lambda: self.sb["comb"][:],
                 [o for o in self.ops if o.tag and o.tag.startswith("cemb")])


    def _ffn_fchunk(self, f, cve, cwob):
        nc, sb, ps, dr = self.nc, self.sb, self.ps, self.dram
        g = f
        mm = None
        for c in range(NCH):
            slot = c % 2
            xld = self.op("dgp", lambda c=c, f=f, slot=slot:
                          nc.gpsimd.dma_start(
                out=sb["xmidld"][:, slot * TG:(slot + 1) * TG],
                in_=dr["xmid"][c, :, f * TG:(f + 1) * TG]),
                self.takewar(f"xmidld{slot}") + self.xmid_stored,
                tag=f"cxl{f}_{c}", stream=f"xm{slot}")
            deps = [xld] + self.wveff_ld
            if c == 0:
                deps += self.takewar("comb_ps") + self.takewar("smbank")
            mm = self.op("tensor", lambda c=c, slot=slot: nc.tensor.matmul(
                ps["sm"][0:EH, 0:TG],
                sb["wveff"][:, c * EH:(c + 1) * EH],
                sb["xmidld"][:, slot * TG:(slot + 1) * TG],
                start=(c == 0), stop=False), deps, tag=f"cm{f}_{c}")
            self.addwar(f"xmidld{slot}", mm)
        corr = self.op("tensor", lambda f=f: nc.tensor.matmul(
            ps["sm"][0:EH, 0:TG], sb["colv"][:],
            sb["m3"][:, f * TG:(f + 1) * TG], start=False, stop=True),
            [mm, cve, self.negm3_ops[f]], tag=f"cmc{f}")
        cemb = self.op("vector", lambda f=f: nc.vector.tensor_copy(
            sb["comb"][:, f * TG:(f + 1) * TG], ps["sm"][0:EH, 0:TG]),
            [corr], tag=f"cemb{f}")
        self.setwar("comb_ps", [cemb])
        self.addwar("smbank", cemb)

        m4m = self.op("tensor", lambda f=f: nc.tensor.matmul(
            ps["big1"][0:1, 512:512 + TG], sb["colwob"][:],
            sb["comb"][:, f * TG:(f + 1) * TG], start=True, stop=True),
            [cemb, cwob] + self.takewar("b1_4"), tag=f"m4m{f}")
        m4 = self.op("scalar", lambda: nc.scalar.activation(
            sb["m4"][:], ps["big1"][0:1, 512:512 + TG], AF.Copy,
            scale=1.0 / D), [m4m] + self.takewar("rowsc"), tag=f"m4_{f}")
        self.addwar("b1_4", m4)
        negm4 = self.op("scalar", lambda: nc.scalar.activation(
            sb["negm4b"][:], sb["m4"][:], AF.Copy, scale=-1.0),
            [m4] + self.takewar("rbf1"))

        ssq_last = None
        evs = []
        for c in range(NCH):
            slot = c % 2
            deps = self.woeff_ld + [cemb, negm4]
            if c == 0:
                deps += self.takewar("big0_0") + self.takewar("big0_1")
            mmf = self.op("tensor", lambda c=c, slot=slot, f=f:
                          nc.tensor.matmul(
                ps["big0"][:, slot * TG:(slot + 1) * TG],
                sb["woeff"][:, c * 128:(c + 1) * 128],
                sb["comb"][:, f * TG:(f + 1) * TG],
                start=True, stop=False),
                deps + self.takewar(f"big0_{slot}"), tag=f"fm{f}_{c}")
            mc = self.op("tensor", lambda c=c, slot=slot: nc.tensor.matmul(
                ps["big0"][:, slot * TG:(slot + 1) * TG],
                sb["ones"][0:1, 0:128], sb["negm4b"][:],
                start=False, stop=True), [mmf], tag=f"fc{f}_{c}")
            self.addwar("rbf1", mc)
            ev = self.op("scalar", lambda c=c, slot=slot:
                         nc.scalar.activation(
                sb["x32"][:, c * TG:(c + 1) * TG],
                ps["big0"][:, slot * TG:(slot + 1) * TG], AF.Copy),
                [mc] + self.takewar(f"x32_{c}"), tag=f"fev{f}_{c}")
            sq4 = self.op("scalar", lambda c=c, slot=slot:
                          nc.scalar.activation(
                sb["fz"][:, slot * TG:(slot + 1) * TG],
                ps["big0"][:, slot * TG:(slot + 1) * TG], AF.Square),
                [mc, ev] + self.takewar(f"fz{slot}"), tag=f"fsq{f}_{c}")
            self.setwar(f"big0_{slot}", [ev, sq4])
            ssq_last = self.op("tensor", lambda slot=slot, c=c:
                               nc.tensor.matmul(
                ps["big1"][0:1, 1024:1024 + TG], sb["ones"][:, 0:1],
                sb["fz"][:, slot * TG:(slot + 1) * TG],
                start=(c == 0), stop=(c == NCH - 1)),
                [sq4] + (self.takewar("b1_5") if c == 0 else []),
                tag=f"fsmm{f}_{c}")
            self.setwar(f"fz{slot}", [ssq_last])
            evs.append(ev)

        msq4 = self.op("scalar", lambda: nc.scalar.activation(
            sb["rowb"][:], ps["big1"][0:1, 1024:1024 + TG], AF.Copy,
            scale=1.0 / D), [ssq_last] + self.takewar("rowsc"),
            tag=f"msq4_{f}")
        self.addwar("b1_5", msq4)
        sd4 = self.op("scalar", lambda: nc.scalar.activation(
            sb["rowa"][:], sb["rowb"][:], AF.Sqrt,
            bias=sb["cst"][0:1, 0:1]), [msq4])
        rstd4 = self.op("vector", lambda: nc.vector.reciprocal(
            sb["rstd4"][:], sb["rowa"][:]), [sd4], tag=f"rstd4_{f}")
        rstd4b = self.op("vector", lambda: nc.vector.tensor_copy(
            sb["rstd4b"][:], sb["rstd4"][:]), [rstd4] + self.takewar("rbf0"))
        bcm = self.op("tensor", lambda: nc.tensor.matmul(
            ps["sm"][:, 0:TG], sb["ones"][0:1, :], sb["rstd4b"][:],
            start=True, stop=True),
            [rstd4b] + self.takewar("smbank") + self.takewar("comb_ps"),
            tag=f"bc4_{f}")
        bc4 = self.op("vector", lambda: nc.vector.tensor_copy(
            sb["bc"][:, 0:TG], ps["sm"][:, 0:TG]),
            [bcm] + self.takewar("bc0"), tag=f"bc4e_{f}")
        self.addwar("smbank", bc4)
        self.addwar("rbf0", bcm)
        self.setwar("rowsc", [msq4, sd4, rstd4, rstd4b, bcm, m4, negm4])
        self.setwar("comb_ps", [bc4])

        for c in range(NCH):
            slot = c % 2
            xld = self.op("dgp", lambda c=c, f=f, slot=slot:
                          nc.gpsimd.dma_start(
                out=sb["xmidld"][:, slot * TG:(slot + 1) * TG],
                in_=dr["xmid"][c, :, f * TG:(f + 1) * TG]),
                self.takewar(f"xmidld{slot}"), tag=f"yxl{f}_{c}", stream=f"xm{slot}")
            u = self.op("vector", lambda c=c: nc.vector.tensor_mul(
                sb["x32"][:, c * TG:(c + 1) * TG],
                sb["x32"][:, c * TG:(c + 1) * TG], sb["bc"][:, 0:TG]),
                [evs[c], bc4], tag=f"yu{f}_{c}")
            self.addwar("bc0", u)
            oslot = c % 2
            y = self.op("vector", lambda c=c, slot=slot, oslot=oslot:
                        nc.vector.tensor_add(
                sb["out32"][:, oslot * TG:(oslot + 1) * TG],
                sb["x32"][:, c * TG:(c + 1) * TG],
                sb["xmidld"][:, slot * TG:(slot + 1) * TG]),
                [u, xld] + self.takewar(f"xo32_{oslot}"),
                tag=f"y{f}_{c}")
            self.addwar(f"xmidld{slot}", y)
            self.addwar(f"x32_{c}", y)
            sto = self.op("dsync", lambda c=c, f=f, oslot=oslot:
                          nc.sync.dma_start(
                out=dr["yT"][c * 128:(c + 1) * 128, f * TG:(f + 1) * TG],
                in_=sb["out32"][:, oslot * TG:(oslot + 1) * TG]),
                [y], tag=f"yst{f}_{c}", stream=f"y{oslot}")
            self.setwar(f"xo32_{oslot}", [sto])
            self.final_deps.append(sto)

    def build(self):
        self._alloc()
        self.emit_consts()
        self.xmid_stored = []
        self.negm3_ops = [None] * NG
        for gg in range(NGALL):
            self.emit_group_proj(gg)
            if gg >= 2:
                self.emit_group_attn(gg)
                self.emit_group_out(gg)
        sb = self.sb
        self.dbg("q1", [128, 8 * TG], BF16, lambda: sb["q1"][:],
                 sum((self.q_ready.get(j, []) for j in range(8)), []))
        self.dbg("q2", [128, 8 * TG], BF16, lambda: sb["q2"][:],
                 sum((self.q_ready.get(j, []) for j in range(8)), []))
        self.dbg("k1", [128, 4 * RING * 128], BF16, lambda: sb["k1"][:],
                 sum((self.kslot_ready.get((s, p), [])
                      for s in range(RING) for p in range(4)), []))
        self.dbg("k2", [128, 4 * RING * 128], BF16, lambda: sb["k2"][:],
                 sum((self.kslot_ready.get((s, p), [])
                      for s in range(RING) for p in range(4)), []))
        self.dbg("vring", [128, RING * KVH * 128], BF16,
                 lambda: sb["vring"][:],
                 [self.vslot_ready[k] for k in self.vslot_ready])
        self.dbg("enc", [128, QH * TG], BF16, lambda: sb["enc"][:],
                 sum((self.enc_ready.get(n, []) for n in range(QH)), []))
        self.dbg("x32", [128, NCH * TG], F32, lambda: sb["x32"][:],
                 [o for o in self.ops if o.tag and o.tag.startswith("sw2c3_")])
        self.dbg("seqa", [128, 16], F32, lambda: sb["seqa"][:],
                 [self.seqa_last])
        self.dbg("P", [128, 2 * 4608], BF16, lambda: sb["PTr"][:],
                 [o for o in self.ops if o.tag and o.tag.startswith("exp5_")])
        if self.stop != "groups":
            self.emit_gating()
            self.emit_ffn()
        return self._emit()

    def _emit(self):
        nc = self.nc
        if self.bank_edges:
            for ia, ib in self.bank_edges:
                self.ops[ib].deps.append(self.ops[ia])
        if self.serial:
            prev = None
            for o in self.ops:
                if prev is not None:
                    o.deps = list(o.deps) + [prev]
                prev = o
        for o in self.ops:
            if o.is_dma:
                o.need_inc = True
            for d in o.deps:
                d.need_inc = True
        for o in self.final_deps:
            o.need_inc = True

        for o in self.ops:
            o.semkey = (ENG_SEM[o.eng] + "_" + o.stream) if o.is_dma \
                else ENG_SEM[o.eng]
        keys = sorted({o.semkey for o in self.ops if o.need_inc})
        for k in keys:
            if k not in self.sems:
                self.sems[k] = self.es.enter_context(nc.semaphore(k))
        counts = {}
        for o in self.ops:
            if o.need_inc:
                counts[o.semkey] = counts.get(o.semkey, 0) + \
                    (1 if o.eng == "coll" else (16 if o.is_dma else 1))
                o.val = counts[o.semkey]
        self.totals = dict(counts)
        know = {}
        per_key = {}
        for o in self.ops:
            k = {}
            for d in o.deps:
                kd = know.get(id(d))
                if kd:
                    for s, v in kd.items():
                        if k.get(s, 0) < v:
                            k[s] = v
                if d.need_inc:
                    if k.get(d.semkey, 0) < d.val:
                        k[d.semkey] = d.val
            know[id(o)] = k
            if o.is_dma and o.need_inc:
                per_key.setdefault(o.semkey, []).append(o)
        for kk, lst in per_key.items():
            i = 0
            while i < len(lst):
                j = i + 1
                while j < len(lst):
                    kj = know.get(id(lst[j]), {})
                    if kj.get(kk, 0) >= lst[j - 1].val:
                        break
                    j += 1
                end_val = lst[j - 1].val
                for t in range(i, j):
                    lst[t].val_wait = end_val
                lst[i].batch_start = True
                i = j
        for o in self.ops:
            if o.need_inc and o.val_wait is None:
                o.val_wait = o.val

        by_exec = {e: [] for e in
                   ("tensor", "vector", "scalar", "gpsimd", "sync")}
        for o in self.ops:
            by_exec[ENG_EXEC[o.eng]].append(o)
        watermark = {e: {} for e in by_exec}
        sems = self.sems

        def run_stream(eng_obj, ops, ename):
            wm = watermark[ename]
            for o in ops:
                for d in o.deps:
                    sk = d.semkey
                    v = d.val_wait
                    assert v is not None, (d.tag, o.tag)
                    if wm.get(sk, 0) < v:
                        eng_obj.wait_ge(sems[sk], v)
                        wm[sk] = v
                inst = o.fn()
                o.iname = inst.ins.name
                if self.capture:
                    o.psum = _extract_psum(inst)
                if o.need_inc:
                    inst.then_inc(sems[o.semkey],
                                  1 if o.eng == "coll" else
                                  (16 if o.is_dma else 1))
            if ename == "sync":
                for o in self.final_deps:
                    sk, v = o.semkey, o.val_wait
                    if wm.get(sk, 0) < v:
                        eng_obj.wait_ge(sems[sk], v)
                        wm[sk] = v
                for sk, tot in sorted(self.totals.items()):
                    if "_" in sk and wm.get(sk, 0) < tot:
                        eng_obj.wait_ge(sems[sk], tot)
                        wm[sk] = tot

        with self.es:
            block = self.es.enter_context(nc.Block())
            block.tensor(lambda e: run_stream(e, by_exec["tensor"], "tensor"))
            block.vector(lambda e: run_stream(e, by_exec["vector"], "vector"))
            block.scalar(lambda e: run_stream(e, by_exec["scalar"], "scalar"))
            block.gpsimd(lambda e: run_stream(e, by_exec["gpsimd"], "gpsimd"))
            block.sync(lambda e: run_stream(e, by_exec["sync"], "sync"))
        return nc



_PSUM_INFO = {"big0": (0, 1536, 4), "big1": (3, 1536, 4),
              "pt": (6, 512, 4), "sm": (7, 512, 4)}


def _extract_psum(bi):
    mi = bi.ins
    acc = []

    def scan(lst, kind):
        for pap in lst:
            memref = getattr(pap, "memref", None)
            if memref not in _PSUM_INFO:
                continue
            b0, fsz, esz = _PSUM_INFO[memref]
            col0 = pap.offset % fsz
            ext = 0
            for pr in pap.ap:
                st, num = int(pr[0]), int(pr[1])
                if st % fsz == 0:
                    continue
                ext += abs(st) * (num - 1)
            lo = b0 + (col0 * esz) // 2048
            hi = b0 + ((col0 + ext) * esz) // 2048
            for b in range(lo, hi + 1):
                acc.append((b, kind))

    scan(mi.ins, "R")
    scan(mi.outs, "W")
    return acc or None


def compute_bank_edges(mk):
    from bisect import bisect_right
    ops = mk.ops
    CE = ("tensor", "vector", "scalar")
    streams = {}
    pos = {}
    for o in ops:
        e = ENG_EXEC[o.eng]
        lst = streams.setdefault(e, [])
        pos[id(o)] = len(lst)
        lst.append(o)
    INF = float("inf")
    icov = {}
    for e in CE:
        lst = streams.get(e, [])
        arr = [INF] * len(lst)
        nxt = INF
        for i in range(len(lst) - 1, -1, -1):
            o = lst[i]
            if (not o.is_dma) and o.need_inc:
                nxt = o.val
            arr[i] = nxt
        icov[e] = arr
    sem_of = {e: ENG_SEM[e] for e in CE}
    wmv = {e: {} for e in streams}
    knowpos = {e: {ae: -1 for ae in CE} for e in streams}
    hist = [{ae: [] for ae in CE} for _ in range(8)]
    edges = []
    for o in ops:
        e = ENG_EXEC[o.eng]
        wv, kp = wmv[e], knowpos[e]
        upd = False
        for d in o.deps:
            v = d.val_wait
            if v is None:
                continue
            sk = d.semkey
            if wv.get(sk, 0) < v:
                wv[sk] = v
                upd = True
        if upd:
            for ae in CE:
                v = wv.get(sem_of[ae], 0)
                if v:
                    p = bisect_right(icov[ae], v) - 1
                    if p > kp[ae]:
                        kp[ae] = p
        if o.psum and e in CE:
            mypos = pos[id(o)]
            for b in sorted({bb for bb, _k in o.psum}):
                hb = hist[b]
                for ae in CE:
                    if ae == e:
                        continue
                    lst = hb[ae]
                    if lst and lst[-1][0] > kp[ae]:
                        ap_, ag = lst[-1]
                        edges.append((ag, o.gidx))
                        kp[ae] = ap_
                hb[e].append((mypos, o.gidx))
    seen = set()
    out = []
    for ed in edges:
        if ed not in seen:
            seen.add(ed)
            out.append(ed)
    return out


def build_banked(**kw):
    m1 = MK(capture=True, **kw)
    m1.build()
    edges = compute_bank_edges(m1)
    m2 = MK(bank_edges=edges, **kw)
    nc = m2.build()
    return m2, nc, edges



_BUILD_CACHE = {}


def kernel(L0_attn_x, L0_attn_q_kernel, L0_attn_kv_kernel, L0_attn_out_kernel,
           L0_attn_q_norm_scale, L0_attn_k_norm_scale, L0_ffn_sim_matrix,
           L0_ffn_gates, L0_ffn_w_q, L0_ffn_w_k, L0_ffn_w_v, L0_ffn_o_weights):
    _ = (L0_ffn_w_q, L0_ffn_w_k)
    x = np.asarray(L0_attn_x, dtype=np.float32)
    shared = host_weights(np.asarray(L0_attn_q_kernel, np.float32),
                          np.asarray(L0_attn_kv_kernel, np.float32),
                          np.asarray(L0_attn_out_kernel, np.float32),
                          np.asarray(L0_ffn_w_v, np.float32),
                          np.asarray(L0_ffn_o_weights, np.float32),
                          np.asarray(L0_ffn_sim_matrix, np.float32))
    shared.update(host_tables())
    in_maps = []
    for core in range(NCORE):
        d = host_core_inputs(core, x,
                             np.asarray(L0_attn_q_norm_scale, np.float32),
                             np.asarray(L0_attn_k_norm_scale, np.float32),
                             np.asarray(L0_ffn_gates, np.float32), shared)
        d["WvT"] = np.ascontiguousarray(
            shared["WvT"][:, core * 16384:(core + 1) * 16384])
        d["WoF"] = np.ascontiguousarray(
            shared["WoF"][:, core * 16384:(core + 1) * 16384])
        in_maps.append(d)
    _mk, nc, _edges = build_banked()
    res = run_bass_kernel_spmd(nc, in_maps, core_ids=list(range(NCORE)))
    out = np.empty((B, T, D), dtype=np.float32)
    for core in range(NCORE):
        b, cs = core // 4, core % 4
        out[b, cs * TO:(cs + 1) * TO, :] = res.results[core]["yT"].T
    return out


# version 07
# speedup vs baseline: 1.2383x; 1.1972x over previous
import math
from contextlib import ExitStack

import numpy as np
import ml_dtypes

import concourse.bass as bass
import concourse.mybir as mybir
from concourse.bass_utils import run_bass_kernel_spmd

F32 = mybir.dt.float32
BF16 = mybir.dt.bfloat16
AF = mybir.ActivationFunctionType
ALU = mybir.AluOpType
AX = mybir.AxisListType
bfdt = ml_dtypes.bfloat16

B, T, D = 2, 8192, 2048
QH, KVH, HD = 16, 8, 128
WIN = 1024
E, EH = 32, 64
ATTN_SCALE = 0.08838834764831845
EPS_RMS = 1e-6
EPS_LN = 1e-5
ROPE_BASE = 10000.0

NCORE = 8
TO = 2048
HALO = 1024
TG = 512
NT = 16
NG = 4
NGALL = 6
NCH = 16
RING = 12
NEG = -1.0e30
QPAIR_A = [0, 1, 4, 5, 8, 9, 12, 13]


def q_block_of(n):
    j = (n // 4) * 2 + (n % 2)
    half = 0 if (n % 4) < 2 else 1
    return j, half



def host_tables():
    c = {}
    c["ident"] = np.eye(128, dtype=np.float32)
    c["ones"] = np.ones((128, 128), dtype=np.float32)
    r = np.arange(128)[:, None]
    col = np.arange(128)[None, :]
    c["tri"] = np.concatenate(
        [np.where(r > col, 0.0, NEG), np.where(r <= col, 0.0, NEG)],
        axis=1).astype(np.float32)
    aqs = np.zeros((16, 8, 128), dtype=np.float32)
    for j in range(8):
        a = QPAIR_A[j]
        aqs[a, j, 0:64] = 1.0
        aqs[a + 2, j, 64:128] = 1.0
    c["aqsel"] = aqs.reshape(16, 8 * 128)
    eq = np.zeros((8, 128, 16), dtype=np.float32)
    for j in range(8):
        a = QPAIR_A[j]
        eq[j, 0:64, a] = 1.0
        eq[j, 64:128, a + 2] = 1.0
    c["eyeq"] = eq.transpose(1, 0, 2).reshape(128, 8 * 16)
    ek = np.zeros((4, 128, 8), dtype=np.float32)
    for p in range(4):
        ek[p, 0:64, 2 * p] = 1.0
        ek[p, 64:128, 2 * p + 1] = 1.0
    c["eyek"] = ek.transpose(1, 0, 2).reshape(128, 4 * 8)
    pk = np.zeros((4, 8, 128), dtype=np.float32)
    for p in range(4):
        pk[p, 2 * p, 0:64] = 1.0
        pk[p, 2 * p + 1, 64:128] = 1.0
    c["pselk"] = pk.transpose(1, 0, 2).reshape(8, 4 * 128)
    ii = np.arange(32)
    c["idxlt"] = (ii[None, :] < ii[:, None]).astype(np.float32)
    return c


def host_weights(qk, kvk, outk, wv, wo, sim):
    s = {}
    qb = np.empty((8, 2, NCH, 128, 128), dtype=np.float32)
    for j in range(8):
        a = QPAIR_A[j]
        for v in range(2):
            blk = np.concatenate([qk[a][:, v * 64:(v + 1) * 64],
                                  qk[a + 2][:, v * 64:(v + 1) * 64]], axis=1)
            qb[j, v] = blk.reshape(NCH, 128, 128)
    s["Wq"] = np.ascontiguousarray(qb.reshape(16, NCH, 128, 128))
    kb = np.empty((16, NCH, 128, 128), dtype=np.float32)
    kk, vv = kvk[0], kvk[1]
    for p in range(4):
        for v in range(2):
            blk = np.concatenate([kk[2 * p][:, v * 64:(v + 1) * 64],
                                  kk[2 * p + 1][:, v * 64:(v + 1) * 64]], axis=1)
            kb[p * 2 + v] = blk.reshape(NCH, 128, 128)
    for h in range(8):
        kb[8 + h] = vv[h].reshape(NCH, 128, 128)
    s["Wkv"] = np.ascontiguousarray(kb)
    ob = np.empty((NCH, QH, 128, 128), dtype=np.float32)
    for n in range(QH):
        for c in range(NCH):
            ob[c, n] = outk[n][:, c * 128:(c + 1) * 128]
    s["Wout"] = np.ascontiguousarray(ob)
    s["WvT"] = np.ascontiguousarray(wv.transpose(0, 2, 1)).reshape(E, D * EH)
    s["WoF"] = np.ascontiguousarray(wo).reshape(E, EH * D)
    s["simM"] = np.ascontiguousarray(sim)
    return s


def host_core_inputs(core, x, qs, ks, gates, shared):
    b, cs = core // 4, core % 4
    t0 = cs * TO
    d = {}
    xs = np.zeros((HALO + TO, D), dtype=np.float32)
    lo = max(0, t0 - HALO)
    xs[HALO - (t0 - lo):HALO] = x[b, lo:t0]
    xs[HALO:] = x[b, t0:t0 + TO]
    d["xT"] = np.ascontiguousarray(xs.T)

    pos = np.arange(t0 - HALO, t0 + TO, dtype=np.float64)
    pos = np.maximum(pos, 0.0)
    frac = np.arange(0, HD, 2, dtype=np.float64) / HD
    ts = ROPE_BASE ** frac
    ang = pos[None, :] / ts[:, None]
    d["costab"] = np.cos(ang).astype(np.float32)
    d["sintab"] = np.sin(ang).astype(np.float32)

    sp = cs * TO + np.arange(2 * TG)
    nd = np.maximum(0, 1023 - sp).astype(np.float32)
    d["negnd"] = (-nd).reshape(1, 2 * TG)

    d["qsstack"] = np.stack([np.concatenate([qs[0:64], qs[0:64]]),
                             np.concatenate([qs[64:128], qs[64:128]])],
                            1).astype(np.float32)
    d["ksstack"] = np.stack([np.concatenate([ks[0:64], ks[0:64]]),
                             np.concatenate([ks[64:128], ks[64:128]])],
                            1).astype(np.float32)
    d["gates"] = gates.reshape(1, E).astype(np.float32)
    bs = np.zeros((128, 2), np.float32)
    bs[:, b] = 1.0
    d["bsel"] = bs
    d.update(shared)
    return d



class Op:
    __slots__ = ("eng", "fn", "deps", "is_dma", "need_inc", "val", "tag",
                 "stream", "semkey", "val_wait", "batch_start", "gidx",
                 "psum", "iname")

    def __init__(self, eng, fn, deps, is_dma, tag, stream):
        self.eng, self.fn, self.deps = eng, fn, deps
        self.is_dma = is_dma
        self.need_inc = False
        self.val = None
        self.val_wait = None
        self.tag = tag
        self.stream = stream
        self.semkey = None
        self.batch_start = False
        self.gidx = -1
        self.psum = None
        self.iname = None


ENG_SEM = {"tensor": "sT", "vector": "sV", "scalar": "sS", "gpsimd": "sG",
           "dsync": "qS", "dscalar": "qA", "dgp": "qG", "coll": "qC"}
ENG_EXEC = {"tensor": "tensor", "vector": "vector", "scalar": "scalar",
            "gpsimd": "gpsimd", "dsync": "sync", "dscalar": "scalar",
            "dgp": "gpsimd", "coll": "gpsimd"}


class MK:
    def __init__(self, debug=(), stop=None, serial=False, capture=False,
                 bank_edges=None):
        self.debug = set(debug)
        self.stop = stop
        self.serial = serial
        self.capture = capture
        self.bank_edges = bank_edges
        self.nc = bass.Bass("TRN2", target_bir_lowering=False)
        self.ops = []
        self.es = ExitStack()
        self.sems = {}
        self.dram = {}
        self.sb = {}
        self.ps = {}
        self.final_deps = []
        self.war = {}
        self.colT_ready = {"colkv": [None] * 16, "colq": [None] * 16}
        self.kslot_ready = {}
        self.vslot_ready = {}
        self.q_ready = {}
        self.acol_ready = [None] * 4
        self.pending_kscale = None
        self.enc_ready = {}

    def op(self, eng, fn, deps=(), tag=None, stream="s"):
        o = Op(eng, fn, [d for d in deps if d is not None],
               eng in ("dsync", "dscalar", "dgp", "coll"), tag, stream)
        o.gidx = len(self.ops)
        self.ops.append(o)
        return o

    def dbg(self, name, shape, dtype, src_fn, deps):
        if name not in self.debug:
            return None
        ap = self.nc.dram_tensor("dbg_" + name, list(shape), dtype,
                                 kind="ExternalOutput").ap()
        o = self.op("dgp", lambda ap=ap, f=src_fn:
                    self.nc.gpsimd.dma_start(out=ap, in_=f()),
                    deps, tag="dbg_" + name, stream="dbg")
        self.final_deps.append(o)
        return o

    def warlist(self, key):
        return self.war.get(key, [])

    def setwar(self, key, ops):
        self.war[key] = list(ops)

    def addwar(self, key, *ops):
        self.war.setdefault(key, []).extend(ops)

    def takewar(self, key):
        v = self.war.get(key, [])
        self.war[key] = []
        return v

    def _alloc(self):
        nc, es = self.nc, self.es

        def sbuf(name, shape, dt):
            self.sb[name] = es.enter_context(
                nc.sbuf_tensor("sb_" + name, list(shape), dt))

        def psum(name, shape, dt):
            self.ps[name] = es.enter_context(
                nc.psum_tensor(name, list(shape), dt))

        def dram_in(name, shape, dt=F32):
            self.dram[name] = nc.dram_tensor(
                name, list(shape), dt, kind="ExternalInput").ap()

        dram_in("xT", [D, HALO + TO])
        dram_in("costab", [64, HALO + TO])
        dram_in("sintab", [64, HALO + TO])
        dram_in("negnd", [1, 2 * TG])
        dram_in("aqsel", [16, 8 * 128])
        dram_in("qsstack", [128, 2])
        dram_in("ksstack", [128, 2])
        dram_in("ident", [128, 128])
        dram_in("ones", [128, 128])
        dram_in("tri", [128, 256])
        dram_in("eyeq", [128, 128])
        dram_in("eyek", [128, 32])
        dram_in("pselk", [8, 512])
        dram_in("idxlt", [32, 32])
        dram_in("bsel", [128, 2])
        dram_in("Wq", [16, NCH, 128, 128])
        dram_in("Wkv", [16, NCH, 128, 128])
        dram_in("Wout", [NCH, QH, 128, 128])
        dram_in("WvT", [E, 16384])
        dram_in("WoF", [E, 16384])
        dram_in("simM", [D, E])
        dram_in("gates", [1, E])

        self.dram["yT"] = nc.dram_tensor("yT", [D, TO], F32,
                                         kind="ExternalOutput").ap()
        self.dram["xmid"] = nc.dram_tensor("xmid", [NCH, 128, TO], BF16,
                                           kind="Internal").ap()
        self.dram["foldin"] = nc.dram_tensor("foldin", [1, 65536], BF16,
                                             kind="Internal").ap()
        self.dram["foldout"] = nc.dram_tensor("foldout", [NCORE, 65536], BF16,
                                              kind="Internal").ap()
        self.dram["seqin"] = nc.dram_tensor("seqin", [128, 18], F32,
                                            kind="Internal").ap()
        self.dram["seqout"] = nc.dram_tensor("seqout", [128 * NCORE, 18], F32,
                                             kind="Internal").ap()

        sbuf("xfm", [128, NCH * TG], BF16)
        sbuf("q1", [128, 8 * TG], BF16)
        sbuf("q2", [128, 8 * TG], BF16)
        sbuf("enc", [128, QH * TG], BF16)
        sbuf("k1", [128, 4 * RING * 128], BF16)
        sbuf("k2", [128, 4 * RING * 128], BF16)
        sbuf("vring", [128, RING * KVH * 128], BF16)
        sbuf("x32", [128, NCH * TG], F32)
        sbuf("PTr", [128, 2 * 4608], BF16)
        sbuf("wbuf", [128, 2 * 2048], BF16)
        sbuf("out32", [128, 2 * TG], F32)
        sbuf("sq", [128, 2 * TG], BF16)
        sbuf("fz", [128, 2 * TG], BF16)
        sbuf("rope", [128, 3 * TG], F32)
        self.sb["recd"] = self.sb["rope"][:, 0:2 * TG]
        self.sb["negnd"] = self.sb["rope"][0:1, 2 * TG:3 * TG]
        sbuf("tbl", [128, 2 * TG], BF16)
        sbuf("bc", [128, 2 * TG], F32)
        sbuf("vh", [128, TG], BF16)
        sbuf("small", [128, 2], F32)
        sbuf("aqsel", [16, 8 * 128], F32)
        sbuf("onesf", [1, 128], F32)
        sbuf("colsc", [128, 1], F32)
        sbuf("colscb", [128, 1], BF16)
        sbuf("ident", [128, 128], BF16)
        sbuf("fident", [128, 128], F32)
        sbuf("pselk", [8, 512], BF16)
        sbuf("ones", [128, 128], BF16)
        sbuf("fones", [128, 1], F32)
        sbuf("tri", [128, 256], F32)
        sbuf("eyeq", [128, 128], BF16)
        sbuf("eyek", [128, 32], BF16)
        sbuf("idxlt", [32, 32], F32)
        sbuf("bsel", [128, 2], F32)
        sbuf("qsstack", [128, 2], F32)
        sbuf("ksstack", [128, 2], F32)
        for nm in ("m1", "rstd1", "rowa", "rowb"):
            sbuf(nm, [1, TG], F32)
        sbuf("rowar", [1, 3 * TG], F32)
        sbuf("rowbf", [1, 2 * TG], BF16)
        sb = self.sb
        sb["m2"] = sb["rowar"][0:1, 0:TG]
        sb["rstd2"] = sb["rowar"][0:1, TG:2 * TG]
        sb["rstd3"] = sb["rowar"][0:1, 2 * TG:3 * TG]
        sb["m4"] = sb["rowar"][0:1, 0:TG]
        sb["rstd4"] = sb["rowar"][0:1, TG:2 * TG]
        sb["rowc"] = sb["rowb"]
        sb["m2b"] = sb["rowbf"][0:1, 0:TG]
        sb["rstd2b"] = sb["rowbf"][0:1, TG:2 * TG]
        sb["rstd3b"] = sb["rowbf"][0:1, 0:TG]
        sb["rstd4b"] = sb["rowbf"][0:1, 0:TG]
        sb["negm4b"] = sb["rowbf"][0:1, TG:2 * TG]
        sb["rstd1b"] = sb["rowbf"][0:1, 0:TG]
        sb["m1b"] = sb["rowbf"][0:1, TG:2 * TG]
        sbuf("m3", [1, NG * TG], BF16)
        sbuf("aq", [16, TG], F32)
        sb["ak"] = sb["aq"][0:8, :]
        sbuf("akb", [8, TG], BF16)
        sbuf("seqp", [128, 16], F32)
        sbuf("seqa", [128, 16], F32)
        sbuf("seqsb", [128, 18], F32)
        sbuf("seqg", [128, NCORE * 18], F32)
        sbuf("sumw", [1, 8], F32)
        sb["simsb"] = sb["x32"][:, 0:512]
        sbuf("cst", [128, 2], F32)
        sbuf("rmb", [32, 4], BF16)
        sb["srep"] = sb["rope"][:, 0:36]
        sb["snorm"] = sb["rope"][:, 36:52]
        sb["simsq"] = sb["rope"][:, 52:84]
        sb["gat"] = sb["rope"][0:32, 84:96]
        sb["gatm"] = sb["rope"][0:32, 96:160]
        sb["gatr"] = sb["rope"][0:1, 160:256]
        sb["grow"] = sb["rope"][0:1, 256:320]
        sb["gsc"] = sb["rope"][0:1, 320:328]
        sb["wveff"] = sb["PTr"][:, 0:NCH * EH]
        sb["colv"] = sb["PTr"][0:1, NCH * EH:NCH * EH + EH]
        sbuf("colwo", [64, 1], F32)
        sbuf("colwob", [64, 1], BF16)
        sb["x32ld"] = sb["out32"][:]
        sb["comb"] = sb["enc"][0:64, 0:TO]
        sb["woeff"] = sb["q1"][0:64, 0:D]
        sb["xmidld"] = sb["q1"][:, 2048:2048 + 2 * TG]
        sb["wfold"] = sb["q2"][0:32, 0:2048]
        sb["foldrow"] = sb["q2"][0:1, 2048:4096]
        sb["wob1"] = sb["enc"][0:64, 2048:2048 + D]

        psum("big0", [128, 1536], F32)
        psum("big1", [128, 1536], F32)
        psum("pt", [128, 512], F32)
        psum("sm", [128, 512], F32)

        for s in sorted(set(ENG_SEM.values())):
            self.sems[s] = es.enter_context(nc.semaphore(s))

    def emit_consts(self):
        nc, sb, dr = self.nc, self.sb, self.dram
        self.cload = []
        for nm in ("ident", "ones", "eyeq", "eyek", "pselk"):
            o = self.op("dgp", lambda nm=nm: nc.gpsimd.dma_start(
                out=self.sb[nm][:], in_=self.dram[nm]), (), tag="ld_" + nm, stream="c")
            self.cload.append(o)
        for nm in ("idxlt", "qsstack", "ksstack", "bsel", "tri", "aqsel"):
            o = self.op("dsync", lambda nm=nm: nc.sync.dma_start(
                out=self.sb[nm][:], in_=self.dram[nm]), (), tag="ld_" + nm, stream="c2")
            self.cload.append(o)
        o = self.op("dsync", lambda: nc.sync.dma_start(
            out=self.sb["fident"][:], in_=self.dram["ident"]), (), tag="ld_fid", stream="c2")
        self.cload.append(o)
        m0 = self.op("vector", lambda: nc.vector.memset(sb["seqa"][:], 0.0))
        c0 = self.op("vector", lambda: nc.vector.memset(
            sb["cst"][:, 0:1], EPS_LN))
        c1 = self.op("vector", lambda: nc.vector.memset(
            sb["cst"][:, 1:2], EPS_RMS))
        self.cload += [c0, c1]
        m1 = self.op("vector", lambda: nc.vector.memset(sb["fones"][:], 1.0))
        m2 = self.op("vector", lambda: nc.vector.memset(sb["onesf"][:], 1.0))
        self.cload.append(m2)
        self.memsets = [m0, m1]
        self.seqa_last = m0

    def emit_group_proj(self, gg):
        nc, sb, ps, dr = self.nc, self.sb, self.ps, self.dram
        own = gg >= 2
        g = gg - 2
        col0 = gg * TG
        gt0 = gg * 4 - 8

        src = dr["xT"][:, col0:col0 + TG].rearrange("(c p) t -> p c t", p=128)
        ld_x = self.op("dgp", lambda src=src: nc.gpsimd.dma_start(
            out=sb["xfm"][:].rearrange("p (c t) -> p c t", c=NCH), in_=src),
            self.takewar("xfm") + self.cload, tag=f"ldx{gg}", stream="x")
        self.ld_x = ld_x

        tbl_deps = self.takewar("tbl")
        ld_t = []
        for half in (0, 1):
            for j, nm in ((0, "costab"), (1, "sintab")):
                o = self.op("dgp", lambda half=half, j=j, nm=nm:
                            nc.gpsimd.dma_start(
                                out=sb["tbl"][64 * half:64 * half + 64,
                                              j * TG:(j + 1) * TG],
                                in_=dr[nm][:, col0:col0 + TG]),
                            tbl_deps, tag=f"ldt{gg}", stream="tbl")
                ld_t.append(o)
        self.ld_t = ld_t

        mean_mms, ssq_mms = [], []
        for c in range(NCH):
            mean_mms.append(self.op("tensor", lambda c=c: nc.tensor.matmul(
                ps["big1"][0:1, 512:512 + TG],
                sb["ones"][:, 0:1], sb["xfm"][:, c * TG:(c + 1) * TG],
                start=(c == 0), stop=(c == NCH - 1)),
                [ld_x] + (self.takewar("b1_4") if c == 0 else []),
                tag=f"meanmm{gg}_{c}"))
        sq_ops = []
        for c in range(NCH):
            slot = c % 2
            sqo = self.op("scalar", lambda c=c, slot=slot:
                          nc.scalar.activation(
                              sb["sq"][:, slot * TG:(slot + 1) * TG],
                              sb["xfm"][:, c * TG:(c + 1) * TG], AF.Square),
                          [ld_x] + self.takewar(f"sq{slot}"),
                          tag=f"sq{gg}_{c}")
            sq_ops.append(sqo)
            mo = self.op("tensor", lambda c=c, slot=slot:
                         nc.tensor.matmul(
                ps["big1"][0:1, 1024:1024 + TG],
                sb["ones"][:, 0:1], sb["sq"][:, slot * TG:(slot + 1) * TG],
                start=(c == 0), stop=(c == NCH - 1)),
                [sqo] + (self.takewar("b1_5") if c == 0 else []),
                tag=f"ssqmm{gg}_{c}")
            ssq_mms.append(mo)
            self.setwar(f"sq{slot}", [mo])

        m1 = self.op("scalar", lambda: nc.scalar.activation(
            sb["m1"][:], ps["big1"][0:1, 512:512 + TG], AF.Copy,
            scale=1.0 / D), [mean_mms[-1]] + self.takewar("rowsc"),
            tag=f"m1_{gg}")
        self.addwar("b1_4", m1)
        m1sq = self.op("scalar", lambda: nc.scalar.activation(
            sb["rowa"][:], sb["m1"][:], AF.Square), [m1])
        msq = self.op("scalar", lambda: nc.scalar.activation(
            sb["rowb"][:], ps["big1"][0:1, 1024:1024 + TG], AF.Copy,
            scale=1.0 / D), [ssq_mms[-1]], tag=f"msq{gg}")
        self.addwar("b1_5", msq)
        var = self.op("vector", lambda: nc.vector.tensor_sub(
            sb["rowb"][:], sb["rowb"][:], sb["rowa"][:]), [msq, m1sq])
        sd = self.op("scalar", lambda: nc.scalar.activation(
            sb["rowa"][:], sb["rowb"][:], AF.Sqrt, bias=sb["cst"][0:1, 0:1]),
            [var], tag=f"sd{gg}")
        rstd1 = self.op("vector", lambda: nc.vector.reciprocal(
            sb["rstd1"][:], sb["rowa"][:]), [sd], tag=f"rstd1_{gg}")
        rstd1b = self.op("vector", lambda: nc.vector.tensor_copy(
            sb["rstd1b"][:], sb["rstd1"][:]),
            [rstd1] + self.takewar("rbf0"), tag=f"rstd1b_{gg}")
        m1b = self.op("vector", lambda: nc.vector.tensor_copy(
            sb["m1b"][:], sb["m1"][:]), [m1] + self.takewar("rbf1"),
            tag=f"m1b_{gg}")
        bcmm = self.op("tensor", lambda: nc.tensor.matmul(
            ps["sm"][:, 0:TG], sb["ones"][0:1, :], sb["rstd1b"][:],
            start=True, stop=True),
            [rstd1b] + self.takewar("smbank"), tag=f"bc1mm{gg}")
        self.addwar("rbf0", bcmm)
        bc1 = self.op("vector", lambda: nc.vector.tensor_copy(
            sb["bc"][:, 0:TG], ps["sm"][:, 0:TG]),
            [bcmm] + self.takewar("bc0"), tag=f"bc1_{gg}")
        bcmm2 = self.op("tensor", lambda: nc.tensor.matmul(
            ps["sm"][:, 0:TG], sb["ones"][0:1, :], sb["m1b"][:],
            start=True, stop=True), [m1b, bc1], tag=f"bcm1mm{gg}")
        self.addwar("rbf1", bcmm2)
        m1bc = self.op("vector", lambda: nc.vector.tensor_copy(
            sb["bc"][:, TG:2 * TG], ps["sm"][:, 0:TG]),
            [bcmm2] + self.takewar("bc1dst"), tag=f"m1bc_{gg}")
        self.addwar("smbank", m1bc)
        self.bc1_op = bc1
        self.setwar("rowsc", [m1, m1sq, msq, var, sd, rstd1, rstd1b, m1b,
                              bcmm, bcmm2])

        xz_ops = []
        for c in range(NCH):
            o = self.op("vector", lambda c=c: nc.vector.tensor_sub(
                sb["xfm"][:, c * TG:(c + 1) * TG],
                sb["xfm"][:, c * TG:(c + 1) * TG], sb["bc"][:, TG:2 * TG]),
                [mean_mms[c], sq_ops[c], m1bc], tag=f"xz{gg}_{c}")
            xz_ops.append(o)
            self.addwar("bc1dst", o)
        self.xz_ops = xz_ops

        self._proj_pass(gg, kv=True)
        if own:
            self._proj_pass(gg, kv=False)
            self._emit_aq_rows_finalize(gg)

    def _stream_block(self, wname, gg, b):
        nc, sb, dr = self.nc, self.sb, self.dram
        slot = b % 2
        deps = self.takewar(f"wslot{slot}")
        ops = []
        for h8 in range(2):
            src = dr[wname][b, h8 * 8:(h8 + 1) * 8].rearrange(
                "c p q -> p c q")
            o = self.op("dgp", lambda src=src, slot=slot, h8=h8:
                        nc.gpsimd.dma_start(
                            out=sb["wbuf"][:, slot * 2048 + h8 * 1024:
                                           slot * 2048 + (h8 + 1) * 1024
                                           ].rearrange("p (c q) -> p c q",
                                                       c=8),
                            in_=src), deps, tag=f"w{wname}{gg}_{b}_{h8}", stream=f"w{slot}")
            ops.append(o)
        return ops

    def _block_mms(self, wname, gg, b, pslot, wdma):
        nc, sb, ps = self.nc, self.sb, self.ps
        slot = b % 2
        mm = None
        for c in range(NCH):
            deps = [wdma[c // 8], self.xz_ops[c]]
            if c == 0:
                deps += self.takewar(f"big0_{pslot}")
            mm = self.op("tensor", lambda c=c, slot=slot, pslot=pslot:
                         nc.tensor.matmul(
                             ps["big0"][:, pslot * TG:(pslot + 1) * TG],
                             sb["wbuf"][:, slot * 2048 + c * 128:
                                        slot * 2048 + (c + 1) * 128],
                             sb["xfm"][:, c * TG:(c + 1) * TG],
                             start=(c == 0), stop=(c == NCH - 1)),
                         deps, tag=f"mm{wname}{gg}_{b}_{c}")
        self.setwar(f"wslot{slot}", [mm])
        self.addwar("xfm", mm)
        return mm

    def _proj_pass(self, gg, kv):
        nc, sb, ps = self.nc, self.sb, self.ps
        wname = "Wkv" if kv else "Wq"
        gt0 = gg * 4 - 8
        slot0 = gt0 % RING

        if kv:
            units = [("k", p) for p in range(4)] + [("v", h) for h in range(8)]
        else:
            units = [("q", j) for j in range(8)]

        for kind, idx in units:
            if kind in ("k", "q"):
                b1 = 2 * idx
                wd1 = self._stream_block(wname, gg, b1)
                wd2 = self._stream_block(wname, gg, b1 + 1)
                mm1 = self._block_mms(wname, gg, b1, 0, wd1)
                mm2 = self._block_mms(wname, gg, b1 + 1, 1, wd2)
                if self.pending_kscale is not None:
                    self.pending_kscale()
                    self.pending_kscale = None
                self._pair_post(gg, kind, idx, mm1, mm2, slot0)
            else:
                b = 8 + idx
                wd = self._stream_block(wname, gg, b)
                mm = self._block_mms(wname, gg, b, idx % 2, wd)
                if self.pending_kscale is not None:
                    self.pending_kscale()
                    self.pending_kscale = None
                self._v_post(gg, idx, mm, slot0)
        if self.pending_kscale is not None:
            self.pending_kscale()
            self.pending_kscale = None

    def _pair_post(self, gg, kind, idx, mm1, mm2, slot0):
        nc, sb, ps = self.nc, self.sb, self.ps
        isq = kind == "q"
        stack = sb["qsstack"] if isq else sb["ksstack"]
        nunits = 8 if isq else 4
        rows_n = 16 if isq else 8
        eye = sb["eyeq"] if isq else sb["eyek"]
        eyew = 16 if isq else 8
        rowkey = "aqrows" if isq else "akrows"

        sq_ops = []
        for half, mm in ((0, mm1), (1, mm2)):
            so = self.op("scalar", lambda half=half: nc.scalar.activation(
                sb["sq"][:, half * TG:(half + 1) * TG],
                ps["big0"][:, half * TG:(half + 1) * TG], AF.Square),
                [mm] + self.takewar(f"sq{half}"),
                tag=f"rsq{gg}{kind}{idx}_{half}")
            sq_ops.append(so)
        eye_last = self.war.get(rowkey + "_last", None)
        for half in (0, 1):
            deps = [sq_ops[half]]
            if eye_last is not None:
                deps.append(eye_last)
            if isq:
                start = (idx == 0 and half == 0)
                stop = (idx == nunits - 1 and half == 1)
            else:
                start = (half == 0)
                stop = (half == 1)
            if start:
                deps += self.takewar("b1_3")
            eye_last = self.op("tensor", lambda half=half, idx=idx,
                               start=start, stop=stop, rows_n=rows_n,
                               eye=eye, eyew=eyew:
                               nc.tensor.matmul(
                ps["big1"][0:rows_n, 0:TG],
                eye[:, idx * eyew:(idx + 1) * eyew],
                sb["sq"][:, half * TG:(half + 1) * TG],
                start=start, stop=stop),
                deps, tag=f"eye{gg}{kind}{idx}_{half}")
            self.setwar(f"sq{half}", [eye_last])
        self.war[rowkey + "_last"] = eye_last
        if idx == nunits - 1:
            self.war[rowkey + "_done"] = eye_last

        rope_war = self.takewar("rope")
        t1 = self.op("vector", lambda: nc.vector.scalar_tensor_tensor(
            sb["rope"][:, 0:TG], ps["big0"][:, 0:TG], stack[:, 0:1],
            sb["tbl"][:, 0:TG], ALU.mult, ALU.mult),
            [mm1] + self.ld_t + self.cload + rope_war,
            tag=f"t1_{gg}{kind}{idx}")
        t2 = self.op("vector", lambda: nc.vector.scalar_tensor_tensor(
            sb["rope"][:, TG:2 * TG], ps["big0"][:, TG:2 * TG],
            stack[:, 1:2], sb["tbl"][:, TG:2 * TG], ALU.mult, ALU.mult),
            [mm2] + self.ld_t + rope_war, tag=f"t2_{gg}{kind}{idx}")
        r1 = self.op("vector", lambda: nc.vector.tensor_sub(
            sb["rope"][:, 0:TG], sb["rope"][:, 0:TG], sb["rope"][:, TG:2 * TG]),
            [t1, t2], tag=f"r1_{gg}{kind}{idx}")
        t3 = self.op("vector", lambda: nc.vector.scalar_tensor_tensor(
            sb["rope"][:, TG:2 * TG], ps["big0"][:, TG:2 * TG],
            stack[:, 1:2], sb["tbl"][:, 0:TG], ALU.mult, ALU.mult),
            [mm2, r1], tag=f"t3_{gg}{kind}{idx}")
        t4 = self.op("vector", lambda: nc.vector.scalar_tensor_tensor(
            sb["rope"][:, 2 * TG:3 * TG], ps["big0"][:, 0:TG],
            stack[:, 0:1], sb["tbl"][:, TG:2 * TG], ALU.mult, ALU.mult),
            [mm1], tag=f"t4_{gg}{kind}{idx}")
        r2 = self.op("vector", lambda: nc.vector.tensor_add(
            sb["rope"][:, TG:2 * TG], sb["rope"][:, TG:2 * TG],
            sb["rope"][:, 2 * TG:3 * TG]),
            [t3, t4], tag=f"r2_{gg}{kind}{idx}")
        self.addwar("tbl", t1, t2, t3, t4)
        self.setwar("big0_0", [t1, t4, sq_ops[0]])
        self.setwar("big0_1", [t2, t3, sq_ops[1]])

        if isq:
            d1 = self.takewar(f"q1_{idx}")
            o1 = self.op("vector", lambda idx=idx: nc.vector.tensor_copy(
                sb["q1"][:, idx * TG:(idx + 1) * TG], sb["rope"][:, 0:TG]),
                [r1] + d1, tag=f"qw1_{gg}_{idx}")
            d2 = self.takewar(f"q2_{idx}")
            o2 = self.op("vector", lambda idx=idx: nc.vector.tensor_copy(
                sb["q2"][:, idx * TG:(idx + 1) * TG],
                sb["rope"][:, TG:2 * TG]), [r2] + d2,
                tag=f"qw2_{gg}_{idx}")
            self.q_ready[idx] = [o1, o2]
            self.setwar("rope", [o1, o2])
        else:
            pk = idx
            base = pk * RING * 128 + slot0 * 128
            dst_deps = []
            for s in range(4):
                dst_deps += self.takewar(f"kw_{slot0 + s}_{pk}")
            e5 = self._ak_pair_finalize(gg, pk)
            cps = []
            for (src_slot, tens) in ((0, "k1"), (1, "k2")):
                o = self.op("vector", lambda src_slot=src_slot, tens=tens,
                            base=base: nc.vector.tensor_copy(
                    sb[tens][:, base:base + TG],
                    sb["rope"][:, src_slot * TG:(src_slot + 1) * TG]),
                    [r1, r2] + dst_deps, tag=f"kcp{gg}_{idx}_{tens}")
                cps.append(o)
            self.setwar("rope", cps)

            def scale_pass(pk=pk, base=base, cps=cps, gg=gg, e5=e5,
                           slot0=slot0):
                nc, sb, ps = self.nc, self.sb, self.ps
                bcd = self.takewar("smbank")
                bm = self.op("tensor", lambda pk=pk: nc.tensor.matmul(
                    ps["sm"][:, 0:TG], sb["pselk"][:, pk * 128:(pk + 1) * 128],
                    sb["akb"][:], start=True, stop=True),
                    [e5] + bcd, tag=f"akbc{gg}{pk}")
                self.addwar("aq_war", bm)
                ev = self.op("vector", lambda: nc.vector.tensor_copy(
                    sb["bc"][:, TG:2 * TG], ps["sm"][:, 0:TG]),
                    [bm] + self.takewar("bc1dst"), tag=f"akbcev{gg}{pk}")
                self.addwar("smbank", ev)
                outs = []
                for ti, tens in enumerate(("k1", "k2")):
                    o = self.op("vector", lambda tens=tens, base=base:
                                nc.vector.tensor_mul(
                        sb[tens][:, base:base + TG],
                        sb[tens][:, base:base + TG],
                        sb["bc"][:, TG:2 * TG]),
                        [cps[ti], ev], tag=f"ksc{gg}_{pk}_{tens}")
                    outs.append(o)
                    self.addwar("bc1dst", o)
                for s in range(4):
                    self.kslot_ready[(slot0 + s, pk)] = outs
                    self.addwar(f"kw_{slot0 + s}_{pk}", *outs)
            self.pending_kscale = scale_pass

    def _ak_pair_finalize(self, gg, p):
        nc, sb, ps = self.nc, self.sb, self.ps
        eye_last = self.war["akrows_last"]
        e1 = self.op("scalar", lambda: nc.scalar.activation(
            sb["ak"][:], sb["bc"][0:8, 0:TG], AF.Square),
            [self.bc1_op] + self.takewar("aq_war"), tag=f"ak1_{gg}{p}")
        self.addwar("bc0", e1)
        e2 = self.op("vector", lambda: nc.vector.scalar_tensor_tensor(
            sb["ak"][:], sb["ak"][:], 1.0 / HD,
            ps["big1"][0:8, 0:TG], ALU.mult, ALU.mult),
            [e1, eye_last], tag=f"ak2_{gg}{p}")
        self.addwar("b1_3", e2)
        e3 = self.op("scalar", lambda: nc.scalar.activation(
            sb["ak"][:], sb["ak"][:], AF.Ln, bias=sb["cst"][0:8, 1:2]),
            [e2], tag=f"ak3_{gg}{p}")
        e4 = self.op("scalar", lambda: nc.scalar.activation(
            sb["ak"][:], sb["ak"][:], AF.Exp, scale=-0.5),
            [e3], tag=f"ak4_{gg}{p}")
        e5 = self.op("vector", lambda: nc.vector.tensor_mul(
            sb["akb"][:], sb["ak"][:], sb["bc"][0:8, 0:TG]),
            [e4, self.bc1_op], tag=f"ak5_{gg}{p}")
        self.addwar("bc0", e5)
        self.setwar("aq_war", [e5])
        return e5

    def _v_post(self, gg, h, mm, slot0):
        nc, sb, ps = self.nc, self.sb, self.ps
        pslot = h % 2
        ev = self.op("vector", lambda pslot=pslot:
                     nc.vector.scalar_tensor_tensor(
            sb["vh"][:], ps["big0"][:, pslot * TG:(pslot + 1) * TG], 0.0,
            sb["bc"][:, 0:TG], ALU.bypass, ALU.mult),
            [mm, self.bc1_op] + self.takewar("vh"), tag=f"vev{gg}_{h}")
        self.addwar("bc0", ev)
        self.setwar(f"big0_{pslot}", [ev])
        tps = []
        ptw = self.takewar("ptbank")
        for ti in range(4):
            o = self.op("tensor", lambda ti=ti: nc.tensor.matmul(
                ps["pt"][:, ti * 128:(ti + 1) * 128],
                sb["vh"][:, ti * 128:(ti + 1) * 128], sb["ident"][:],
                start=True, stop=True),
                [ev] + (ptw if ti == 0 else []), tag=f"vtp{gg}_{h}_{ti}")
            tps.append(o)
        dep_ring = []
        for s in range(4):
            dep_ring += self.takewar(f"vw_{slot0 + s}_{h}")
        evv = self.op("vector", lambda h=h, slot0=slot0:
                      nc.vector.tensor_copy(
            sb["vring"][:].rearrange("p (s h q) -> p s h q",
                                     s=RING, h=KVH)[:, slot0:slot0 + 4, h],
            ps["pt"][:, 0:512].rearrange("p (s q) -> p s q", s=4)),
            tps + dep_ring, tag=f"vrev{gg}_{h}")
        self.setwar("ptbank", [evv])
        self.addwar("vh", tps[-1])
        for s in range(4):
            self.vslot_ready[(slot0 + s, h)] = evv

    def _emit_aq_rows_finalize(self, gg):
        nc, sb, ps = self.nc, self.sb, self.ps
        eye_done = self.war["aqrows_done"]
        e1 = self.op("scalar", lambda: nc.scalar.activation(
            sb["aq"][:], sb["bc"][0:16, 0:TG], AF.Square),
            [self.bc1_op] + self.takewar("aq_war"), tag=f"aq1_{gg}")
        self.addwar("bc0", e1)
        e2 = self.op("vector", lambda: nc.vector.scalar_tensor_tensor(
            sb["aq"][:], sb["aq"][:], 1.0 / HD,
            ps["big1"][0:16, 0:TG], ALU.mult, ALU.mult),
            [e1, eye_done], tag=f"aq2_{gg}")
        self.addwar("b1_3", e2)
        e3 = self.op("scalar", lambda: nc.scalar.activation(
            sb["aq"][:], sb["aq"][:], AF.Ln,
            bias=sb["cst"][0:16, 1:2]), [e2])
        e4 = self.op("scalar", lambda: nc.scalar.activation(
            sb["aq"][:], sb["aq"][:], AF.Exp, scale=-0.5), [e3])
        e5 = self.op("vector", lambda: nc.vector.scalar_tensor_tensor(
            sb["aq"][:], sb["bc"][0:16, 0:TG], ATTN_SCALE,
            sb["aq"][:], ALU.mult, ALU.mult), [e4, self.bc1_op],
            tag=f"aq5_{gg}")
        self.addwar("bc0", e5)
        self.setwar("aq_war", [])
        for j in range(8):
            bankkey = "ptbank" if j % 2 == 0 else "smbank"
            dps = ps["pt"] if j % 2 == 0 else ps["sm"]
            deps = [e5] + self.cload + self.takewar(bankkey)
            bm = self.op("tensor", lambda j=j, dps=dps: nc.tensor.matmul(
                dps[:, 0:TG], sb["aqsel"][:, j * 128:(j + 1) * 128],
                sb["aq"][0:16, 0:TG], start=True, stop=True),
                deps, tag=f"qsc{gg}_{j}")
            self.addwar("aq_war", bm)
            m1d = self.q_ready.get(j, [])
            o1 = self.op("vector", lambda j=j, dps=dps: nc.vector.tensor_mul(
                sb["q1"][:, j * TG:(j + 1) * TG],
                sb["q1"][:, j * TG:(j + 1) * TG], dps[:, 0:TG]),
                [bm] + list(m1d), tag=f"qm1{gg}_{j}")
            o2 = self.op("vector", lambda j=j, dps=dps: nc.vector.tensor_mul(
                sb["q2"][:, j * TG:(j + 1) * TG],
                sb["q2"][:, j * TG:(j + 1) * TG], dps[:, 0:TG]),
                [bm] + list(m1d), tag=f"qm2{gg}_{j}")
            self.setwar(bankkey, [o1, o2])
            self.addwar(f"q1_{j}", o1)
            self.addwar(f"q2_{j}", o2)
            self.q_ready[j] = [o1, o2]

    def emit_group_attn(self, gg):
        nc, sb, ps = self.nc, self.sb, self.ps
        g = gg - 2
        t0 = g * 4
        self.enc_ready = {}

        entry = {0: (self.takewar("big0_0") + self.takewar("big0_1") +
                     self.takewar("S0")),
                 1: (self.takewar("b1_3") + self.takewar("b1_4") +
                     self.takewar("b1_5") + self.takewar("S1"))}
        entry_used = {0: False, 1: False}

        ndld = None
        if g < 2:
            nddeps = (list(self.q_ready.get(7, ())) +
                      self.takewar("ndld") + self.cload)
            ndld = self.op("dsync", lambda g=g: nc.sync.dma_start(
                out=sb["negnd"][0:1, 0:TG],
                in_=self.dram["negnd"][0:1, g * TG:(g + 1) * TG]),
                nddeps, tag=f"ndld{gg}", stream="c2")

        ALIVE = []
        for jj in range(12):
            if jj < 4:
                ALIVE.append((0, (jj + 1) * 128))
            elif jj < 8:
                ALIVE.append((0, 512))
            else:
                ALIVE.append(((jj - 8) * 128, 512))
        PCOL = []
        c = 0
        for jj in range(12):
            PCOL.append(c)
            lo, hi = ALIVE[jj]
            c += hi - lo
        DORD = [3, 0, 1, 2] + list(range(4, 12))

        exps = {}
        enc_last = {}

        def s_block(h, jj):
            par = h % 2
            pk = h // 4
            qj, qhalf = q_block_of(h)
            base = 64 * qhalf
            slot = (t0 - 8 + jj) % RING
            lo, hi = ALIVE[jj]
            big = ps["big0"] if par == 0 else ps["big1"]
            pp = jj % 2
            soff = pp * TG
            last = None
            for half, (qt, kt) in enumerate((("q1", "k1"), ("q2", "k2"))):
                deps = list(self.q_ready.get(qj, ())) + \
                    self.kslot_ready.get((slot, pk), [])
                if half == 0:
                    deps += self.takewar(f"sbuf{par}{pp}")
                    if not entry_used[par]:
                        deps += entry[par]
                o = self.op("tensor", lambda par=par, lo=lo, hi=hi, soff=soff,
                            base=base, qt=qt, kt=kt, qj=qj, pk=pk, slot=slot,
                            half=half, big=big: nc.tensor.matmul(
                    big[:, soff + lo:soff + hi],
                    sb[kt][base:base + 64,
                           pk * RING * 128 + slot * 128:
                           pk * RING * 128 + (slot + 1) * 128],
                    sb[qt][base:base + 64, qj * TG + lo:qj * TG + hi],
                    start=(half == 0), stop=(half == 1)),
                    deps, tag=f"S{gg}_{h}_{jj}_{half}")
                self.addwar(f"kw_{slot}_{pk}", o)
                self.addwar(f"q1_{qj}" if half == 0 else f"q2_{qj}", o)
                last = o
            mask = None
            if jj < 4:
                mask = self.op("vector", lambda big=big, soff=soff, jj=jj:
                               nc.vector.tensor_add(
                    big[:, soff + jj * 128:soff + (jj + 1) * 128],
                    big[:, soff + jj * 128:soff + (jj + 1) * 128],
                    sb["tri"][:, 0:128]), [last] + self.cload,
                    tag=f"mask{gg}_{h}_{jj}")
            elif jj >= 8:
                mask = self.op("vector", lambda big=big, soff=soff, jj=jj:
                               nc.vector.tensor_add(
                    big[:, soff + (jj - 8) * 128:soff + (jj - 7) * 128],
                    big[:, soff + (jj - 8) * 128:soff + (jj - 7) * 128],
                    sb["tri"][:, 128:256]), [last],
                    tag=f"mask{gg}_{h}_{jj}")
            pwar = self.takewar(f"ptr{par}_{jj}")
            ex = self.op("scalar", lambda par=par, lo=lo, hi=hi, soff=soff,
                         jj=jj, big=big: nc.scalar.activation(
                sb["PTr"][:, par * 4608 + PCOL[jj]:
                          par * 4608 + PCOL[jj] + (hi - lo)],
                big[:, soff + lo:soff + hi], AF.Exp),
                ([mask] if mask is not None else [last]) + pwar,
                tag=f"exp{gg}_{h}_{jj}")
            self.setwar(f"sbuf{par}{pp}", [ex])
            return ex

        def pvd_block(h, jj, first, last_pv, last_d):
            par = h % 2
            kvh = h // 2
            slot = (t0 - 8 + jj) % RING
            lo, hi = ALIVE[jj]
            big = ps["big0"] if par == 0 else ps["big1"]
            ex = exps[h][jj]
            vr = self.vslot_ready.get((slot, kvh))
            deps = [ex] + ([vr] if vr is not None else [])
            if first:
                deps += self.takewar(f"encp{par}")
                if not entry_used[par]:
                    deps += entry[par]
                    entry_used[par] = True
            pv = self.op("tensor", lambda par=par, lo=lo, hi=hi, jj=jj,
                         slot=slot, kvh=kvh, first=first, last_pv=last_pv,
                         big=big: nc.tensor.matmul(
                big[:, 1024 + lo:1024 + hi],
                sb["vring"][:, slot * KVH * 128 + kvh * 128:
                            slot * KVH * 128 + (kvh + 1) * 128],
                sb["PTr"][:, par * 4608 + PCOL[jj]:
                          par * 4608 + PCOL[jj] + (hi - lo)],
                start=first, stop=last_pv), deps, tag=f"pv{gg}_{h}_{jj}")
            self.addwar(f"vw_{slot}_{kvh}", pv)
            dps = ps["pt"] if par == 0 else ps["sm"]
            ddeps = [ex] + self.cload
            if first:
                ddeps += self.takewar("ptbank" if par == 0 else "smbank")
            dm = self.op("tensor", lambda par=par, lo=lo, hi=hi, jj=jj,
                         first=first, last_d=last_d, dps=dps:
                         nc.tensor.matmul(
                dps[:, lo:hi], sb["ones"][:, 0:128],
                sb["PTr"][:, par * 4608 + PCOL[jj]:
                          par * 4608 + PCOL[jj] + (hi - lo)],
                start=first, stop=last_d), ddeps, tag=f"dm{gg}_{h}_{jj}")
            self.addwar(f"ptr{par}_{jj}", pv, dm)
            return pv, dm

        def finish_head(h, pvlast, dlast):
            par = h % 2
            dps = ps["pt"] if par == 0 else ps["sm"]
            if g < 2:
                dlast = self.op("tensor", lambda dps=dps:
                                nc.tensor.matmul(
                    dps[:, 0:TG], sb["onesf"][0:1, 0:128],
                    sb["negnd"][0:1, 0:TG],
                    start=False, stop=True), [dlast, ndld] + self.cload,
                    tag=f"dnd{gg}_{h}")
                self.addwar("ndld", dlast)
                self.addwar("rope", dlast)
            rc1 = self.op("scalar", lambda par=par, dps=dps:
                          nc.scalar.activation(
                sb["recd"][:, par * TG:(par + 1) * TG], dps[:, 0:TG],
                AF.Ln), [dlast], tag=f"rcl{gg}_{h}")
            rc = self.op("scalar", lambda par=par: nc.scalar.activation(
                sb["recd"][:, par * TG:(par + 1) * TG],
                sb["recd"][:, par * TG:(par + 1) * TG], AF.Exp,
                scale=-1.0), [rc1], tag=f"rcp{gg}_{h}")
            self.setwar("ptbank" if par == 0 else "smbank", [rc])
            big = ps["big0"] if par == 0 else ps["big1"]
            edeps = [pvlast, rc]
            for t in range(4):
                edeps += self.takewar(f"encdst_{h}_{t}")
            ev = self.op("vector", lambda par=par, h=h, big=big:
                         nc.vector.tensor_mul(
                sb["enc"][:, h * TG:(h + 1) * TG],
                sb["recd"][:, par * TG:(par + 1) * TG],
                big[:, 1024:1536]), edeps, tag=f"encev{gg}_{h}")
            self.setwar(f"encp{par}", [ev])
            self.addwar("rope", ev)
            self.enc_ready.setdefault(h, []).append(ev)
            enc_last[par] = ev

        for step in range(QH + 1):
            h = step
            hp = step - 1
            for jj in range(12):
                if h < QH:
                    exps.setdefault(h, {})[jj] = s_block(h, jj)
                if hp >= 0:
                    dj = DORD[jj]
                    pv, dm = pvd_block(hp, dj, jj == 0,
                                       jj == 11, jj == 11 and (g >= 2))
                    if jj == 11:
                        finish_head(hp, pv, dm)

        self.setwar("S0", self.warlist("sbuf00") + self.warlist("sbuf01") +
                    [enc_last[0]])
        self.setwar("S1", self.warlist("sbuf10") + self.warlist("sbuf11") +
                    [enc_last[1]])

    def emit_group_out(self, gg):
        nc, sb, ps = self.nc, self.sb, self.ps
        g = gg - 2
        col0 = gg * TG

        s0war = self.takewar("S0")
        self.setwar("big0_0", s0war + self.warlist("big0_0"))
        self.addwar("big0_1", *s0war)
        s1war = self.takewar("S1")
        self.addwar("b1_3", *s1war)
        self.addwar("b1_4", *s1war)
        self.addwar("b1_5", *s1war)
        self.setwar("smbank", self.takewar("encbank") +
                    self.warlist("smbank"))

        mean_last = None
        ssq_last = None
        evac_ops = []
        for c in range(NCH):
            slot = c % 2
            wd = []
            wdeps = self.takewar(f"wslot{slot}")
            for h8 in range(2):
                src = self.dram["Wout"][c, h8 * 8:(h8 + 1) * 8].rearrange(
                    "n p q -> p n q")
                o = self.op("dgp", lambda src=src, slot=slot, h8=h8:
                            nc.gpsimd.dma_start(
                    out=sb["wbuf"][:, slot * 2048 + h8 * 1024:
                                   slot * 2048 + (h8 + 1) * 1024
                                   ].rearrange("p (n q) -> p n q", n=8),
                    in_=src), wdeps, tag=f"wout{g}_{c}_{h8}", stream=f"w{slot}")
                wd.append(o)
            mm = None
            for n in range(QH):
                deps = [wd[n // 8]] + self.enc_ready.get(n, [])
                if n == 0:
                    deps += self.takewar(f"big0_{slot}")
                mm = self.op("tensor", lambda n=n, slot=slot: nc.tensor.matmul(
                    ps["big0"][:, slot * TG:(slot + 1) * TG],
                    sb["wbuf"][:, slot * 2048 + n * 128:
                               slot * 2048 + (n + 1) * 128],
                    sb["enc"][:, n * TG:(n + 1) * TG],
                    start=(n == 0), stop=(n == QH - 1)),
                    deps, tag=f"op{g}_{c}_{n}")
                for tix4 in range(4):
                    self.addwar(f"encdst_{n}_{tix4}", mm)
            self.setwar(f"wslot{slot}", [mm])
            ev = self.op("scalar", lambda c=c, slot=slot: nc.scalar.activation(
                sb["x32"][:, c * TG:(c + 1) * TG],
                ps["big0"][:, slot * TG:(slot + 1) * TG], AF.Copy),
                [mm] + self.takewar(f"x32_{c}"), tag=f"atev{g}_{c}")
            cb = self.op("vector", lambda c=c, slot=slot: nc.vector.tensor_copy(
                sb["sq"][:, slot * TG:(slot + 1) * TG],
                sb["x32"][:, c * TG:(c + 1) * TG]),
                [ev] + self.takewar(f"sq{slot}"), tag=f"atb{g}_{c}")
            mean_last = self.op("tensor", lambda slot=slot, c=c:
                                nc.tensor.matmul(
                ps["big1"][0:1, 512:512 + TG], sb["ones"][:, 0:1],
                sb["sq"][:, slot * TG:(slot + 1) * TG],
                start=(c == 0), stop=(c == NCH - 1)),
                [cb] + (self.takewar("b1_4") if c == 0 else []),
                tag=f"m2mm{g}_{c}")
            self.setwar(f"sq{slot}", [mean_last])
            sq2 = self.op("scalar", lambda c=c, slot=slot: nc.scalar.activation(
                sb["fz"][:, slot * TG:(slot + 1) * TG],
                ps["big0"][:, slot * TG:(slot + 1) * TG], AF.Square),
                [mm, ev] + self.takewar(f"fz{slot}"), tag=f"atsq{g}_{c}")
            self.setwar(f"big0_{slot}", [ev, sq2])
            ssq_last = self.op("tensor", lambda slot=slot, c=c:
                               nc.tensor.matmul(
                ps["big1"][0:1, 1024:1024 + TG], sb["ones"][:, 0:1],
                sb["fz"][:, slot * TG:(slot + 1) * TG],
                start=(c == 0), stop=(c == NCH - 1)),
                [sq2] + (self.takewar("b1_5") if c == 0 else []),
                tag=f"s2mm{g}_{c}")
            self.setwar(f"fz{slot}", [ssq_last])
            evac_ops.append(ev)

        m2 = self.op("scalar", lambda: nc.scalar.activation(
            sb["m2"][:], ps["big1"][0:1, 512:512 + TG], AF.Copy,
            scale=1.0 / D), [mean_last] + self.takewar("rowsc"),
            tag=f"m2_{g}")
        self.addwar("b1_4", m2)
        m2sq = self.op("scalar", lambda: nc.scalar.activation(
            sb["rowa"][:], sb["m2"][:], AF.Square), [m2])
        msq2 = self.op("scalar", lambda: nc.scalar.activation(
            sb["rowb"][:], ps["big1"][0:1, 1024:1024 + TG], AF.Copy,
            scale=1.0 / D), [ssq_last], tag=f"msq2_{g}")
        self.addwar("b1_5", msq2)
        var2 = self.op("vector", lambda: nc.vector.tensor_sub(
            sb["rowa"][:], sb["rowb"][:], sb["rowa"][:]), [msq2, m2sq])
        sd2 = self.op("scalar", lambda: nc.scalar.activation(
            sb["rowb"][:], sb["rowa"][:], AF.Sqrt,
            bias=sb["cst"][0:1, 0:1]), [var2])
        rstd2 = self.op("vector", lambda: nc.vector.reciprocal(
            sb["rstd2"][:], sb["rowb"][:]), [sd2], tag=f"rstd2_{g}")
        m2b = self.op("vector", lambda: nc.vector.tensor_copy(
            sb["m2b"][:], sb["m2"][:]), [m2] + self.takewar("rbf0"))
        rstd2b = self.op("vector", lambda: nc.vector.tensor_copy(
            sb["rstd2b"][:], sb["rstd2"][:]), [rstd2] + self.takewar("rbf1"))
        bcm = self.op("tensor", lambda: nc.tensor.matmul(
            ps["sm"][:, 0:TG], sb["ones"][0:1, :], sb["m2b"][:],
            start=True, stop=True), [m2b] + self.takewar("smbank"),
            tag=f"bcm2_{g}")
        bc_m2 = self.op("vector", lambda: nc.vector.tensor_copy(
            sb["bc"][:, 0:TG], ps["sm"][:, 0:TG]),
            [bcm] + self.takewar("bc0"), tag=f"bcm2e_{g}")
        bcr = self.op("tensor", lambda: nc.tensor.matmul(
            ps["sm"][:, 0:TG], sb["ones"][0:1, :], sb["rstd2b"][:],
            start=True, stop=True), [rstd2b, bc_m2], tag=f"bcr2_{g}")
        self.addwar("rbf1", bcr)
        bc_r2 = self.op("vector", lambda: nc.vector.tensor_copy(
            sb["bc"][:, TG:2 * TG], ps["sm"][:, 0:TG]),
            [bcr] + self.takewar("bc1dst"), tag=f"bcr2e_{g}")
        self.addwar("smbank", bc_r2)
        self.setwar("rowsc", [m2, m2sq, msq2, var2, sd2, rstd2, m2b,
                              rstd2b, bcm, bcr])

        mean3_last = None
        ssq3_last = None
        for c in range(NCH):
            slot = c % 2
            xld = self.op("dsync", lambda c=c, slot=slot, col0=col0:
                          nc.sync.dma_start(
                out=sb["x32ld"][:, slot * TG:(slot + 1) * TG],
                in_=self.dram["xT"][c * 128:(c + 1) * 128,
                                    col0:col0 + TG]),
                self.takewar(f"xo32_{slot}"), tag=f"xld{g}_{c}", stream=f"xl{slot}")
            u1 = self.op("vector", lambda c=c: nc.vector.tensor_sub(
                sb["x32"][:, c * TG:(c + 1) * TG],
                sb["x32"][:, c * TG:(c + 1) * TG], sb["bc"][:, 0:TG]),
                [evac_ops[c], bc_m2], tag=f"sw2a{g}_{c}")
            u2 = self.op("vector", lambda c=c: nc.vector.tensor_mul(
                sb["x32"][:, c * TG:(c + 1) * TG],
                sb["x32"][:, c * TG:(c + 1) * TG], sb["bc"][:, TG:2 * TG]),
                [u1, bc_r2], tag=f"sw2b{g}_{c}")
            self.addwar("bc0", u1)
            self.addwar("bc1dst", u2)
            u3 = self.op("vector", lambda c=c, slot=slot: nc.vector.tensor_add(
                sb["x32"][:, c * TG:(c + 1) * TG],
                sb["x32"][:, c * TG:(c + 1) * TG],
                sb["x32ld"][:, slot * TG:(slot + 1) * TG]),
                [u2, xld], tag=f"sw2c{g}_{c}")
            self.addwar(f"xo32_{slot}", u3)
            self.sweep2_u3 = getattr(self, "sweep2_u3", {})
            self.sweep2_u3[c] = u3
            xmb = self.op("vector", lambda c=c, slot=slot: nc.vector.tensor_copy(
                sb["sq"][:, slot * TG:(slot + 1) * TG],
                sb["x32"][:, c * TG:(c + 1) * TG]),
                [u3] + self.takewar(f"sq{slot}"), tag=f"xmb{g}_{c}")
            st = self.op("dsync", lambda c=c, slot=slot, g=g:
                         nc.sync.dma_start(
                out=self.dram["xmid"][c, :, g * TG:(g + 1) * TG],
                in_=sb["sq"][:, slot * TG:(slot + 1) * TG]),
                [xmb], tag=f"xmst{g}_{c}", stream=f"xms{slot}")
            self.xmid_stored.append(st)
            mean3_last = self.op("tensor", lambda slot=slot, c=c:
                                 nc.tensor.matmul(
                ps["big1"][0:1, 512:512 + TG], sb["ones"][:, 0:1],
                sb["sq"][:, slot * TG:(slot + 1) * TG],
                start=(c == 0), stop=(c == NCH - 1)),
                [xmb] + (self.takewar("b1_4") if c == 0 else []),
                tag=f"m3mm{g}_{c}")
            self.setwar(f"sq{slot}", [mean3_last, st])
            sq3 = self.op("scalar", lambda c=c, slot=slot: nc.scalar.activation(
                sb["fz"][:, slot * TG:(slot + 1) * TG],
                sb["x32"][:, c * TG:(c + 1) * TG], AF.Square),
                [u3] + self.takewar(f"fz{slot}"), tag=f"sq3{g}_{c}")
            ssq3_last = self.op("tensor", lambda slot=slot, c=c:
                                nc.tensor.matmul(
                ps["big1"][0:1, 1024:1024 + TG], sb["ones"][:, 0:1],
                sb["fz"][:, slot * TG:(slot + 1) * TG],
                start=(c == 0), stop=(c == NCH - 1)),
                [sq3] + (self.takewar("b1_5") if c == 0 else []),
                tag=f"s3mm{g}_{c}")
            self.setwar(f"fz{slot}", [ssq3_last])

        m3 = self.op("scalar", lambda: nc.scalar.activation(
            sb["rowc"][:], ps["big1"][0:1, 512:512 + TG], AF.Copy,
            scale=1.0 / D), [mean3_last] + self.takewar("rowsc"),
            tag=f"m3_{g}")
        self.addwar("b1_4", m3)
        negm3 = self.op("scalar", lambda g=g: nc.scalar.activation(
            sb["m3"][:, g * TG:(g + 1) * TG], sb["rowc"][:], AF.Copy,
            scale=-1.0), [m3], tag=f"negm3_{g}")
        self.negm3_ops[g] = negm3
        m3sq = self.op("scalar", lambda: nc.scalar.activation(
            sb["rowa"][:], sb["rowc"][:], AF.Square), [m3])
        msq3 = self.op("scalar", lambda: nc.scalar.activation(
            sb["rowb"][:], ps["big1"][0:1, 1024:1024 + TG], AF.Copy,
            scale=1.0 / D), [ssq3_last, negm3, m3sq], tag=f"msq3_{g}")
        self.addwar("b1_5", msq3)
        var3 = self.op("vector", lambda: nc.vector.tensor_sub(
            sb["rowa"][:], sb["rowb"][:], sb["rowa"][:]), [msq3, m3sq])
        sd3 = self.op("scalar", lambda: nc.scalar.activation(
            sb["rowb"][:], sb["rowa"][:], AF.Sqrt,
            bias=sb["cst"][0:1, 0:1]), [var3])
        rstd3 = self.op("vector", lambda: nc.vector.reciprocal(
            sb["rstd3"][:], sb["rowb"][:]), [sd3], tag=f"rstd3_{g}")
        rstd3b = self.op("vector", lambda: nc.vector.tensor_copy(
            sb["rstd3b"][:], sb["rstd3"][:]),
            [rstd3] + self.takewar("rbf0"))
        bcm3 = self.op("tensor", lambda: nc.tensor.matmul(
            ps["sm"][:, 0:TG], sb["ones"][0:1, :], sb["rstd3b"][:],
            start=True, stop=True), [rstd3b] + self.takewar("smbank"),
            tag=f"bc3_{g}")
        self.addwar("rbf0", bcm3)
        bc_r3 = self.op("vector", lambda: nc.vector.tensor_copy(
            sb["bc"][:, 0:TG], ps["sm"][:, 0:TG]),
            [bcm3, bc_m2] + self.takewar("bc0"), tag=f"bc3e_{g}")
        self.addwar("smbank", bc_r3)
        self.setwar("rowsc", [m3, negm3, m3sq, msq3, var3, sd3, rstd3,
                              rstd3b, bcm3])

        stt_ops = []
        u3s = self.sweep2_u3
        for c in range(NCH):
            slot = c % 2
            o = self.op("vector", lambda c=c, slot=slot:
                        nc.vector.scalar_tensor_tensor(
                sb["fz"][:, slot * TG:(slot + 1) * TG],
                sb["x32"][:, c * TG:(c + 1) * TG], 0.0,
                sb["bc"][:, 0:TG], ALU.bypass, ALU.mult,
                accum_out=sb["seqp"][:, c:c + 1]),
                [bc_r3, u3s[c]] + self.takewar(f"fz{slot}") +
                self.takewar(f"seqp{c}"), tag=f"stt{g}_{c}")
            self.addwar("bc0", o)
            self.setwar(f"fz{slot}", [o])
            self.addwar(f"x32_{c}", o)
            stt_ops.append(o)
        acc = self.op("vector", lambda: nc.vector.tensor_add(
            sb["seqa"][:], sb["seqa"][:], sb["seqp"][:]),
            stt_ops + [self.seqa_last], tag=f"seqacc{g}")
        self.seqa_last = acc
        for c in range(NCH):
            self.addwar(f"seqp{c}", acc)
        sg = self.op("vector", lambda g=g: nc.vector.tensor_mul(
            sb["rowc"][:], sb["rstd3"][:], sb["m3"][:, g * TG:(g + 1) * TG]),
            [rstd3, negm3, m3sq], tag=f"sig1_{g}")
        sgs = self.op("vector", lambda g=g: nc.vector.reduce_sum(
            sb["sumw"][0:1, g:g + 1], sb["rowc"][:], axis=AX.X),
            [sg], tag=f"sig2_{g}")
        self.sumw_ops = getattr(self, "sumw_ops", []) + [sgs]
        self.addwar("rowsc", sg, sgs)

    def _row_transpose(self, tag, src_fn, dst_fn, deps, region):
        nc, sb, ps = self.nc, self.sb, self.ps
        smw = self.takewar("smbank")
        tp = self.op("tensor", lambda: nc.tensor.transpose(
            ps["sm"][0:32, region:region + 1] if True else None,
            src_fn(), sb["fident"][0:1, 0:1]), deps + smw,
            tag=tag + "_tp")
        ev = self.op("vector", lambda: nc.vector.tensor_copy(
            dst_fn(), ps["sm"][0:32, region:region + 1]), [tp],
            tag=tag + "_ev")
        self.addwar("smbank", ev)
        return ev

    def _col_transpose(self, tag, src_fn, dst_fn, deps, region):
        nc, sb, ps = self.nc, self.sb, self.ps
        smw = self.takewar("smbank")
        tp = self.op("tensor", lambda: nc.tensor.transpose(
            ps["sm"][0:1, region:region + 32],
            src_fn(), sb["fident"][0:32, 0:32]), deps + smw,
            tag=tag + "_tp")
        ev = self.op("vector", lambda: nc.vector.tensor_copy(
            dst_fn(), ps["sm"][0:1, region:region + 32]), [tp],
            tag=tag + "_ev")
        self.addwar("smbank", ev)
        return ev

    def emit_gating(self):
        nc, sb, ps, dr = self.nc, self.sb, self.ps, self.dram

        freed = []
        for j in range(8):
            freed += self.takewar(f"q1_{j}") + self.takewar(f"q2_{j}")
        for n in range(QH):
            for t4 in range(4):
                freed += self.takewar(f"encdst_{n}_{t4}")
        for p in (0, 1):
            for jj in range(12):
                freed += self.takewar(f"ptr{p}_{jj}")
        for k in ("wfold0", "foldrow0", "woeff", "wob1", "wveff",
                  "xmidld0", "xmidld1", "comb_ps"):
            self.setwar(k, freed + self.warlist(k))

        ganchor = self.takewar("rope")
        self.ganchor = ganchor

        z = self.op("vector", lambda: nc.vector.memset(
            sb["seqsb"][:], 0.0), (), tag="seqz")
        cp = self.op("vector", lambda: nc.vector.tensor_copy(
            sb["seqsb"][:, 0:16], sb["seqa"][:]), [z, self.seqa_last],
            tag="seqcp")
        ssum = self.op("vector", lambda: nc.vector.reduce_sum(
            sb["sumw"][0:1, 6:7], sb["sumw"][0:1, 0:4], axis=AX.X),
            self.sumw_ops, tag="sigsum")
        sput = self.op("vector", lambda: nc.vector.tensor_copy(
            sb["seqsb"][0:1, 16:17], sb["sumw"][0:1, 6:7]), [ssum, z],
            tag="sigput")
        st = self.op("dsync", lambda: nc.sync.dma_start(
            out=dr["seqin"], in_=sb["seqsb"][:]), [cp, sput], tag="seqst", stream="g")
        coll = self.op("coll", lambda: nc.gpsimd.collective_compute(
            "AllGather", ALU.bypass,
            replica_groups=[list(range(NCORE))],
            ins=[dr["seqin"][:]], outs=[dr["seqout"][:]]), [st], tag="ag")
        ldg = self.op("dsync", lambda: nc.sync.dma_start(
            out=sb["seqg"][:].rearrange("p (r c) -> p r c", r=NCORE),
            in_=dr["seqout"].rearrange("(r p) c -> p r c", p=128)),
            [coll], tag="seqld", stream="g")

        sums = {}
        for b in range(2):
            t1 = self.op("vector", lambda b=b: nc.vector.tensor_add(
                sb["srep"][:, b * 18:(b + 1) * 18],
                sb["seqg"][:, (4 * b) * 18:(4 * b) * 18 + 18],
                sb["seqg"][:, (4 * b + 1) * 18:(4 * b + 1) * 18 + 18]),
                [ldg], tag=f"bs{b}a")
            t2 = self.op("vector", lambda b=b: nc.vector.tensor_add(
                sb["srep"][:, b * 18:(b + 1) * 18],
                sb["srep"][:, b * 18:(b + 1) * 18],
                sb["seqg"][:, (4 * b + 2) * 18:(4 * b + 2) * 18 + 18]),
                [t1], tag=f"bs{b}b")
            t3 = self.op("vector", lambda b=b: nc.vector.tensor_add(
                sb["srep"][:, b * 18:(b + 1) * 18],
                sb["srep"][:, b * 18:(b + 1) * 18],
                sb["seqg"][:, (4 * b + 3) * 18:(4 * b + 3) * 18 + 18]),
                [t2], tag=f"bs{b}c")
            sums[b] = t3

        sld = self.op("dsync", lambda: nc.sync.dma_start(
            out=sb["simsb"][:].rearrange("p (c e) -> p c e", c=NCH),
            in_=dr["simM"].rearrange("(c p) e -> p c e", p=128)),
            self.takewar("x32_0"), tag="simld", stream="g")
        gld = self.op("dsync", lambda: nc.sync.dma_start(
            out=sb["gatr"][0:1, 0:32], in_=dr["gates"]), ganchor, tag="gld",
            stream="g")
        sig = self.op("scalar", lambda: nc.scalar.activation(
            sb["gatr"][0:1, 32:64], sb["gatr"][0:1, 0:32], AF.Sigmoid),
            [gld], tag="sig")
        sigc = self._row_transpose("sigT", lambda: sb["gatr"][0:1, 32:64],
                                   lambda: sb["gat"][0:32, 8:9],
                                   [sig], 160)

        simsq_last = None
        for c in range(NCH):
            sqq = self.op("scalar", lambda c=c: nc.scalar.activation(
                sb["simsq"][:], sb["simsb"][:, c * 32:(c + 1) * 32],
                AF.Square), [sld] + ([simsq_last] if simsq_last else
                                     ganchor),
                tag=f"simsq{c}")
            self.addwar("x32_0", sqq)
            simsq_last = self.op("tensor", lambda c=c: nc.tensor.matmul(
                ps["sm"][0:1, 8:40], sb["fones"][:], sb["simsq"][:],
                start=(c == 0), stop=(c == NCH - 1)),
                [sqq, self.memsets[1]] +
                (self.takewar("smbank") if c == 0 else []),
                tag=f"simmm{c}")
        scn = self.op("scalar", lambda: nc.scalar.activation(
            sb["grow"][0:1, 0:32], ps["sm"][0:1, 8:40], AF.Sqrt),
            [simsq_last], tag="scn")
        scr = self.op("vector", lambda: nc.vector.reciprocal(
            sb["grow"][0:1, 32:64], sb["grow"][0:1, 0:32]), [scn], tag="scr")
        self.addwar("smbank", scn)
        scrc = self._row_transpose("scrT", lambda: sb["grow"][0:1, 32:64],
                                   lambda: sb["gat"][0:32, 9:10],
                                   [scr], 128)
        self.sim_ops = [simsq_last, scrc, sigc]

        for b in range(2):
            self._gate_batch(b, sums[b])

        self.dbg("routing", [32, 8], F32,
                 lambda: sb["gat"][0:32, 0:8], self.routing_ops)
        self._fold()

    def _gate_batch(self, b, sum_op):
        nc, sb, ps, dr = self.nc, self.sb, self.ps, self.dram
        gw = self.takewar("gatew")
        sgbb = self.op("vector", lambda b=b: nc.vector.tensor_copy(
            sb["colscb"][0:1, 0:1], sb["srep"][0:1, b * 18 + 16:b * 18 + 17]),
            [sum_op] + self.warlist("colscb_war") + gw, tag=f"sgbb{b}")
        smw = self.takewar("smbank")
        sgm = self.op("tensor", lambda: nc.tensor.matmul(
            ps["sm"][:, 0:1], sb["ones"][0:1, :], sb["colscb"][0:1, 0:1],
            start=True, stop=True), [sgbb] + smw, tag=f"sgmm{b}")
        sgev = self.op("vector", lambda: nc.vector.tensor_copy(
            sb["small"][:, 0:1], ps["sm"][:, 0:1]), [sgm] + gw,
            tag=f"sgev{b}")
        self.addwar("smbank", sgev)
        sn = self.op("vector", lambda b=b: nc.vector.tensor_scalar(
            sb["snorm"][:], sb["srep"][:, b * 18:b * 18 + 16],
            sb["small"][:, 0:1], 1.0 / T, ALU.add, ALU.mult),
            [sum_op, sgev] + gw, tag=f"snorm{b}")
        nsq = self.op("scalar", lambda: nc.scalar.activation(
            sb["bc"][:, 0:16], sb["snorm"][:], AF.Square,
            accum_out=sb["colsc"][:, 0:1]),
            [sn] + self.takewar("bc0") + self.takewar("colsc_g") + gw,
            tag=f"nsq{b}")
        nmm = self.op("tensor", lambda: nc.tensor.matmul(
            ps["sm"][0:1, 4:5], sb["colsc"][:], sb["fones"][:],
            start=True, stop=True),
            [nsq, self.memsets[1], sgev] + self.takewar("smbank2"),
            tag=f"nmm{b}")
        nev = self.op("scalar", lambda: nc.scalar.activation(
            sb["gsc"][0:1, 0:1], ps["sm"][0:1, 4:5], AF.Sqrt),
            [nmm] + gw, tag=f"nev{b}")
        nrc = self.op("vector", lambda: nc.vector.reciprocal(
            sb["gsc"][0:1, 1:2], sb["gsc"][0:1, 0:1]), [nev], tag=f"nrc{b}")
        self.addwar("smbank2", nev)
        nrb = self.op("vector", lambda: nc.vector.tensor_copy(
            sb["colscb"][0:1, 0:1], sb["gsc"][0:1, 1:2]),
            [nrc, sgm] + self.warlist("colscb_war"), tag=f"nrb{b}")
        smw2 = self.takewar("smbank")
        nrm = self.op("tensor", lambda: nc.tensor.matmul(
            ps["sm"][:, 1:2], sb["ones"][0:1, :], sb["colscb"][0:1, 0:1],
            start=True, stop=True), [nrb] + smw2, tag=f"nrm{b}")
        nrbc = self.op("vector", lambda: nc.vector.tensor_copy(
            sb["small"][:, 1:2], ps["sm"][:, 1:2]), [nrm] + gw,
            tag=f"nrbc{b}")
        self.addwar("smbank", nrbc)
        aff_r = None
        aff_c = None
        b00 = self.takewar("big0_0")
        b01 = self.takewar("big0_1")
        for c in range(NCH):
            aff_r = self.op("tensor", lambda c=c: nc.tensor.matmul(
                ps["big0"][0:1, 0:32],
                sb["snorm"][:, c:c + 1], sb["simsb"][:, c * 32:(c + 1) * 32],
                start=(c == 0), stop=(c == NCH - 1)),
                [sn, self.sim_ops[0]] + (b00 if c == 0 else []),
                tag=f"affr{b}_{c}")
            aff_c = self.op("tensor", lambda c=c: nc.tensor.matmul(
                ps["big0"][0:32, 512:513],
                sb["simsb"][:, c * 32:(c + 1) * 32], sb["snorm"][:, c:c + 1],
                start=(c == 0), stop=(c == NCH - 1)),
                [sn] + (b01 if c == 0 else []), tag=f"affc{b}_{c}")
            self.addwar("x32_0", aff_r, aff_c)
        lr1 = self.op("vector", lambda: nc.vector.tensor_scalar_mul(
            sb["gatr"][0:1, 64:96], ps["big0"][0:1, 0:32],
            sb["small"][0:1, 1:2]), [aff_r, nrbc] + gw, tag=f"lr1{b}")
        self.addwar("big0_0", lr1)
        lr2 = self.op("vector", lambda: nc.vector.tensor_mul(
            sb["gatr"][0:1, 64:96], sb["gatr"][0:1, 64:96],
            sb["grow"][0:1, 32:64]), [lr1, self.sim_ops[1]], tag=f"lr2{b}")
        lrow = self.op("vector", lambda: nc.vector.tensor_sub(
            sb["gatr"][0:1, 64:96], sb["gatr"][0:1, 64:96],
            sb["gatr"][0:1, 32:64]), [lr2], tag=f"lrow{b}")
        lc1 = self.op("vector", lambda: nc.vector.tensor_scalar_mul(
            sb["gat"][0:32, 2:3], ps["big0"][0:32, 512:513],
            sb["small"][0:32, 1:2]), [aff_c, nrbc] + gw, tag=f"lc1{b}")
        self.addwar("big0_1", lc1)
        lc2 = self.op("vector", lambda: nc.vector.tensor_mul(
            sb["gat"][0:32, 2:3], sb["gat"][0:32, 2:3],
            sb["gat"][0:32, 9:10]), [lc1, self.sim_ops[1]], tag=f"lc2{b}")
        lcol = self.op("vector", lambda: nc.vector.tensor_sub(
            sb["gat"][0:32, 2:3], sb["gat"][0:32, 2:3], sb["gat"][0:32, 8:9]),
            [lc2, self.sim_ops[2]], tag=f"lcol{b}")
        smw = self.takewar("smbank")
        rmm = self.op("tensor", lambda: nc.tensor.matmul(
            ps["sm"][0:32, 192:224], sb["onesf"][0:1, 0:32],
            sb["gatr"][0:1, 64:96], start=True, stop=True),
            [lrow] + self.cload + smw, tag=f"rmm{b}")
        rev = self.op("vector", lambda: nc.vector.tensor_copy(
            sb["gatm"][:, 0:32], ps["sm"][0:32, 192:224]), [rmm] + gw,
            tag=f"rev{b}")
        self.addwar("smbank", rev)
        gt_ = self.op("vector", lambda: nc.vector.tensor_single_scalar(
            sb["gatm"][:, 32:64], sb["gatm"][:, 0:32], sb["gat"][0:32, 2:3],
            ALU.is_gt), [rev, lcol], tag=f"ggt{b}")
        eq_ = self.op("vector", lambda: nc.vector.tensor_single_scalar(
            sb["gatm"][:, 0:32], sb["gatm"][:, 0:32], sb["gat"][0:32, 2:3],
            ALU.is_equal), [gt_], tag=f"geq{b}")
        tie = self.op("vector", lambda: nc.vector.tensor_mul(
            sb["gatm"][:, 0:32], sb["gatm"][:, 0:32], sb["idxlt"][:]),
            [eq_] + self.cload, tag=f"gtie{b}")
        rkp = self.op("vector", lambda: nc.vector.tensor_add(
            sb["gatm"][:, 0:32], sb["gatm"][:, 0:32], sb["gatm"][:, 32:64]),
            [tie], tag=f"grk{b}")
        rank = self.op("vector", lambda: nc.vector.reduce_sum(
            sb["gat"][0:32, 3:4], sb["gatm"][:, 0:32], axis=AX.X), [rkp],
            tag=f"grank{b}")
        fb_ = self.op("vector", lambda: nc.vector.tensor_single_scalar(
            sb["gat"][0:32, 4:5], sb["gat"][0:32, 3:4], 15.0, ALU.is_le),
            [rank], tag=f"gfb{b}")
        mc = self.op("vector", lambda: nc.vector.tensor_single_scalar(
            sb["gat"][0:32, 5:6], sb["gat"][0:32, 2:3], 0.0, ALU.is_gt),
            [lcol], tag=f"gmc{b}")
        cmm = self.op("tensor", lambda: nc.tensor.matmul(
            ps["sm"][0:1, 228:229], sb["gat"][0:32, 5:6],
            sb["fones"][0:32, 0:1], start=True, stop=True),
            [mc, rmm, self.memsets[1]], tag=f"cmm{b}")
        cev = self.op("vector", lambda: nc.vector.tensor_copy(
            sb["gsc"][0:1, 2:3], ps["sm"][0:1, 228:229]), [cmm] + gw,
            tag=f"cev{b}")
        self.addwar("smbank", cev)
        inz = self.op("vector", lambda: nc.vector.tensor_single_scalar(
            sb["gsc"][0:1, 3:4], sb["gsc"][0:1, 2:3], 0.0, ALU.is_equal),
            [cev], tag=f"ginz{b}")
        smw = self.takewar("smbank")
        ibm = self.op("tensor", lambda: nc.tensor.matmul(
            ps["sm"][0:32, 232:233], sb["onesf"][0:1, 0:32],
            sb["gsc"][0:1, 3:4], start=True, stop=True),
            [inz] + smw, tag=f"gibm{b}")
        ibe = self.op("vector", lambda: nc.vector.tensor_copy(
            sb["gat"][0:32, 6:7], ps["sm"][0:32, 232:233]), [ibm] + gw,
            tag=f"gibe{b}")
        self.addwar("smbank", ibe)
        fbt = self.op("vector", lambda: nc.vector.tensor_mul(
            sb["gat"][0:32, 6:7], sb["gat"][0:32, 6:7], sb["gat"][0:32, 4:5]),
            [ibe, fb_], tag=f"gfbt{b}")
        mfin = self.op("vector", lambda: nc.vector.tensor_add(
            sb["gat"][0:32, 5:6], sb["gat"][0:32, 5:6], sb["gat"][0:32, 6:7]),
            [fbt, mc], tag=f"gmfin{b}")
        gcol = self.op("vector", lambda: nc.vector.tensor_relu(
            sb["gat"][0:32, 7:8], sb["gat"][0:32, 2:3]), [lcol, gt_, eq_],
            tag=f"ggcol{b}")
        z1 = self.op("vector", lambda: nc.vector.tensor_mul(
            sb["gat"][0:32, 7:8], sb["gat"][0:32, 7:8], sb["gat"][0:32, 5:6]),
            [gcol, mfin], tag=f"gz1{b}")
        z2 = self.op("vector", lambda: nc.vector.tensor_scalar(
            sb["gat"][0:32, 6:7], sb["gat"][0:32, 5:6], -1.0, 1.0,
            ALU.mult, ALU.add), [mfin, fbt], tag=f"gz2{b}")
        z3 = self.op("vector", lambda: nc.vector.tensor_scalar_mul(
            sb["gat"][0:32, 6:7], sb["gat"][0:32, 6:7], -1.0e38), [z2],
            tag=f"gz3{b}")
        zz = self.op("vector", lambda: nc.vector.tensor_add(
            sb["gat"][0:32, 7:8], sb["gat"][0:32, 7:8], sb["gat"][0:32, 6:7]),
            [z3, z1], tag=f"gzz{b}")
        zrow = self._col_transpose(f"zT{b}", lambda: sb["gat"][0:32, 7:8],
                                   lambda: sb["grow"][0:1, 0:32],
                                   [zz, self.sim_ops[1]], 240)
        mx = self.op("vector", lambda: nc.vector.reduce_max(
            sb["gsc"][0:1, 5:6], sb["grow"][0:1, 0:32], axis=AX.X,
            negate=True), [zrow], tag=f"gmx{b}")
        ex = self.op("scalar", lambda: nc.scalar.activation(
            sb["grow"][0:1, 0:32], sb["grow"][0:1, 0:32], AF.Exp,
            bias=sb["gsc"][0:1, 5:6], accum_out=sb["gsc"][0:1, 6:7]),
            [mx], tag=f"gex{b}")
        rcp = self.op("vector", lambda: nc.vector.reciprocal(
            sb["gsc"][0:1, 7:8], sb["gsc"][0:1, 6:7]), [ex], tag=f"grcp{b}")
        rt = self.op("vector", lambda: nc.vector.tensor_scalar_mul(
            sb["grow"][0:1, 0:32], sb["grow"][0:1, 0:32],
            sb["gsc"][0:1, 7:8]), [rcp, ex], tag=f"grt{b}")
        rce = self._row_transpose(f"rT{b}", lambda: sb["grow"][0:1, 0:32],
                                  lambda: sb["gat"][0:32, 0:1],
                                  [rt] + gw, 272)
        rmk = self.op("vector", lambda: nc.vector.tensor_mul(
            sb["gat"][0:32, 1:2], sb["gat"][0:32, 0:1], sb["gat"][0:32, 5:6]),
            [rce, mfin] + gw, tag=f"grmk{b}")
        rbM = self.op("vector", lambda b=b: nc.vector.tensor_copy(
            sb["rmb"][0:32, 2 * b:2 * b + 1], sb["gat"][0:32, 1:2]),
            [rmk], tag=f"grb{b}")
        rbP = self.op("vector", lambda b=b: nc.vector.tensor_copy(
            sb["rmb"][0:32, 2 * b + 1:2 * b + 2], sb["gat"][0:32, 0:1]),
            [rce, rmk], tag=f"grb2{b}")
        self.routing_ops = getattr(self, "routing_ops", []) + [rbM, rbP]
        self.setwar("gatew", [rbM, rbP, lrow, zrow, mx, rt, sn, nsq, nrc,
                              mfin, fbt, gcol, rank, inz, cev])

    def _fold(self):
        nc, sb, ps, dr = self.nc, self.sb, self.ps, self.dram
        NCHUNK = 8
        self.setwar("fold0", self.takewar("big0_0") + self.takewar("big0_1") +
                    self.takewar("S0"))
        self.setwar("fold1", self.takewar("b1_3") + self.takewar("b1_4") +
                    self.takewar("b1_5") + self.takewar("S1"))
        stores = []
        for wi, (wnm, bb) in enumerate((("WvT", 0), ("WvT", 1),
                                        ("WoF", 0), ("WoF", 1))):
            rcol = 2 * bb + (0 if wnm == "WvT" else 1)
            rop = self.routing_ops[rcol]
            for ch in range(NCHUNK):
                slot = 0
                wd = self.op("dgp", lambda ch=ch, slot=slot, wnm=wnm:
                             nc.gpsimd.dma_start(
                    out=sb["wfold"][:, slot * 2048:(slot + 1) * 2048],
                    in_=dr[wnm][:, ch * 2048:(ch + 1) * 2048]),
                    self.takewar(f"wfold{slot}"), tag=f"f{wnm}{bb}_{ch}",
                    stream="wf")
                mms = []
                for pp in range(4):
                    bigt = ps["big0"] if pp < 3 else ps["big1"]
                    pcol = (pp % 3) * 512
                    deps = [wd, rop]
                    if pp == 0:
                        deps += self.takewar("fold0")
                    if pp == 3:
                        deps += self.takewar("fold1")
                    o = self.op("tensor", lambda pp=pp, slot=slot, bigt=bigt,
                                pcol=pcol, rcol=rcol: nc.tensor.matmul(
                        bigt[0:1, pcol:pcol + 512],
                        sb["rmb"][0:32, rcol:rcol + 1],
                        sb["wfold"][0:32, slot * 2048 + pp * 512:
                                    slot * 2048 + (pp + 1) * 512],
                        start=True, stop=True), deps,
                        tag=f"fm{wnm}{bb}_{ch}_{pp}")
                    mms.append(o)
                ev1 = self.op("vector", lambda slot=slot:
                              nc.vector.tensor_copy(
                    sb["foldrow"][0:1, 0:1536],
                    ps["big0"][0:1, 0:1536]), mms[0:3] +
                    self.takewar("foldrow0"), tag=f"fe1{wnm}{bb}_{ch}")
                ev2 = self.op("vector", lambda slot=slot:
                              nc.vector.tensor_copy(
                    sb["foldrow"][0:1, 1536:2048],
                    ps["big1"][0:1, 0:512]), [mms[3]],
                    tag=f"fe2{wnm}{bb}_{ch}")
                self.setwar("fold0", [ev1])
                self.setwar("fold1", [ev2])
                sto = self.op("dsync", lambda ch=ch, wi=wi:
                              nc.sync.dma_start(
                    out=dr["foldin"][0:1, wi * 16384 + ch * 2048:
                                     wi * 16384 + (ch + 1) * 2048],
                    in_=sb["foldrow"][0:1, 0:2048]),
                    [ev1, ev2], tag=f"fs{wnm}{bb}_{ch}", stream="fst")
                self.setwar(f"wfold{slot}", mms)
                self.setwar("foldrow0", [sto])
                stores.append(sto)
        self.addwar("big0_0", *self.takewar("fold0"))
        self.addwar("b1_3", *self.takewar("fold1"))
        co = self.op("coll", lambda: nc.gpsimd.collective_compute(
            "AllGather", ALU.bypass,
            replica_groups=[list(range(NCORE))],
            ins=[dr["foldin"][:]], outs=[dr["foldout"][:]]), stores,
            tag="agw")
        fo = dr["foldout"]
        self.wveff_ld = []
        ld_b1 = []
        for r in range(NCORE):
            for cc in range(2):
                c = 2 * r + cc
                for bb, dstn in ((0, "wveff"), (1, "fz")):
                    src_ap = fo[r:r + 1,
                                bb * 16384 + cc * 8192:
                                bb * 16384 + (cc + 1) * 8192].rearrange(
                        "one (p h) -> (one p) h", p=128)
                    o = self.op("dgp", lambda src_ap=src_ap, c=c, dstn=dstn:
                                nc.gpsimd.dma_start(
                        out=sb[dstn][:, c * EH:(c + 1) * EH], in_=src_ap),
                        [co] + (self.takewar("wveff") if dstn == "wveff"
                                else self.takewar("fz0") +
                                self.takewar("fz1")),
                        tag=f"wveffld{bb}_{c}", stream="we")
                    (self.wveff_ld if bb == 0 else ld_b1).append(o)
        bl1 = self.op("vector", lambda: nc.vector.tensor_scalar_mul(
            sb["wveff"][:], sb["wveff"][:], sb["bsel"][:, 0:1]),
            self.wveff_ld + self.cload, tag="wvbl1")
        bl2 = self.op("vector", lambda: nc.vector.scalar_tensor_tensor(
            sb["wveff"][:], sb["fz"][:, 0:2 * TG], sb["bsel"][:, 1:2],
            sb["wveff"][:], ALU.mult, ALU.add),
            [bl1] + ld_b1, tag="wvbl2")
        self.addwar("fz0", bl2)
        self.addwar("fz1", bl2)
        self.wveff_ld = [bl2]
        self.woeff_ld = []
        ldo_b1 = []
        for r in range(NCORE):
            for bb, dstn in ((0, "woeff"), (1, "wob1")):
                src_ap = fo[r:r + 1,
                            32768 + bb * 16384:32768 + bb * 16384 + 16384
                            ][0:1, 0:16384].rearrange(
                    "one (h d) -> (one h) d", h=8)
                o = self.op("dgp", lambda src_ap=src_ap, r=r, dstn=dstn:
                            nc.gpsimd.dma_start(
                    out=sb[dstn][8 * r:8 * r + 8, :], in_=src_ap),
                    [co] + self.takewar("woeff" if dstn == "woeff"
                                        else "wob1"),
                    tag=f"woeffld{bb}_{r}", stream="we")
                (self.woeff_ld if bb == 0 else ldo_b1).append(o)
        bo1 = self.op("vector", lambda: nc.vector.tensor_scalar_mul(
            sb["woeff"][:], sb["woeff"][:], sb["bsel"][0:64, 0:1]),
            self.woeff_ld + self.cload, tag="wobl1")
        bo2 = self.op("vector", lambda: nc.vector.scalar_tensor_tensor(
            sb["woeff"][:], sb["wob1"][:], sb["bsel"][0:64, 1:2],
            sb["woeff"][:], ALU.mult, ALU.add),
            [bo1] + ldo_b1, tag="wobl2")
        self.addwar("wob1_done", bo2)
        self.woeff_ld = [bo2]

    def emit_ffn(self):
        nc, sb, ps, dr = self.nc, self.sb, self.ps, self.dram

        cvm = None
        for c in range(NCH):
            cvm = self.op("tensor", lambda c=c: nc.tensor.matmul(
                ps["sm"][0:1, 448:448 + EH], sb["ones"][:, 0:1],
                sb["wveff"][:, c * EH:(c + 1) * EH],
                start=(c == 0), stop=(c == NCH - 1)),
                self.wveff_ld + (self.takewar("smbank") if c == 0 else []),
                tag=f"cvm{c}")
        cve = self.op("vector", lambda: nc.vector.tensor_copy(
            sb["colv"][:], ps["sm"][0:1, 448:448 + EH]), [cvm], tag="cve")
        self.addwar("smbank", cve)
        cwo = self.op("scalar", lambda: nc.scalar.activation(
            sb["bc"][0:64, 0:TG], sb["woeff"][:, 0:TG], AF.Copy,
            accum_out=sb["colwo"][:]), self.woeff_ld + self.takewar("bc0"),
            tag="cwo1")
        cwos = [cwo]
        for q in range(1, 4):
            o = self.op("scalar", lambda q=q: nc.scalar.activation(
                sb["bc"][0:64, 0:TG], sb["woeff"][:, q * TG:(q + 1) * TG],
                AF.Copy, accum_out=sb["colsc"][0:64, 0:1]),
                [cwos[-1]] + self.takewar("colsc_g"), tag=f"cwo{q}")
            a = self.op("vector", lambda: nc.vector.tensor_add(
                sb["colwo"][:], sb["colwo"][:], sb["colsc"][0:64, 0:1]),
                [o, cwos[-1] if q == 1 else cwos[-1]], tag=f"cwoa{q}")
            self.addwar("colsc_g", a)
            cwos.append(a)
        cwob = self.op("vector", lambda: nc.vector.tensor_copy(
            sb["colwob"][:], sb["colwo"][:]), [cwos[-1]], tag="cwob")
        self.addwar("bc0", cwos[-1])

        self.dbg("wveff", [128, NCH * EH], BF16, lambda: self.sb["wveff"][:],
                 self.wveff_ld)
        self.dbg("woeff", [64, D], BF16, lambda: self.sb["woeff"][:],
                 self.woeff_ld)
        for f in range(4):
            self._ffn_fchunk(f, cve, cwob)
        self.dbg("comb", [64, TO], BF16, lambda: self.sb["comb"][:],
                 [o for o in self.ops if o.tag and o.tag.startswith("cemb")])


    def _ffn_fchunk(self, f, cve, cwob):
        nc, sb, ps, dr = self.nc, self.sb, self.ps, self.dram
        g = f
        mm = None
        for c in range(NCH):
            slot = c % 2
            xld = self.op("dgp", lambda c=c, f=f, slot=slot:
                          nc.gpsimd.dma_start(
                out=sb["xmidld"][:, slot * TG:(slot + 1) * TG],
                in_=dr["xmid"][c, :, f * TG:(f + 1) * TG]),
                self.takewar(f"xmidld{slot}") + self.xmid_stored,
                tag=f"cxl{f}_{c}", stream=f"xm{slot}")
            deps = [xld] + self.wveff_ld
            if c == 0:
                deps += self.takewar("comb_ps") + self.takewar("smbank")
            mm = self.op("tensor", lambda c=c, slot=slot: nc.tensor.matmul(
                ps["sm"][0:EH, 0:TG],
                sb["wveff"][:, c * EH:(c + 1) * EH],
                sb["xmidld"][:, slot * TG:(slot + 1) * TG],
                start=(c == 0), stop=False), deps, tag=f"cm{f}_{c}")
            self.addwar(f"xmidld{slot}", mm)
        corr = self.op("tensor", lambda f=f: nc.tensor.matmul(
            ps["sm"][0:EH, 0:TG], sb["colv"][:],
            sb["m3"][:, f * TG:(f + 1) * TG], start=False, stop=True),
            [mm, cve, self.negm3_ops[f]], tag=f"cmc{f}")
        cemb = self.op("vector", lambda f=f: nc.vector.tensor_copy(
            sb["comb"][:, f * TG:(f + 1) * TG], ps["sm"][0:EH, 0:TG]),
            [corr], tag=f"cemb{f}")
        self.setwar("comb_ps", [cemb])
        self.addwar("smbank", cemb)

        m4m = self.op("tensor", lambda f=f: nc.tensor.matmul(
            ps["big1"][0:1, 512:512 + TG], sb["colwob"][:],
            sb["comb"][:, f * TG:(f + 1) * TG], start=True, stop=True),
            [cemb, cwob] + self.takewar("b1_4"), tag=f"m4m{f}")
        m4 = self.op("scalar", lambda: nc.scalar.activation(
            sb["m4"][:], ps["big1"][0:1, 512:512 + TG], AF.Copy,
            scale=1.0 / D), [m4m] + self.takewar("rowsc"), tag=f"m4_{f}")
        self.addwar("b1_4", m4)
        negm4 = self.op("scalar", lambda: nc.scalar.activation(
            sb["negm4b"][:], sb["m4"][:], AF.Copy, scale=-1.0),
            [m4] + self.takewar("rbf1"))

        ssq_last = None
        evs = []
        for c in range(NCH):
            slot = c % 2
            deps = self.woeff_ld + [cemb, negm4]
            if c == 0:
                deps += self.takewar("big0_0") + self.takewar("big0_1")
            mmf = self.op("tensor", lambda c=c, slot=slot, f=f:
                          nc.tensor.matmul(
                ps["big0"][:, slot * TG:(slot + 1) * TG],
                sb["woeff"][:, c * 128:(c + 1) * 128],
                sb["comb"][:, f * TG:(f + 1) * TG],
                start=True, stop=False),
                deps + self.takewar(f"big0_{slot}"), tag=f"fm{f}_{c}")
            mc = self.op("tensor", lambda c=c, slot=slot: nc.tensor.matmul(
                ps["big0"][:, slot * TG:(slot + 1) * TG],
                sb["ones"][0:1, 0:128], sb["negm4b"][:],
                start=False, stop=True), [mmf], tag=f"fc{f}_{c}")
            self.addwar("rbf1", mc)
            ev = self.op("scalar", lambda c=c, slot=slot:
                         nc.scalar.activation(
                sb["x32"][:, c * TG:(c + 1) * TG],
                ps["big0"][:, slot * TG:(slot + 1) * TG], AF.Copy),
                [mc] + self.takewar(f"x32_{c}"), tag=f"fev{f}_{c}")
            sq4 = self.op("scalar", lambda c=c, slot=slot:
                          nc.scalar.activation(
                sb["fz"][:, slot * TG:(slot + 1) * TG],
                ps["big0"][:, slot * TG:(slot + 1) * TG], AF.Square),
                [mc, ev] + self.takewar(f"fz{slot}"), tag=f"fsq{f}_{c}")
            self.setwar(f"big0_{slot}", [ev, sq4])
            ssq_last = self.op("tensor", lambda slot=slot, c=c:
                               nc.tensor.matmul(
                ps["big1"][0:1, 1024:1024 + TG], sb["ones"][:, 0:1],
                sb["fz"][:, slot * TG:(slot + 1) * TG],
                start=(c == 0), stop=(c == NCH - 1)),
                [sq4] + (self.takewar("b1_5") if c == 0 else []),
                tag=f"fsmm{f}_{c}")
            self.setwar(f"fz{slot}", [ssq_last])
            evs.append(ev)

        msq4 = self.op("scalar", lambda: nc.scalar.activation(
            sb["rowb"][:], ps["big1"][0:1, 1024:1024 + TG], AF.Copy,
            scale=1.0 / D), [ssq_last] + self.takewar("rowsc"),
            tag=f"msq4_{f}")
        self.addwar("b1_5", msq4)
        sd4 = self.op("scalar", lambda: nc.scalar.activation(
            sb["rowa"][:], sb["rowb"][:], AF.Sqrt,
            bias=sb["cst"][0:1, 0:1]), [msq4])
        rstd4 = self.op("vector", lambda: nc.vector.reciprocal(
            sb["rstd4"][:], sb["rowa"][:]), [sd4], tag=f"rstd4_{f}")
        rstd4b = self.op("vector", lambda: nc.vector.tensor_copy(
            sb["rstd4b"][:], sb["rstd4"][:]), [rstd4] + self.takewar("rbf0"))
        bcm = self.op("tensor", lambda: nc.tensor.matmul(
            ps["sm"][:, 0:TG], sb["ones"][0:1, :], sb["rstd4b"][:],
            start=True, stop=True),
            [rstd4b] + self.takewar("smbank") + self.takewar("comb_ps"),
            tag=f"bc4_{f}")
        bc4 = self.op("vector", lambda: nc.vector.tensor_copy(
            sb["bc"][:, 0:TG], ps["sm"][:, 0:TG]),
            [bcm] + self.takewar("bc0"), tag=f"bc4e_{f}")
        self.addwar("smbank", bc4)
        self.addwar("rbf0", bcm)
        self.setwar("rowsc", [msq4, sd4, rstd4, rstd4b, bcm, m4, negm4])
        self.setwar("comb_ps", [bc4])

        for c in range(NCH):
            slot = c % 2
            xld = self.op("dgp", lambda c=c, f=f, slot=slot:
                          nc.gpsimd.dma_start(
                out=sb["xmidld"][:, slot * TG:(slot + 1) * TG],
                in_=dr["xmid"][c, :, f * TG:(f + 1) * TG]),
                self.takewar(f"xmidld{slot}"), tag=f"yxl{f}_{c}", stream=f"xm{slot}")
            u = self.op("vector", lambda c=c: nc.vector.tensor_mul(
                sb["x32"][:, c * TG:(c + 1) * TG],
                sb["x32"][:, c * TG:(c + 1) * TG], sb["bc"][:, 0:TG]),
                [evs[c], bc4], tag=f"yu{f}_{c}")
            self.addwar("bc0", u)
            oslot = c % 2
            y = self.op("vector", lambda c=c, slot=slot, oslot=oslot:
                        nc.vector.tensor_add(
                sb["out32"][:, oslot * TG:(oslot + 1) * TG],
                sb["x32"][:, c * TG:(c + 1) * TG],
                sb["xmidld"][:, slot * TG:(slot + 1) * TG]),
                [u, xld] + self.takewar(f"xo32_{oslot}"),
                tag=f"y{f}_{c}")
            self.addwar(f"xmidld{slot}", y)
            self.addwar(f"x32_{c}", y)
            sto = self.op("dsync", lambda c=c, f=f, oslot=oslot:
                          nc.sync.dma_start(
                out=dr["yT"][c * 128:(c + 1) * 128, f * TG:(f + 1) * TG],
                in_=sb["out32"][:, oslot * TG:(oslot + 1) * TG]),
                [y], tag=f"yst{f}_{c}", stream=f"y{oslot}")
            self.setwar(f"xo32_{oslot}", [sto])
            self.final_deps.append(sto)

    def build(self):
        self._alloc()
        self.emit_consts()
        self.xmid_stored = []
        self.negm3_ops = [None] * NG
        for gg in range(NGALL):
            self.emit_group_proj(gg)
            if gg >= 2:
                self.emit_group_attn(gg)
                self.emit_group_out(gg)
        sb = self.sb
        self.dbg("q1", [128, 8 * TG], BF16, lambda: sb["q1"][:],
                 sum((self.q_ready.get(j, []) for j in range(8)), []))
        self.dbg("q2", [128, 8 * TG], BF16, lambda: sb["q2"][:],
                 sum((self.q_ready.get(j, []) for j in range(8)), []))
        self.dbg("k1", [128, 4 * RING * 128], BF16, lambda: sb["k1"][:],
                 sum((self.kslot_ready.get((s, p), [])
                      for s in range(RING) for p in range(4)), []))
        self.dbg("k2", [128, 4 * RING * 128], BF16, lambda: sb["k2"][:],
                 sum((self.kslot_ready.get((s, p), [])
                      for s in range(RING) for p in range(4)), []))
        self.dbg("vring", [128, RING * KVH * 128], BF16,
                 lambda: sb["vring"][:],
                 [self.vslot_ready[k] for k in self.vslot_ready])
        self.dbg("enc", [128, QH * TG], BF16, lambda: sb["enc"][:],
                 sum((self.enc_ready.get(n, []) for n in range(QH)), []))
        self.dbg("x32", [128, NCH * TG], F32, lambda: sb["x32"][:],
                 [o for o in self.ops if o.tag and o.tag.startswith("sw2c3_")])
        self.dbg("seqa", [128, 16], F32, lambda: sb["seqa"][:],
                 [self.seqa_last])
        self.dbg("P", [128, 2 * 4608], BF16, lambda: sb["PTr"][:],
                 [o for o in self.ops if o.tag and o.tag.startswith("exp5_")])
        if self.stop != "groups":
            self.emit_gating()
            self.emit_ffn()
        return self._emit()

    def _emit(self):
        nc = self.nc
        if self.bank_edges:
            for ia, ib in self.bank_edges:
                self.ops[ib].deps.append(self.ops[ia])
        if self.serial:
            prev = None
            for o in self.ops:
                if prev is not None:
                    o.deps = list(o.deps) + [prev]
                prev = o
        for o in self.ops:
            if o.is_dma:
                o.need_inc = True
            for d in o.deps:
                d.need_inc = True
        for o in self.final_deps:
            o.need_inc = True

        for o in self.ops:
            o.semkey = (ENG_SEM[o.eng] + "_" + o.stream) if o.is_dma \
                else ENG_SEM[o.eng]
        keys = sorted({o.semkey for o in self.ops if o.need_inc})
        for k in keys:
            if k not in self.sems:
                self.sems[k] = self.es.enter_context(nc.semaphore(k))
        counts = {}
        for o in self.ops:
            if o.need_inc:
                counts[o.semkey] = counts.get(o.semkey, 0) + \
                    (1 if o.eng == "coll" else (16 if o.is_dma else 1))
                o.val = counts[o.semkey]
        self.totals = dict(counts)
        know = {}
        per_key = {}
        for o in self.ops:
            k = {}
            for d in o.deps:
                kd = know.get(id(d))
                if kd:
                    for s, v in kd.items():
                        if k.get(s, 0) < v:
                            k[s] = v
                if d.need_inc:
                    if k.get(d.semkey, 0) < d.val:
                        k[d.semkey] = d.val
            know[id(o)] = k
            if o.is_dma and o.need_inc:
                per_key.setdefault(o.semkey, []).append(o)
        for kk, lst in per_key.items():
            i = 0
            while i < len(lst):
                j = i + 1
                while j < len(lst):
                    kj = know.get(id(lst[j]), {})
                    if kj.get(kk, 0) >= lst[j - 1].val:
                        break
                    j += 1
                end_val = lst[j - 1].val
                for t in range(i, j):
                    lst[t].val_wait = end_val
                lst[i].batch_start = True
                i = j
        for o in self.ops:
            if o.need_inc and o.val_wait is None:
                o.val_wait = o.val

        by_exec = {e: [] for e in
                   ("tensor", "vector", "scalar", "gpsimd", "sync")}
        for o in self.ops:
            by_exec[ENG_EXEC[o.eng]].append(o)
        watermark = {e: {} for e in by_exec}
        sems = self.sems

        def run_stream(eng_obj, ops, ename):
            wm = watermark[ename]
            for o in ops:
                for d in o.deps:
                    sk = d.semkey
                    v = d.val_wait
                    assert v is not None, (d.tag, o.tag)
                    if wm.get(sk, 0) < v:
                        eng_obj.wait_ge(sems[sk], v)
                        wm[sk] = v
                inst = o.fn()
                o.iname = inst.ins.name
                if self.capture:
                    o.psum = _extract_psum(inst)
                if o.need_inc:
                    inst.then_inc(sems[o.semkey],
                                  1 if o.eng == "coll" else
                                  (16 if o.is_dma else 1))
            if ename == "sync":
                for o in self.final_deps:
                    sk, v = o.semkey, o.val_wait
                    if wm.get(sk, 0) < v:
                        eng_obj.wait_ge(sems[sk], v)
                        wm[sk] = v
                for sk, tot in sorted(self.totals.items()):
                    if "_" in sk and wm.get(sk, 0) < tot:
                        eng_obj.wait_ge(sems[sk], tot)
                        wm[sk] = tot

        with self.es:
            block = self.es.enter_context(nc.Block())
            block.tensor(lambda e: run_stream(e, by_exec["tensor"], "tensor"))
            block.vector(lambda e: run_stream(e, by_exec["vector"], "vector"))
            block.scalar(lambda e: run_stream(e, by_exec["scalar"], "scalar"))
            block.gpsimd(lambda e: run_stream(e, by_exec["gpsimd"], "gpsimd"))
            block.sync(lambda e: run_stream(e, by_exec["sync"], "sync"))
        return nc



_PSUM_INFO = {"big0": (0, 1536, 4), "big1": (3, 1536, 4),
              "pt": (6, 512, 4), "sm": (7, 512, 4)}


def _extract_psum(bi):
    mi = bi.ins
    acc = []

    def scan(lst, kind):
        for pap in lst:
            memref = getattr(pap, "memref", None)
            if memref not in _PSUM_INFO:
                continue
            b0, fsz, esz = _PSUM_INFO[memref]
            col0 = pap.offset % fsz
            ext = 0
            for pr in pap.ap:
                st, num = int(pr[0]), int(pr[1])
                if st % fsz == 0:
                    continue
                ext += abs(st) * (num - 1)
            lo = b0 + (col0 * esz) // 2048
            hi = b0 + ((col0 + ext) * esz) // 2048
            for b in range(lo, hi + 1):
                acc.append((b, kind))

    scan(mi.ins, "R")
    scan(mi.outs, "W")
    return acc or None


def compute_bank_edges(mk):
    from bisect import bisect_right
    ops = mk.ops
    CE = ("tensor", "vector", "scalar")
    streams = {}
    pos = {}
    for o in ops:
        e = ENG_EXEC[o.eng]
        lst = streams.setdefault(e, [])
        pos[id(o)] = len(lst)
        lst.append(o)
    INF = float("inf")
    icov = {}
    for e in CE:
        lst = streams.get(e, [])
        arr = [INF] * len(lst)
        nxt = INF
        for i in range(len(lst) - 1, -1, -1):
            o = lst[i]
            if (not o.is_dma) and o.need_inc:
                nxt = o.val
            arr[i] = nxt
        icov[e] = arr
    sem_of = {e: ENG_SEM[e] for e in CE}
    wmv = {e: {} for e in streams}
    knowpos = {e: {ae: -1 for ae in CE} for e in streams}
    hist = [{ae: [] for ae in CE} for _ in range(8)]
    edges = []
    for o in ops:
        e = ENG_EXEC[o.eng]
        wv, kp = wmv[e], knowpos[e]
        upd = False
        for d in o.deps:
            v = d.val_wait
            if v is None:
                continue
            sk = d.semkey
            if wv.get(sk, 0) < v:
                wv[sk] = v
                upd = True
        if upd:
            for ae in CE:
                v = wv.get(sem_of[ae], 0)
                if v:
                    p = bisect_right(icov[ae], v) - 1
                    if p > kp[ae]:
                        kp[ae] = p
        if o.psum and e in CE:
            mypos = pos[id(o)]
            for b in sorted({bb for bb, _k in o.psum}):
                hb = hist[b]
                for ae in CE:
                    if ae == e:
                        continue
                    lst = hb[ae]
                    if lst and lst[-1][0] > kp[ae]:
                        ap_, ag = lst[-1]
                        edges.append((ag, o.gidx))
                        kp[ae] = ap_
                hb[e].append((mypos, o.gidx))
    seen = set()
    out = []
    for ed in edges:
        if ed not in seen:
            seen.add(ed)
            out.append(ed)
    return out


def build_banked(**kw):
    m1 = MK(capture=True, **kw)
    m1.build()
    edges = compute_bank_edges(m1)
    m2 = MK(bank_edges=edges, **kw)
    nc = m2.build()
    return m2, nc, edges



_BUILD_CACHE = {}


def kernel(L0_attn_x, L0_attn_q_kernel, L0_attn_kv_kernel, L0_attn_out_kernel,
           L0_attn_q_norm_scale, L0_attn_k_norm_scale, L0_ffn_sim_matrix,
           L0_ffn_gates, L0_ffn_w_q, L0_ffn_w_k, L0_ffn_w_v, L0_ffn_o_weights):
    _ = (L0_ffn_w_q, L0_ffn_w_k)
    x = np.asarray(L0_attn_x, dtype=np.float32)
    shared = host_weights(np.asarray(L0_attn_q_kernel, np.float32),
                          np.asarray(L0_attn_kv_kernel, np.float32),
                          np.asarray(L0_attn_out_kernel, np.float32),
                          np.asarray(L0_ffn_w_v, np.float32),
                          np.asarray(L0_ffn_o_weights, np.float32),
                          np.asarray(L0_ffn_sim_matrix, np.float32))
    shared.update(host_tables())
    in_maps = []
    for core in range(NCORE):
        d = host_core_inputs(core, x,
                             np.asarray(L0_attn_q_norm_scale, np.float32),
                             np.asarray(L0_attn_k_norm_scale, np.float32),
                             np.asarray(L0_ffn_gates, np.float32), shared)
        d["WvT"] = np.ascontiguousarray(
            shared["WvT"][:, core * 16384:(core + 1) * 16384])
        d["WoF"] = np.ascontiguousarray(
            shared["WoF"][:, core * 16384:(core + 1) * 16384])
        in_maps.append(d)
    _mk, nc, _edges = build_banked()
    res = run_bass_kernel_spmd(nc, in_maps, core_ids=list(range(NCORE)))
    out = np.empty((B, T, D), dtype=np.float32)
    for core in range(NCORE):
        b, cs = core // 4, core % 4
        out[b, cs * TO:(cs + 1) * TO, :] = res.results[core]["yT"].T
    return out


# version 08
# speedup vs baseline: 1.3026x; 1.0519x over previous
import math
from contextlib import ExitStack

import numpy as np
import ml_dtypes

import concourse.bass as bass
import concourse.mybir as mybir
from concourse.bass_utils import run_bass_kernel_spmd

F32 = mybir.dt.float32
BF16 = mybir.dt.bfloat16
AF = mybir.ActivationFunctionType
ALU = mybir.AluOpType
AX = mybir.AxisListType
bfdt = ml_dtypes.bfloat16

B, T, D = 2, 8192, 2048
QH, KVH, HD = 16, 8, 128
WIN = 1024
E, EH = 32, 64
ATTN_SCALE = 0.08838834764831845
EPS_RMS = 1e-6
EPS_LN = 1e-5
ROPE_BASE = 10000.0

NCORE = 8
TO = 2048
HALO = 1024
TG = 512
NT = 16
NG = 4
NGALL = 6
NCH = 16
RING = 12
NEG = -1.0e30
QPAIR_A = [0, 1, 4, 5, 8, 9, 12, 13]


def q_block_of(n):
    j = (n // 4) * 2 + (n % 2)
    half = 0 if (n % 4) < 2 else 1
    return j, half



def host_tables():
    c = {}
    c["ident"] = np.eye(128, dtype=np.float32)
    c["ones"] = np.ones((128, 128), dtype=np.float32)
    r = np.arange(128)[:, None]
    col = np.arange(128)[None, :]
    c["tri"] = np.concatenate(
        [np.where(r > col, 0.0, NEG), np.where(r <= col, 0.0, NEG)],
        axis=1).astype(np.float32)
    aqs = np.zeros((16, 8, 128), dtype=np.float32)
    for j in range(8):
        a = QPAIR_A[j]
        aqs[a, j, 0:64] = 1.0
        aqs[a + 2, j, 64:128] = 1.0
    c["aqsel"] = aqs.reshape(16, 8 * 128)
    eq = np.zeros((8, 128, 16), dtype=np.float32)
    for j in range(8):
        a = QPAIR_A[j]
        eq[j, 0:64, a] = 1.0
        eq[j, 64:128, a + 2] = 1.0
    c["eyeq"] = eq.transpose(1, 0, 2).reshape(128, 8 * 16)
    ek = np.zeros((4, 128, 8), dtype=np.float32)
    for p in range(4):
        ek[p, 0:64, 2 * p] = 1.0
        ek[p, 64:128, 2 * p + 1] = 1.0
    c["eyek"] = ek.transpose(1, 0, 2).reshape(128, 4 * 8)
    pk = np.zeros((4, 8, 128), dtype=np.float32)
    for p in range(4):
        pk[p, 2 * p, 0:64] = 1.0
        pk[p, 2 * p + 1, 64:128] = 1.0
    c["pselk"] = pk.transpose(1, 0, 2).reshape(8, 4 * 128)
    ii = np.arange(32)
    c["idxlt"] = (ii[None, :] < ii[:, None]).astype(np.float32)
    return c


def host_weights(qk, kvk, outk, wv, wo, sim):
    s = {}
    qb = np.empty((8, 2, NCH, 128, 128), dtype=np.float32)
    for j in range(8):
        a = QPAIR_A[j]
        for v in range(2):
            blk = np.concatenate([qk[a][:, v * 64:(v + 1) * 64],
                                  qk[a + 2][:, v * 64:(v + 1) * 64]], axis=1)
            qb[j, v] = blk.reshape(NCH, 128, 128)
    s["Wq"] = np.ascontiguousarray(qb.reshape(16, NCH, 128, 128))
    kb = np.empty((16, NCH, 128, 128), dtype=np.float32)
    kk, vv = kvk[0], kvk[1]
    for p in range(4):
        for v in range(2):
            blk = np.concatenate([kk[2 * p][:, v * 64:(v + 1) * 64],
                                  kk[2 * p + 1][:, v * 64:(v + 1) * 64]], axis=1)
            kb[p * 2 + v] = blk.reshape(NCH, 128, 128)
    for h in range(8):
        kb[8 + h] = vv[h].reshape(NCH, 128, 128)
    s["Wkv"] = np.ascontiguousarray(kb)
    ob = np.empty((NCH, QH, 128, 128), dtype=np.float32)
    for n in range(QH):
        for c in range(NCH):
            ob[c, n] = outk[n][:, c * 128:(c + 1) * 128]
    s["Wout"] = np.ascontiguousarray(ob)
    s["WvT"] = np.ascontiguousarray(wv.transpose(0, 2, 1)).reshape(E, D * EH)
    s["WoF"] = np.ascontiguousarray(wo).reshape(E, EH * D)
    s["simM"] = np.ascontiguousarray(sim)
    return s


def host_core_inputs(core, x, qs, ks, gates, shared):
    b, cs = core // 4, core % 4
    t0 = cs * TO
    d = {}
    xs = np.zeros((HALO + TO, D), dtype=np.float32)
    lo = max(0, t0 - HALO)
    xs[HALO - (t0 - lo):HALO] = x[b, lo:t0]
    xs[HALO:] = x[b, t0:t0 + TO]
    d["xT"] = np.ascontiguousarray(xs.T)

    pos = np.arange(t0 - HALO, t0 + TO, dtype=np.float64)
    pos = np.maximum(pos, 0.0)
    frac = np.arange(0, HD, 2, dtype=np.float64) / HD
    ts = ROPE_BASE ** frac
    ang = pos[None, :] / ts[:, None]
    d["costab"] = np.cos(ang).astype(np.float32)
    d["sintab"] = np.sin(ang).astype(np.float32)

    sp = cs * TO + np.arange(2 * TG)
    nd = np.maximum(0, 1023 - sp).astype(np.float32)
    d["negnd"] = (-nd).reshape(1, 2 * TG)

    d["qsstack"] = np.stack([np.concatenate([qs[0:64], qs[0:64]]),
                             np.concatenate([qs[64:128], qs[64:128]])],
                            1).astype(np.float32)
    d["ksstack"] = np.stack([np.concatenate([ks[0:64], ks[0:64]]),
                             np.concatenate([ks[64:128], ks[64:128]])],
                            1).astype(np.float32)
    d["gates"] = gates.reshape(1, E).astype(np.float32)
    bs = np.zeros((128, 2), np.float32)
    bs[:, b] = 1.0
    d["bsel"] = bs
    d.update(shared)
    return d



class Op:
    __slots__ = ("eng", "fn", "deps", "is_dma", "need_inc", "val", "tag",
                 "stream", "semkey", "val_wait", "batch_start", "gidx",
                 "psum", "iname")

    def __init__(self, eng, fn, deps, is_dma, tag, stream):
        self.eng, self.fn, self.deps = eng, fn, deps
        self.is_dma = is_dma
        self.need_inc = False
        self.val = None
        self.val_wait = None
        self.tag = tag
        self.stream = stream
        self.semkey = None
        self.batch_start = False
        self.gidx = -1
        self.psum = None
        self.iname = None


ENG_SEM = {"tensor": "sT", "vector": "sV", "scalar": "sS", "gpsimd": "sG",
           "dsync": "qS", "dscalar": "qA", "dgp": "qG", "coll": "qC"}
ENG_EXEC = {"tensor": "tensor", "vector": "vector", "scalar": "scalar",
            "gpsimd": "gpsimd", "dsync": "sync", "dscalar": "scalar",
            "dgp": "gpsimd", "coll": "gpsimd"}


class MK:
    def __init__(self, debug=(), stop=None, serial=False, capture=False,
                 bank_edges=None):
        self.debug = set(debug)
        self.stop = stop
        self.serial = serial
        self.capture = capture
        self.bank_edges = bank_edges
        self.nc = bass.Bass("TRN2", target_bir_lowering=False)
        self.ops = []
        self.es = ExitStack()
        self.sems = {}
        self.dram = {}
        self.sb = {}
        self.ps = {}
        self.final_deps = []
        self.war = {}
        self.colT_ready = {"colkv": [None] * 16, "colq": [None] * 16}
        self.kslot_ready = {}
        self.vslot_ready = {}
        self.q_ready = {}
        self.acol_ready = [None] * 4
        self.pending_kscale = None
        self.enc_ready = {}

    def op(self, eng, fn, deps=(), tag=None, stream="s"):
        o = Op(eng, fn, [d for d in deps if d is not None],
               eng in ("dsync", "dscalar", "dgp", "coll"), tag, stream)
        o.gidx = len(self.ops)
        self.ops.append(o)
        return o

    def dbg(self, name, shape, dtype, src_fn, deps):
        if name not in self.debug:
            return None
        ap = self.nc.dram_tensor("dbg_" + name, list(shape), dtype,
                                 kind="ExternalOutput").ap()
        o = self.op("dgp", lambda ap=ap, f=src_fn:
                    self.nc.gpsimd.dma_start(out=ap, in_=f()),
                    deps, tag="dbg_" + name, stream="dbg")
        self.final_deps.append(o)
        return o

    def warlist(self, key):
        return self.war.get(key, [])

    def setwar(self, key, ops):
        self.war[key] = list(ops)

    def addwar(self, key, *ops):
        self.war.setdefault(key, []).extend(ops)

    def takewar(self, key):
        v = self.war.get(key, [])
        self.war[key] = []
        return v

    def _alloc(self):
        nc, es = self.nc, self.es

        def sbuf(name, shape, dt):
            self.sb[name] = es.enter_context(
                nc.sbuf_tensor("sb_" + name, list(shape), dt))

        def psum(name, shape, dt):
            self.ps[name] = es.enter_context(
                nc.psum_tensor(name, list(shape), dt))

        def dram_in(name, shape, dt=F32):
            self.dram[name] = nc.dram_tensor(
                name, list(shape), dt, kind="ExternalInput").ap()

        dram_in("xT", [D, HALO + TO])
        dram_in("costab", [64, HALO + TO])
        dram_in("sintab", [64, HALO + TO])
        dram_in("negnd", [1, 2 * TG])
        dram_in("aqsel", [16, 8 * 128])
        dram_in("qsstack", [128, 2])
        dram_in("ksstack", [128, 2])
        dram_in("ident", [128, 128])
        dram_in("ones", [128, 128])
        dram_in("tri", [128, 256])
        dram_in("eyeq", [128, 128])
        dram_in("eyek", [128, 32])
        dram_in("pselk", [8, 512])
        dram_in("idxlt", [32, 32])
        dram_in("bsel", [128, 2])
        dram_in("Wq", [16, NCH, 128, 128])
        dram_in("Wkv", [16, NCH, 128, 128])
        dram_in("Wout", [NCH, QH, 128, 128])
        dram_in("WvT", [E, 16384])
        dram_in("WoF", [E, 16384])
        dram_in("simM", [D, E])
        dram_in("gates", [1, E])

        self.dram["yT"] = nc.dram_tensor("yT", [D, TO], F32,
                                         kind="ExternalOutput").ap()
        self.dram["xmid"] = nc.dram_tensor("xmid", [NCH, 128, TO], BF16,
                                           kind="Internal").ap()
        self.dram["foldin"] = nc.dram_tensor("foldin", [1, 65536], BF16,
                                             kind="Internal").ap()
        self.dram["foldout"] = nc.dram_tensor("foldout", [NCORE, 65536], BF16,
                                              kind="Internal").ap()
        self.dram["seqin"] = nc.dram_tensor("seqin", [128, 18], F32,
                                            kind="Internal").ap()
        self.dram["seqout"] = nc.dram_tensor("seqout", [128 * NCORE, 18], F32,
                                             kind="Internal").ap()

        sbuf("xfm", [128, NCH * TG], BF16)
        sbuf("q1", [128, 8 * TG], BF16)
        sbuf("q2", [128, 8 * TG], BF16)
        sbuf("enc", [128, QH * TG], BF16)
        sbuf("k1", [128, 4 * RING * 128], BF16)
        sbuf("k2", [128, 4 * RING * 128], BF16)
        sbuf("vring", [128, RING * KVH * 128], BF16)
        sbuf("x32", [128, NCH * TG], F32)
        sbuf("PTr", [128, 2 * 4608], BF16)
        sbuf("wbuf", [128, 2 * 2048], BF16)
        sbuf("out32", [128, 2 * TG], F32)
        sbuf("sq", [128, 2 * TG], BF16)
        sbuf("fz", [128, 2 * TG], BF16)
        sbuf("rope", [128, 3 * TG], F32)
        self.sb["recd"] = self.sb["rope"][:, 0:2 * TG]
        self.sb["negnd"] = self.sb["rope"][0:1, 2 * TG:3 * TG]
        sbuf("tbl", [128, 2 * TG], BF16)
        sbuf("bc", [128, 2 * TG], F32)
        sbuf("vh", [128, TG], BF16)
        sbuf("small", [128, 2], F32)
        sbuf("aqsel", [16, 8 * 128], F32)
        sbuf("onesf", [1, 128], F32)
        sbuf("colsc", [128, 1], F32)
        sbuf("colscb", [128, 1], BF16)
        sbuf("ident", [128, 128], BF16)
        sbuf("fident", [128, 128], F32)
        sbuf("pselk", [8, 512], BF16)
        sbuf("ones", [128, 128], BF16)
        sbuf("fones", [128, 1], F32)
        sbuf("tri", [128, 256], F32)
        sbuf("eyeq", [128, 128], BF16)
        sbuf("eyek", [128, 32], BF16)
        sbuf("idxlt", [32, 32], F32)
        sbuf("bsel", [128, 2], F32)
        sbuf("qsstack", [128, 2], F32)
        sbuf("ksstack", [128, 2], F32)
        for nm in ("m1", "rstd1", "rowa", "rowb"):
            sbuf(nm, [1, TG], F32)
        sbuf("rowar", [1, 3 * TG], F32)
        sbuf("rowbf", [1, 2 * TG], BF16)
        sb = self.sb
        sb["m2"] = sb["rowar"][0:1, 0:TG]
        sb["rstd2"] = sb["rowar"][0:1, TG:2 * TG]
        sb["rstd3"] = sb["rowar"][0:1, 2 * TG:3 * TG]
        sb["m4"] = sb["rowar"][0:1, 0:TG]
        sb["rstd4"] = sb["rowar"][0:1, TG:2 * TG]
        sb["rowc"] = sb["rowb"]
        sb["m2b"] = sb["rowbf"][0:1, 0:TG]
        sb["rstd2b"] = sb["rowbf"][0:1, TG:2 * TG]
        sb["rstd3b"] = sb["rowbf"][0:1, 0:TG]
        sb["rstd4b"] = sb["rowbf"][0:1, 0:TG]
        sb["negm4b"] = sb["rowbf"][0:1, TG:2 * TG]
        sb["rstd1b"] = sb["rowbf"][0:1, 0:TG]
        sb["m1b"] = sb["rowbf"][0:1, TG:2 * TG]
        sbuf("m3", [1, NG * TG], BF16)
        sbuf("aq", [16, TG], F32)
        sb["ak"] = sb["aq"][0:8, :]
        sbuf("akb", [8, TG], BF16)
        sbuf("seqp", [128, 16], F32)
        sbuf("seqa", [128, 16], F32)
        sbuf("seqsb", [128, 18], F32)
        sbuf("seqg", [128, NCORE * 18], F32)
        sbuf("sumw", [1, 8], F32)
        sb["simsb"] = sb["x32"][:, 0:512]
        sbuf("cst", [128, 2], F32)
        sbuf("rmb", [32, 4], BF16)
        sb["srep"] = sb["rope"][:, 0:36]
        sb["snorm"] = sb["rope"][:, 36:52]
        sb["simsq"] = sb["rope"][:, 52:84]
        sb["gat"] = sb["rope"][0:32, 84:96]
        sb["gatm"] = sb["rope"][0:32, 96:160]
        sb["gatr"] = sb["rope"][0:1, 160:256]
        sb["grow"] = sb["rope"][0:1, 256:320]
        sb["gsc"] = sb["rope"][0:1, 320:328]
        sb["wveff"] = sb["PTr"][:, 0:NCH * EH]
        sb["colv"] = sb["PTr"][0:1, NCH * EH:NCH * EH + EH]
        sbuf("colwo", [64, 1], F32)
        sbuf("colwob", [64, 1], BF16)
        sb["x32ld"] = sb["out32"][:]
        sb["comb"] = sb["enc"][0:64, 0:TO]
        sb["woeff"] = sb["q1"][0:64, 0:D]
        sb["xmidld"] = sb["q1"][:, 2048:2048 + 2 * TG]
        sb["wfold"] = sb["q2"][0:32, 0:2048]
        sb["foldrow"] = sb["q2"][0:1, 2048:4096]
        sb["wob1"] = sb["enc"][0:64, 2048:2048 + D]

        psum("big0", [128, 1536], F32)
        psum("big1", [128, 1536], F32)
        psum("pt", [128, 512], F32)
        psum("sm", [128, 512], F32)

        for s in sorted(set(ENG_SEM.values())):
            self.sems[s] = es.enter_context(nc.semaphore(s))

    def emit_consts(self):
        nc, sb, dr = self.nc, self.sb, self.dram
        self.cload = []
        for nm in ("ident", "ones", "eyeq", "eyek", "pselk"):
            o = self.op("dgp", lambda nm=nm: nc.gpsimd.dma_start(
                out=self.sb[nm][:], in_=self.dram[nm]), (), tag="ld_" + nm, stream="c")
            self.cload.append(o)
        for nm in ("idxlt", "qsstack", "ksstack", "bsel", "tri", "aqsel"):
            o = self.op("dsync", lambda nm=nm: nc.sync.dma_start(
                out=self.sb[nm][:], in_=self.dram[nm]), (), tag="ld_" + nm, stream="c2")
            self.cload.append(o)
        o = self.op("dsync", lambda: nc.sync.dma_start(
            out=self.sb["fident"][:], in_=self.dram["ident"]), (), tag="ld_fid", stream="c2")
        self.cload.append(o)
        m0 = self.op("vector", lambda: nc.vector.memset(sb["seqa"][:], 0.0))
        c0 = self.op("vector", lambda: nc.vector.memset(
            sb["cst"][:, 0:1], EPS_LN))
        c1 = self.op("vector", lambda: nc.vector.memset(
            sb["cst"][:, 1:2], EPS_RMS))
        self.cload += [c0, c1]
        m1 = self.op("vector", lambda: nc.vector.memset(sb["fones"][:], 1.0))
        m2 = self.op("vector", lambda: nc.vector.memset(sb["onesf"][:], 1.0))
        self.cload.append(m2)
        self.memsets = [m0, m1]
        self.seqa_last = m0

    def emit_group_proj(self, gg):
        nc, sb, ps, dr = self.nc, self.sb, self.ps, self.dram
        own = gg >= 2
        g = gg - 2
        col0 = gg * TG
        gt0 = gg * 4 - 8

        src = dr["xT"][:, col0:col0 + TG].rearrange("(c p) t -> p c t", p=128)
        ld_x = self.op("dgp", lambda src=src: nc.gpsimd.dma_start(
            out=sb["xfm"][:].rearrange("p (c t) -> p c t", c=NCH), in_=src),
            self.takewar("xfm") + self.cload, tag=f"ldx{gg}", stream="x")
        self.ld_x = ld_x

        tbl_deps = self.takewar("tbl")
        ld_t = []
        for half in (0, 1):
            for j, nm in ((0, "costab"), (1, "sintab")):
                o = self.op("dgp", lambda half=half, j=j, nm=nm:
                            nc.gpsimd.dma_start(
                                out=sb["tbl"][64 * half:64 * half + 64,
                                              j * TG:(j + 1) * TG],
                                in_=dr[nm][:, col0:col0 + TG]),
                            tbl_deps, tag=f"ldt{gg}", stream="tbl")
                ld_t.append(o)
        self.ld_t = ld_t

        mean_mms, ssq_mms = [], []
        for c in range(NCH):
            mean_mms.append(self.op("tensor", lambda c=c: nc.tensor.matmul(
                ps["big1"][0:1, 512:512 + TG],
                sb["ones"][:, 0:1], sb["xfm"][:, c * TG:(c + 1) * TG],
                start=(c == 0), stop=(c == NCH - 1)),
                [ld_x] + (self.takewar("b1_4") if c == 0 else []),
                tag=f"meanmm{gg}_{c}"))
        sq_ops = []
        for c in range(NCH):
            slot = c % 2
            sqo = self.op("scalar", lambda c=c, slot=slot:
                          nc.scalar.activation(
                              sb["sq"][:, slot * TG:(slot + 1) * TG],
                              sb["xfm"][:, c * TG:(c + 1) * TG], AF.Square),
                          [ld_x] + self.takewar(f"sq{slot}"),
                          tag=f"sq{gg}_{c}")
            sq_ops.append(sqo)
            mo = self.op("tensor", lambda c=c, slot=slot:
                         nc.tensor.matmul(
                ps["big1"][0:1, 1024:1024 + TG],
                sb["ones"][:, 0:1], sb["sq"][:, slot * TG:(slot + 1) * TG],
                start=(c == 0), stop=(c == NCH - 1)),
                [sqo] + (self.takewar("b1_5") if c == 0 else []),
                tag=f"ssqmm{gg}_{c}")
            ssq_mms.append(mo)
            self.setwar(f"sq{slot}", [mo])

        m1 = self.op("scalar", lambda: nc.scalar.activation(
            sb["m1"][:], ps["big1"][0:1, 512:512 + TG], AF.Copy,
            scale=1.0 / D), [mean_mms[-1]] + self.takewar("rowsc"),
            tag=f"m1_{gg}")
        self.addwar("b1_4", m1)
        m1sq = self.op("scalar", lambda: nc.scalar.activation(
            sb["rowa"][:], sb["m1"][:], AF.Square), [m1])
        msq = self.op("scalar", lambda: nc.scalar.activation(
            sb["rowb"][:], ps["big1"][0:1, 1024:1024 + TG], AF.Copy,
            scale=1.0 / D), [ssq_mms[-1]], tag=f"msq{gg}")
        self.addwar("b1_5", msq)
        var = self.op("vector", lambda: nc.vector.tensor_sub(
            sb["rowb"][:], sb["rowb"][:], sb["rowa"][:]), [msq, m1sq])
        sd = self.op("scalar", lambda: nc.scalar.activation(
            sb["rowa"][:], sb["rowb"][:], AF.Sqrt, bias=sb["cst"][0:1, 0:1]),
            [var], tag=f"sd{gg}")
        rstd1 = self.op("vector", lambda: nc.vector.reciprocal(
            sb["rstd1"][:], sb["rowa"][:]), [sd], tag=f"rstd1_{gg}")
        rstd1b = self.op("vector", lambda: nc.vector.tensor_copy(
            sb["rstd1b"][:], sb["rstd1"][:]),
            [rstd1] + self.takewar("rbf0"), tag=f"rstd1b_{gg}")
        m1b = self.op("vector", lambda: nc.vector.tensor_copy(
            sb["m1b"][:], sb["m1"][:]), [m1] + self.takewar("rbf1"),
            tag=f"m1b_{gg}")
        bcmm = self.op("tensor", lambda: nc.tensor.matmul(
            ps["sm"][:, 0:TG], sb["ones"][0:1, :], sb["rstd1b"][:],
            start=True, stop=True),
            [rstd1b] + self.takewar("smbank"), tag=f"bc1mm{gg}")
        self.addwar("rbf0", bcmm)
        bc1 = self.op("vector", lambda: nc.vector.tensor_copy(
            sb["bc"][:, 0:TG], ps["sm"][:, 0:TG]),
            [bcmm] + self.takewar("bc0"), tag=f"bc1_{gg}")
        bcmm2 = self.op("tensor", lambda: nc.tensor.matmul(
            ps["sm"][:, 0:TG], sb["ones"][0:1, :], sb["m1b"][:],
            start=True, stop=True), [m1b, bc1], tag=f"bcm1mm{gg}")
        self.addwar("rbf1", bcmm2)
        m1bc = self.op("vector", lambda: nc.vector.tensor_copy(
            sb["bc"][:, TG:2 * TG], ps["sm"][:, 0:TG]),
            [bcmm2] + self.takewar("bc1dst"), tag=f"m1bc_{gg}")
        self.addwar("smbank", m1bc)
        self.bc1_op = bc1
        self.setwar("rowsc", [m1, m1sq, msq, var, sd, rstd1, rstd1b, m1b,
                              bcmm, bcmm2])

        xz_ops = []
        for c in range(NCH):
            o = self.op("vector", lambda c=c: nc.vector.tensor_sub(
                sb["xfm"][:, c * TG:(c + 1) * TG],
                sb["xfm"][:, c * TG:(c + 1) * TG], sb["bc"][:, TG:2 * TG]),
                [mean_mms[c], sq_ops[c], m1bc], tag=f"xz{gg}_{c}")
            xz_ops.append(o)
            self.addwar("bc1dst", o)
        self.xz_ops = xz_ops

        self._proj_pass(gg, kv=True)
        if own:
            self._proj_pass(gg, kv=False)
            self._emit_aq_rows_finalize(gg)

    def _stream_block(self, wname, gg, b):
        nc, sb, dr = self.nc, self.sb, self.dram
        slot = b % 2
        deps = self.takewar(f"wslot{slot}")
        ops = []
        for h8 in range(2):
            src = dr[wname][b, h8 * 8:(h8 + 1) * 8].rearrange(
                "c p q -> p c q")
            o = self.op("dgp", lambda src=src, slot=slot, h8=h8:
                        nc.gpsimd.dma_start(
                            out=sb["wbuf"][:, slot * 2048 + h8 * 1024:
                                           slot * 2048 + (h8 + 1) * 1024
                                           ].rearrange("p (c q) -> p c q",
                                                       c=8),
                            in_=src), deps, tag=f"w{wname}{gg}_{b}_{h8}", stream=f"w{slot}")
            ops.append(o)
        return ops

    def _block_mms(self, wname, gg, b, pslot, wdma):
        nc, sb, ps = self.nc, self.sb, self.ps
        slot = b % 2
        mm = None
        for c in range(NCH):
            deps = [wdma[c // 8], self.xz_ops[c]]
            if c == 0:
                deps += self.takewar(f"big0_{pslot}")
            mm = self.op("tensor", lambda c=c, slot=slot, pslot=pslot:
                         nc.tensor.matmul(
                             ps["big0"][:, pslot * TG:(pslot + 1) * TG],
                             sb["wbuf"][:, slot * 2048 + c * 128:
                                        slot * 2048 + (c + 1) * 128],
                             sb["xfm"][:, c * TG:(c + 1) * TG],
                             start=(c == 0), stop=(c == NCH - 1)),
                         deps, tag=f"mm{wname}{gg}_{b}_{c}")
        self.setwar(f"wslot{slot}", [mm])
        self.addwar("xfm", mm)
        return mm

    def _proj_pass(self, gg, kv):
        nc, sb, ps = self.nc, self.sb, self.ps
        wname = "Wkv" if kv else "Wq"
        gt0 = gg * 4 - 8
        slot0 = gt0 % RING

        if kv:
            units = [("k", p) for p in range(4)] + [("v", h) for h in range(8)]
        else:
            units = [("q", j) for j in range(8)]

        for kind, idx in units:
            if kind in ("k", "q"):
                b1 = 2 * idx
                wd1 = self._stream_block(wname, gg, b1)
                wd2 = self._stream_block(wname, gg, b1 + 1)
                mm1 = self._block_mms(wname, gg, b1, 0, wd1)
                mm2 = self._block_mms(wname, gg, b1 + 1, 1, wd2)
                if self.pending_kscale is not None:
                    self.pending_kscale()
                    self.pending_kscale = None
                self._pair_post(gg, kind, idx, mm1, mm2, slot0)
            else:
                b = 8 + idx
                wd = self._stream_block(wname, gg, b)
                mm = self._block_mms(wname, gg, b, idx % 2, wd)
                if self.pending_kscale is not None:
                    self.pending_kscale()
                    self.pending_kscale = None
                self._v_post(gg, idx, mm, slot0)
        if self.pending_kscale is not None:
            self.pending_kscale()
            self.pending_kscale = None

    def _pair_post(self, gg, kind, idx, mm1, mm2, slot0):
        nc, sb, ps = self.nc, self.sb, self.ps
        isq = kind == "q"
        stack = sb["qsstack"] if isq else sb["ksstack"]
        nunits = 8 if isq else 4
        rows_n = 16 if isq else 8
        eye = sb["eyeq"] if isq else sb["eyek"]
        eyew = 16 if isq else 8
        rowkey = "aqrows" if isq else "akrows"

        sq_ops = []
        for half, mm in ((0, mm1), (1, mm2)):
            so = self.op("scalar", lambda half=half: nc.scalar.activation(
                sb["sq"][:, half * TG:(half + 1) * TG],
                ps["big0"][:, half * TG:(half + 1) * TG], AF.Square),
                [mm] + self.takewar(f"sq{half}"),
                tag=f"rsq{gg}{kind}{idx}_{half}")
            sq_ops.append(so)
        eye_last = self.war.get(rowkey + "_last", None)
        for half in (0, 1):
            deps = [sq_ops[half]]
            if eye_last is not None:
                deps.append(eye_last)
            if isq:
                start = (idx == 0 and half == 0)
                stop = (idx == nunits - 1 and half == 1)
            else:
                start = (half == 0)
                stop = (half == 1)
            if start:
                deps += self.takewar("b1_3")
            eye_last = self.op("tensor", lambda half=half, idx=idx,
                               start=start, stop=stop, rows_n=rows_n,
                               eye=eye, eyew=eyew:
                               nc.tensor.matmul(
                ps["big1"][0:rows_n, 0:TG],
                eye[:, idx * eyew:(idx + 1) * eyew],
                sb["sq"][:, half * TG:(half + 1) * TG],
                start=start, stop=stop),
                deps, tag=f"eye{gg}{kind}{idx}_{half}")
            self.setwar(f"sq{half}", [eye_last])
        self.war[rowkey + "_last"] = eye_last
        if idx == nunits - 1:
            self.war[rowkey + "_done"] = eye_last

        rope_war = self.takewar("rope")
        t1 = self.op("vector", lambda: nc.vector.scalar_tensor_tensor(
            sb["rope"][:, 0:TG], ps["big0"][:, 0:TG], stack[:, 0:1],
            sb["tbl"][:, 0:TG], ALU.mult, ALU.mult),
            [mm1] + self.ld_t + self.cload + rope_war,
            tag=f"t1_{gg}{kind}{idx}")
        t2 = self.op("vector", lambda: nc.vector.scalar_tensor_tensor(
            sb["rope"][:, TG:2 * TG], ps["big0"][:, TG:2 * TG],
            stack[:, 1:2], sb["tbl"][:, TG:2 * TG], ALU.mult, ALU.mult),
            [mm2] + self.ld_t + rope_war, tag=f"t2_{gg}{kind}{idx}")
        r1 = self.op("vector", lambda: nc.vector.tensor_sub(
            sb["rope"][:, 0:TG], sb["rope"][:, 0:TG], sb["rope"][:, TG:2 * TG]),
            [t1, t2], tag=f"r1_{gg}{kind}{idx}")
        t3 = self.op("vector", lambda: nc.vector.scalar_tensor_tensor(
            sb["rope"][:, TG:2 * TG], ps["big0"][:, TG:2 * TG],
            stack[:, 1:2], sb["tbl"][:, 0:TG], ALU.mult, ALU.mult),
            [mm2, r1], tag=f"t3_{gg}{kind}{idx}")
        t4 = self.op("vector", lambda: nc.vector.scalar_tensor_tensor(
            sb["rope"][:, 2 * TG:3 * TG], ps["big0"][:, 0:TG],
            stack[:, 0:1], sb["tbl"][:, TG:2 * TG], ALU.mult, ALU.mult),
            [mm1], tag=f"t4_{gg}{kind}{idx}")
        r2 = self.op("vector", lambda: nc.vector.tensor_add(
            sb["rope"][:, TG:2 * TG], sb["rope"][:, TG:2 * TG],
            sb["rope"][:, 2 * TG:3 * TG]),
            [t3, t4], tag=f"r2_{gg}{kind}{idx}")
        self.addwar("tbl", t1, t2, t3, t4)
        self.setwar("big0_0", [t1, t4, sq_ops[0]])
        self.setwar("big0_1", [t2, t3, sq_ops[1]])

        if isq:
            d1 = self.takewar(f"q1_{idx}")
            o1 = self.op("vector", lambda idx=idx: nc.vector.tensor_copy(
                sb["q1"][:, idx * TG:(idx + 1) * TG], sb["rope"][:, 0:TG]),
                [r1] + d1, tag=f"qw1_{gg}_{idx}")
            d2 = self.takewar(f"q2_{idx}")
            o2 = self.op("vector", lambda idx=idx: nc.vector.tensor_copy(
                sb["q2"][:, idx * TG:(idx + 1) * TG],
                sb["rope"][:, TG:2 * TG]), [r2] + d2,
                tag=f"qw2_{gg}_{idx}")
            self.q_ready[idx] = [o1, o2]
            self.setwar("rope", [o1, o2])
        else:
            pk = idx
            base = pk * RING * 128 + slot0 * 128
            dst_deps = []
            for s in range(4):
                dst_deps += self.takewar(f"kw_{slot0 + s}_{pk}")
            e5 = self._ak_pair_finalize(gg, pk)
            cps = []
            for (src_slot, tens) in ((0, "k1"), (1, "k2")):
                o = self.op("vector", lambda src_slot=src_slot, tens=tens,
                            base=base: nc.vector.tensor_copy(
                    sb[tens][:, base:base + TG],
                    sb["rope"][:, src_slot * TG:(src_slot + 1) * TG]),
                    [r1, r2] + dst_deps, tag=f"kcp{gg}_{idx}_{tens}")
                cps.append(o)
            self.setwar("rope", cps)

            def scale_pass(pk=pk, base=base, cps=cps, gg=gg, e5=e5,
                           slot0=slot0):
                nc, sb, ps = self.nc, self.sb, self.ps
                bcd = self.takewar("smbank")
                bm = self.op("tensor", lambda pk=pk: nc.tensor.matmul(
                    ps["sm"][:, 0:TG], sb["pselk"][:, pk * 128:(pk + 1) * 128],
                    sb["akb"][:], start=True, stop=True),
                    [e5] + bcd, tag=f"akbc{gg}{pk}")
                self.addwar("aq_war", bm)
                ev = self.op("vector", lambda: nc.vector.tensor_copy(
                    sb["bc"][:, TG:2 * TG], ps["sm"][:, 0:TG]),
                    [bm] + self.takewar("bc1dst"), tag=f"akbcev{gg}{pk}")
                self.addwar("smbank", ev)
                outs = []
                for ti, tens in enumerate(("k1", "k2")):
                    o = self.op("vector", lambda tens=tens, base=base:
                                nc.vector.tensor_mul(
                        sb[tens][:, base:base + TG],
                        sb[tens][:, base:base + TG],
                        sb["bc"][:, TG:2 * TG]),
                        [cps[ti], ev], tag=f"ksc{gg}_{pk}_{tens}")
                    outs.append(o)
                    self.addwar("bc1dst", o)
                for s in range(4):
                    self.kslot_ready[(slot0 + s, pk)] = outs
                    self.addwar(f"kw_{slot0 + s}_{pk}", *outs)
            self.pending_kscale = scale_pass

    def _ak_pair_finalize(self, gg, p):
        nc, sb, ps = self.nc, self.sb, self.ps
        eye_last = self.war["akrows_last"]
        e1 = self.op("scalar", lambda: nc.scalar.activation(
            sb["ak"][:], sb["bc"][0:8, 0:TG], AF.Square),
            [self.bc1_op] + self.takewar("aq_war"), tag=f"ak1_{gg}{p}")
        self.addwar("bc0", e1)
        e2 = self.op("vector", lambda: nc.vector.scalar_tensor_tensor(
            sb["ak"][:], sb["ak"][:], 1.0 / HD,
            ps["big1"][0:8, 0:TG], ALU.mult, ALU.mult),
            [e1, eye_last], tag=f"ak2_{gg}{p}")
        self.addwar("b1_3", e2)
        e3 = self.op("scalar", lambda: nc.scalar.activation(
            sb["ak"][:], sb["ak"][:], AF.Ln, bias=sb["cst"][0:8, 1:2]),
            [e2], tag=f"ak3_{gg}{p}")
        e4 = self.op("scalar", lambda: nc.scalar.activation(
            sb["ak"][:], sb["ak"][:], AF.Exp, scale=-0.5),
            [e3], tag=f"ak4_{gg}{p}")
        e5 = self.op("vector", lambda: nc.vector.tensor_mul(
            sb["akb"][:], sb["ak"][:], sb["bc"][0:8, 0:TG]),
            [e4, self.bc1_op], tag=f"ak5_{gg}{p}")
        self.addwar("bc0", e5)
        self.setwar("aq_war", [e5])
        return e5

    def _v_post(self, gg, h, mm, slot0):
        nc, sb, ps = self.nc, self.sb, self.ps
        pslot = h % 2
        ev = self.op("vector", lambda pslot=pslot:
                     nc.vector.scalar_tensor_tensor(
            sb["vh"][:], ps["big0"][:, pslot * TG:(pslot + 1) * TG], 0.0,
            sb["bc"][:, 0:TG], ALU.bypass, ALU.mult),
            [mm, self.bc1_op] + self.takewar("vh"), tag=f"vev{gg}_{h}")
        self.addwar("bc0", ev)
        self.setwar(f"big0_{pslot}", [ev])
        tps = []
        ptw = self.takewar("ptbank")
        for ti in range(4):
            o = self.op("tensor", lambda ti=ti: nc.tensor.matmul(
                ps["pt"][:, ti * 128:(ti + 1) * 128],
                sb["vh"][:, ti * 128:(ti + 1) * 128], sb["ident"][:],
                start=True, stop=True),
                [ev] + (ptw if ti == 0 else []), tag=f"vtp{gg}_{h}_{ti}")
            tps.append(o)
        dep_ring = []
        for s in range(4):
            dep_ring += self.takewar(f"vw_{slot0 + s}_{h}")
        evv = self.op("vector", lambda h=h, slot0=slot0:
                      nc.vector.tensor_copy(
            sb["vring"][:].rearrange("p (s h q) -> p s h q",
                                     s=RING, h=KVH)[:, slot0:slot0 + 4, h],
            ps["pt"][:, 0:512].rearrange("p (s q) -> p s q", s=4)),
            tps + dep_ring, tag=f"vrev{gg}_{h}")
        self.setwar("ptbank", [evv])
        self.addwar("vh", tps[-1])
        for s in range(4):
            self.vslot_ready[(slot0 + s, h)] = evv

    def _emit_aq_rows_finalize(self, gg):
        nc, sb, ps = self.nc, self.sb, self.ps
        eye_done = self.war["aqrows_done"]
        e1 = self.op("scalar", lambda: nc.scalar.activation(
            sb["aq"][:], sb["bc"][0:16, 0:TG], AF.Square),
            [self.bc1_op] + self.takewar("aq_war"), tag=f"aq1_{gg}")
        self.addwar("bc0", e1)
        e2 = self.op("vector", lambda: nc.vector.scalar_tensor_tensor(
            sb["aq"][:], sb["aq"][:], 1.0 / HD,
            ps["big1"][0:16, 0:TG], ALU.mult, ALU.mult),
            [e1, eye_done], tag=f"aq2_{gg}")
        self.addwar("b1_3", e2)
        e3 = self.op("scalar", lambda: nc.scalar.activation(
            sb["aq"][:], sb["aq"][:], AF.Ln,
            bias=sb["cst"][0:16, 1:2]), [e2])
        e4 = self.op("scalar", lambda: nc.scalar.activation(
            sb["aq"][:], sb["aq"][:], AF.Exp, scale=-0.5), [e3])
        e5 = self.op("vector", lambda: nc.vector.scalar_tensor_tensor(
            sb["aq"][:], sb["bc"][0:16, 0:TG], ATTN_SCALE,
            sb["aq"][:], ALU.mult, ALU.mult), [e4, self.bc1_op],
            tag=f"aq5_{gg}")
        self.addwar("bc0", e5)
        self.setwar("aq_war", [])
        for j in range(8):
            bankkey = "ptbank" if j % 2 == 0 else "smbank"
            dps = ps["pt"] if j % 2 == 0 else ps["sm"]
            deps = [e5] + self.cload + self.takewar(bankkey)
            bm = self.op("tensor", lambda j=j, dps=dps: nc.tensor.matmul(
                dps[:, 0:TG], sb["aqsel"][:, j * 128:(j + 1) * 128],
                sb["aq"][0:16, 0:TG], start=True, stop=True),
                deps, tag=f"qsc{gg}_{j}")
            self.addwar("aq_war", bm)
            m1d = self.q_ready.get(j, [])
            o1 = self.op("vector", lambda j=j, dps=dps: nc.vector.tensor_mul(
                sb["q1"][:, j * TG:(j + 1) * TG],
                sb["q1"][:, j * TG:(j + 1) * TG], dps[:, 0:TG]),
                [bm] + list(m1d), tag=f"qm1{gg}_{j}")
            o2 = self.op("vector", lambda j=j, dps=dps: nc.vector.tensor_mul(
                sb["q2"][:, j * TG:(j + 1) * TG],
                sb["q2"][:, j * TG:(j + 1) * TG], dps[:, 0:TG]),
                [bm] + list(m1d), tag=f"qm2{gg}_{j}")
            self.setwar(bankkey, [o1, o2])
            self.addwar(f"q1_{j}", o1)
            self.addwar(f"q2_{j}", o2)
            self.q_ready[j] = [o1, o2]

    def emit_group_attn(self, gg):
        nc, sb, ps = self.nc, self.sb, self.ps
        g = gg - 2
        t0 = g * 4
        self.enc_ready = {}

        entry = {0: (self.takewar("big0_0") + self.takewar("big0_1") +
                     self.takewar("S0")),
                 1: (self.takewar("b1_3") + self.takewar("b1_4") +
                     self.takewar("b1_5") + self.takewar("S1"))}
        entry_used = {0: False, 1: False}

        ndld = None
        if g < 2:
            nddeps = (list(self.q_ready.get(7, ())) +
                      self.takewar("ndld") + self.cload)
            ndld = self.op("dsync", lambda g=g: nc.sync.dma_start(
                out=sb["negnd"][0:1, 0:TG],
                in_=self.dram["negnd"][0:1, g * TG:(g + 1) * TG]),
                nddeps, tag=f"ndld{gg}", stream="c2")

        ALIVE = []
        for jj in range(12):
            if jj < 4:
                ALIVE.append((0, (jj + 1) * 128))
            elif jj < 8:
                ALIVE.append((0, 512))
            else:
                ALIVE.append(((jj - 8) * 128, 512))
        PCOL = []
        c = 0
        for jj in range(12):
            PCOL.append(c)
            lo, hi = ALIVE[jj]
            c += hi - lo
        DORD = [3, 0, 1, 2] + list(range(4, 12))

        exps = {}
        enc_last = {}

        def s_block(h, jj):
            par = h % 2
            pk = h // 4
            qj, qhalf = q_block_of(h)
            base = 64 * qhalf
            slot = (t0 - 8 + jj) % RING
            lo, hi = ALIVE[jj]
            big = ps["big0"] if par == 0 else ps["big1"]
            pp = jj % 2
            soff = pp * TG
            last = None
            for half, (qt, kt) in enumerate((("q1", "k1"), ("q2", "k2"))):
                deps = list(self.q_ready.get(qj, ())) + \
                    self.kslot_ready.get((slot, pk), [])
                if half == 0:
                    deps += self.takewar(f"sbuf{par}{pp}")
                    if not entry_used[par]:
                        deps += entry[par]
                o = self.op("tensor", lambda par=par, lo=lo, hi=hi, soff=soff,
                            base=base, qt=qt, kt=kt, qj=qj, pk=pk, slot=slot,
                            half=half, big=big: nc.tensor.matmul(
                    big[:, soff + lo:soff + hi],
                    sb[kt][base:base + 64,
                           pk * RING * 128 + slot * 128:
                           pk * RING * 128 + (slot + 1) * 128],
                    sb[qt][base:base + 64, qj * TG + lo:qj * TG + hi],
                    start=(half == 0), stop=(half == 1)),
                    deps, tag=f"S{gg}_{h}_{jj}_{half}")
                self.addwar(f"kw_{slot}_{pk}", o)
                self.addwar(f"q1_{qj}" if half == 0 else f"q2_{qj}", o)
                last = o
            mask = None
            if jj < 4:
                mask = self.op("vector", lambda big=big, soff=soff, jj=jj:
                               nc.vector.tensor_add(
                    big[:, soff + jj * 128:soff + (jj + 1) * 128],
                    big[:, soff + jj * 128:soff + (jj + 1) * 128],
                    sb["tri"][:, 0:128]), [last] + self.cload,
                    tag=f"mask{gg}_{h}_{jj}")
            elif jj >= 8:
                mask = self.op("vector", lambda big=big, soff=soff, jj=jj:
                               nc.vector.tensor_add(
                    big[:, soff + (jj - 8) * 128:soff + (jj - 7) * 128],
                    big[:, soff + (jj - 8) * 128:soff + (jj - 7) * 128],
                    sb["tri"][:, 128:256]), [last],
                    tag=f"mask{gg}_{h}_{jj}")
            pwar = self.takewar(f"ptr{par}_{jj}")
            ex = self.op("scalar", lambda par=par, lo=lo, hi=hi, soff=soff,
                         jj=jj, big=big: nc.scalar.activation(
                sb["PTr"][:, par * 4608 + PCOL[jj]:
                          par * 4608 + PCOL[jj] + (hi - lo)],
                big[:, soff + lo:soff + hi], AF.Exp),
                ([mask] if mask is not None else [last]) + pwar,
                tag=f"exp{gg}_{h}_{jj}")
            self.setwar(f"sbuf{par}{pp}", [ex])
            return ex

        def pvd_block(h, jj, first, last_pv, last_d):
            par = h % 2
            kvh = h // 2
            slot = (t0 - 8 + jj) % RING
            lo, hi = ALIVE[jj]
            big = ps["big0"] if par == 0 else ps["big1"]
            ex = exps[h][jj]
            vr = self.vslot_ready.get((slot, kvh))
            deps = [ex] + ([vr] if vr is not None else [])
            if first:
                deps += self.takewar(f"encp{par}")
                if not entry_used[par]:
                    deps += entry[par]
                    entry_used[par] = True
            pv = self.op("tensor", lambda par=par, lo=lo, hi=hi, jj=jj,
                         slot=slot, kvh=kvh, first=first, last_pv=last_pv,
                         big=big: nc.tensor.matmul(
                big[:, 1024 + lo:1024 + hi],
                sb["vring"][:, slot * KVH * 128 + kvh * 128:
                            slot * KVH * 128 + (kvh + 1) * 128],
                sb["PTr"][:, par * 4608 + PCOL[jj]:
                          par * 4608 + PCOL[jj] + (hi - lo)],
                start=first, stop=last_pv), deps, tag=f"pv{gg}_{h}_{jj}")
            self.addwar(f"vw_{slot}_{kvh}", pv)
            dps = ps["pt"] if par == 0 else ps["sm"]
            ddeps = [ex] + self.cload
            if first:
                ddeps += self.takewar("ptbank" if par == 0 else "smbank")
            dm = self.op("tensor", lambda par=par, lo=lo, hi=hi, jj=jj,
                         first=first, last_d=last_d, dps=dps:
                         nc.tensor.matmul(
                dps[:, lo:hi], sb["ones"][:, 0:128],
                sb["PTr"][:, par * 4608 + PCOL[jj]:
                          par * 4608 + PCOL[jj] + (hi - lo)],
                start=first, stop=last_d), ddeps, tag=f"dm{gg}_{h}_{jj}")
            self.addwar(f"ptr{par}_{jj}", pv, dm)
            return pv, dm

        def finish_head(h, pvlast, dlast):
            par = h % 2
            dps = ps["pt"] if par == 0 else ps["sm"]
            if g < 2:
                dlast = self.op("tensor", lambda dps=dps:
                                nc.tensor.matmul(
                    dps[:, 0:TG], sb["onesf"][0:1, 0:128],
                    sb["negnd"][0:1, 0:TG],
                    start=False, stop=True), [dlast, ndld] + self.cload,
                    tag=f"dnd{gg}_{h}")
                self.addwar("ndld", dlast)
                self.addwar("rope", dlast)
            rc1 = self.op("scalar", lambda par=par, dps=dps:
                          nc.scalar.activation(
                sb["recd"][:, par * TG:(par + 1) * TG], dps[:, 0:TG],
                AF.Ln), [dlast], tag=f"rcl{gg}_{h}")
            rc = self.op("scalar", lambda par=par: nc.scalar.activation(
                sb["recd"][:, par * TG:(par + 1) * TG],
                sb["recd"][:, par * TG:(par + 1) * TG], AF.Exp,
                scale=-1.0), [rc1], tag=f"rcp{gg}_{h}")
            self.setwar("ptbank" if par == 0 else "smbank", [rc])
            big = ps["big0"] if par == 0 else ps["big1"]
            edeps = [pvlast, rc]
            for t in range(4):
                edeps += self.takewar(f"encdst_{h}_{t}")
            ev = self.op("vector", lambda par=par, h=h, big=big:
                         nc.vector.tensor_mul(
                sb["enc"][:, h * TG:(h + 1) * TG],
                sb["recd"][:, par * TG:(par + 1) * TG],
                big[:, 1024:1536]), edeps, tag=f"encev{gg}_{h}")
            self.setwar(f"encp{par}", [ev])
            self.addwar("rope", ev)
            self.enc_ready.setdefault(h, []).append(ev)
            enc_last[par] = ev

        for step in range(QH + 1):
            h = step
            hp = step - 1
            for jj in range(12):
                if h < QH:
                    exps.setdefault(h, {})[jj] = s_block(h, jj)
                if hp >= 0:
                    dj = DORD[jj]
                    pv, dm = pvd_block(hp, dj, jj == 0,
                                       jj == 11, jj == 11 and (g >= 2))
                    if jj == 11:
                        finish_head(hp, pv, dm)

        self.setwar("S0", self.warlist("sbuf00") + self.warlist("sbuf01") +
                    [enc_last[0]])
        self.setwar("S1", self.warlist("sbuf10") + self.warlist("sbuf11") +
                    [enc_last[1]])

    def emit_group_out(self, gg):
        nc, sb, ps = self.nc, self.sb, self.ps
        g = gg - 2
        col0 = gg * TG

        s0war = self.takewar("S0")
        self.setwar("big0_0", s0war + self.warlist("big0_0"))
        self.addwar("big0_1", *s0war)
        s1war = self.takewar("S1")
        self.addwar("b1_3", *s1war)
        self.addwar("b1_4", *s1war)
        self.addwar("b1_5", *s1war)
        self.setwar("smbank", self.takewar("encbank") +
                    self.warlist("smbank"))

        mean_last = None
        ssq_last = None
        evac_ops = []
        for c in range(NCH):
            slot = c % 2
            wd = []
            wdeps = self.takewar(f"wslot{slot}")
            for h8 in range(2):
                src = self.dram["Wout"][c, h8 * 8:(h8 + 1) * 8].rearrange(
                    "n p q -> p n q")
                o = self.op("dgp", lambda src=src, slot=slot, h8=h8:
                            nc.gpsimd.dma_start(
                    out=sb["wbuf"][:, slot * 2048 + h8 * 1024:
                                   slot * 2048 + (h8 + 1) * 1024
                                   ].rearrange("p (n q) -> p n q", n=8),
                    in_=src), wdeps, tag=f"wout{g}_{c}_{h8}", stream=f"w{slot}")
                wd.append(o)
            mm = None
            for n in range(QH):
                deps = [wd[n // 8]] + self.enc_ready.get(n, [])
                if n == 0:
                    deps += self.takewar(f"big0_{slot}")
                mm = self.op("tensor", lambda n=n, slot=slot: nc.tensor.matmul(
                    ps["big0"][:, slot * TG:(slot + 1) * TG],
                    sb["wbuf"][:, slot * 2048 + n * 128:
                               slot * 2048 + (n + 1) * 128],
                    sb["enc"][:, n * TG:(n + 1) * TG],
                    start=(n == 0), stop=(n == QH - 1)),
                    deps, tag=f"op{g}_{c}_{n}")
                for tix4 in range(4):
                    self.addwar(f"encdst_{n}_{tix4}", mm)
            self.setwar(f"wslot{slot}", [mm])
            ev = self.op("scalar", lambda c=c, slot=slot: nc.scalar.activation(
                sb["x32"][:, c * TG:(c + 1) * TG],
                ps["big0"][:, slot * TG:(slot + 1) * TG], AF.Copy),
                [mm] + self.takewar(f"x32_{c}"), tag=f"atev{g}_{c}")
            cb = self.op("vector", lambda c=c, slot=slot: nc.vector.tensor_copy(
                sb["sq"][:, slot * TG:(slot + 1) * TG],
                sb["x32"][:, c * TG:(c + 1) * TG]),
                [ev] + self.takewar(f"sq{slot}"), tag=f"atb{g}_{c}")
            mean_last = self.op("tensor", lambda slot=slot, c=c:
                                nc.tensor.matmul(
                ps["big1"][0:1, 512:512 + TG], sb["ones"][:, 0:1],
                sb["sq"][:, slot * TG:(slot + 1) * TG],
                start=(c == 0), stop=(c == NCH - 1)),
                [cb] + (self.takewar("b1_4") if c == 0 else []),
                tag=f"m2mm{g}_{c}")
            self.setwar(f"sq{slot}", [mean_last])
            sq2 = self.op("scalar", lambda c=c, slot=slot: nc.scalar.activation(
                sb["fz"][:, slot * TG:(slot + 1) * TG],
                ps["big0"][:, slot * TG:(slot + 1) * TG], AF.Square),
                [mm, ev] + self.takewar(f"fz{slot}"), tag=f"atsq{g}_{c}")
            self.setwar(f"big0_{slot}", [ev, sq2])
            ssq_last = self.op("tensor", lambda slot=slot, c=c:
                               nc.tensor.matmul(
                ps["big1"][0:1, 1024:1024 + TG], sb["ones"][:, 0:1],
                sb["fz"][:, slot * TG:(slot + 1) * TG],
                start=(c == 0), stop=(c == NCH - 1)),
                [sq2] + (self.takewar("b1_5") if c == 0 else []),
                tag=f"s2mm{g}_{c}")
            self.setwar(f"fz{slot}", [ssq_last])
            evac_ops.append(ev)

        m2 = self.op("scalar", lambda: nc.scalar.activation(
            sb["m2"][:], ps["big1"][0:1, 512:512 + TG], AF.Copy,
            scale=1.0 / D), [mean_last] + self.takewar("rowsc"),
            tag=f"m2_{g}")
        self.addwar("b1_4", m2)
        m2sq = self.op("scalar", lambda: nc.scalar.activation(
            sb["rowa"][:], sb["m2"][:], AF.Square), [m2])
        msq2 = self.op("scalar", lambda: nc.scalar.activation(
            sb["rowb"][:], ps["big1"][0:1, 1024:1024 + TG], AF.Copy,
            scale=1.0 / D), [ssq_last], tag=f"msq2_{g}")
        self.addwar("b1_5", msq2)
        var2 = self.op("vector", lambda: nc.vector.tensor_sub(
            sb["rowa"][:], sb["rowb"][:], sb["rowa"][:]), [msq2, m2sq])
        sd2 = self.op("scalar", lambda: nc.scalar.activation(
            sb["rowb"][:], sb["rowa"][:], AF.Sqrt,
            bias=sb["cst"][0:1, 0:1]), [var2])
        rstd2 = self.op("vector", lambda: nc.vector.reciprocal(
            sb["rstd2"][:], sb["rowb"][:]), [sd2], tag=f"rstd2_{g}")
        m2b = self.op("vector", lambda: nc.vector.tensor_copy(
            sb["m2b"][:], sb["m2"][:]), [m2] + self.takewar("rbf0"))
        rstd2b = self.op("vector", lambda: nc.vector.tensor_copy(
            sb["rstd2b"][:], sb["rstd2"][:]), [rstd2] + self.takewar("rbf1"))
        bcm = self.op("tensor", lambda: nc.tensor.matmul(
            ps["sm"][:, 0:TG], sb["ones"][0:1, :], sb["m2b"][:],
            start=True, stop=True), [m2b] + self.takewar("smbank"),
            tag=f"bcm2_{g}")
        bc_m2 = self.op("vector", lambda: nc.vector.tensor_copy(
            sb["bc"][:, 0:TG], ps["sm"][:, 0:TG]),
            [bcm] + self.takewar("bc0"), tag=f"bcm2e_{g}")
        bcr = self.op("tensor", lambda: nc.tensor.matmul(
            ps["sm"][:, 0:TG], sb["ones"][0:1, :], sb["rstd2b"][:],
            start=True, stop=True), [rstd2b, bc_m2], tag=f"bcr2_{g}")
        self.addwar("rbf1", bcr)
        bc_r2 = self.op("vector", lambda: nc.vector.tensor_copy(
            sb["bc"][:, TG:2 * TG], ps["sm"][:, 0:TG]),
            [bcr] + self.takewar("bc1dst"), tag=f"bcr2e_{g}")
        self.addwar("smbank", bc_r2)
        self.setwar("rowsc", [m2, m2sq, msq2, var2, sd2, rstd2, m2b,
                              rstd2b, bcm, bcr])

        mean3_last = None
        ssq3_last = None
        for c in range(NCH):
            slot = c % 2
            xld = self.op("dsync", lambda c=c, slot=slot, col0=col0:
                          nc.sync.dma_start(
                out=sb["x32ld"][:, slot * TG:(slot + 1) * TG],
                in_=self.dram["xT"][c * 128:(c + 1) * 128,
                                    col0:col0 + TG]),
                self.takewar(f"xo32_{slot}"), tag=f"xld{g}_{c}", stream=f"xl{slot}")
            u1 = self.op("vector", lambda c=c: nc.vector.tensor_sub(
                sb["x32"][:, c * TG:(c + 1) * TG],
                sb["x32"][:, c * TG:(c + 1) * TG], sb["bc"][:, 0:TG]),
                [evac_ops[c], bc_m2], tag=f"sw2a{g}_{c}")
            u2 = self.op("vector", lambda c=c: nc.vector.tensor_mul(
                sb["x32"][:, c * TG:(c + 1) * TG],
                sb["x32"][:, c * TG:(c + 1) * TG], sb["bc"][:, TG:2 * TG]),
                [u1, bc_r2], tag=f"sw2b{g}_{c}")
            self.addwar("bc0", u1)
            self.addwar("bc1dst", u2)
            u3 = self.op("vector", lambda c=c, slot=slot: nc.vector.tensor_add(
                sb["x32"][:, c * TG:(c + 1) * TG],
                sb["x32"][:, c * TG:(c + 1) * TG],
                sb["x32ld"][:, slot * TG:(slot + 1) * TG]),
                [u2, xld], tag=f"sw2c{g}_{c}")
            self.addwar(f"xo32_{slot}", u3)
            self.sweep2_u3 = getattr(self, "sweep2_u3", {})
            self.sweep2_u3[c] = u3
            xmb = self.op("vector", lambda c=c, slot=slot: nc.vector.tensor_copy(
                sb["sq"][:, slot * TG:(slot + 1) * TG],
                sb["x32"][:, c * TG:(c + 1) * TG]),
                [u3] + self.takewar(f"sq{slot}"), tag=f"xmb{g}_{c}")
            st = self.op("dsync", lambda c=c, slot=slot, g=g:
                         nc.sync.dma_start(
                out=self.dram["xmid"][c, :, g * TG:(g + 1) * TG],
                in_=sb["sq"][:, slot * TG:(slot + 1) * TG]),
                [xmb], tag=f"xmst{g}_{c}", stream=f"xms{slot}")
            self.xmid_stored.append(st)
            self.xmid_store_map[(g, c)] = st
            mean3_last = self.op("tensor", lambda slot=slot, c=c:
                                 nc.tensor.matmul(
                ps["big1"][0:1, 512:512 + TG], sb["ones"][:, 0:1],
                sb["sq"][:, slot * TG:(slot + 1) * TG],
                start=(c == 0), stop=(c == NCH - 1)),
                [xmb] + (self.takewar("b1_4") if c == 0 else []),
                tag=f"m3mm{g}_{c}")
            self.setwar(f"sq{slot}", [mean3_last, st])
            sq3 = self.op("scalar", lambda c=c, slot=slot: nc.scalar.activation(
                sb["fz"][:, slot * TG:(slot + 1) * TG],
                sb["x32"][:, c * TG:(c + 1) * TG], AF.Square),
                [u3] + self.takewar(f"fz{slot}"), tag=f"sq3{g}_{c}")
            ssq3_last = self.op("tensor", lambda slot=slot, c=c:
                                nc.tensor.matmul(
                ps["big1"][0:1, 1024:1024 + TG], sb["ones"][:, 0:1],
                sb["fz"][:, slot * TG:(slot + 1) * TG],
                start=(c == 0), stop=(c == NCH - 1)),
                [sq3] + (self.takewar("b1_5") if c == 0 else []),
                tag=f"s3mm{g}_{c}")
            self.setwar(f"fz{slot}", [ssq3_last])

        m3 = self.op("scalar", lambda: nc.scalar.activation(
            sb["rowc"][:], ps["big1"][0:1, 512:512 + TG], AF.Copy,
            scale=1.0 / D), [mean3_last] + self.takewar("rowsc"),
            tag=f"m3_{g}")
        self.addwar("b1_4", m3)
        negm3 = self.op("scalar", lambda g=g: nc.scalar.activation(
            sb["m3"][:, g * TG:(g + 1) * TG], sb["rowc"][:], AF.Copy,
            scale=-1.0), [m3], tag=f"negm3_{g}")
        self.negm3_ops[g] = negm3
        m3sq = self.op("scalar", lambda: nc.scalar.activation(
            sb["rowa"][:], sb["rowc"][:], AF.Square), [m3])
        msq3 = self.op("scalar", lambda: nc.scalar.activation(
            sb["rowb"][:], ps["big1"][0:1, 1024:1024 + TG], AF.Copy,
            scale=1.0 / D), [ssq3_last, negm3, m3sq], tag=f"msq3_{g}")
        self.addwar("b1_5", msq3)
        var3 = self.op("vector", lambda: nc.vector.tensor_sub(
            sb["rowa"][:], sb["rowb"][:], sb["rowa"][:]), [msq3, m3sq])
        sd3 = self.op("scalar", lambda: nc.scalar.activation(
            sb["rowb"][:], sb["rowa"][:], AF.Sqrt,
            bias=sb["cst"][0:1, 0:1]), [var3])
        rstd3 = self.op("vector", lambda: nc.vector.reciprocal(
            sb["rstd3"][:], sb["rowb"][:]), [sd3], tag=f"rstd3_{g}")
        rstd3b = self.op("vector", lambda: nc.vector.tensor_copy(
            sb["rstd3b"][:], sb["rstd3"][:]),
            [rstd3] + self.takewar("rbf0"))
        bcm3 = self.op("tensor", lambda: nc.tensor.matmul(
            ps["sm"][:, 0:TG], sb["ones"][0:1, :], sb["rstd3b"][:],
            start=True, stop=True), [rstd3b] + self.takewar("smbank"),
            tag=f"bc3_{g}")
        self.addwar("rbf0", bcm3)
        bc_r3 = self.op("vector", lambda: nc.vector.tensor_copy(
            sb["bc"][:, 0:TG], ps["sm"][:, 0:TG]),
            [bcm3, bc_m2] + self.takewar("bc0"), tag=f"bc3e_{g}")
        self.addwar("smbank", bc_r3)
        self.setwar("rowsc", [m3, negm3, m3sq, msq3, var3, sd3, rstd3,
                              rstd3b, bcm3])

        stt_ops = []
        u3s = self.sweep2_u3
        for c in range(NCH):
            slot = c % 2
            o = self.op("vector", lambda c=c, slot=slot:
                        nc.vector.scalar_tensor_tensor(
                sb["fz"][:, slot * TG:(slot + 1) * TG],
                sb["x32"][:, c * TG:(c + 1) * TG], 0.0,
                sb["bc"][:, 0:TG], ALU.bypass, ALU.mult,
                accum_out=sb["seqp"][:, c:c + 1]),
                [bc_r3, u3s[c]] + self.takewar(f"fz{slot}") +
                self.takewar(f"seqp{c}"), tag=f"stt{g}_{c}")
            self.addwar("bc0", o)
            self.setwar(f"fz{slot}", [o])
            self.addwar(f"x32_{c}", o)
            stt_ops.append(o)
        acc = self.op("vector", lambda: nc.vector.tensor_add(
            sb["seqa"][:], sb["seqa"][:], sb["seqp"][:]),
            stt_ops + [self.seqa_last], tag=f"seqacc{g}")
        self.seqa_last = acc
        for c in range(NCH):
            self.addwar(f"seqp{c}", acc)
        sg = self.op("vector", lambda g=g: nc.vector.tensor_mul(
            sb["rowc"][:], sb["rstd3"][:], sb["m3"][:, g * TG:(g + 1) * TG]),
            [rstd3, negm3, m3sq], tag=f"sig1_{g}")
        sgs = self.op("vector", lambda g=g: nc.vector.reduce_sum(
            sb["sumw"][0:1, g:g + 1], sb["rowc"][:], axis=AX.X),
            [sg], tag=f"sig2_{g}")
        self.sumw_ops = getattr(self, "sumw_ops", []) + [sgs]
        self.addwar("rowsc", sg, sgs)

    def _row_transpose(self, tag, src_fn, dst_fn, deps, region):
        nc, sb, ps = self.nc, self.sb, self.ps
        smw = self.takewar("smbank")
        tp = self.op("tensor", lambda: nc.tensor.transpose(
            ps["sm"][0:32, region:region + 1] if True else None,
            src_fn(), sb["fident"][0:1, 0:1]), deps + smw,
            tag=tag + "_tp")
        ev = self.op("vector", lambda: nc.vector.tensor_copy(
            dst_fn(), ps["sm"][0:32, region:region + 1]), [tp],
            tag=tag + "_ev")
        self.addwar("smbank", ev)
        return ev

    def _col_transpose(self, tag, src_fn, dst_fn, deps, region):
        nc, sb, ps = self.nc, self.sb, self.ps
        smw = self.takewar("smbank")
        tp = self.op("tensor", lambda: nc.tensor.transpose(
            ps["sm"][0:1, region:region + 32],
            src_fn(), sb["fident"][0:32, 0:32]), deps + smw,
            tag=tag + "_tp")
        ev = self.op("vector", lambda: nc.vector.tensor_copy(
            dst_fn(), ps["sm"][0:1, region:region + 32]), [tp],
            tag=tag + "_ev")
        self.addwar("smbank", ev)
        return ev

    def emit_gating(self):
        nc, sb, ps, dr = self.nc, self.sb, self.ps, self.dram

        freed = []
        for j in range(8):
            freed += self.takewar(f"q1_{j}") + self.takewar(f"q2_{j}")
        for n in range(QH):
            for t4 in range(4):
                freed += self.takewar(f"encdst_{n}_{t4}")
        for p in (0, 1):
            for jj in range(12):
                freed += self.takewar(f"ptr{p}_{jj}")
        for k in ("wfold0", "foldrow0", "woeff", "wob1", "wveff",
                  "xmidld0", "xmidld1", "comb_ps"):
            self.setwar(k, freed + self.warlist(k))

        ganchor = self.takewar("rope")
        self.ganchor = ganchor

        z = self.op("vector", lambda: nc.vector.memset(
            sb["seqsb"][:], 0.0), (), tag="seqz")
        cp = self.op("vector", lambda: nc.vector.tensor_copy(
            sb["seqsb"][:, 0:16], sb["seqa"][:]), [z, self.seqa_last],
            tag="seqcp")
        ssum = self.op("vector", lambda: nc.vector.reduce_sum(
            sb["sumw"][0:1, 6:7], sb["sumw"][0:1, 0:4], axis=AX.X),
            self.sumw_ops, tag="sigsum")
        sput = self.op("vector", lambda: nc.vector.tensor_copy(
            sb["seqsb"][0:1, 16:17], sb["sumw"][0:1, 6:7]), [ssum, z],
            tag="sigput")
        st = self.op("dsync", lambda: nc.sync.dma_start(
            out=dr["seqin"], in_=sb["seqsb"][:]), [cp, sput], tag="seqst", stream="g")
        coll = self.op("coll", lambda: nc.gpsimd.collective_compute(
            "AllGather", ALU.bypass,
            replica_groups=[list(range(NCORE))],
            ins=[dr["seqin"][:]], outs=[dr["seqout"][:]]), [st], tag="ag")
        ldg = self.op("dsync", lambda: nc.sync.dma_start(
            out=sb["seqg"][:].rearrange("p (r c) -> p r c", r=NCORE),
            in_=dr["seqout"].rearrange("(r p) c -> p r c", p=128)),
            [coll], tag="seqld", stream="g")

        sums = {}
        for b in range(2):
            t1 = self.op("vector", lambda b=b: nc.vector.tensor_add(
                sb["srep"][:, b * 18:(b + 1) * 18],
                sb["seqg"][:, (4 * b) * 18:(4 * b) * 18 + 18],
                sb["seqg"][:, (4 * b + 1) * 18:(4 * b + 1) * 18 + 18]),
                [ldg], tag=f"bs{b}a")
            t2 = self.op("vector", lambda b=b: nc.vector.tensor_add(
                sb["srep"][:, b * 18:(b + 1) * 18],
                sb["srep"][:, b * 18:(b + 1) * 18],
                sb["seqg"][:, (4 * b + 2) * 18:(4 * b + 2) * 18 + 18]),
                [t1], tag=f"bs{b}b")
            t3 = self.op("vector", lambda b=b: nc.vector.tensor_add(
                sb["srep"][:, b * 18:(b + 1) * 18],
                sb["srep"][:, b * 18:(b + 1) * 18],
                sb["seqg"][:, (4 * b + 3) * 18:(4 * b + 3) * 18 + 18]),
                [t2], tag=f"bs{b}c")
            sums[b] = t3

        sld = self.op("dsync", lambda: nc.sync.dma_start(
            out=sb["simsb"][:].rearrange("p (c e) -> p c e", c=NCH),
            in_=dr["simM"].rearrange("(c p) e -> p c e", p=128)),
            self.takewar("x32_0"), tag="simld", stream="g")
        gld = self.op("dsync", lambda: nc.sync.dma_start(
            out=sb["gatr"][0:1, 0:32], in_=dr["gates"]), ganchor, tag="gld",
            stream="g")
        sig = self.op("scalar", lambda: nc.scalar.activation(
            sb["gatr"][0:1, 32:64], sb["gatr"][0:1, 0:32], AF.Sigmoid),
            [gld], tag="sig")
        sigc = self._row_transpose("sigT", lambda: sb["gatr"][0:1, 32:64],
                                   lambda: sb["gat"][0:32, 8:9],
                                   [sig], 160)

        simsq_last = None
        for c in range(NCH):
            sqq = self.op("scalar", lambda c=c: nc.scalar.activation(
                sb["simsq"][:], sb["simsb"][:, c * 32:(c + 1) * 32],
                AF.Square), [sld] + ([simsq_last] if simsq_last else
                                     ganchor),
                tag=f"simsq{c}")
            self.addwar("x32_0", sqq)
            simsq_last = self.op("tensor", lambda c=c: nc.tensor.matmul(
                ps["sm"][0:1, 8:40], sb["fones"][:], sb["simsq"][:],
                start=(c == 0), stop=(c == NCH - 1)),
                [sqq, self.memsets[1]] +
                (self.takewar("smbank") if c == 0 else []),
                tag=f"simmm{c}")
        scn = self.op("scalar", lambda: nc.scalar.activation(
            sb["grow"][0:1, 0:32], ps["sm"][0:1, 8:40], AF.Sqrt),
            [simsq_last], tag="scn")
        scr = self.op("vector", lambda: nc.vector.reciprocal(
            sb["grow"][0:1, 32:64], sb["grow"][0:1, 0:32]), [scn], tag="scr")
        self.addwar("smbank", scn)
        scrc = self._row_transpose("scrT", lambda: sb["grow"][0:1, 32:64],
                                   lambda: sb["gat"][0:32, 9:10],
                                   [scr], 128)
        self.sim_ops = [simsq_last, scrc, sigc]

        for b in range(2):
            self._gate_batch(b, sums[b])

        self.dbg("routing", [32, 8], F32,
                 lambda: sb["gat"][0:32, 0:8], self.routing_ops)
        self._fold()

    def _gate_batch(self, b, sum_op):
        nc, sb, ps, dr = self.nc, self.sb, self.ps, self.dram
        gw = self.takewar("gatew")
        sgbb = self.op("vector", lambda b=b: nc.vector.tensor_copy(
            sb["colscb"][0:1, 0:1], sb["srep"][0:1, b * 18 + 16:b * 18 + 17]),
            [sum_op] + self.warlist("colscb_war") + gw, tag=f"sgbb{b}")
        smw = self.takewar("smbank")
        sgm = self.op("tensor", lambda: nc.tensor.matmul(
            ps["sm"][:, 0:1], sb["ones"][0:1, :], sb["colscb"][0:1, 0:1],
            start=True, stop=True), [sgbb] + smw, tag=f"sgmm{b}")
        sgev = self.op("vector", lambda: nc.vector.tensor_copy(
            sb["small"][:, 0:1], ps["sm"][:, 0:1]), [sgm] + gw,
            tag=f"sgev{b}")
        self.addwar("smbank", sgev)
        sn = self.op("vector", lambda b=b: nc.vector.tensor_scalar(
            sb["snorm"][:], sb["srep"][:, b * 18:b * 18 + 16],
            sb["small"][:, 0:1], 1.0 / T, ALU.add, ALU.mult),
            [sum_op, sgev] + gw, tag=f"snorm{b}")
        nsq = self.op("scalar", lambda: nc.scalar.activation(
            sb["bc"][:, 0:16], sb["snorm"][:], AF.Square,
            accum_out=sb["colsc"][:, 0:1]),
            [sn] + self.takewar("bc0") + self.takewar("colsc_g") + gw,
            tag=f"nsq{b}")
        nmm = self.op("tensor", lambda: nc.tensor.matmul(
            ps["sm"][0:1, 4:5], sb["colsc"][:], sb["fones"][:],
            start=True, stop=True),
            [nsq, self.memsets[1], sgev] + self.takewar("smbank2"),
            tag=f"nmm{b}")
        nev = self.op("scalar", lambda: nc.scalar.activation(
            sb["gsc"][0:1, 0:1], ps["sm"][0:1, 4:5], AF.Sqrt),
            [nmm] + gw, tag=f"nev{b}")
        nrc = self.op("vector", lambda: nc.vector.reciprocal(
            sb["gsc"][0:1, 1:2], sb["gsc"][0:1, 0:1]), [nev], tag=f"nrc{b}")
        self.addwar("smbank2", nev)
        nrb = self.op("vector", lambda: nc.vector.tensor_copy(
            sb["colscb"][0:1, 0:1], sb["gsc"][0:1, 1:2]),
            [nrc, sgm] + self.warlist("colscb_war"), tag=f"nrb{b}")
        smw2 = self.takewar("smbank")
        nrm = self.op("tensor", lambda: nc.tensor.matmul(
            ps["sm"][:, 1:2], sb["ones"][0:1, :], sb["colscb"][0:1, 0:1],
            start=True, stop=True), [nrb] + smw2, tag=f"nrm{b}")
        nrbc = self.op("vector", lambda: nc.vector.tensor_copy(
            sb["small"][:, 1:2], ps["sm"][:, 1:2]), [nrm] + gw,
            tag=f"nrbc{b}")
        self.addwar("smbank", nrbc)
        aff_r = None
        aff_c = None
        b00 = self.takewar("big0_0")
        b01 = self.takewar("big0_1")
        for c in range(NCH):
            aff_r = self.op("tensor", lambda c=c: nc.tensor.matmul(
                ps["big0"][0:1, 0:32],
                sb["snorm"][:, c:c + 1], sb["simsb"][:, c * 32:(c + 1) * 32],
                start=(c == 0), stop=(c == NCH - 1)),
                [sn, self.sim_ops[0]] + (b00 if c == 0 else []),
                tag=f"affr{b}_{c}")
            aff_c = self.op("tensor", lambda c=c: nc.tensor.matmul(
                ps["big0"][0:32, 512:513],
                sb["simsb"][:, c * 32:(c + 1) * 32], sb["snorm"][:, c:c + 1],
                start=(c == 0), stop=(c == NCH - 1)),
                [sn] + (b01 if c == 0 else []), tag=f"affc{b}_{c}")
            self.addwar("x32_0", aff_r, aff_c)
        lr1 = self.op("vector", lambda: nc.vector.tensor_scalar_mul(
            sb["gatr"][0:1, 64:96], ps["big0"][0:1, 0:32],
            sb["small"][0:1, 1:2]), [aff_r, nrbc] + gw, tag=f"lr1{b}")
        self.addwar("big0_0", lr1)
        lr2 = self.op("vector", lambda: nc.vector.tensor_mul(
            sb["gatr"][0:1, 64:96], sb["gatr"][0:1, 64:96],
            sb["grow"][0:1, 32:64]), [lr1, self.sim_ops[1]], tag=f"lr2{b}")
        lrow = self.op("vector", lambda: nc.vector.tensor_sub(
            sb["gatr"][0:1, 64:96], sb["gatr"][0:1, 64:96],
            sb["gatr"][0:1, 32:64]), [lr2], tag=f"lrow{b}")
        lc1 = self.op("vector", lambda: nc.vector.tensor_scalar_mul(
            sb["gat"][0:32, 2:3], ps["big0"][0:32, 512:513],
            sb["small"][0:32, 1:2]), [aff_c, nrbc] + gw, tag=f"lc1{b}")
        self.addwar("big0_1", lc1)
        lc2 = self.op("vector", lambda: nc.vector.tensor_mul(
            sb["gat"][0:32, 2:3], sb["gat"][0:32, 2:3],
            sb["gat"][0:32, 9:10]), [lc1, self.sim_ops[1]], tag=f"lc2{b}")
        lcol = self.op("vector", lambda: nc.vector.tensor_sub(
            sb["gat"][0:32, 2:3], sb["gat"][0:32, 2:3], sb["gat"][0:32, 8:9]),
            [lc2, self.sim_ops[2]], tag=f"lcol{b}")
        smw = self.takewar("smbank")
        rmm = self.op("tensor", lambda: nc.tensor.matmul(
            ps["sm"][0:32, 192:224], sb["onesf"][0:1, 0:32],
            sb["gatr"][0:1, 64:96], start=True, stop=True),
            [lrow] + self.cload + smw, tag=f"rmm{b}")
        rev = self.op("vector", lambda: nc.vector.tensor_copy(
            sb["gatm"][:, 0:32], ps["sm"][0:32, 192:224]), [rmm] + gw,
            tag=f"rev{b}")
        self.addwar("smbank", rev)
        gt_ = self.op("vector", lambda: nc.vector.tensor_single_scalar(
            sb["gatm"][:, 32:64], sb["gatm"][:, 0:32], sb["gat"][0:32, 2:3],
            ALU.is_gt), [rev, lcol], tag=f"ggt{b}")
        eq_ = self.op("vector", lambda: nc.vector.tensor_single_scalar(
            sb["gatm"][:, 0:32], sb["gatm"][:, 0:32], sb["gat"][0:32, 2:3],
            ALU.is_equal), [gt_], tag=f"geq{b}")
        tie = self.op("vector", lambda: nc.vector.tensor_mul(
            sb["gatm"][:, 0:32], sb["gatm"][:, 0:32], sb["idxlt"][:]),
            [eq_] + self.cload, tag=f"gtie{b}")
        rkp = self.op("vector", lambda: nc.vector.tensor_add(
            sb["gatm"][:, 0:32], sb["gatm"][:, 0:32], sb["gatm"][:, 32:64]),
            [tie], tag=f"grk{b}")
        rank = self.op("vector", lambda: nc.vector.reduce_sum(
            sb["gat"][0:32, 3:4], sb["gatm"][:, 0:32], axis=AX.X), [rkp],
            tag=f"grank{b}")
        fb_ = self.op("vector", lambda: nc.vector.tensor_single_scalar(
            sb["gat"][0:32, 4:5], sb["gat"][0:32, 3:4], 15.0, ALU.is_le),
            [rank], tag=f"gfb{b}")
        mc = self.op("vector", lambda: nc.vector.tensor_single_scalar(
            sb["gat"][0:32, 5:6], sb["gat"][0:32, 2:3], 0.0, ALU.is_gt),
            [lcol], tag=f"gmc{b}")
        cmm = self.op("tensor", lambda: nc.tensor.matmul(
            ps["sm"][0:1, 228:229], sb["gat"][0:32, 5:6],
            sb["fones"][0:32, 0:1], start=True, stop=True),
            [mc, rmm, self.memsets[1]], tag=f"cmm{b}")
        cev = self.op("vector", lambda: nc.vector.tensor_copy(
            sb["gsc"][0:1, 2:3], ps["sm"][0:1, 228:229]), [cmm] + gw,
            tag=f"cev{b}")
        self.addwar("smbank", cev)
        inz = self.op("vector", lambda: nc.vector.tensor_single_scalar(
            sb["gsc"][0:1, 3:4], sb["gsc"][0:1, 2:3], 0.0, ALU.is_equal),
            [cev], tag=f"ginz{b}")
        smw = self.takewar("smbank")
        ibm = self.op("tensor", lambda: nc.tensor.matmul(
            ps["sm"][0:32, 232:233], sb["onesf"][0:1, 0:32],
            sb["gsc"][0:1, 3:4], start=True, stop=True),
            [inz] + smw, tag=f"gibm{b}")
        ibe = self.op("vector", lambda: nc.vector.tensor_copy(
            sb["gat"][0:32, 6:7], ps["sm"][0:32, 232:233]), [ibm] + gw,
            tag=f"gibe{b}")
        self.addwar("smbank", ibe)
        fbt = self.op("vector", lambda: nc.vector.tensor_mul(
            sb["gat"][0:32, 6:7], sb["gat"][0:32, 6:7], sb["gat"][0:32, 4:5]),
            [ibe, fb_], tag=f"gfbt{b}")
        mfin = self.op("vector", lambda: nc.vector.tensor_add(
            sb["gat"][0:32, 5:6], sb["gat"][0:32, 5:6], sb["gat"][0:32, 6:7]),
            [fbt, mc], tag=f"gmfin{b}")
        gcol = self.op("vector", lambda: nc.vector.tensor_relu(
            sb["gat"][0:32, 7:8], sb["gat"][0:32, 2:3]), [lcol, gt_, eq_],
            tag=f"ggcol{b}")
        z1 = self.op("vector", lambda: nc.vector.tensor_mul(
            sb["gat"][0:32, 7:8], sb["gat"][0:32, 7:8], sb["gat"][0:32, 5:6]),
            [gcol, mfin], tag=f"gz1{b}")
        z2 = self.op("vector", lambda: nc.vector.tensor_scalar(
            sb["gat"][0:32, 6:7], sb["gat"][0:32, 5:6], -1.0, 1.0,
            ALU.mult, ALU.add), [mfin, fbt], tag=f"gz2{b}")
        z3 = self.op("vector", lambda: nc.vector.tensor_scalar_mul(
            sb["gat"][0:32, 6:7], sb["gat"][0:32, 6:7], -1.0e38), [z2],
            tag=f"gz3{b}")
        zz = self.op("vector", lambda: nc.vector.tensor_add(
            sb["gat"][0:32, 7:8], sb["gat"][0:32, 7:8], sb["gat"][0:32, 6:7]),
            [z3, z1], tag=f"gzz{b}")
        zrow = self._col_transpose(f"zT{b}", lambda: sb["gat"][0:32, 7:8],
                                   lambda: sb["grow"][0:1, 0:32],
                                   [zz, self.sim_ops[1]], 240)
        mx = self.op("vector", lambda: nc.vector.reduce_max(
            sb["gsc"][0:1, 5:6], sb["grow"][0:1, 0:32], axis=AX.X,
            negate=True), [zrow], tag=f"gmx{b}")
        ex = self.op("scalar", lambda: nc.scalar.activation(
            sb["grow"][0:1, 0:32], sb["grow"][0:1, 0:32], AF.Exp,
            bias=sb["gsc"][0:1, 5:6], accum_out=sb["gsc"][0:1, 6:7]),
            [mx], tag=f"gex{b}")
        rcp = self.op("vector", lambda: nc.vector.reciprocal(
            sb["gsc"][0:1, 7:8], sb["gsc"][0:1, 6:7]), [ex], tag=f"grcp{b}")
        rt = self.op("vector", lambda: nc.vector.tensor_scalar_mul(
            sb["grow"][0:1, 0:32], sb["grow"][0:1, 0:32],
            sb["gsc"][0:1, 7:8]), [rcp, ex], tag=f"grt{b}")
        rce = self._row_transpose(f"rT{b}", lambda: sb["grow"][0:1, 0:32],
                                  lambda: sb["gat"][0:32, 0:1],
                                  [rt] + gw, 272)
        rmk = self.op("vector", lambda: nc.vector.tensor_mul(
            sb["gat"][0:32, 1:2], sb["gat"][0:32, 0:1], sb["gat"][0:32, 5:6]),
            [rce, mfin] + gw, tag=f"grmk{b}")
        rbM = self.op("vector", lambda b=b: nc.vector.tensor_copy(
            sb["rmb"][0:32, 2 * b:2 * b + 1], sb["gat"][0:32, 1:2]),
            [rmk], tag=f"grb{b}")
        rbP = self.op("vector", lambda b=b: nc.vector.tensor_copy(
            sb["rmb"][0:32, 2 * b + 1:2 * b + 2], sb["gat"][0:32, 0:1]),
            [rce, rmk], tag=f"grb2{b}")
        self.routing_ops = getattr(self, "routing_ops", []) + [rbM, rbP]
        self.setwar("gatew", [rbM, rbP, lrow, zrow, mx, rt, sn, nsq, nrc,
                              mfin, fbt, gcol, rank, inz, cev])

    def _fold(self):
        nc, sb, ps, dr = self.nc, self.sb, self.ps, self.dram
        NCHUNK = 8
        self.setwar("fold0", self.takewar("big0_0") + self.takewar("big0_1") +
                    self.takewar("S0"))
        self.setwar("fold1", self.takewar("b1_3") + self.takewar("b1_4") +
                    self.takewar("b1_5") + self.takewar("S1"))
        stores = []
        for wi, (wnm, bb) in enumerate((("WvT", 0), ("WvT", 1),
                                        ("WoF", 0), ("WoF", 1))):
            rcol = 2 * bb + (0 if wnm == "WvT" else 1)
            rop = self.routing_ops[rcol]
            for ch in range(NCHUNK):
                slot = 0
                wd = self.op("dgp", lambda ch=ch, slot=slot, wnm=wnm:
                             nc.gpsimd.dma_start(
                    out=sb["wfold"][:, slot * 2048:(slot + 1) * 2048],
                    in_=dr[wnm][:, ch * 2048:(ch + 1) * 2048]),
                    self.takewar(f"wfold{slot}"), tag=f"f{wnm}{bb}_{ch}",
                    stream="wf")
                mms = []
                for pp in range(4):
                    bigt = ps["big0"] if pp < 3 else ps["big1"]
                    pcol = (pp % 3) * 512
                    deps = [wd, rop]
                    if pp == 0:
                        deps += self.takewar("fold0")
                    if pp == 3:
                        deps += self.takewar("fold1")
                    o = self.op("tensor", lambda pp=pp, slot=slot, bigt=bigt,
                                pcol=pcol, rcol=rcol: nc.tensor.matmul(
                        bigt[0:1, pcol:pcol + 512],
                        sb["rmb"][0:32, rcol:rcol + 1],
                        sb["wfold"][0:32, slot * 2048 + pp * 512:
                                    slot * 2048 + (pp + 1) * 512],
                        start=True, stop=True), deps,
                        tag=f"fm{wnm}{bb}_{ch}_{pp}")
                    mms.append(o)
                ev1 = self.op("vector", lambda slot=slot:
                              nc.vector.tensor_copy(
                    sb["foldrow"][0:1, 0:1536],
                    ps["big0"][0:1, 0:1536]), mms[0:3] +
                    self.takewar("foldrow0"), tag=f"fe1{wnm}{bb}_{ch}")
                ev2 = self.op("vector", lambda slot=slot:
                              nc.vector.tensor_copy(
                    sb["foldrow"][0:1, 1536:2048],
                    ps["big1"][0:1, 0:512]), [mms[3]],
                    tag=f"fe2{wnm}{bb}_{ch}")
                self.setwar("fold0", [ev1])
                self.setwar("fold1", [ev2])
                sto = self.op("dsync", lambda ch=ch, wi=wi:
                              nc.sync.dma_start(
                    out=dr["foldin"][0:1, wi * 16384 + ch * 2048:
                                     wi * 16384 + (ch + 1) * 2048],
                    in_=sb["foldrow"][0:1, 0:2048]),
                    [ev1, ev2], tag=f"fs{wnm}{bb}_{ch}", stream="fst")
                self.setwar(f"wfold{slot}", mms)
                self.setwar("foldrow0", [sto])
                stores.append(sto)
        self.addwar("big0_0", *self.takewar("fold0"))
        self.addwar("b1_3", *self.takewar("fold1"))
        co = self.op("coll", lambda: nc.gpsimd.collective_compute(
            "AllGather", ALU.bypass,
            replica_groups=[list(range(NCORE))],
            ins=[dr["foldin"][:]], outs=[dr["foldout"][:]]), stores,
            tag="agw")
        fo = dr["foldout"]
        self.wveff_ld = []
        ld_b1 = []
        for r in range(NCORE):
            for cc in range(2):
                c = 2 * r + cc
                for bb, dstn in ((0, "wveff"), (1, "fz")):
                    src_ap = fo[r:r + 1,
                                bb * 16384 + cc * 8192:
                                bb * 16384 + (cc + 1) * 8192].rearrange(
                        "one (p h) -> (one p) h", p=128)
                    o = self.op("dgp", lambda src_ap=src_ap, c=c, dstn=dstn:
                                nc.gpsimd.dma_start(
                        out=sb[dstn][:, c * EH:(c + 1) * EH], in_=src_ap),
                        [co] + (self.takewar("wveff") if dstn == "wveff"
                                else self.takewar("fz0") +
                                self.takewar("fz1")),
                        tag=f"wveffld{bb}_{c}", stream="we")
                    (self.wveff_ld if bb == 0 else ld_b1).append(o)
        bl1 = self.op("vector", lambda: nc.vector.tensor_scalar_mul(
            sb["wveff"][:], sb["wveff"][:], sb["bsel"][:, 0:1]),
            self.wveff_ld + self.cload, tag="wvbl1")
        bl2 = self.op("vector", lambda: nc.vector.scalar_tensor_tensor(
            sb["wveff"][:], sb["fz"][:, 0:2 * TG], sb["bsel"][:, 1:2],
            sb["wveff"][:], ALU.mult, ALU.add),
            [bl1] + ld_b1, tag="wvbl2")
        self.addwar("fz0", bl2)
        self.addwar("fz1", bl2)
        self.wveff_ld = [bl2]
        self.woeff_ld = []
        ldo_b1 = []
        for r in range(NCORE):
            for bb, dstn in ((0, "woeff"), (1, "wob1")):
                src_ap = fo[r:r + 1,
                            32768 + bb * 16384:32768 + bb * 16384 + 16384
                            ][0:1, 0:16384].rearrange(
                    "one (h d) -> (one h) d", h=8)
                o = self.op("dgp", lambda src_ap=src_ap, r=r, dstn=dstn:
                            nc.gpsimd.dma_start(
                    out=sb[dstn][8 * r:8 * r + 8, :], in_=src_ap),
                    [co] + self.takewar("woeff" if dstn == "woeff"
                                        else "wob1"),
                    tag=f"woeffld{bb}_{r}", stream="we")
                (self.woeff_ld if bb == 0 else ldo_b1).append(o)
        bo1 = self.op("vector", lambda: nc.vector.tensor_scalar_mul(
            sb["woeff"][:], sb["woeff"][:], sb["bsel"][0:64, 0:1]),
            self.woeff_ld + self.cload, tag="wobl1")
        bo2 = self.op("vector", lambda: nc.vector.scalar_tensor_tensor(
            sb["woeff"][:], sb["wob1"][:], sb["bsel"][0:64, 1:2],
            sb["woeff"][:], ALU.mult, ALU.add),
            [bo1] + ldo_b1, tag="wobl2")
        self.addwar("wob1_done", bo2)
        self.woeff_ld = [bo2]

    def emit_ffn(self):
        nc, sb, ps, dr = self.nc, self.sb, self.ps, self.dram

        self.xpf = {}
        self._emit_xloads(0)
        self._emit_xloads(1)

        cvm = None
        for c in range(NCH):
            cvm = self.op("tensor", lambda c=c: nc.tensor.matmul(
                ps["sm"][0:1, 448:448 + EH], sb["ones"][:, 0:1],
                sb["wveff"][:, c * EH:(c + 1) * EH],
                start=(c == 0), stop=(c == NCH - 1)),
                self.wveff_ld + (self.takewar("smbank") if c == 0 else []),
                tag=f"cvm{c}")
        cve = self.op("vector", lambda: nc.vector.tensor_copy(
            sb["colv"][:], ps["sm"][0:1, 448:448 + EH]), [cvm], tag="cve")
        self.addwar("smbank", cve)
        cwo = self.op("scalar", lambda: nc.scalar.activation(
            sb["bc"][0:64, 0:TG], sb["woeff"][:, 0:TG], AF.Copy,
            accum_out=sb["colwo"][:]), self.woeff_ld + self.takewar("bc0"),
            tag="cwo1")
        cwos = [cwo]
        for q in range(1, 4):
            o = self.op("scalar", lambda q=q: nc.scalar.activation(
                sb["bc"][0:64, 0:TG], sb["woeff"][:, q * TG:(q + 1) * TG],
                AF.Copy, accum_out=sb["colsc"][0:64, 0:1]),
                [cwos[-1]] + self.takewar("colsc_g"), tag=f"cwo{q}")
            a = self.op("vector", lambda: nc.vector.tensor_add(
                sb["colwo"][:], sb["colwo"][:], sb["colsc"][0:64, 0:1]),
                [o, cwos[-1] if q == 1 else cwos[-1]], tag=f"cwoa{q}")
            self.addwar("colsc_g", a)
            cwos.append(a)
        cwob = self.op("vector", lambda: nc.vector.tensor_copy(
            sb["colwob"][:], sb["colwo"][:]), [cwos[-1]], tag="cwob")
        self.addwar("bc0", cwos[-1])

        self.dbg("wveff", [128, NCH * EH], BF16, lambda: self.sb["wveff"][:],
                 self.wveff_ld)
        self.dbg("woeff", [64, D], BF16, lambda: self.sb["woeff"][:],
                 self.woeff_ld)
        for f in range(4):
            self._ffn_fchunk(f, cve, cwob)
        self.dbg("comb", [64, TO], BF16, lambda: self.sb["comb"][:],
                 [o for o in self.ops if o.tag and o.tag.startswith("cemb")])


    def _xbuf(self, f, c):
        sb = self.sb
        if f % 2 == 0:
            return sb["vring"][:, c * TG:(c + 1) * TG]
        if c < 12:
            return sb["k1"][:, c * TG:(c + 1) * TG]
        return sb["k2"][:, (c - 12) * TG:(c - 11) * TG]

    def _emit_xloads(self, f):
        nc, dr = self.nc, self.dram
        wars = []
        if f == 0:
            for s in range(RING):
                for kvh in range(KVH):
                    wars += self.takewar(f"vw_{s}_{kvh}")
        elif f == 1:
            for s in range(RING):
                for pk in range(4):
                    wars += self.takewar(f"kw_{s}_{pk}")
        else:
            wars = self.takewar(f"xbuf{f % 2}")
        lds = []
        for c in range(NCH):
            ld = self.op("dsync", lambda c=c, f=f: nc.sync.dma_start(
                out=self._xbuf(f, c),
                in_=dr["xmid"][c, :, f * TG:(f + 1) * TG]),
                [self.xmid_store_map[(f, c)]] + (wars if c == 0 else []),
                tag=f"xpf{f}_{c}", stream=f"xp{f % 2}")
            lds.append(ld)
        self.xpf[f] = lds

    def _ffn_fchunk(self, f, cve, cwob):
        nc, sb, ps, dr = self.nc, self.sb, self.ps, self.dram
        g = f
        mm = None
        for c in range(NCH):
            deps = [self.xpf[f][c]] + self.wveff_ld
            if c == 0:
                deps += self.takewar("comb_ps") + self.takewar("smbank")
            mm = self.op("tensor", lambda c=c, f=f: nc.tensor.matmul(
                ps["sm"][0:EH, 0:TG],
                sb["wveff"][:, c * EH:(c + 1) * EH],
                self._xbuf(f, c),
                start=(c == 0), stop=False), deps, tag=f"cm{f}_{c}")
            self.addwar(f"xbuf{f % 2}", mm)
        corr = self.op("tensor", lambda f=f: nc.tensor.matmul(
            ps["sm"][0:EH, 0:TG], sb["colv"][:],
            sb["m3"][:, f * TG:(f + 1) * TG], start=False, stop=True),
            [mm, cve, self.negm3_ops[f]], tag=f"cmc{f}")
        cemb = self.op("vector", lambda f=f: nc.vector.tensor_copy(
            sb["comb"][:, f * TG:(f + 1) * TG], ps["sm"][0:EH, 0:TG]),
            [corr], tag=f"cemb{f}")
        self.setwar("comb_ps", [cemb])
        self.addwar("smbank", cemb)

        m4m = self.op("tensor", lambda f=f: nc.tensor.matmul(
            ps["big1"][0:1, 512:512 + TG], sb["colwob"][:],
            sb["comb"][:, f * TG:(f + 1) * TG], start=True, stop=True),
            [cemb, cwob] + self.takewar("b1_4"), tag=f"m4m{f}")
        m4 = self.op("scalar", lambda: nc.scalar.activation(
            sb["m4"][:], ps["big1"][0:1, 512:512 + TG], AF.Copy,
            scale=1.0 / D), [m4m] + self.takewar("rowsc"), tag=f"m4_{f}")
        self.addwar("b1_4", m4)
        negm4 = self.op("scalar", lambda: nc.scalar.activation(
            sb["negm4b"][:], sb["m4"][:], AF.Copy, scale=-1.0),
            [m4] + self.takewar("rbf1"))

        ssq_last = None
        evs = []
        for c in range(NCH):
            slot = c % 2
            deps = self.woeff_ld + [cemb, negm4]
            if c == 0:
                deps += self.takewar("big0_0") + self.takewar("big0_1")
            mmf = self.op("tensor", lambda c=c, slot=slot, f=f:
                          nc.tensor.matmul(
                ps["big0"][:, slot * TG:(slot + 1) * TG],
                sb["woeff"][:, c * 128:(c + 1) * 128],
                sb["comb"][:, f * TG:(f + 1) * TG],
                start=True, stop=False),
                deps + self.takewar(f"big0_{slot}"), tag=f"fm{f}_{c}")
            mc = self.op("tensor", lambda c=c, slot=slot: nc.tensor.matmul(
                ps["big0"][:, slot * TG:(slot + 1) * TG],
                sb["ones"][0:1, 0:128], sb["negm4b"][:],
                start=False, stop=True), [mmf], tag=f"fc{f}_{c}")
            self.addwar("rbf1", mc)
            ev = self.op("scalar", lambda c=c, slot=slot:
                         nc.scalar.activation(
                sb["x32"][:, c * TG:(c + 1) * TG],
                ps["big0"][:, slot * TG:(slot + 1) * TG], AF.Copy),
                [mc] + self.takewar(f"x32_{c}"), tag=f"fev{f}_{c}")
            sq4 = self.op("scalar", lambda c=c, slot=slot:
                          nc.scalar.activation(
                sb["fz"][:, slot * TG:(slot + 1) * TG],
                ps["big0"][:, slot * TG:(slot + 1) * TG], AF.Square),
                [mc, ev] + self.takewar(f"fz{slot}"), tag=f"fsq{f}_{c}")
            self.setwar(f"big0_{slot}", [ev, sq4])
            ssq_last = self.op("tensor", lambda slot=slot, c=c:
                               nc.tensor.matmul(
                ps["big1"][0:1, 1024:1024 + TG], sb["ones"][:, 0:1],
                sb["fz"][:, slot * TG:(slot + 1) * TG],
                start=(c == 0), stop=(c == NCH - 1)),
                [sq4] + (self.takewar("b1_5") if c == 0 else []),
                tag=f"fsmm{f}_{c}")
            self.setwar(f"fz{slot}", [ssq_last])
            evs.append(ev)

        msq4 = self.op("scalar", lambda: nc.scalar.activation(
            sb["rowb"][:], ps["big1"][0:1, 1024:1024 + TG], AF.Copy,
            scale=1.0 / D), [ssq_last] + self.takewar("rowsc"),
            tag=f"msq4_{f}")
        self.addwar("b1_5", msq4)
        sd4 = self.op("scalar", lambda: nc.scalar.activation(
            sb["rowa"][:], sb["rowb"][:], AF.Sqrt,
            bias=sb["cst"][0:1, 0:1]), [msq4])
        rstd4 = self.op("vector", lambda: nc.vector.reciprocal(
            sb["rstd4"][:], sb["rowa"][:]), [sd4], tag=f"rstd4_{f}")
        rstd4b = self.op("vector", lambda: nc.vector.tensor_copy(
            sb["rstd4b"][:], sb["rstd4"][:]), [rstd4] + self.takewar("rbf0"))
        bcm = self.op("tensor", lambda: nc.tensor.matmul(
            ps["sm"][:, 0:TG], sb["ones"][0:1, :], sb["rstd4b"][:],
            start=True, stop=True),
            [rstd4b] + self.takewar("smbank") + self.takewar("comb_ps"),
            tag=f"bc4_{f}")
        bc4 = self.op("vector", lambda: nc.vector.tensor_copy(
            sb["bc"][:, 0:TG], ps["sm"][:, 0:TG]),
            [bcm] + self.takewar("bc0"), tag=f"bc4e_{f}")
        self.addwar("smbank", bc4)
        self.addwar("rbf0", bcm)
        self.setwar("rowsc", [msq4, sd4, rstd4, rstd4b, bcm, m4, negm4])
        self.setwar("comb_ps", [bc4])

        for c in range(NCH):
            u = self.op("vector", lambda c=c: nc.vector.tensor_mul(
                sb["x32"][:, c * TG:(c + 1) * TG],
                sb["x32"][:, c * TG:(c + 1) * TG], sb["bc"][:, 0:TG]),
                [evs[c], bc4], tag=f"yu{f}_{c}")
            self.addwar("bc0", u)
            oslot = c % 2
            y = self.op("vector", lambda c=c, f=f, oslot=oslot:
                        nc.vector.tensor_add(
                sb["out32"][:, oslot * TG:(oslot + 1) * TG],
                sb["x32"][:, c * TG:(c + 1) * TG],
                self._xbuf(f, c)),
                [u, self.xpf[f][c]] + self.takewar(f"xo32_{oslot}"),
                tag=f"y{f}_{c}")
            self.addwar(f"xbuf{f % 2}", y)
            self.addwar(f"x32_{c}", y)
            sto = self.op("dsync", lambda c=c, f=f, oslot=oslot:
                          nc.sync.dma_start(
                out=dr["yT"][c * 128:(c + 1) * 128, f * TG:(f + 1) * TG],
                in_=sb["out32"][:, oslot * TG:(oslot + 1) * TG]),
                [y], tag=f"yst{f}_{c}", stream=f"y{oslot}")
            self.setwar(f"xo32_{oslot}", [sto])
            self.final_deps.append(sto)
        if f + 2 < 4:
            self._emit_xloads(f + 2)

    def build(self):
        self._alloc()
        self.emit_consts()
        self.xmid_stored = []
        self.xmid_store_map = {}
        self.negm3_ops = [None] * NG
        for gg in range(NGALL):
            self.emit_group_proj(gg)
            if gg >= 2:
                self.emit_group_attn(gg)
                self.emit_group_out(gg)
        sb = self.sb
        self.dbg("q1", [128, 8 * TG], BF16, lambda: sb["q1"][:],
                 sum((self.q_ready.get(j, []) for j in range(8)), []))
        self.dbg("q2", [128, 8 * TG], BF16, lambda: sb["q2"][:],
                 sum((self.q_ready.get(j, []) for j in range(8)), []))
        self.dbg("k1", [128, 4 * RING * 128], BF16, lambda: sb["k1"][:],
                 sum((self.kslot_ready.get((s, p), [])
                      for s in range(RING) for p in range(4)), []))
        self.dbg("k2", [128, 4 * RING * 128], BF16, lambda: sb["k2"][:],
                 sum((self.kslot_ready.get((s, p), [])
                      for s in range(RING) for p in range(4)), []))
        self.dbg("vring", [128, RING * KVH * 128], BF16,
                 lambda: sb["vring"][:],
                 [self.vslot_ready[k] for k in self.vslot_ready])
        self.dbg("enc", [128, QH * TG], BF16, lambda: sb["enc"][:],
                 sum((self.enc_ready.get(n, []) for n in range(QH)), []))
        self.dbg("x32", [128, NCH * TG], F32, lambda: sb["x32"][:],
                 [o for o in self.ops if o.tag and o.tag.startswith("sw2c3_")])
        self.dbg("seqa", [128, 16], F32, lambda: sb["seqa"][:],
                 [self.seqa_last])
        self.dbg("P", [128, 2 * 4608], BF16, lambda: sb["PTr"][:],
                 [o for o in self.ops if o.tag and o.tag.startswith("exp5_")])
        if self.stop != "groups":
            self.emit_gating()
            self.emit_ffn()
        return self._emit()

    def _emit(self):
        nc = self.nc
        if self.bank_edges:
            for ia, ib in self.bank_edges:
                self.ops[ib].deps.append(self.ops[ia])
        if self.serial:
            prev = None
            for o in self.ops:
                if prev is not None:
                    o.deps = list(o.deps) + [prev]
                prev = o
        for o in self.ops:
            if o.is_dma:
                o.need_inc = True
            for d in o.deps:
                d.need_inc = True
        for o in self.final_deps:
            o.need_inc = True

        for o in self.ops:
            o.semkey = (ENG_SEM[o.eng] + "_" + o.stream) if o.is_dma \
                else ENG_SEM[o.eng]
        keys = sorted({o.semkey for o in self.ops if o.need_inc})
        for k in keys:
            if k not in self.sems:
                self.sems[k] = self.es.enter_context(nc.semaphore(k))
        counts = {}
        for o in self.ops:
            if o.need_inc:
                counts[o.semkey] = counts.get(o.semkey, 0) + \
                    (1 if o.eng == "coll" else (16 if o.is_dma else 1))
                o.val = counts[o.semkey]
        self.totals = dict(counts)
        know = {}
        per_key = {}
        for o in self.ops:
            k = {}
            for d in o.deps:
                kd = know.get(id(d))
                if kd:
                    for s, v in kd.items():
                        if k.get(s, 0) < v:
                            k[s] = v
                if d.need_inc:
                    if k.get(d.semkey, 0) < d.val:
                        k[d.semkey] = d.val
            know[id(o)] = k
            if o.is_dma and o.need_inc:
                per_key.setdefault(o.semkey, []).append(o)
        for kk, lst in per_key.items():
            i = 0
            while i < len(lst):
                j = i + 1
                while j < len(lst):
                    kj = know.get(id(lst[j]), {})
                    if kj.get(kk, 0) >= lst[j - 1].val:
                        break
                    j += 1
                end_val = lst[j - 1].val
                for t in range(i, j):
                    lst[t].val_wait = end_val
                lst[i].batch_start = True
                i = j
        for o in self.ops:
            if o.need_inc and o.val_wait is None:
                o.val_wait = o.val

        by_exec = {e: [] for e in
                   ("tensor", "vector", "scalar", "gpsimd", "sync")}
        for o in self.ops:
            by_exec[ENG_EXEC[o.eng]].append(o)
        watermark = {e: {} for e in by_exec}
        sems = self.sems

        def run_stream(eng_obj, ops, ename):
            wm = watermark[ename]
            for o in ops:
                for d in o.deps:
                    sk = d.semkey
                    v = d.val_wait
                    assert v is not None, (d.tag, o.tag)
                    if wm.get(sk, 0) < v:
                        eng_obj.wait_ge(sems[sk], v)
                        wm[sk] = v
                inst = o.fn()
                o.iname = inst.ins.name
                if self.capture:
                    o.psum = _extract_psum(inst)
                if o.need_inc:
                    inst.then_inc(sems[o.semkey],
                                  1 if o.eng == "coll" else
                                  (16 if o.is_dma else 1))
            if ename == "sync":
                for o in self.final_deps:
                    sk, v = o.semkey, o.val_wait
                    if wm.get(sk, 0) < v:
                        eng_obj.wait_ge(sems[sk], v)
                        wm[sk] = v
                for sk, tot in sorted(self.totals.items()):
                    if "_" in sk and wm.get(sk, 0) < tot:
                        eng_obj.wait_ge(sems[sk], tot)
                        wm[sk] = tot

        with self.es:
            block = self.es.enter_context(nc.Block())
            block.tensor(lambda e: run_stream(e, by_exec["tensor"], "tensor"))
            block.vector(lambda e: run_stream(e, by_exec["vector"], "vector"))
            block.scalar(lambda e: run_stream(e, by_exec["scalar"], "scalar"))
            block.gpsimd(lambda e: run_stream(e, by_exec["gpsimd"], "gpsimd"))
            block.sync(lambda e: run_stream(e, by_exec["sync"], "sync"))
        return nc



_PSUM_INFO = {"big0": (0, 1536, 4), "big1": (3, 1536, 4),
              "pt": (6, 512, 4), "sm": (7, 512, 4)}


def _extract_psum(bi):
    mi = bi.ins
    acc = []

    def scan(lst, kind):
        for pap in lst:
            memref = getattr(pap, "memref", None)
            if memref not in _PSUM_INFO:
                continue
            b0, fsz, esz = _PSUM_INFO[memref]
            col0 = pap.offset % fsz
            ext = 0
            for pr in pap.ap:
                st, num = int(pr[0]), int(pr[1])
                if st % fsz == 0:
                    continue
                ext += abs(st) * (num - 1)
            lo = b0 + (col0 * esz) // 2048
            hi = b0 + ((col0 + ext) * esz) // 2048
            for b in range(lo, hi + 1):
                acc.append((b, kind))

    scan(mi.ins, "R")
    scan(mi.outs, "W")
    return acc or None


def compute_bank_edges(mk):
    from bisect import bisect_right
    ops = mk.ops
    CE = ("tensor", "vector", "scalar")
    streams = {}
    pos = {}
    for o in ops:
        e = ENG_EXEC[o.eng]
        lst = streams.setdefault(e, [])
        pos[id(o)] = len(lst)
        lst.append(o)
    INF = float("inf")
    icov = {}
    for e in CE:
        lst = streams.get(e, [])
        arr = [INF] * len(lst)
        nxt = INF
        for i in range(len(lst) - 1, -1, -1):
            o = lst[i]
            if (not o.is_dma) and o.need_inc:
                nxt = o.val
            arr[i] = nxt
        icov[e] = arr
    sem_of = {e: ENG_SEM[e] for e in CE}
    wmv = {e: {} for e in streams}
    knowpos = {e: {ae: -1 for ae in CE} for e in streams}
    hist = [{ae: [] for ae in CE} for _ in range(8)]
    edges = []
    for o in ops:
        e = ENG_EXEC[o.eng]
        wv, kp = wmv[e], knowpos[e]
        upd = False
        for d in o.deps:
            v = d.val_wait
            if v is None:
                continue
            sk = d.semkey
            if wv.get(sk, 0) < v:
                wv[sk] = v
                upd = True
        if upd:
            for ae in CE:
                v = wv.get(sem_of[ae], 0)
                if v:
                    p = bisect_right(icov[ae], v) - 1
                    if p > kp[ae]:
                        kp[ae] = p
        if o.psum and e in CE:
            mypos = pos[id(o)]
            for b in sorted({bb for bb, _k in o.psum}):
                hb = hist[b]
                for ae in CE:
                    if ae == e:
                        continue
                    lst = hb[ae]
                    if lst and lst[-1][0] > kp[ae]:
                        ap_, ag = lst[-1]
                        edges.append((ag, o.gidx))
                        kp[ae] = ap_
                hb[e].append((mypos, o.gidx))
    seen = set()
    out = []
    for ed in edges:
        if ed not in seen:
            seen.add(ed)
            out.append(ed)
    return out


def build_banked(**kw):
    m1 = MK(capture=True, **kw)
    m1.build()
    edges = compute_bank_edges(m1)
    m2 = MK(bank_edges=edges, **kw)
    nc = m2.build()
    return m2, nc, edges



_BUILD_CACHE = {}


def kernel(L0_attn_x, L0_attn_q_kernel, L0_attn_kv_kernel, L0_attn_out_kernel,
           L0_attn_q_norm_scale, L0_attn_k_norm_scale, L0_ffn_sim_matrix,
           L0_ffn_gates, L0_ffn_w_q, L0_ffn_w_k, L0_ffn_w_v, L0_ffn_o_weights):
    _ = (L0_ffn_w_q, L0_ffn_w_k)
    x = np.asarray(L0_attn_x, dtype=np.float32)
    shared = host_weights(np.asarray(L0_attn_q_kernel, np.float32),
                          np.asarray(L0_attn_kv_kernel, np.float32),
                          np.asarray(L0_attn_out_kernel, np.float32),
                          np.asarray(L0_ffn_w_v, np.float32),
                          np.asarray(L0_ffn_o_weights, np.float32),
                          np.asarray(L0_ffn_sim_matrix, np.float32))
    shared.update(host_tables())
    in_maps = []
    for core in range(NCORE):
        d = host_core_inputs(core, x,
                             np.asarray(L0_attn_q_norm_scale, np.float32),
                             np.asarray(L0_attn_k_norm_scale, np.float32),
                             np.asarray(L0_ffn_gates, np.float32), shared)
        d["WvT"] = np.ascontiguousarray(
            shared["WvT"][:, core * 16384:(core + 1) * 16384])
        d["WoF"] = np.ascontiguousarray(
            shared["WoF"][:, core * 16384:(core + 1) * 16384])
        in_maps.append(d)
    _mk, nc, _edges = build_banked()
    res = run_bass_kernel_spmd(nc, in_maps, core_ids=list(range(NCORE)))
    out = np.empty((B, T, D), dtype=np.float32)
    for core in range(NCORE):
        b, cs = core // 4, core % 4
        out[b, cs * TO:(cs + 1) * TO, :] = res.results[core]["yT"].T
    return out
